# version 01
# baseline (speedup 1.0000x reference)
import numpy as np
import ml_dtypes
import concourse.bass as bass
import concourse.mybir as mybir
from concourse.bass_utils import run_bass_kernel_spmd

F32 = mybir.dt.float32
BF16 = mybir.dt.bfloat16
U8 = mybir.dt.uint8
I32 = mybir.dt.int32
AF = mybir.ActivationFunctionType
ALU = mybir.AluOpType

ENGS = ("tensor", "scalar", "vector", "sync", "gpsimd")


def make_cfg(small=False):
    if small:
        c = dict(B=2, S=1024, D=1024, HKV=8, G=1, HD=128, F=1536)
    else:
        c = dict(B=2, S=4096, D=4096, HKV=8, G=4, HD=128, F=11008)
    c["CH"] = c["D"] // 128
    c["QB"] = c["S"] // 8
    c["TCH"] = c["QB"]
    c["NCHUNK"] = c["S"] // c["TCH"]
    c["DIAG"] = c["QB"] // 128
    c["UNITS"] = c["B"] * c["HKV"] // 8
    c["DQL"] = c["UNITS"] * c["G"] * 128
    c["QNC"] = c["DQL"] // 128
    c["WC"] = c["UNITS"] * 128 * 2
    c["TC"] = c["B"] * c["S"] // 8
    c["TH"] = max(1, c["TC"] // 512)
    c["TN"] = c["TC"] // c["TH"]
    c["FC"] = c["F"] // 128
    c["SW"] = 3
    base = c["FC"] // c["SW"]
    rem = c["FC"] - base * c["SW"]
    c["FCL"] = [base + (1 if i < rem else 0) for i in range(c["SW"])]
    c["FCMAX"] = max(c["FCL"])
    c["SCALE"] = float(c["HD"]) ** -0.5
    return c


class Walk:

    def __init__(self):
        self.ops = {e: [] for e in ENGS}
        self.cnt = {}

    def at(self, sem):
        return self.cnt.get(sem, 0)

    def op(self, eng, emit, waits=(), inc=None, inck=1):
        ws = tuple((s, v) for (s, v) in waits if v > 0)
        self.ops[eng].append((emit, ws, inc, inck))
        if inc is not None:
            self.cnt[inc] = self.cnt.get(inc, 0) + inck
        return self.cnt.get(inc, 0) if inc else None

    def emit(self, eng_name, eng, sems):
        last = {}
        for (emit, waits, inc, inck) in self.ops[eng_name]:
            for (s, v) in waits:
                if last.get(s, -1) >= v:
                    continue
                eng.wait_ge(sems[s], v)
                last[s] = v
            ins = emit(eng)
            if inc is not None:
                ins.then_inc(sems[inc], inck)


def build_program(cfg):
    B, S, D, HKV, G, HD, F = (cfg[k] for k in ("B", "S", "D", "HKV", "G", "HD", "F"))
    CH, QB, TCH, NCHUNK, DIAG = (cfg[k] for k in ("CH", "QB", "TCH", "NCHUNK", "DIAG"))
    UNITS, DQL, QNC, WC = (cfg[k] for k in ("UNITS", "DQL", "QNC", "WC"))
    TC, TH, TN, FC = (cfg[k] for k in ("TC", "TH", "TN", "FC"))
    SW, FCL, FCMAX = cfg["SW"], cfg["FCL"], cfg["FCMAX"]
    SCALE = cfg["SCALE"]
    VSUB = TCH // 128

    from contextlib import ExitStack
    stack = ExitStack()
    nc = bass.Bass("TRN2", target_bir_lowering=False)

    xseq = nc.dram_tensor("xseq", [NCHUNK, 128, CH * TCH], F32, kind="ExternalInput").ap()
    xown = nc.dram_tensor("xown", [D, TC], F32, kind="ExternalInput").ap()
    wq = nc.dram_tensor("wq", [128, CH * DQL], F32, kind="ExternalInput").ap()
    wkv = nc.dram_tensor("wkv", [128, CH * WC], F32, kind="ExternalInput").ap()
    wo = nc.dram_tensor("wo", [CH, 128, CH * 128], F32, kind="ExternalInput").ap()
    wg = nc.dram_tensor("wg", [FC, 128, CH * 128], F32, kind="ExternalInput").ap()
    wu = nc.dram_tensor("wu", [FC, 128, CH * 128], F32, kind="ExternalInput").ap()
    wd = nc.dram_tensor("wd", [CH, 128, FC * 128], F32, kind="ExternalInput").ap()
    masks_in = nc.dram_tensor("masks", [128, DIAG * QB], BF16, kind="ExternalInput").ap()
    sel = nc.dram_tensor("sel", [1, 1], I32, kind="ExternalInput").ap()
    out_fm = nc.dram_tensor("out_fm", [D, TC], F32, kind="ExternalOutput").ap()

    q_scr = nc.dram_tensor("q_scr", [DQL, S], BF16).ap()
    k_scr = nc.dram_tensor("k_scr", [UNITS, 128, S], BF16).ap()
    v_scr = nc.dram_tensor("v_scr", [UNITS, 128, S], BF16).ap()
    attn_scr_h = nc.dram_tensor("attn_scr", [DQL, S], BF16)
    gath_scr_h = nc.dram_tensor("gath_scr", [8 * DQL, S], BF16)
    attn_scr = attn_scr_h.ap()
    out1_scr = nc.dram_tensor("out1_scr", [D, TC], F32).ap()
    da_scr = nc.dram_tensor("da_scr", [D, TC], F32).ap()
    db_scr = nc.dram_tensor("db_scr", [D, TC], F32).ap()

    ones_bf = stack.enter_context(nc.sbuf_tensor("ones_bf", [128, 1], BF16))
    onesp_f = stack.enter_context(nc.sbuf_tensor("onesp_f", [128, 1], F32))
    ones_f = stack.enter_context(nc.sbuf_tensor("ones_f", [1, 128], F32))
    masks_sb = stack.enter_context(nc.sbuf_tensor("masks_sb", [128, DIAG * QB], BF16))
    sel_sb = stack.enter_context(nc.sbuf_tensor("sel_sb", [1, 1], I32))

    def roundup(x, a=32):
        return (x + a - 1) // a * a

    class Arena:
        def __init__(self):
            cap = nc.sbuf_bytes_remaining - 2048
            self.big = stack.enter_context(
                nc.sbuf_tensor("arena", [nc.NUM_PARTITIONS, cap], U8))
            self.base = nc.lookup_mloc(self.big).addr
            self.cap = cap
            self.off = 0
            self.maxoff = 0

        def take(self, name, shape, dtype):
            per = int(np.prod(shape[1:])) * mybir.dt.size(dtype)
            o = self.off
            self.off = roundup(self.off + per)
            self.maxoff = max(self.maxoff, self.off)
            assert self.off <= self.cap, (name, self.off, self.cap)
            return nc.alloc_sbuf_tensor_at(name, list(shape), dtype,
                                           offset=self.base + o)

        def reset(self):
            self.off = 0

    ar = Arena()
    wq_sb = ar.take("wq_sb", [128, CH * DQL], BF16)
    wkv_sb = ar.take("wkv_sb", [128, CH * WC], BF16)
    xc_sb = ar.take("xc_sb", [128, CH * TCH], BF16)
    xs_sb = ar.take("xs_sb", [128, CH * TCH], BF16)
    sq_sb = [ar.take(f"sq{i}", [128, TCH], BF16) for i in range(2)]
    qev_sb = [ar.take(f"qev{i}", [128, TCH], BF16) for i in range(2)]
    kev_sb = [ar.take(f"kev{i}", [128, TCH], BF16) for i in range(2)]
    vev_sb = [ar.take(f"vev{i}", [128, UNITS * 128], BF16) for i in range(2)]
    t1_sb = [ar.take(f"t1{i}", [128, TCH], F32) for i in range(2)]
    mb_sb = ar.take("mb_sb", [128, TCH], F32)
    rb_sb = ar.take("rb_sb", [128, TCH], F32)
    stat1 = ar.take("stat1", [1, 4 * TCH], F32)
    ar.reset()
    qt_sb = [ar.take(f"qt{u}", [128, G * S], BF16) for u in range(UNITS)]
    kt_sb = [ar.take(f"kt{u}", [128, S], BF16) for u in range(UNITS)]
    vt_sb = [ar.take(f"vt{u}", [128, S], BF16) for u in range(UNITS)]
    pr_sb = [ar.take(f"pr{i}", [128, QB], BF16) for i in range(4)]
    pm_sb = [ar.take(f"pm{i}", [128, QB], BF16) for i in range(4)]
    ib_sb = [ar.take(f"ib{i}", [128, QB], F32) for i in range(2)]
    aev_sb = [ar.take(f"aev{i}", [128, QB], BF16) for i in range(2)]
    inv_sb = [ar.take(f"inv{i}", [1, QB], F32) for i in range(2)]
    ar.reset()
    xs2_sb = ar.take("xs2_sb", [128, CH * TC], BF16)
    xs2_end = ar.off
    ao_sb = ar.take("ao_sb", [128, CH * TC], BF16)
    wo_sb = [ar.take(f"wo{i}", [128, CH * 128], BF16) for i in range(2)]
    xo_sb = [ar.take(f"xo{i}", [128, TC], F32) for i in range(2)]
    o1_sb = [ar.take(f"o1{i}", [128, TC], F32) for i in range(2)]
    sq2_sb = [ar.take(f"sq2{i}", [128, TC], BF16) for i in range(2)]
    o1r_sb = [ar.take(f"o1r{i}", [128, TC], F32) for i in range(2)]
    t2_sb = ar.take("t2_sb", [128, TC], F32)
    mb2_sb = ar.take("mb2_sb", [128, TC], F32)
    rb2_sb = ar.take("rb2_sb", [128, TC], F32)
    stat2 = ar.take("stat2", [1, 4 * TC], F32)
    ar.reset()
    ar.off = xs2_end
    hA_sb = ar.take("hA_sb", [128, FCMAX * TC], BF16)
    wg_sb = [ar.take(f"wgs{i}", [128, CH * 128], BF16) for i in range(2)]
    wu_sb = [ar.take(f"wus{i}", [128, CH * 128], BF16) for i in range(2)]
    wd_sb = [ar.take(f"wds{i}", [128, FCMAX * 128], BF16) for i in range(2)]
    sg_sb = [ar.take(f"sg{i}", [128, TC], BF16) for i in range(2)]
    o2r_sb = [ar.take(f"o2r{i}", [128, TC], F32) for i in range(2)]
    dat_sb = [ar.take(f"dat{i}", [128, TC], F32) for i in range(2)]

    pb = [stack.enter_context(nc.psum_tensor(f"pb{i}", [128, 512], F32))
          for i in range(8)]

    semnames = ["g_w", "g_x", "g_ao", "g_wo0", "g_wo1", "g_fw0", "g_fw1",
                "g_wd0", "g_wd1",
                "s_ms", "s_kv", "s_xo0", "s_xo1", "s_or0", "s_or1",
                "s_r20", "s_r21", "s_at0", "s_at1", "s_o1a", "s_o1b",
                "s_da0", "s_da1", "s_db0", "s_db1", "s_ou0", "s_ou1",
                "a_q0", "a_q1", "a_k0", "a_k1", "a_v0", "a_v1",
                "cc", "g_ms",
                "act_sq", "pe_st", "act_mv", "dve_mv", "pe_bc", "act_bc",
                "dve_x", "dve_xs", "pe_mm", "act_ev",
                "pe_s", "act_pr", "dve_pm", "pe_a", "dve_iv", "pe_b",
                "act_ib", "dve_no",
                "pe_o", "dve_o1", "act_sq2", "pe_o2", "dve_x2",
                "pe_f", "act_sl", "dve_h", "pe_d", "dve_d"]
    sems = {n: stack.enter_context(nc.semaphore(n)) for n in semnames}

    w = Walk()

    w.op("gpsimd", lambda E: E.memset(ones_bf[:, :], 1.0))
    w.op("gpsimd", lambda E: E.memset(onesp_f[:, :], 1.0))
    w.op("gpsimd", lambda E: E.memset(ones_f[:, :], 1.0), inc="g_ms")
    w.op("sync", lambda E: E.dma_start(out=masks_sb[:, :], in_=masks_in[:, :]),
         inc="s_ms", inck=16)
    w.op("sync", lambda E: E.dma_start(out=sel_sb[:, :], in_=sel[:, :]),
         inc="s_ms", inck=16)
    S_MS = w.at("s_ms")
    nsp = 4
    for i in range(nsp):
        lo, hi = i * (CH // nsp), (i + 1) * (CH // nsp)
        w.op("gpsimd", lambda E, lo=lo, hi=hi: E.dma_start(
            out=wq_sb[:, lo * DQL:hi * DQL], in_=wq[:, lo * DQL:hi * DQL]),
            inc="g_w", inck=16)
    w.op("gpsimd", lambda E: E.dma_start(out=wkv_sb[:, :], in_=wkv[:, :]),
         inc="g_w", inck=16)
    G_W = w.at("g_w")

    rec_xsfree = 0
    rec_qg, rec_kg, rec_vg = [], [], []
    n_groups = QNC + UNITS + VSUB

    for t in range(NCHUNK):
        w.op("gpsimd", lambda E, t=t: E.dma_start(out=xc_sb[:, :], in_=xseq[t, :, :]),
             waits=[("act_sq", t * CH), ("pe_st", t * CH), ("dve_x", t * CH)],
             inc="g_x", inck=16)
        ld_n = w.at("g_x")

        for ci in range(CH):
            w.op("scalar", lambda E, ci=ci: E.activation(
                sq_sb[ci % 2][:, :], xc_sb[:, ci * TCH:(ci + 1) * TCH], AF.Square),
                waits=[("g_x", ld_n), ("pe_st", t * CH + ci - 1)],
                inc="act_sq")
            w.op("tensor", lambda E, ci=ci: E.matmul(
                pb[6][0:1, 0:TCH], ones_bf[:, :], xc_sb[:, ci * TCH:(ci + 1) * TCH],
                start=(ci == 0), stop=(ci == CH - 1)),
                waits=[("g_x", ld_n), ("act_mv", 3 * t), ("g_ms", 1)])
            w.op("tensor", lambda E, ci=ci: E.matmul(
                pb[7][0:1, 0:TCH], ones_bf[:, :], sq_sb[ci % 2][:, :],
                start=(ci == 0), stop=(ci == CH - 1)),
                waits=[("act_sq", t * CH + ci + 1)],
                inc="pe_st")
        pe_st_n = w.at("pe_st")

        m_sl = stat1[0:1, 0:TCH]
        q_sl = stat1[0:1, TCH:2 * TCH]
        r_sl = stat1[0:1, 2 * TCH:3 * TCH]
        tp_sl = stat1[0:1, 3 * TCH:4 * TCH]
        w.op("scalar", lambda E: E.mul(m_sl, pb[6][0:1, 0:TCH], 1.0 / D),
             waits=[("pe_st", pe_st_n)])
        w.op("scalar", lambda E: E.mul(q_sl, pb[7][0:1, 0:TCH], 1.0 / D),
             inc="act_mv")
        a_mv = w.at("act_mv")
        w.op("vector", lambda E: E.tensor_mul(tp_sl, m_sl, m_sl),
             waits=[("act_mv", a_mv)], inc="dve_mv")
        w.op("vector", lambda E: E.tensor_sub(r_sl, q_sl, tp_sl),
             waits=[("dve_mv", w.at("dve_mv"))], inc="dve_mv")
        w.op("vector", lambda E: E.tensor_scalar_add(tp_sl, r_sl, 1e-5),
             waits=[("dve_mv", w.at("dve_mv"))], inc="dve_mv")
        d_mv = w.at("dve_mv")
        w.op("scalar", lambda E: E.activation(q_sl, tp_sl, AF.Ln),
             waits=[("dve_mv", d_mv)], inc="act_mv")
        w.op("scalar", lambda E: E.activation(r_sl, q_sl, AF.Exp, scale=-0.5),
             waits=[("act_mv", w.at("act_mv"))], inc="act_mv")
        a_mv = w.at("act_mv")
        w.op("tensor", lambda E: E.matmul(pb[4][:, 0:TCH], ones_f[:, :], m_sl,
                                          start=True, stop=True),
             waits=[("act_mv", a_mv), ("act_bc", 2 * t)])
        w.op("tensor", lambda E: E.matmul(pb[5][:, 0:TCH], ones_f[:, :], r_sl,
                                          start=True, stop=True), inc="pe_bc")
        p_bc = w.at("pe_bc")
        w.op("scalar", lambda E: E.copy(mb_sb[:, :], pb[4][:, 0:TCH]),
             waits=[("pe_bc", p_bc), ("dve_xs", t * CH)])
        w.op("scalar", lambda E: E.copy(rb_sb[:, :], pb[5][:, 0:TCH]),
             inc="act_bc", inck=2)
        a_bc = w.at("act_bc")

        xs = xs_sb
        for ci in range(CH):
            w.op("vector", lambda E, ci=ci: E.tensor_sub(
                t1_sb[ci % 2][:, :], xc_sb[:, ci * TCH:(ci + 1) * TCH], mb_sb[:, :]),
                waits=[("act_bc", a_bc), ("pe_mm", rec_xsfree),
                       ("dve_xs", t * CH + ci - 1)],
                inc="dve_x")
            w.op("vector", lambda E, ci=ci, xs=xs: E.tensor_mul(
                xs[:, ci * TCH:(ci + 1) * TCH], t1_sb[ci % 2][:, :], rb_sb[:, :]),
                waits=[("dve_x", t * CH + ci + 1)],
                inc="dve_xs")

        gbase = t * n_groups
        gi = 0
        for nq in range(QNC):
            bank = gi % 2
            qg = t * QNC + nq
            for ci in range(CH):
                w.op("tensor", lambda E, ci=ci, nq=nq, bank=bank, xs=xs: E.matmul(
                    pb[bank][:, 0:TCH],
                    wq_sb[:, ci * DQL + nq * 128: ci * DQL + (nq + 1) * 128],
                    xs[:, ci * TCH:(ci + 1) * TCH],
                    start=(ci == 0), stop=(ci == CH - 1)),
                    waits=[("g_w", G_W), ("dve_xs", t * CH + ci + 1),
                           ("act_ev", gbase + gi - 1)],
                    inc=("pe_mm" if ci == CH - 1 else None))
            mm_n = w.at("pe_mm")
            w.op("scalar", lambda E, bank=bank, qg=qg: E.copy(
                qev_sb[qg % 2][:, :], pb[bank][:, 0:TCH]),
                waits=[("pe_mm", mm_n),
                       ("a_q%d" % (qg % 2), rec_qg[qg - 2] if qg >= 2 else 0)],
                inc="act_ev")
            a_ev = w.at("act_ev")
            stn = w.op("scalar", lambda E, nq=nq, t=t, qg=qg: E.dma_start(
                out=q_scr[nq * 128:(nq + 1) * 128, t * TCH:(t + 1) * TCH],
                in_=qev_sb[qg % 2][:, :]),
                waits=[("act_ev", a_ev)], inc="a_q%d" % (qg % 2), inck=16)
            rec_qg.append(stn)
            gi += 1
        for u in range(UNITS):
            bank = gi % 2
            kg = t * UNITS + u
            for ci in range(CH):
                w.op("tensor", lambda E, ci=ci, u=u, bank=bank, xs=xs: E.matmul(
                    pb[bank][:, 0:TCH],
                    wkv_sb[:, ci * WC + u * 128: ci * WC + (u + 1) * 128],
                    xs[:, ci * TCH:(ci + 1) * TCH],
                    start=(ci == 0), stop=(ci == CH - 1)),
                    waits=[("g_w", G_W), ("dve_xs", t * CH + ci + 1),
                           ("act_ev", gbase + gi - 1)],
                    inc=("pe_mm" if ci == CH - 1 else None))
            mm_n = w.at("pe_mm")
            w.op("scalar", lambda E, bank=bank, kg=kg: E.copy(
                kev_sb[kg % 2][:, :], pb[bank][:, 0:TCH]),
                waits=[("pe_mm", mm_n),
                       ("a_k%d" % (kg % 2), rec_kg[kg - 2] if kg >= 2 else 0)],
                inc="act_ev")
            a_ev = w.at("act_ev")
            stn = w.op("scalar", lambda E, u=u, t=t, kg=kg: E.dma_start(
                out=k_scr[u, :, t * TCH:(t + 1) * TCH], in_=kev_sb[kg % 2][:, :]),
                waits=[("act_ev", a_ev)], inc="a_k%d" % (kg % 2), inck=16)
            rec_kg.append(stn)
            gi += 1
        for sub in range(VSUB):
            bank = 2 + sub % 2
            vg = t * VSUB + sub
            for ci in range(CH):
                w.op("tensor", lambda E, ci=ci, sub=sub, bank=bank, xs=xs: E.matmul(
                    pb[bank][:, 0:UNITS * 128],
                    xs[:, ci * TCH + sub * 128: ci * TCH + (sub + 1) * 128],
                    wkv_sb[:, ci * WC + UNITS * 128:(ci + 1) * WC],
                    start=(ci == 0), stop=(ci == CH - 1)),
                    waits=[("g_w", G_W), ("dve_xs", t * CH + ci + 1),
                           ("act_ev", gbase + gi - 1)],
                    inc=("pe_mm" if ci == CH - 1 else None))
            mm_n = w.at("pe_mm")
            w.op("scalar", lambda E, bank=bank, vg=vg: E.copy(
                vev_sb[vg % 2][:, :], pb[bank][:, 0:UNITS * 128]),
                waits=[("pe_mm", mm_n),
                       ("a_v%d" % (vg % 2), rec_vg[vg - 2] if vg >= 2 else 0)],
                inc="act_ev")
            a_ev = w.at("act_ev")
            for u in range(UNITS):
                stn = w.op("scalar", lambda E, u=u, vg=vg: E.dma_start(
                    out=v_scr[u, :, vg * 128:(vg + 1) * 128],
                    in_=vev_sb[vg % 2][:, u * 128:(u + 1) * 128]),
                    waits=[("act_ev", a_ev)], inc="a_v%d" % (vg % 2), inck=16)
            rec_vg.append(stn)
            gi += 1
        rec_xsfree = w.at("pe_mm")

    P1T = {n: w.at(n) for n in ("a_q0", "a_q1", "a_k0", "a_k1", "a_v0", "a_v1")}

    p1waits = [(n, v) for n, v in P1T.items()]
    for u in range(UNITS):
        w.op("sync", lambda E, u=u: E.dma_start(
            out=qt_sb[u][:, :].rearrange("p (g t) -> p g t", g=G),
            in_=q_scr[u * G * 128:(u + 1) * G * 128, :].rearrange(
                "(g p) t -> p g t", p=128)),
            waits=p1waits, inc="s_kv", inck=16)
        w.op("sync", lambda E, u=u: E.dma_start(
            out=kt_sb[u][:, :], in_=k_scr[u, :, :]),
            waits=p1waits, inc="s_kv", inck=16)
        w.op("sync", lambda E, u=u: E.dma_start(
            out=vt_sb[u][:, :], in_=v_scr[u, :, :]),
            waits=p1waits, inc="s_kv", inck=16)
    P2_LD = w.at("s_kv")

    it = 0
    qbg = 0
    rec_av = []
    rec_no = [0, 0]
    rec_pb2 = [0, 0]
    rec_aib = [0, 0]
    rec_st2 = [0, 0]
    for u in range(UNITS):
        for qb in range(8):
            nkc = (qb + 1) * DIAG
            kc0d = qb * DIAG
            for g in range(G):
                par = qbg % 2
                for kc in range(nkc):
                    rot2, rot4 = it % 2, it % 4
                    w.op("tensor", lambda E, u=u, kc=kc, g=g, qb=qb, rot2=rot2: E.matmul(
                        pb[rot2][:, 0:QB], kt_sb[u][:, kc * 128:(kc + 1) * 128],
                        qt_sb[u][:, g * S + qb * QB: g * S + (qb + 1) * QB],
                        start=True, stop=True),
                        waits=[("s_kv", P2_LD), ("act_pr", it - 1)],
                        inc="pe_s")
                    w.op("scalar", lambda E, rot2=rot2, rot4=rot4: E.activation(
                        pr_sb[rot4][:, :], pb[rot2][:, 0:QB], AF.Exp, scale=SCALE),
                        waits=[("pe_s", it + 1),
                               ("pe_a", rec_av[it - 4] if it >= 4 else 0)],
                        inc="act_pr")
                    diag = kc >= kc0d
                    if diag:
                        r = kc - kc0d
                        w.op("vector", lambda E, rot4=rot4, r=r: E.tensor_mul(
                            pm_sb[rot4][:, :], pr_sb[rot4][:, :],
                            masks_sb[:, r * QB:(r + 1) * QB]),
                            waits=[("act_pr", it + 1), ("s_ms", S_MS),
                                   ("pe_a", rec_av[it - 4] if it >= 4 else 0)],
                            inc="dve_pm")
                        rhs = pm_sb[rot4]
                        av_w = [("dve_pm", w.at("dve_pm"))]
                    else:
                        rhs = pr_sb[rot4]
                        av_w = [("act_pr", it + 1)]
                    if kc == 0:
                        av_w.append(("dve_no", rec_no[par]))
                    w.op("tensor", lambda E, u=u, kc=kc, par=par, rhs=rhs,
                         nkc=nkc: E.matmul(
                        pb[2 + par][:, 0:QB], vt_sb[u][:, kc * 128:(kc + 1) * 128],
                        rhs[:, :], start=(kc == 0), stop=(kc == nkc - 1)),
                        waits=av_w)
                    w.op("tensor", lambda E, par=par, rhs=rhs, kc=kc, nkc=nkc: E.matmul(
                        pb[4 + par][0:1, 0:QB], ones_bf[:, :], rhs[:, :],
                        start=(kc == 0), stop=(kc == nkc - 1)),
                        inc="pe_a")
                    rec_av.append(w.at("pe_a"))
                    it += 1
                pe_a_n = w.at("pe_a")
                w.op("vector", lambda E, par=par: E.reciprocal(
                    inv_sb[par][:, :], pb[4 + par][0:1, 0:QB]),
                    waits=[("pe_a", pe_a_n), ("pe_b", rec_pb2[par])],
                    inc="dve_iv")
                d_iv = w.at("dve_iv")
                w.op("tensor", lambda E, par=par: E.matmul(
                    pb[6 + par][:, 0:QB], ones_f[:, :], inv_sb[par][:, :],
                    start=True, stop=True),
                    waits=[("dve_iv", d_iv), ("act_ib", rec_aib[par]),
                           ("act_bc", 2 * NCHUNK), ("act_mv", 3 * NCHUNK)],
                    inc="pe_b")
                rec_pb2[par] = w.at("pe_b")
                w.op("scalar", lambda E, par=par: E.copy(
                    ib_sb[par][:, :], pb[6 + par][:, 0:QB]),
                    waits=[("pe_b", rec_pb2[par])], inc="act_ib")
                rec_aib[par] = w.at("act_ib")
                w.op("vector", lambda E, par=par: E.tensor_mul(
                    aev_sb[par][:, :], pb[2 + par][:, 0:QB], ib_sb[par][:, :]),
                    waits=[("act_ib", rec_aib[par]),
                           ("s_at%d" % par, rec_st2[par])],
                    inc="dve_no")
                rec_no[par] = w.at("dve_no")
                stn = w.op("sync", lambda E, u=u, g=g, qb=qb, par=par: E.dma_start(
                    out=attn_scr[(u * G + g) * 128:(u * G + g + 1) * 128,
                                 qb * QB:(qb + 1) * QB],
                    in_=aev_sb[par][:, :]),
                    waits=[("dve_no", rec_no[par])], inc="s_at%d" % par, inck=16)
                rec_st2[par] = stn
                qbg += 1
    P2_ST0, P2_ST1 = w.at("s_at0"), w.at("s_at1")

    w.op("gpsimd", lambda E: E.collective_compute(
        "AllGather", ALU.bypass, replica_groups=[list(range(8))],
        ins=[attn_scr_h.ap().opt()], outs=[gath_scr_h.ap().opt()]),
        waits=[("s_at0", P2_ST0), ("s_at1", P2_ST1)], inc="cc")

    gre = {"base": None, "tmp": None}
    GSPL = 4 if CH % 4 == 0 else 1
    CPS = CH // GSPL

    def p3_gather(E, k):
        if gre["base"] is None:
            gre["base"] = E.alloc_register("gbase")
            gre["tmp"] = E.alloc_register("gtmp")
            E.reg_load(gre["base"], sel_sb[0:1, 0:1])
        E.reg_mov(gre["tmp"], k * CPS * 128 * S)
        E.reg_add(gre["tmp"], gre["tmp"], gre["base"])
        return E.dma_start(
            out=ao_sb[:, k * CPS * TC:(k + 1) * CPS * TC].rearrange(
                "p (c t) -> p c t", c=CPS),
            in_=bass.AP(gath_scr_h, gre["tmp"],
                        [[S, 128], [128 * S, CPS], [1, TC]]))

    for k in range(GSPL):
        w.op("gpsimd", lambda E, k=k: p3_gather(E, k),
             waits=[("cc", 1), ("s_ms", S_MS)], inc="g_ao", inck=16)
    P3G_LD = w.at("g_ao")

    rec_wofree = [0, 0]
    rec_wog = [0, 0]
    rec_o1st = []
    rec_o1pe = {}
    for i in range(CH):
        stn = w.op("gpsimd", lambda E, i=i: E.dma_start(
            out=wo_sb[i % 2][:, :], in_=wo[i, :, :]),
            waits=[("cc", 1), ("pe_o", rec_wofree[i % 2])],
            inc="g_wo%d" % (i % 2), inck=16)
        rec_wog[i % 2] = stn
        w.op("sync", lambda E, i=i: E.dma_start(
            out=xo_sb[i % 2][:, :], in_=xown[i * 128:(i + 1) * 128, :]),
            waits=[("cc", 1), ("dve_o1", max(0, TH * (i - 1)))],
            inc="s_xo%d" % (i % 2), inck=16)
        sld = w.at("s_xo%d" % (i % 2))
        for th in range(TH):
            bank = (2 * i + th) % 4 if TH > 1 else i % 4
            for ci in range(CH):
                w.op("tensor", lambda E, i=i, th=th, ci=ci, bank=bank: E.matmul(
                    pb[bank][:, 0:TN],
                    wo_sb[i % 2][:, ci * 128:(ci + 1) * 128],
                    ao_sb[:, ci * TC + th * TN: ci * TC + (th + 1) * TN],
                    start=(ci == 0), stop=(ci == CH - 1)),
                    waits=[("g_wo%d" % (i % 2), rec_wog[i % 2]),
                           ("g_ao", P3G_LD),
                           ("dve_o1", max(0, TH * (i - 2) + th + 1))],
                    inc=("pe_o" if ci == CH - 1 else None))
            pe_o_n = w.at("pe_o")
            w.op("vector", lambda E, i=i, th=th, bank=bank: E.tensor_add(
                o1_sb[i % 2][:, th * TN:(th + 1) * TN], pb[bank][:, 0:TN],
                xo_sb[i % 2][:, th * TN:(th + 1) * TN]),
                waits=[("pe_o", pe_o_n), ("s_xo%d" % (i % 2), sld),
                       ("s_o1%s" % "ab"[i % 2], rec_o1st[i - 2] if i >= 2 else 0),
                       ("pe_o2", rec_o1pe.get(i - 2, 0)),
                       ("act_sq2", max(0, i - 1))],
                inc="dve_o1")
        d_o1 = w.at("dve_o1")
        w.op("scalar", lambda E, i=i: E.activation(
            sq2_sb[i % 2][:, :], o1_sb[i % 2][:, :], AF.Square),
            waits=[("dve_o1", d_o1), ("pe_o2", max(0, i - 1))], inc="act_sq2")
        a_sq2 = w.at("act_sq2")
        for th in range(TH):
            w.op("tensor", lambda E, i=i, th=th: E.matmul(
                pb[4 + th][0:1, 0:TN], onesp_f[:, :],
                o1_sb[i % 2][:, th * TN:(th + 1) * TN],
                start=(i == 0), stop=(i == CH - 1)),
                waits=[("dve_o1", d_o1)])
            w.op("tensor", lambda E, i=i, th=th: E.matmul(
                pb[6 + th][0:1, 0:TN], ones_bf[:, :],
                sq2_sb[i % 2][:, th * TN:(th + 1) * TN],
                start=(i == 0), stop=(i == CH - 1)),
                waits=[("act_sq2", a_sq2)],
                inc=("pe_o2" if th == TH - 1 else None))
        rec_o1pe[i] = w.at("pe_o2")
        stn = w.op("sync", lambda E, i=i: E.dma_start(
            out=out1_scr[i * 128:(i + 1) * 128, :], in_=o1_sb[i % 2][:, :]),
            waits=[("dve_o1", d_o1)], inc="s_o1%s" % "ab"[i % 2], inck=16)
        rec_o1st.append(stn)
        rec_wofree[i % 2] = w.at("pe_o")
    P3_PEO2 = w.at("pe_o2")
    P3_O1A, P3_O1B = w.at("s_o1a"), w.at("s_o1b")

    m2_sl = stat2[0:1, 0:TC]
    q2_sl = stat2[0:1, TC:2 * TC]
    r2_sl = stat2[0:1, 2 * TC:3 * TC]
    t2_sl = stat2[0:1, 3 * TC:4 * TC]
    for th in range(TH):
        w.op("scalar", lambda E, th=th: E.mul(
            m2_sl[0:1, th * TN:(th + 1) * TN], pb[4 + th][0:1, 0:TN], 1.0 / D),
            waits=[("pe_o2", P3_PEO2)])
        w.op("scalar", lambda E, th=th: E.mul(
            q2_sl[0:1, th * TN:(th + 1) * TN], pb[6 + th][0:1, 0:TN], 1.0 / D),
            inc=("act_sq2" if th == TH - 1 else None))
    A_MV2 = w.at("act_sq2")
    w.op("vector", lambda E: E.tensor_mul(t2_sl, m2_sl, m2_sl),
         waits=[("act_sq2", A_MV2)], inc="dve_x2")
    w.op("vector", lambda E: E.tensor_sub(r2_sl, q2_sl, t2_sl),
         waits=[("dve_x2", w.at("dve_x2"))], inc="dve_x2")
    w.op("vector", lambda E: E.tensor_scalar_add(t2_sl, r2_sl, 1e-5),
         waits=[("dve_x2", w.at("dve_x2"))], inc="dve_x2")
    D_MV2 = w.at("dve_x2")
    w.op("scalar", lambda E: E.activation(q2_sl, t2_sl, AF.Ln),
         waits=[("dve_x2", D_MV2)], inc="act_sq2")
    w.op("scalar", lambda E: E.activation(r2_sl, q2_sl, AF.Exp, scale=-0.5),
         waits=[("act_sq2", w.at("act_sq2"))], inc="act_sq2")
    A_MV2 = w.at("act_sq2")
    for th in range(TH):
        w.op("tensor", lambda E, th=th: E.matmul(
            pb[th][:, 0:TN], ones_f[:, :], m2_sl[0:1, th * TN:(th + 1) * TN],
            start=True, stop=True),
            waits=[("act_sq2", A_MV2), ("dve_o1", TH * CH)])
        w.op("tensor", lambda E, th=th: E.matmul(
            pb[2 + th][:, 0:TN], ones_f[:, :], r2_sl[0:1, th * TN:(th + 1) * TN],
            start=True, stop=True),
            inc=("pe_o" if th == TH - 1 else None))
    PE_BC2 = w.at("pe_o")
    for th in range(TH):
        w.op("scalar", lambda E, th=th: E.copy(
            mb2_sb[:, th * TN:(th + 1) * TN], pb[th][:, 0:TN]),
            waits=[("pe_o", PE_BC2)])
        w.op("scalar", lambda E, th=th: E.copy(
            rb2_sb[:, th * TN:(th + 1) * TN], pb[2 + th][:, 0:TN]),
            inc=("act_sq2" if th == TH - 1 else None))
    A_BC2 = w.at("act_sq2")

    X2B = w.at("dve_x2")
    for i in range(CH):
        w.op("sync", lambda E, i=i: E.dma_start(
            out=o1r_sb[i % 2][:, :], in_=out1_scr[i * 128:(i + 1) * 128, :]),
            waits=[("s_o1a", P3_O1A), ("s_o1b", P3_O1B),
                   ("dve_x2", max(0, X2B + 2 * (i - 1)))],
            inc="s_or%d" % (i % 2), inck=16)
        sld = w.at("s_or%d" % (i % 2))
        w.op("vector", lambda E, i=i: E.tensor_sub(
            t2_sb[:, :], o1r_sb[i % 2][:, :], mb2_sb[:, :]),
            waits=[("s_or%d" % (i % 2), sld), ("act_sq2", A_BC2),
                   ("dve_x2", X2B + 2 * i)],
            inc="dve_x2")
        w.op("vector", lambda E, i=i: E.tensor_mul(
            xs2_sb[:, i * TC:(i + 1) * TC], t2_sb[:, :], rb2_sb[:, :]),
            waits=[("dve_x2", X2B + 2 * i + 1)],
            inc="dve_x2")
    P3_X2 = w.at("dve_x2")

    P4_FENCE = [("pe_o2", P3_PEO2), ("dve_x2", P3_X2), ("pe_o", PE_BC2),
                ("act_sq2", A_BC2), ("s_o1a", P3_O1A), ("s_o1b", P3_O1B)]

    def xs2_ready(ci):
        return X2B + 2 * ci + 2

    rd_srcs = [out1_scr, da_scr, db_scr]
    st_dsts = [da_scr, db_scr, None]
    st_name = ["s_da", "s_db", "s_ou"]
    rec_dd = [0, 0]
    rec_fwg = [0, 0]
    rec_wdg = [0, 0]
    rec_dast = {}
    prev_tot = None
    gfc = 0
    for sw in range(SW):
        NF = FCL[sw]
        fb = sum(FCL[:sw])
        for fc in range(NF):
            w.op("gpsimd", lambda E, gfc=gfc: E.dma_start(
                out=wg_sb[gfc % 2][:, :], in_=wg[gfc, :, :]),
                waits=P4_FENCE + [("pe_f", max(0, 2 * TH * (gfc - 1)))],
                inc="g_fw%d" % (gfc % 2), inck=16)
            w.op("gpsimd", lambda E, gfc=gfc: E.dma_start(
                out=wu_sb[gfc % 2][:, :], in_=wu[gfc, :, :]),
                inc="g_fw%d" % (gfc % 2), inck=16)
            rec_fwg[gfc % 2] = w.at("g_fw%d" % (gfc % 2))
            base = 4 * (gfc % 2)
            pf0 = w.at("pe_f")
            for role in range(2):
                wsb = wg_sb[gfc % 2] if role == 0 else wu_sb[gfc % 2]
                for th in range(TH):
                    bank = base + 2 * role + th
                    for ci in range(CH):
                        fw = [("g_fw%d" % (gfc % 2), rec_fwg[gfc % 2]),
                              ("dve_x2", xs2_ready(ci)),
                              ("act_sl", max(0, TH * (gfc - 1))),
                              ("dve_h", max(0, TH * (gfc - 1))),
                              ("dve_d", rec_dd[0] if fc == 0 else 0)]
                        w.op("tensor", lambda E, ci=ci, th=th, bank=bank,
                             wsb=wsb: E.matmul(
                            pb[bank][:, 0:TN],
                            wsb[:, ci * 128:(ci + 1) * 128],
                            xs2_sb[:, ci * TC + th * TN: ci * TC + (th + 1) * TN],
                            start=(ci == 0), stop=(ci == CH - 1)),
                            waits=fw,
                            inc=("pe_f" if ci == CH - 1 else None))
            for th in range(TH):
                w.op("scalar", lambda E, th=th, base=base, gfc=gfc: E.activation(
                    sg_sb[gfc % 2][:, th * TN:(th + 1) * TN],
                    pb[base + th][:, 0:TN], AF.Silu),
                    waits=[("pe_f", pf0 + th + 1),
                           ("dve_h", max(0, TH * (gfc - 2) + TH))],
                    inc="act_sl")
            asl = w.at("act_sl")
            for th in range(TH):
                w.op("vector", lambda E, th=th, base=base, gfc=gfc, fc=fc: E.tensor_mul(
                    hA_sb[:, fc * TC + th * TN: fc * TC + (th + 1) * TN],
                    pb[base + 2 + th][:, 0:TN],
                    sg_sb[gfc % 2][:, th * TN:(th + 1) * TN]),
                    waits=[("pe_f", pf0 + TH + th + 1), ("act_sl", asl),
                           ("pe_d", rec_dd[1] if (sw >= 1 and fc < 2) else 0)],
                    inc="dve_h")
            gfc += 1
        DVE_H_S = w.at("dve_h")

        for dmc in range(CH):
            dg = sw * CH + dmc
            stn = w.op("gpsimd", lambda E, dmc=dmc, fb=fb, NF=NF, dg=dg: E.dma_start(
                out=wd_sb[dg % 2][:, 0:NF * 128],
                in_=wd[dmc, :, fb * 128:(fb + NF) * 128]),
                waits=[("pe_d", max(0, TH * (dg - 1)))],
                inc="g_wd%d" % (dg % 2), inck=16)
            rec_wdg[dg % 2] = stn
            if sw == 0:
                w.op("sync", lambda E, dmc=dmc, dg=dg: E.dma_start(
                    out=o2r_sb[dg % 2][:, :],
                    in_=rd_srcs[0][dmc * 128:(dmc + 1) * 128, :]),
                    waits=P4_FENCE + [("dve_d", max(0, TH * (dg - 1)))],
                    inc="s_r2%d" % (dg % 2), inck=16)
            else:
                w.op("sync", lambda E, dmc=dmc, dg=dg, sw=sw: E.dma_start(
                    out=o2r_sb[dg % 2][:, :],
                    in_=rd_srcs[sw][dmc * 128:(dmc + 1) * 128, :]),
                    waits=[(st_name[sw - 1] + "0", prev_tot[0]),
                           (st_name[sw - 1] + "1", prev_tot[1]),
                           ("dve_d", max(0, TH * (dg - 1)))],
                    inc="s_r2%d" % (dg % 2), inck=16)
            sld = w.at("s_r2%d" % (dg % 2))
            for th in range(TH):
                bank = (2 * dg + th) % 4 if TH > 1 else dg % 4
                for fc in range(NF):
                    w.op("tensor", lambda E, fc=fc, th=th, bank=bank, dg=dg,
                         NF=NF: E.matmul(
                        pb[bank][:, 0:TN],
                        wd_sb[dg % 2][:, fc * 128:(fc + 1) * 128],
                        hA_sb[:, fc * TC + th * TN: fc * TC + (th + 1) * TN],
                        start=(fc == 0), stop=(fc == NF - 1)),
                        waits=[("g_wd%d" % (dg % 2), rec_wdg[dg % 2]),
                               ("dve_h", DVE_H_S),
                               ("dve_d", max(0, TH * (dg - 2) + th + 1))],
                        inc=("pe_d" if fc == NF - 1 else None))
                pd = w.at("pe_d")
                ds_sem, ds_cnt = rec_dast.get(dg - 2, ("s_da0", 0))
                w.op("vector", lambda E, th=th, bank=bank, dg=dg: E.tensor_add(
                    dat_sb[dg % 2][:, th * TN:(th + 1) * TN], pb[bank][:, 0:TN],
                    o2r_sb[dg % 2][:, th * TN:(th + 1) * TN]),
                    waits=[("pe_d", pd), ("s_r2%d" % (dg % 2), sld),
                           (ds_sem, ds_cnt)],
                    inc="dve_d")
            dd = w.at("dve_d")
            sname = "%s%d" % (st_name[sw], dg % 2)
            if sw < SW - 1:
                stn = w.op("sync", lambda E, dmc=dmc, dg=dg, sw=sw: E.dma_start(
                    out=st_dsts[sw][dmc * 128:(dmc + 1) * 128, :],
                    in_=dat_sb[dg % 2][:, :]),
                    waits=[("dve_d", dd)], inc=sname, inck=16)
            else:
                stn = w.op("sync", lambda E, dmc=dmc, dg=dg: E.dma_start(
                    out=out_fm[dmc * 128:(dmc + 1) * 128, :],
                    in_=dat_sb[dg % 2][:, :]),
                    waits=[("dve_d", dd)], inc=sname, inck=16)
            rec_dast[dg] = (sname, stn)
        rec_dd = [w.at("dve_d"), w.at("pe_d")]
        prev_tot = [w.at(st_name[sw] + "0"), w.at(st_name[sw] + "1")]

    w.op("sync", lambda E: E.nop(),
         waits=[("s_ou0", w.at("s_ou0")), ("s_ou1", w.at("s_ou1"))])

    with nc.Block() as block:
        @block.tensor
        def _(E):
            w.emit("tensor", E, sems)

        @block.scalar
        def _(E):
            w.emit("scalar", E, sems)

        @block.vector
        def _(E):
            w.emit("vector", E, sems)

        @block.sync
        def _(E):
            w.emit("sync", E, sems)

        @block.gpsimd
        def _(E):
            w.emit("gpsimd", E, sems)

    stack.pop_all()
    return nc


def prep_inputs(cfg, x, w_q, w_k, w_v, w_o, w_gate, w_up, w_down):
    B, S, D, HKV, G, HD, F = (cfg[k] for k in ("B", "S", "D", "HKV", "G", "HD", "F"))
    CH, TCH, NCHUNK, DIAG, QB = (cfg[k] for k in ("CH", "TCH", "NCHUNK", "DIAG", "QB"))
    UNITS, DQL, WC, TC, FC = (cfg[k] for k in ("UNITS", "DQL", "WC", "TC", "FC"))

    f32 = np.float32
    x = np.asarray(x, dtype=f32)
    w_q = np.asarray(w_q, dtype=f32)
    w_k = np.asarray(w_k, dtype=f32)
    w_v = np.asarray(w_v, dtype=f32)
    w_o = np.asarray(w_o, dtype=f32)
    w_gate = np.asarray(w_gate, dtype=f32)
    w_up = np.asarray(w_up, dtype=f32)
    w_down = np.asarray(w_down, dtype=f32)

    wo_r = np.ascontiguousarray(
        w_o.reshape(CH, 128, CH, 128).transpose(2, 1, 0, 3).reshape(CH, 128, CH * 128))
    wgT = w_gate.T
    wuT = w_up.T
    wdT = w_down.T
    wg_r = np.ascontiguousarray(
        wgT.reshape(CH, 128, FC, 128).transpose(2, 1, 0, 3).reshape(FC, 128, CH * 128))
    wu_r = np.ascontiguousarray(
        wuT.reshape(CH, 128, FC, 128).transpose(2, 1, 0, 3).reshape(FC, 128, CH * 128))
    wd_r = np.ascontiguousarray(
        wdT.reshape(FC, 128, CH, 128).transpose(2, 1, 0, 3).reshape(CH, 128, FC * 128))

    masks = np.zeros((128, DIAG * QB), dtype=ml_dtypes.bfloat16)
    for r in range(DIAG):
        for p in range(128):
            q0 = r * 128 + p
            if q0 < QB:
                masks[p, r * QB + q0:(r + 1) * QB] = 1.0

    xseq_b = []
    for b in range(B):
        xb = np.ascontiguousarray(x[b].T)
        xr = np.ascontiguousarray(
            xb.reshape(CH, 128, NCHUNK, TCH).transpose(2, 1, 0, 3)
            .reshape(NCHUNK, 128, CH * TCH))
        xseq_b.append(xr)

    in_maps = []
    for c in range(8):
        b, j = c // 4, c % 4
        wq_sl = np.ascontiguousarray(
            w_q[:, j * DQL:(j + 1) * DQL].reshape(CH, 128, DQL)
            .transpose(1, 0, 2).reshape(128, CH * DQL))
        h0 = UNITS * j
        kcols = w_k[:, h0 * 128:(h0 + UNITS) * 128]
        vcols = w_v[:, h0 * 128:(h0 + UNITS) * 128]
        wkv_sl = np.concatenate([kcols, vcols], axis=1)
        wkv_sl = np.ascontiguousarray(
            wkv_sl.reshape(CH, 128, WC).transpose(1, 0, 2).reshape(128, CH * WC))
        xown = np.ascontiguousarray(x[b, j * TC:(j + 1) * TC, :].T)
        selv = np.array([[4 * b * DQL * S + j * TC]], dtype=np.int32)
        in_maps.append({
            "xseq": xseq_b[b], "xown": xown, "wq": wq_sl, "wkv": wkv_sl,
            "wo": wo_r, "wg": wg_r, "wu": wu_r, "wd": wd_r,
            "masks": masks, "sel": selv,
        })
    return in_maps


def assemble_output(cfg, results):
    B, S, D, TC = cfg["B"], cfg["S"], cfg["D"], cfg["TC"]
    out = np.empty((B, S, D), dtype=np.float32)
    for c in range(8):
        b, j = c // 4, c % 4
        out[b, j * TC:(j + 1) * TC, :] = results[c]["out_fm"].T
    return out


def kernel(L0_attn_hidden_states, L0_attn_w_q, L0_attn_w_k, L0_attn_w_v,
           L0_attn_w_o, L0_ffn_W_gate, L0_ffn_W_up, L0_ffn_W_down):
    cfg = make_cfg(small=False)
    in_maps = prep_inputs(cfg, L0_attn_hidden_states, L0_attn_w_q, L0_attn_w_k,
                          L0_attn_w_v, L0_attn_w_o, L0_ffn_W_gate, L0_ffn_W_up,
                          L0_ffn_W_down)
    nc = build_program(cfg)
    res = run_bass_kernel_spmd(nc, in_maps, core_ids=list(range(8)))
    return assemble_output(cfg, res.results)


# version 03
# speedup vs baseline: 1.0454x; 1.0454x over previous
import numpy as np
import ml_dtypes
import concourse.bass as bass
import concourse.mybir as mybir
from concourse.bass_utils import run_bass_kernel_spmd

F32 = mybir.dt.float32
BF16 = mybir.dt.bfloat16
U8 = mybir.dt.uint8
I32 = mybir.dt.int32
AF = mybir.ActivationFunctionType
ALU = mybir.AluOpType

ENGS = ("tensor", "scalar", "vector", "sync", "gpsimd")


def make_cfg(small=False):
    if small:
        c = dict(B=2, S=1024, D=1024, HKV=8, G=1, HD=128, F=1536)
    else:
        c = dict(B=2, S=4096, D=4096, HKV=8, G=4, HD=128, F=11008)
    c["CH"] = c["D"] // 128
    c["QB"] = c["S"] // 8
    c["TCH"] = c["QB"]
    c["NCHUNK"] = c["S"] // c["TCH"]
    c["DIAG"] = c["QB"] // 128
    c["UNITS"] = c["B"] * c["HKV"] // 8
    c["DQL"] = c["UNITS"] * c["G"] * 128
    c["QNC"] = c["DQL"] // 128
    c["WC"] = c["UNITS"] * 128 * 2
    c["TC"] = c["B"] * c["S"] // 8
    c["TH"] = max(1, c["TC"] // 512)
    c["TN"] = c["TC"] // c["TH"]
    c["FC"] = c["F"] // 128
    c["SW"] = 3
    base = c["FC"] // c["SW"]
    rem = c["FC"] - base * c["SW"]
    c["FCL"] = [base + (1 if i < rem else 0) for i in range(c["SW"])]
    c["FCMAX"] = max(c["FCL"])
    c["SCALE"] = float(c["HD"]) ** -0.5
    return c


class Walk:

    def __init__(self):
        self.ops = {e: [] for e in ENGS}
        self.cnt = {}

    def at(self, sem):
        return self.cnt.get(sem, 0)

    def op(self, eng, emit, waits=(), inc=None, inck=1):
        ws = tuple((s, v) for (s, v) in waits if v > 0)
        self.ops[eng].append((emit, ws, inc, inck))
        if inc is not None:
            self.cnt[inc] = self.cnt.get(inc, 0) + inck
        return self.cnt.get(inc, 0) if inc else None

    def emit(self, eng_name, eng, sems):
        last = {}
        for (emit, waits, inc, inck) in self.ops[eng_name]:
            for (s, v) in waits:
                if last.get(s, -1) >= v:
                    continue
                eng.wait_ge(sems[s], v)
                last[s] = v
            ins = emit(eng)
            if inc is not None:
                ins.then_inc(sems[inc], inck)


def build_program(cfg):
    B, S, D, HKV, G, HD, F = (cfg[k] for k in ("B", "S", "D", "HKV", "G", "HD", "F"))
    CH, QB, TCH, NCHUNK, DIAG = (cfg[k] for k in ("CH", "QB", "TCH", "NCHUNK", "DIAG"))
    UNITS, DQL, QNC, WC = (cfg[k] for k in ("UNITS", "DQL", "QNC", "WC"))
    TC, TH, TN, FC = (cfg[k] for k in ("TC", "TH", "TN", "FC"))
    SW, FCL, FCMAX = cfg["SW"], cfg["FCL"], cfg["FCMAX"]
    SCALE = cfg["SCALE"]
    VSUB = TCH // 128

    from contextlib import ExitStack
    stack = ExitStack()
    nc = bass.Bass("TRN2", target_bir_lowering=False)

    xseq = nc.dram_tensor("xseq", [NCHUNK, 128, CH * TCH], F32, kind="ExternalInput").ap()
    xown = nc.dram_tensor("xown", [D, TC], F32, kind="ExternalInput").ap()
    wq = nc.dram_tensor("wq", [128, CH * DQL], F32, kind="ExternalInput").ap()
    wkv = nc.dram_tensor("wkv", [128, CH * WC], F32, kind="ExternalInput").ap()
    wo = nc.dram_tensor("wo", [CH, 128, CH * 128], F32, kind="ExternalInput").ap()
    wg = nc.dram_tensor("wg", [FC, 128, CH * 128], F32, kind="ExternalInput").ap()
    wu = nc.dram_tensor("wu", [FC, 128, CH * 128], F32, kind="ExternalInput").ap()
    wd = nc.dram_tensor("wd", [CH, 128, FC * 128], F32, kind="ExternalInput").ap()
    masks_in = nc.dram_tensor("masks", [128, DIAG * QB], BF16, kind="ExternalInput").ap()
    sel = nc.dram_tensor("sel", [1, 1], I32, kind="ExternalInput").ap()
    out_fm = nc.dram_tensor("out_fm", [D, TC], F32, kind="ExternalOutput").ap()

    q_scr = nc.dram_tensor("q_scr", [DQL, S], BF16).ap()
    k_scr = nc.dram_tensor("k_scr", [UNITS, 128, S], BF16).ap()
    v_scr = nc.dram_tensor("v_scr", [UNITS, 128, S], BF16).ap()
    attn_scr_h = nc.dram_tensor("attn_scr", [4, DQL, TC], BF16)
    gath_scr_h = nc.dram_tensor("gath_scr", [4, 8 * DQL, TC], BF16)
    attn_scr = attn_scr_h.ap()
    out1_scr = nc.dram_tensor("out1_scr", [D, TC], F32).ap()
    da_scr = nc.dram_tensor("da_scr", [D, TC], F32).ap()
    db_scr = nc.dram_tensor("db_scr", [D, TC], F32).ap()

    ones_bf = stack.enter_context(nc.sbuf_tensor("ones_bf", [128, 1], BF16))
    onesp_f = stack.enter_context(nc.sbuf_tensor("onesp_f", [128, 1], F32))
    ones_f = stack.enter_context(nc.sbuf_tensor("ones_f", [1, 128], F32))
    sel_sb = stack.enter_context(nc.sbuf_tensor("sel_sb", [1, 1], I32))

    def roundup(x, a=32):
        return (x + a - 1) // a * a

    class Arena:
        def __init__(self):
            cap = nc.sbuf_bytes_remaining - 64
            self.big = stack.enter_context(
                nc.sbuf_tensor("arena", [nc.NUM_PARTITIONS, cap], U8))
            self.base = nc.lookup_mloc(self.big).addr
            self.cap = cap
            self.off = 0
            self.maxoff = 0

        def take(self, name, shape, dtype):
            per = int(np.prod(shape[1:])) * mybir.dt.size(dtype)
            o = self.off
            self.off = roundup(self.off + per)
            self.maxoff = max(self.maxoff, self.off)
            assert self.off <= self.cap, (name, self.off, self.cap)
            return nc.alloc_sbuf_tensor_at(name, list(shape), dtype,
                                           offset=self.base + o)

        def reset(self):
            self.off = 0

    ar = Arena()
    wq_sb = ar.take("wq_sb", [128, CH * DQL], BF16)
    wkv_sb = ar.take("wkv_sb", [128, CH * WC], BF16)
    xc_sb = ar.take("xc_sb", [128, CH * TCH], BF16)
    xs_sb = [ar.take(f"xs{i}", [128, CH * TCH], BF16) for i in range(2)]
    sq_sb = [ar.take(f"sq{i}", [128, TCH], BF16) for i in range(2)]
    EVW = max(TCH, UNITS * 128)
    ev_sb = [ar.take(f"ev{i}", [128, EVW], BF16) for i in range(2)]
    t1_sb = ar.take("t1_sb", [128, TCH], BF16)
    mb_sb = ar.take("mb_sb", [128, TCH], BF16)
    rb_sb = ar.take("rb_sb", [128, TCH], BF16)
    stat1 = ar.take("stat1", [1, 4 * TCH], F32)
    ar.reset()
    qt_sb = [ar.take(f"qt{u}", [128, G * S], BF16) for u in range(UNITS)]
    kt_sb = [ar.take(f"kt{u}", [128, S], BF16) for u in range(UNITS)]
    vt_sb = [ar.take(f"vt{u}", [128, S], BF16) for u in range(UNITS)]
    pr_sb = [ar.take(f"pr{i}", [128, QB], BF16) for i in range(4)]
    pm_sb = [ar.take(f"pm{i}", [128, QB], BF16) for i in range(4)]
    ib_sb = [ar.take(f"ib{i}", [128, QB], F32) for i in range(2)]
    GPQ = 2 * UNITS * G
    aev_sb = [ar.take(f"aev{i}", [128, QB], BF16) for i in range(GPQ)]
    inv_sb = [ar.take(f"inv{i}", [1, QB], F32) for i in range(2)]
    masks_sb = ar.take("masks_sb", [128, DIAG * QB], BF16)
    ar.reset()
    xs2_sb = ar.take("xs2_sb", [128, CH * TC], BF16)
    xs2_end = ar.off
    ao_sb = ar.take("ao_sb", [128, CH * TC], BF16)
    wo_sb = [ar.take(f"wo{i}", [128, CH * 128], BF16) for i in range(2)]
    xo_sb = [ar.take(f"xo{i}", [128, TC], F32) for i in range(2)]
    o1_sb = [ar.take(f"o1{i}", [128, TC], F32) for i in range(2)]
    sq2_sb = [ar.take(f"sq2{i}", [128, TC], BF16) for i in range(2)]
    o1r_sb = [ar.take(f"o1r{i}", [128, TC], F32) for i in range(2)]
    t2_sb = ar.take("t2_sb", [128, TC], BF16)
    mb2_sb = ar.take("mb2_sb", [128, TC], BF16)
    rb2_sb = ar.take("rb2_sb", [128, TC], BF16)
    stat2 = ar.take("stat2", [1, 4 * TC], F32)
    ar.reset()
    ar.off = xs2_end
    hA_sb = ar.take("hA_sb", [128, FCMAX * TC], BF16)
    wg_sb = [ar.take(f"wgs{i}", [128, CH * 128], BF16) for i in range(2)]
    wu_sb = [ar.take(f"wus{i}", [128, CH * 128], BF16) for i in range(2)]
    wd_sb = [ar.take(f"wds{i}", [128, FCMAX * 128], BF16) for i in range(2)]
    sg_sb = [ar.take(f"sg{i}", [128, TC], BF16) for i in range(2)]
    o2r_sb = [ar.take(f"o2r{i}", [128, TC], F32) for i in range(2)]
    dat_sb = [ar.take(f"dat{i}", [128, TC], F32) for i in range(2)]

    pb = [stack.enter_context(nc.psum_tensor(f"pb{i}", [128, 512], F32))
          for i in range(8)]

    semnames = ["g_w", "g_x", "g_ao", "g_wo0", "g_wo1", "g_fw0", "g_fw1",
                "g_wd0", "g_wd1",
                "s_ms", "s_kv", "s_xo0", "s_xo1", "s_or0", "s_or1",
                "s_r20", "s_r21", "s_q0", "s_q1", "s_q2", "s_q3",
                "s_o1a", "s_o1b",
                "s_da0", "s_da1", "s_db0", "s_db1", "s_ou0", "s_ou1",
                "a_q0", "a_q1",
                "cc", "g_ms",
                "act_sq", "pe_st", "act_mv", "dve_mv", "pe_bc", "act_bc",
                "dve_x", "dve_xs", "pe_mm", "act_ev",
                "pe_s", "act_pr", "dve_pm", "pe_a", "dve_iv", "pe_b",
                "act_ib", "dve_no",
                "pe_o", "dve_o1", "act_sq2", "pe_o2", "dve_x2",
                "pe_f", "act_sl", "dve_h", "pe_d", "dve_d"]
    sems = {n: stack.enter_context(nc.semaphore(n)) for n in semnames}

    w = Walk()

    w.op("gpsimd", lambda E: E.memset(ones_bf[:, :], 1.0))
    w.op("gpsimd", lambda E: E.memset(onesp_f[:, :], 1.0))
    w.op("gpsimd", lambda E: E.memset(ones_f[:, :], 1.0), inc="g_ms")
    w.op("sync", lambda E: E.dma_start(out=sel_sb[:, :], in_=sel[:, :]),
         inc="s_ms", inck=16)
    nsp = 4
    for i in range(nsp):
        lo, hi = i * (CH // nsp), (i + 1) * (CH // nsp)
        w.op("gpsimd", lambda E, lo=lo, hi=hi: E.dma_start(
            out=wq_sb[:, lo * DQL:hi * DQL], in_=wq[:, lo * DQL:hi * DQL]),
            inc="g_w", inck=16)
    w.op("gpsimd", lambda E: E.dma_start(out=wkv_sb[:, :], in_=wkv[:, :]),
         inc="g_w", inck=16)
    G_W = w.at("g_w")

    rec_xsfree = [0, 0]
    rec_ev = []
    evg = 0
    n_groups = QNC + UNITS + VSUB

    for t in range(NCHUNK):
        w.op("gpsimd", lambda E, t=t: E.dma_start(out=xc_sb[:, :], in_=xseq[t, :, :]),
             waits=[("act_sq", t * CH), ("pe_st", t * CH), ("dve_x", t * CH)],
             inc="g_x", inck=16)
        ld_n = w.at("g_x")

        for ci in range(CH):
            w.op("scalar", lambda E, ci=ci: E.activation(
                sq_sb[ci % 2][:, :], xc_sb[:, ci * TCH:(ci + 1) * TCH], AF.Square),
                waits=[("g_x", ld_n), ("pe_st", t * CH + ci - 1)],
                inc="act_sq")
            w.op("tensor", lambda E, ci=ci: E.matmul(
                pb[6][0:1, 0:TCH], ones_bf[:, :], xc_sb[:, ci * TCH:(ci + 1) * TCH],
                start=(ci == 0), stop=(ci == CH - 1)),
                waits=[("g_x", ld_n), ("act_mv", 3 * t), ("g_ms", 1)])
            w.op("tensor", lambda E, ci=ci: E.matmul(
                pb[7][0:1, 0:TCH], ones_bf[:, :], sq_sb[ci % 2][:, :],
                start=(ci == 0), stop=(ci == CH - 1)),
                waits=[("act_sq", t * CH + ci + 1)],
                inc="pe_st")
        pe_st_n = w.at("pe_st")

        m_sl = stat1[0:1, 0:TCH]
        q_sl = stat1[0:1, TCH:2 * TCH]
        r_sl = stat1[0:1, 2 * TCH:3 * TCH]
        tp_sl = stat1[0:1, 3 * TCH:4 * TCH]
        w.op("scalar", lambda E: E.mul(m_sl, pb[6][0:1, 0:TCH], 1.0 / D),
             waits=[("pe_st", pe_st_n)])
        w.op("scalar", lambda E: E.mul(q_sl, pb[7][0:1, 0:TCH], 1.0 / D),
             inc="act_mv")
        a_mv = w.at("act_mv")
        w.op("vector", lambda E: E.tensor_mul(tp_sl, m_sl, m_sl),
             waits=[("act_mv", a_mv)], inc="dve_mv")
        w.op("vector", lambda E: E.tensor_sub(r_sl, q_sl, tp_sl),
             waits=[("dve_mv", w.at("dve_mv"))], inc="dve_mv")
        w.op("vector", lambda E: E.tensor_scalar_add(tp_sl, r_sl, 1e-5),
             waits=[("dve_mv", w.at("dve_mv"))], inc="dve_mv")
        d_mv = w.at("dve_mv")
        w.op("scalar", lambda E: E.activation(q_sl, tp_sl, AF.Ln),
             waits=[("dve_mv", d_mv)], inc="act_mv")
        w.op("scalar", lambda E: E.activation(r_sl, q_sl, AF.Exp, scale=-0.5),
             waits=[("act_mv", w.at("act_mv"))], inc="act_mv")
        a_mv = w.at("act_mv")
        w.op("tensor", lambda E: E.matmul(pb[4][:, 0:TCH], ones_f[:, :], m_sl,
                                          start=True, stop=True),
             waits=[("act_mv", a_mv), ("act_bc", 2 * t)])
        w.op("tensor", lambda E: E.matmul(pb[5][:, 0:TCH], ones_f[:, :], r_sl,
                                          start=True, stop=True), inc="pe_bc")
        p_bc = w.at("pe_bc")
        w.op("scalar", lambda E: E.copy(mb_sb[:, :], pb[4][:, 0:TCH]),
             waits=[("pe_bc", p_bc), ("dve_xs", t * CH)])
        w.op("scalar", lambda E: E.copy(rb_sb[:, :], pb[5][:, 0:TCH]),
             inc="act_bc", inck=2)
        a_bc = w.at("act_bc")

        xs = xs_sb[t % 2]
        for ci in range(CH):
            w.op("vector", lambda E, ci=ci: E.tensor_sub(
                t1_sb[:, :], xc_sb[:, ci * TCH:(ci + 1) * TCH], mb_sb[:, :]),
                waits=[("act_bc", a_bc), ("pe_mm", rec_xsfree[t % 2]),
                       ("dve_xs", t * CH + ci)],
                inc="dve_x")
            w.op("vector", lambda E, ci=ci, xs=xs: E.tensor_mul(
                xs[:, ci * TCH:(ci + 1) * TCH], t1_sb[:, :], rb_sb[:, :]),
                waits=[("dve_x", t * CH + ci + 1)],
                inc="dve_xs")

        gbase = t * n_groups
        gi = 0
        for nq in range(QNC):
            bank = gi % 2
            qg = t * QNC + nq
            for ci in range(CH):
                w.op("tensor", lambda E, ci=ci, nq=nq, bank=bank, xs=xs: E.matmul(
                    pb[bank][:, 0:TCH],
                    wq_sb[:, ci * DQL + nq * 128: ci * DQL + (nq + 1) * 128],
                    xs[:, ci * TCH:(ci + 1) * TCH],
                    start=(ci == 0), stop=(ci == CH - 1)),
                    waits=[("g_w", G_W), ("dve_xs", t * CH + ci + 1),
                           ("act_ev", gbase + gi - 1)],
                    inc=("pe_mm" if ci == CH - 1 else None))
            mm_n = w.at("pe_mm")
            w.op("scalar", lambda E, bank=bank, evg=evg: E.copy(
                ev_sb[evg % 2][:, 0:TCH], pb[bank][:, 0:TCH]),
                waits=[("pe_mm", mm_n),
                       ("a_q%d" % (evg % 2), rec_ev[evg - 2] if evg >= 2 else 0)],
                inc="act_ev")
            a_ev = w.at("act_ev")
            stn = w.op("scalar", lambda E, nq=nq, t=t, evg=evg: E.dma_start(
                out=q_scr[nq * 128:(nq + 1) * 128, t * TCH:(t + 1) * TCH],
                in_=ev_sb[evg % 2][:, 0:TCH]),
                waits=[("act_ev", a_ev)], inc="a_q%d" % (evg % 2), inck=16)
            rec_ev.append(stn)
            evg += 1
            gi += 1
        for u in range(UNITS):
            bank = gi % 2
            kg = t * UNITS + u
            for ci in range(CH):
                w.op("tensor", lambda E, ci=ci, u=u, bank=bank, xs=xs: E.matmul(
                    pb[bank][:, 0:TCH],
                    wkv_sb[:, ci * WC + u * 128: ci * WC + (u + 1) * 128],
                    xs[:, ci * TCH:(ci + 1) * TCH],
                    start=(ci == 0), stop=(ci == CH - 1)),
                    waits=[("g_w", G_W), ("dve_xs", t * CH + ci + 1),
                           ("act_ev", gbase + gi - 1)],
                    inc=("pe_mm" if ci == CH - 1 else None))
            mm_n = w.at("pe_mm")
            w.op("scalar", lambda E, bank=bank, evg=evg: E.copy(
                ev_sb[evg % 2][:, 0:TCH], pb[bank][:, 0:TCH]),
                waits=[("pe_mm", mm_n),
                       ("a_q%d" % (evg % 2), rec_ev[evg - 2] if evg >= 2 else 0)],
                inc="act_ev")
            a_ev = w.at("act_ev")
            stn = w.op("scalar", lambda E, u=u, t=t, evg=evg: E.dma_start(
                out=k_scr[u, :, t * TCH:(t + 1) * TCH], in_=ev_sb[evg % 2][:, 0:TCH]),
                waits=[("act_ev", a_ev)], inc="a_q%d" % (evg % 2), inck=16)
            rec_ev.append(stn)
            evg += 1
            gi += 1
        for sub in range(VSUB):
            bank = 2 + sub % 2
            vg = t * VSUB + sub
            for ci in range(CH):
                w.op("tensor", lambda E, ci=ci, sub=sub, bank=bank, xs=xs: E.matmul(
                    pb[bank][:, 0:UNITS * 128],
                    xs[:, ci * TCH + sub * 128: ci * TCH + (sub + 1) * 128],
                    wkv_sb[:, ci * WC + UNITS * 128:(ci + 1) * WC],
                    start=(ci == 0), stop=(ci == CH - 1)),
                    waits=[("g_w", G_W), ("dve_xs", t * CH + ci + 1),
                           ("act_ev", gbase + gi - 1)],
                    inc=("pe_mm" if ci == CH - 1 else None))
            mm_n = w.at("pe_mm")
            w.op("scalar", lambda E, bank=bank, evg=evg: E.copy(
                ev_sb[evg % 2][:, 0:UNITS * 128], pb[bank][:, 0:UNITS * 128]),
                waits=[("pe_mm", mm_n),
                       ("a_q%d" % (evg % 2), rec_ev[evg - 2] if evg >= 2 else 0)],
                inc="act_ev")
            a_ev = w.at("act_ev")
            for u in range(UNITS):
                stn = w.op("scalar", lambda E, u=u, vg=vg, evg=evg: E.dma_start(
                    out=v_scr[u, :, vg * 128:(vg + 1) * 128],
                    in_=ev_sb[evg % 2][:, u * 128:(u + 1) * 128]),
                    waits=[("act_ev", a_ev)], inc="a_q%d" % (evg % 2), inck=16)
            rec_ev.append(stn)
            evg += 1
            gi += 1
        rec_xsfree[t % 2] = w.at("pe_mm")

    P1T = {n: w.at(n) for n in ("a_q0", "a_q1")}

    p1waits = [(n, v) for n, v in P1T.items()]
    w.op("sync", lambda E: E.dma_start(out=masks_sb[:, :], in_=masks_in[:, :]),
         waits=p1waits, inc="s_ms", inck=16)
    S_MS = w.at("s_ms")
    for u in range(UNITS):
        w.op("sync", lambda E, u=u: E.dma_start(
            out=qt_sb[u][:, :].rearrange("p (g t) -> p g t", g=G),
            in_=q_scr[u * G * 128:(u + 1) * G * 128, :].rearrange(
                "(g p) t -> p g t", p=128)),
            waits=p1waits, inc="s_kv", inck=16)
        w.op("sync", lambda E, u=u: E.dma_start(
            out=kt_sb[u][:, :], in_=k_scr[u, :, :]),
            waits=p1waits, inc="s_kv", inck=16)
        w.op("sync", lambda E, u=u: E.dma_start(
            out=vt_sb[u][:, :], in_=v_scr[u, :, :]),
            waits=p1waits, inc="s_kv", inck=16)
    P2_LD = w.at("s_kv")

    it = 0
    qbg = 0
    rec_av = []
    rec_no = [0, 0]
    rec_pb2 = [0, 0]
    rec_aib = [0, 0]
    qtot = [0, 0, 0, 0]
    for qb in range(8):
        k = qb // 2
        nkc = (qb + 1) * DIAG
        kc0d = qb * DIAG
        for u in range(UNITS):
            for g in range(G):
                par = qbg % 2
                for kc in range(nkc):
                    rot2, rot4 = it % 2, it % 4
                    w.op("tensor", lambda E, u=u, kc=kc, g=g, qb=qb, rot2=rot2: E.matmul(
                        pb[rot2][:, 0:QB], kt_sb[u][:, kc * 128:(kc + 1) * 128],
                        qt_sb[u][:, g * S + qb * QB: g * S + (qb + 1) * QB],
                        start=True, stop=True),
                        waits=[("s_kv", P2_LD), ("act_pr", it - 1)],
                        inc="pe_s")
                    w.op("scalar", lambda E, rot2=rot2, rot4=rot4: E.activation(
                        pr_sb[rot4][:, :], pb[rot2][:, 0:QB], AF.Exp, scale=SCALE),
                        waits=[("pe_s", it + 1),
                               ("pe_a", rec_av[it - 4] if it >= 4 else 0)],
                        inc="act_pr")
                    diag = kc >= kc0d
                    if diag:
                        r = kc - kc0d
                        w.op("vector", lambda E, rot4=rot4, r=r: E.tensor_mul(
                            pm_sb[rot4][:, :], pr_sb[rot4][:, :],
                            masks_sb[:, r * QB:(r + 1) * QB]),
                            waits=[("act_pr", it + 1), ("s_ms", S_MS),
                                   ("pe_a", rec_av[it - 4] if it >= 4 else 0)],
                            inc="dve_pm")
                        rhs = pm_sb[rot4]
                        av_w = [("dve_pm", w.at("dve_pm"))]
                    else:
                        rhs = pr_sb[rot4]
                        av_w = [("act_pr", it + 1)]
                    if kc == 0:
                        av_w.append(("dve_no", rec_no[par]))
                    w.op("tensor", lambda E, u=u, kc=kc, par=par, rhs=rhs,
                         nkc=nkc: E.matmul(
                        pb[2 + par][:, 0:QB], vt_sb[u][:, kc * 128:(kc + 1) * 128],
                        rhs[:, :], start=(kc == 0), stop=(kc == nkc - 1)),
                        waits=av_w)
                    w.op("tensor", lambda E, par=par, rhs=rhs, kc=kc, nkc=nkc: E.matmul(
                        pb[4 + par][0:1, 0:QB], ones_bf[:, :], rhs[:, :],
                        start=(kc == 0), stop=(kc == nkc - 1)),
                        inc="pe_a")
                    rec_av.append(w.at("pe_a"))
                    it += 1
                pe_a_n = w.at("pe_a")
                w.op("vector", lambda E, par=par: E.reciprocal(
                    inv_sb[par][:, :], pb[4 + par][0:1, 0:QB]),
                    waits=[("pe_a", pe_a_n), ("pe_b", rec_pb2[par])],
                    inc="dve_iv")
                d_iv = w.at("dve_iv")
                w.op("tensor", lambda E, par=par: E.matmul(
                    pb[6 + par][:, 0:QB], ones_f[:, :], inv_sb[par][:, :],
                    start=True, stop=True),
                    waits=[("dve_iv", d_iv), ("act_ib", rec_aib[par]),
                           ("act_bc", 2 * NCHUNK), ("act_mv", 3 * NCHUNK)],
                    inc="pe_b")
                rec_pb2[par] = w.at("pe_b")
                w.op("scalar", lambda E, par=par: E.copy(
                    ib_sb[par][:, :], pb[6 + par][:, 0:QB]),
                    waits=[("pe_b", rec_pb2[par])], inc="act_ib")
                rec_aib[par] = w.at("act_ib")
                ai = qbg % GPQ
                w.op("vector", lambda E, par=par, ai=ai: E.tensor_mul(
                    aev_sb[ai][:, :], pb[2 + par][:, 0:QB], ib_sb[par][:, :]),
                    waits=[("act_ib", rec_aib[par]),
                           ("s_q%d" % (k - 1), qtot[k - 1] if k >= 1 else 0)],
                    inc="dve_no")
                rec_no[par] = w.at("dve_no")
                w.op("sync", lambda E, u=u, g=g, qb=qb, ai=ai, k=k: E.dma_start(
                    out=attn_scr[k, (u * G + g) * 128:(u * G + g + 1) * 128,
                                 (qb % 2) * QB:(qb % 2 + 1) * QB],
                    in_=aev_sb[ai][:, :]),
                    waits=[("dve_no", rec_no[par])], inc="s_q%d" % k, inck=16)
                qbg += 1
        if qb % 2 == 1:
            qtot[k] = w.at("s_q%d" % k)
            w.op("gpsimd", lambda E, k=k: E.collective_compute(
                "AllGather", ALU.bypass, replica_groups=[list(range(8))],
                ins=[attn_scr_h.ap()[k].opt()], outs=[gath_scr_h.ap()[k].opt()]),
                waits=[("s_q%d" % k, qtot[k])], inc="cc")
    NG = w.at("cc")

    gre = {"base": None, "tmp": None}
    GSPL = 4 if CH % 4 == 0 else 1
    CPS = CH // GSPL

    def p3_gather(E, k):
        if gre["base"] is None:
            gre["base"] = E.alloc_register("gbase")
            gre["tmp"] = E.alloc_register("gtmp")
            E.reg_load(gre["base"], sel_sb[0:1, 0:1])
        E.reg_mov(gre["tmp"], k * CPS * 128 * TC)
        E.reg_add(gre["tmp"], gre["tmp"], gre["base"])
        return E.dma_start(
            out=ao_sb[:, k * CPS * TC:(k + 1) * CPS * TC].rearrange(
                "p (c t) -> p c t", c=CPS),
            in_=bass.AP(gath_scr_h, gre["tmp"],
                        [[TC, 128], [128 * TC, CPS], [1, TC]]))

    for k in range(GSPL):
        w.op("gpsimd", lambda E, k=k: p3_gather(E, k),
             waits=[("cc", NG), ("s_ms", S_MS)], inc="g_ao", inck=16)
    P3G_LD = w.at("g_ao")

    rec_wofree = [0, 0]
    rec_wog = [0, 0]
    rec_o1st = []
    rec_o1pe = {}
    for i in range(CH):
        stn = w.op("gpsimd", lambda E, i=i: E.dma_start(
            out=wo_sb[i % 2][:, :], in_=wo[i, :, :]),
            waits=[("cc", NG), ("pe_o", rec_wofree[i % 2])],
            inc="g_wo%d" % (i % 2), inck=16)
        rec_wog[i % 2] = stn
        w.op("sync", lambda E, i=i: E.dma_start(
            out=xo_sb[i % 2][:, :], in_=xown[i * 128:(i + 1) * 128, :]),
            waits=[("cc", NG), ("dve_o1", max(0, TH * (i - 1)))],
            inc="s_xo%d" % (i % 2), inck=16)
        sld = w.at("s_xo%d" % (i % 2))
        for th in range(TH):
            bank = (2 * i + th) % 4 if TH > 1 else i % 4
            for ci in range(CH):
                w.op("tensor", lambda E, i=i, th=th, ci=ci, bank=bank: E.matmul(
                    pb[bank][:, 0:TN],
                    wo_sb[i % 2][:, ci * 128:(ci + 1) * 128],
                    ao_sb[:, ci * TC + th * TN: ci * TC + (th + 1) * TN],
                    start=(ci == 0), stop=(ci == CH - 1)),
                    waits=[("g_wo%d" % (i % 2), rec_wog[i % 2]),
                           ("g_ao", P3G_LD),
                           ("dve_o1", max(0, TH * (i - 2) + th + 1))],
                    inc=("pe_o" if ci == CH - 1 else None))
            pe_o_n = w.at("pe_o")
            w.op("vector", lambda E, i=i, th=th, bank=bank: E.tensor_add(
                o1_sb[i % 2][:, th * TN:(th + 1) * TN], pb[bank][:, 0:TN],
                xo_sb[i % 2][:, th * TN:(th + 1) * TN]),
                waits=[("pe_o", pe_o_n), ("s_xo%d" % (i % 2), sld),
                       ("s_o1%s" % "ab"[i % 2], rec_o1st[i - 2] if i >= 2 else 0),
                       ("pe_o2", rec_o1pe.get(i - 2, 0)),
                       ("act_sq2", max(0, i - 1))],
                inc="dve_o1")
        d_o1 = w.at("dve_o1")
        w.op("scalar", lambda E, i=i: E.activation(
            sq2_sb[i % 2][:, :], o1_sb[i % 2][:, :], AF.Square),
            waits=[("dve_o1", d_o1), ("pe_o2", max(0, i - 1))], inc="act_sq2")
        a_sq2 = w.at("act_sq2")
        for th in range(TH):
            w.op("tensor", lambda E, i=i, th=th: E.matmul(
                pb[4 + th][0:1, 0:TN], onesp_f[:, :],
                o1_sb[i % 2][:, th * TN:(th + 1) * TN],
                start=(i == 0), stop=(i == CH - 1)),
                waits=[("dve_o1", d_o1)])
            w.op("tensor", lambda E, i=i, th=th: E.matmul(
                pb[6 + th][0:1, 0:TN], ones_bf[:, :],
                sq2_sb[i % 2][:, th * TN:(th + 1) * TN],
                start=(i == 0), stop=(i == CH - 1)),
                waits=[("act_sq2", a_sq2)],
                inc=("pe_o2" if th == TH - 1 else None))
        rec_o1pe[i] = w.at("pe_o2")
        stn = w.op("sync", lambda E, i=i: E.dma_start(
            out=out1_scr[i * 128:(i + 1) * 128, :], in_=o1_sb[i % 2][:, :]),
            waits=[("dve_o1", d_o1)], inc="s_o1%s" % "ab"[i % 2], inck=16)
        rec_o1st.append(stn)
        rec_wofree[i % 2] = w.at("pe_o")
    P3_PEO2 = w.at("pe_o2")
    P3_O1A, P3_O1B = w.at("s_o1a"), w.at("s_o1b")

    m2_sl = stat2[0:1, 0:TC]
    q2_sl = stat2[0:1, TC:2 * TC]
    r2_sl = stat2[0:1, 2 * TC:3 * TC]
    t2_sl = stat2[0:1, 3 * TC:4 * TC]
    for th in range(TH):
        w.op("scalar", lambda E, th=th: E.mul(
            m2_sl[0:1, th * TN:(th + 1) * TN], pb[4 + th][0:1, 0:TN], 1.0 / D),
            waits=[("pe_o2", P3_PEO2)])
        w.op("scalar", lambda E, th=th: E.mul(
            q2_sl[0:1, th * TN:(th + 1) * TN], pb[6 + th][0:1, 0:TN], 1.0 / D),
            inc=("act_sq2" if th == TH - 1 else None))
    A_MV2 = w.at("act_sq2")
    w.op("vector", lambda E: E.tensor_mul(t2_sl, m2_sl, m2_sl),
         waits=[("act_sq2", A_MV2)], inc="dve_x2")
    w.op("vector", lambda E: E.tensor_sub(r2_sl, q2_sl, t2_sl),
         waits=[("dve_x2", w.at("dve_x2"))], inc="dve_x2")
    w.op("vector", lambda E: E.tensor_scalar_add(t2_sl, r2_sl, 1e-5),
         waits=[("dve_x2", w.at("dve_x2"))], inc="dve_x2")
    D_MV2 = w.at("dve_x2")
    w.op("scalar", lambda E: E.activation(q2_sl, t2_sl, AF.Ln),
         waits=[("dve_x2", D_MV2)], inc="act_sq2")
    w.op("scalar", lambda E: E.activation(r2_sl, q2_sl, AF.Exp, scale=-0.5),
         waits=[("act_sq2", w.at("act_sq2"))], inc="act_sq2")
    A_MV2 = w.at("act_sq2")
    for th in range(TH):
        w.op("tensor", lambda E, th=th: E.matmul(
            pb[th][:, 0:TN], ones_f[:, :], m2_sl[0:1, th * TN:(th + 1) * TN],
            start=True, stop=True),
            waits=[("act_sq2", A_MV2), ("dve_o1", TH * CH)])
        w.op("tensor", lambda E, th=th: E.matmul(
            pb[2 + th][:, 0:TN], ones_f[:, :], r2_sl[0:1, th * TN:(th + 1) * TN],
            start=True, stop=True),
            inc=("pe_o" if th == TH - 1 else None))
    PE_BC2 = w.at("pe_o")
    for th in range(TH):
        w.op("scalar", lambda E, th=th: E.copy(
            mb2_sb[:, th * TN:(th + 1) * TN], pb[th][:, 0:TN]),
            waits=[("pe_o", PE_BC2)])
        w.op("scalar", lambda E, th=th: E.copy(
            rb2_sb[:, th * TN:(th + 1) * TN], pb[2 + th][:, 0:TN]),
            inc=("act_sq2" if th == TH - 1 else None))
    A_BC2 = w.at("act_sq2")

    X2B = w.at("dve_x2")
    for i in range(CH):
        w.op("sync", lambda E, i=i: E.dma_start(
            out=o1r_sb[i % 2][:, :], in_=out1_scr[i * 128:(i + 1) * 128, :]),
            waits=[("s_o1a", P3_O1A), ("s_o1b", P3_O1B),
                   ("dve_x2", max(0, X2B + 2 * (i - 1)))],
            inc="s_or%d" % (i % 2), inck=16)
        sld = w.at("s_or%d" % (i % 2))
        w.op("vector", lambda E, i=i: E.tensor_sub(
            t2_sb[:, :], o1r_sb[i % 2][:, :], mb2_sb[:, :]),
            waits=[("s_or%d" % (i % 2), sld), ("act_sq2", A_BC2),
                   ("dve_x2", X2B + 2 * i)],
            inc="dve_x2")
        w.op("vector", lambda E, i=i: E.tensor_mul(
            xs2_sb[:, i * TC:(i + 1) * TC], t2_sb[:, :], rb2_sb[:, :]),
            waits=[("dve_x2", X2B + 2 * i + 1)],
            inc="dve_x2")
    P3_X2 = w.at("dve_x2")

    P4_FENCE = [("pe_o2", P3_PEO2), ("dve_x2", P3_X2), ("pe_o", PE_BC2),
                ("act_sq2", A_BC2), ("s_o1a", P3_O1A), ("s_o1b", P3_O1B)]

    def xs2_ready(ci):
        return X2B + 2 * ci + 2

    rd_srcs = [out1_scr, da_scr, db_scr]
    st_dsts = [da_scr, db_scr, None]
    st_name = ["s_da", "s_db", "s_ou"]
    rec_dd = [0, 0]
    rec_fwg = [0, 0]
    rec_wdg = [0, 0]
    rec_dast = {}
    prev_tot = None
    gfc = 0
    for sw in range(SW):
        NF = FCL[sw]
        fb = sum(FCL[:sw])
        for fc in range(NF):
            w.op("gpsimd", lambda E, gfc=gfc: E.dma_start(
                out=wg_sb[gfc % 2][:, :], in_=wg[gfc, :, :]),
                waits=P4_FENCE + [("pe_f", max(0, 2 * TH * (gfc - 1)))],
                inc="g_fw%d" % (gfc % 2), inck=16)
            w.op("gpsimd", lambda E, gfc=gfc: E.dma_start(
                out=wu_sb[gfc % 2][:, :], in_=wu[gfc, :, :]),
                inc="g_fw%d" % (gfc % 2), inck=16)
            rec_fwg[gfc % 2] = w.at("g_fw%d" % (gfc % 2))
            base = 4 * (gfc % 2)
            pf0 = w.at("pe_f")
            for role in range(2):
                wsb = wg_sb[gfc % 2] if role == 0 else wu_sb[gfc % 2]
                for th in range(TH):
                    bank = base + 2 * role + th
                    for ci in range(CH):
                        fw = [("g_fw%d" % (gfc % 2), rec_fwg[gfc % 2]),
                              ("dve_x2", xs2_ready(ci)),
                              ("act_sl", max(0, TH * (gfc - 1))),
                              ("dve_h", max(0, TH * (gfc - 1))),
                              ("dve_d", rec_dd[0] if fc == 0 else 0)]
                        w.op("tensor", lambda E, ci=ci, th=th, bank=bank,
                             wsb=wsb: E.matmul(
                            pb[bank][:, 0:TN],
                            wsb[:, ci * 128:(ci + 1) * 128],
                            xs2_sb[:, ci * TC + th * TN: ci * TC + (th + 1) * TN],
                            start=(ci == 0), stop=(ci == CH - 1)),
                            waits=fw,
                            inc=("pe_f" if ci == CH - 1 else None))
            for th in range(TH):
                w.op("scalar", lambda E, th=th, base=base, gfc=gfc: E.activation(
                    sg_sb[gfc % 2][:, th * TN:(th + 1) * TN],
                    pb[base + th][:, 0:TN], AF.Silu),
                    waits=[("pe_f", pf0 + th + 1),
                           ("dve_h", max(0, TH * (gfc - 2) + TH))],
                    inc="act_sl")
            asl = w.at("act_sl")
            for th in range(TH):
                w.op("vector", lambda E, th=th, base=base, gfc=gfc, fc=fc: E.tensor_mul(
                    hA_sb[:, fc * TC + th * TN: fc * TC + (th + 1) * TN],
                    pb[base + 2 + th][:, 0:TN],
                    sg_sb[gfc % 2][:, th * TN:(th + 1) * TN]),
                    waits=[("pe_f", pf0 + TH + th + 1), ("act_sl", asl),
                           ("pe_d", rec_dd[1] if (sw >= 1 and fc < 2) else 0)],
                    inc="dve_h")
            gfc += 1
        DVE_H_S = w.at("dve_h")

        for dmc in range(CH):
            dg = sw * CH + dmc
            stn = w.op("gpsimd", lambda E, dmc=dmc, fb=fb, NF=NF, dg=dg: E.dma_start(
                out=wd_sb[dg % 2][:, 0:NF * 128],
                in_=wd[dmc, :, fb * 128:(fb + NF) * 128]),
                waits=[("pe_d", max(0, TH * (dg - 1)))],
                inc="g_wd%d" % (dg % 2), inck=16)
            rec_wdg[dg % 2] = stn
            if sw == 0:
                w.op("sync", lambda E, dmc=dmc, dg=dg: E.dma_start(
                    out=o2r_sb[dg % 2][:, :],
                    in_=rd_srcs[0][dmc * 128:(dmc + 1) * 128, :]),
                    waits=P4_FENCE + [("dve_d", max(0, TH * (dg - 1)))],
                    inc="s_r2%d" % (dg % 2), inck=16)
            else:
                w.op("sync", lambda E, dmc=dmc, dg=dg, sw=sw: E.dma_start(
                    out=o2r_sb[dg % 2][:, :],
                    in_=rd_srcs[sw][dmc * 128:(dmc + 1) * 128, :]),
                    waits=[(st_name[sw - 1] + "0", prev_tot[0]),
                           (st_name[sw - 1] + "1", prev_tot[1]),
                           ("dve_d", max(0, TH * (dg - 1)))],
                    inc="s_r2%d" % (dg % 2), inck=16)
            sld = w.at("s_r2%d" % (dg % 2))
            for th in range(TH):
                bank = (2 * dg + th) % 4 if TH > 1 else dg % 4
                for fc in range(NF):
                    w.op("tensor", lambda E, fc=fc, th=th, bank=bank, dg=dg,
                         NF=NF: E.matmul(
                        pb[bank][:, 0:TN],
                        wd_sb[dg % 2][:, fc * 128:(fc + 1) * 128],
                        hA_sb[:, fc * TC + th * TN: fc * TC + (th + 1) * TN],
                        start=(fc == 0), stop=(fc == NF - 1)),
                        waits=[("g_wd%d" % (dg % 2), rec_wdg[dg % 2]),
                               ("dve_h", DVE_H_S),
                               ("dve_d", max(0, TH * (dg - 2) + th + 1))],
                        inc=("pe_d" if fc == NF - 1 else None))
                pd = w.at("pe_d")
                ds_sem, ds_cnt = rec_dast.get(dg - 2, ("s_da0", 0))
                w.op("vector", lambda E, th=th, bank=bank, dg=dg: E.tensor_add(
                    dat_sb[dg % 2][:, th * TN:(th + 1) * TN], pb[bank][:, 0:TN],
                    o2r_sb[dg % 2][:, th * TN:(th + 1) * TN]),
                    waits=[("pe_d", pd), ("s_r2%d" % (dg % 2), sld),
                           (ds_sem, ds_cnt)],
                    inc="dve_d")
            dd = w.at("dve_d")
            sname = "%s%d" % (st_name[sw], dg % 2)
            if sw < SW - 1:
                stn = w.op("sync", lambda E, dmc=dmc, dg=dg, sw=sw: E.dma_start(
                    out=st_dsts[sw][dmc * 128:(dmc + 1) * 128, :],
                    in_=dat_sb[dg % 2][:, :]),
                    waits=[("dve_d", dd)], inc=sname, inck=16)
            else:
                stn = w.op("sync", lambda E, dmc=dmc, dg=dg: E.dma_start(
                    out=out_fm[dmc * 128:(dmc + 1) * 128, :],
                    in_=dat_sb[dg % 2][:, :]),
                    waits=[("dve_d", dd)], inc=sname, inck=16)
            rec_dast[dg] = (sname, stn)
        rec_dd = [w.at("dve_d"), w.at("pe_d")]
        prev_tot = [w.at(st_name[sw] + "0"), w.at(st_name[sw] + "1")]

    w.op("sync", lambda E: E.nop(),
         waits=[("s_ou0", w.at("s_ou0")), ("s_ou1", w.at("s_ou1"))])

    with nc.Block() as block:
        @block.tensor
        def _(E):
            w.emit("tensor", E, sems)

        @block.scalar
        def _(E):
            w.emit("scalar", E, sems)

        @block.vector
        def _(E):
            w.emit("vector", E, sems)

        @block.sync
        def _(E):
            w.emit("sync", E, sems)

        @block.gpsimd
        def _(E):
            w.emit("gpsimd", E, sems)

    stack.pop_all()
    return nc


def prep_inputs(cfg, x, w_q, w_k, w_v, w_o, w_gate, w_up, w_down):
    B, S, D, HKV, G, HD, F = (cfg[k] for k in ("B", "S", "D", "HKV", "G", "HD", "F"))
    CH, TCH, NCHUNK, DIAG, QB = (cfg[k] for k in ("CH", "TCH", "NCHUNK", "DIAG", "QB"))
    UNITS, DQL, WC, TC, FC = (cfg[k] for k in ("UNITS", "DQL", "WC", "TC", "FC"))

    f32 = np.float32
    x = np.asarray(x, dtype=f32)
    w_q = np.asarray(w_q, dtype=f32)
    w_k = np.asarray(w_k, dtype=f32)
    w_v = np.asarray(w_v, dtype=f32)
    w_o = np.asarray(w_o, dtype=f32)
    w_gate = np.asarray(w_gate, dtype=f32)
    w_up = np.asarray(w_up, dtype=f32)
    w_down = np.asarray(w_down, dtype=f32)

    wo_r = np.ascontiguousarray(
        w_o.reshape(CH, 128, CH, 128).transpose(2, 1, 0, 3).reshape(CH, 128, CH * 128))
    wgT = w_gate.T
    wuT = w_up.T
    wdT = w_down.T
    wg_r = np.ascontiguousarray(
        wgT.reshape(CH, 128, FC, 128).transpose(2, 1, 0, 3).reshape(FC, 128, CH * 128))
    wu_r = np.ascontiguousarray(
        wuT.reshape(CH, 128, FC, 128).transpose(2, 1, 0, 3).reshape(FC, 128, CH * 128))
    wd_r = np.ascontiguousarray(
        wdT.reshape(FC, 128, CH, 128).transpose(2, 1, 0, 3).reshape(CH, 128, FC * 128))

    masks = np.zeros((128, DIAG * QB), dtype=ml_dtypes.bfloat16)
    for r in range(DIAG):
        for p in range(128):
            q0 = r * 128 + p
            if q0 < QB:
                masks[p, r * QB + q0:(r + 1) * QB] = 1.0

    xseq_b = []
    for b in range(B):
        xb = np.ascontiguousarray(x[b].T)
        xr = np.ascontiguousarray(
            xb.reshape(CH, 128, NCHUNK, TCH).transpose(2, 1, 0, 3)
            .reshape(NCHUNK, 128, CH * TCH))
        xseq_b.append(xr)

    in_maps = []
    for c in range(8):
        b, j = c // 4, c % 4
        wq_sl = np.ascontiguousarray(
            w_q[:, j * DQL:(j + 1) * DQL].reshape(CH, 128, DQL)
            .transpose(1, 0, 2).reshape(128, CH * DQL))
        h0 = UNITS * j
        kcols = w_k[:, h0 * 128:(h0 + UNITS) * 128]
        vcols = w_v[:, h0 * 128:(h0 + UNITS) * 128]
        wkv_sl = np.concatenate([kcols, vcols], axis=1)
        wkv_sl = np.ascontiguousarray(
            wkv_sl.reshape(CH, 128, WC).transpose(1, 0, 2).reshape(128, CH * WC))
        xown = np.ascontiguousarray(x[b, j * TC:(j + 1) * TC, :].T)
        selv = np.array([[j * 8 * DQL * TC + 4 * b * DQL * TC]], dtype=np.int32)
        in_maps.append({
            "xseq": xseq_b[b], "xown": xown, "wq": wq_sl, "wkv": wkv_sl,
            "wo": wo_r, "wg": wg_r, "wu": wu_r, "wd": wd_r,
            "masks": masks, "sel": selv,
        })
    return in_maps


def assemble_output(cfg, results):
    B, S, D, TC = cfg["B"], cfg["S"], cfg["D"], cfg["TC"]
    out = np.empty((B, S, D), dtype=np.float32)
    for c in range(8):
        b, j = c // 4, c % 4
        out[b, j * TC:(j + 1) * TC, :] = results[c]["out_fm"].T
    return out


def kernel(L0_attn_hidden_states, L0_attn_w_q, L0_attn_w_k, L0_attn_w_v,
           L0_attn_w_o, L0_ffn_W_gate, L0_ffn_W_up, L0_ffn_W_down):
    cfg = make_cfg(small=False)
    in_maps = prep_inputs(cfg, L0_attn_hidden_states, L0_attn_w_q, L0_attn_w_k,
                          L0_attn_w_v, L0_attn_w_o, L0_ffn_W_gate, L0_ffn_W_up,
                          L0_ffn_W_down)
    nc = build_program(cfg)
    res = run_bass_kernel_spmd(nc, in_maps, core_ids=list(range(8)))
    return assemble_output(cfg, res.results)


# version 04
# speedup vs baseline: 1.0758x; 1.0291x over previous
import numpy as np
import ml_dtypes
import concourse.bass as bass
import concourse.mybir as mybir
from concourse.bass_utils import run_bass_kernel_spmd

F32 = mybir.dt.float32
BF16 = mybir.dt.bfloat16
U8 = mybir.dt.uint8
I32 = mybir.dt.int32
AF = mybir.ActivationFunctionType
ALU = mybir.AluOpType

ENGS = ("tensor", "scalar", "vector", "sync", "gpsimd")


def make_cfg(small=False):
    if small:
        c = dict(B=2, S=1024, D=1024, HKV=8, G=1, HD=128, F=1536)
    else:
        c = dict(B=2, S=4096, D=4096, HKV=8, G=4, HD=128, F=11008)
    c["CH"] = c["D"] // 128
    c["QB"] = c["S"] // 8
    c["TCH"] = c["QB"]
    c["NCHUNK"] = c["S"] // c["TCH"]
    c["DIAG"] = c["QB"] // 128
    c["UNITS"] = c["B"] * c["HKV"] // 8
    c["DQL"] = c["UNITS"] * c["G"] * 128
    c["QNC"] = c["DQL"] // 128
    c["WC"] = c["UNITS"] * 128 * 2
    c["TC"] = c["B"] * c["S"] // 8
    c["TH"] = max(1, c["TC"] // 512)
    c["TN"] = c["TC"] // c["TH"]
    c["FC"] = c["F"] // 128
    c["SW"] = 3
    base = c["FC"] // c["SW"]
    rem = c["FC"] - base * c["SW"]
    c["FCL"] = [base + (1 if i < rem else 0) for i in range(c["SW"])]
    c["FCMAX"] = max(c["FCL"])
    c["SCALE"] = float(c["HD"]) ** -0.5
    return c


class Walk:

    def __init__(self):
        self.ops = {e: [] for e in ENGS}
        self.cnt = {}

    def at(self, sem):
        return self.cnt.get(sem, 0)

    def op(self, eng, emit, waits=(), inc=None, inck=1):
        ws = tuple((s, v) for (s, v) in waits if v > 0)
        self.ops[eng].append((emit, ws, inc, inck))
        if inc is not None:
            self.cnt[inc] = self.cnt.get(inc, 0) + inck
        return self.cnt.get(inc, 0) if inc else None

    def emit(self, eng_name, eng, sems):
        last = {}
        for (emit, waits, inc, inck) in self.ops[eng_name]:
            for (s, v) in waits:
                if last.get(s, -1) >= v:
                    continue
                eng.wait_ge(sems[s], v)
                last[s] = v
            ins = emit(eng)
            if inc is not None:
                ins.then_inc(sems[inc], inck)


def build_program(cfg):
    B, S, D, HKV, G, HD, F = (cfg[k] for k in ("B", "S", "D", "HKV", "G", "HD", "F"))
    CH, QB, TCH, NCHUNK, DIAG = (cfg[k] for k in ("CH", "QB", "TCH", "NCHUNK", "DIAG"))
    UNITS, DQL, QNC, WC = (cfg[k] for k in ("UNITS", "DQL", "QNC", "WC"))
    TC, TH, TN, FC = (cfg[k] for k in ("TC", "TH", "TN", "FC"))
    SW, FCL, FCMAX = cfg["SW"], cfg["FCL"], cfg["FCMAX"]
    SCALE = cfg["SCALE"]
    VSUB = TCH // 128

    from contextlib import ExitStack
    stack = ExitStack()
    nc = bass.Bass("TRN2", target_bir_lowering=False)

    xseq = nc.dram_tensor("xseq", [NCHUNK, 128, CH * TCH], F32, kind="ExternalInput").ap()
    xown = nc.dram_tensor("xown", [D, TC], F32, kind="ExternalInput").ap()
    wq = nc.dram_tensor("wq", [128, CH * DQL], F32, kind="ExternalInput").ap()
    wkv = nc.dram_tensor("wkv", [128, CH * WC], F32, kind="ExternalInput").ap()
    wo = nc.dram_tensor("wo", [CH, 128, CH * 128], F32, kind="ExternalInput").ap()
    wg = nc.dram_tensor("wg", [FC, 128, CH * 128], F32, kind="ExternalInput").ap()
    wu = nc.dram_tensor("wu", [FC, 128, CH * 128], F32, kind="ExternalInput").ap()
    wd = nc.dram_tensor("wd", [CH, 128, FC * 128], F32, kind="ExternalInput").ap()
    masks_in = nc.dram_tensor("masks", [128, DIAG * QB], BF16, kind="ExternalInput").ap()
    sel = nc.dram_tensor("sel", [1, 1], I32, kind="ExternalInput").ap()
    out_fm = nc.dram_tensor("out_fm", [D, TC], F32, kind="ExternalOutput").ap()

    q_scr = nc.dram_tensor("q_scr", [DQL, S], BF16).ap()
    k_scr = nc.dram_tensor("k_scr", [UNITS, 128, S], BF16).ap()
    v_scr = nc.dram_tensor("v_scr", [UNITS, 128, S], BF16).ap()
    attn_scr_h = nc.dram_tensor("attn_scr", [4, DQL, TC], BF16)
    gath_scr_h = nc.dram_tensor("gath_scr", [4, 8 * DQL, TC], BF16)
    attn_scr = attn_scr_h.ap()
    out1_scr = nc.dram_tensor("out1_scr", [D, TC], F32).ap()
    da_scr = nc.dram_tensor("da_scr", [D, TC], F32).ap()
    db_scr = nc.dram_tensor("db_scr", [D, TC], F32).ap()

    ones_bf = stack.enter_context(nc.sbuf_tensor("ones_bf", [128, 1], BF16))
    onesp_f = stack.enter_context(nc.sbuf_tensor("onesp_f", [128, 1], F32))
    ones_f = stack.enter_context(nc.sbuf_tensor("ones_f", [1, 128], F32))
    sel_sb = stack.enter_context(nc.sbuf_tensor("sel_sb", [1, 1], I32))

    def roundup(x, a=32):
        return (x + a - 1) // a * a

    class Arena:
        def __init__(self):
            cap = nc.sbuf_bytes_remaining - 64
            self.big = stack.enter_context(
                nc.sbuf_tensor("arena", [nc.NUM_PARTITIONS, cap], U8))
            self.base = nc.lookup_mloc(self.big).addr
            self.cap = cap
            self.off = 0
            self.maxoff = 0

        def take(self, name, shape, dtype):
            per = int(np.prod(shape[1:])) * mybir.dt.size(dtype)
            o = self.off
            self.off = roundup(self.off + per)
            self.maxoff = max(self.maxoff, self.off)
            assert self.off <= self.cap, (name, self.off, self.cap)
            return nc.alloc_sbuf_tensor_at(name, list(shape), dtype,
                                           offset=self.base + o)

        def reset(self):
            self.off = 0

    ar = Arena()
    wq_sb = ar.take("wq_sb", [128, CH * DQL], BF16)
    wkv_sb = ar.take("wkv_sb", [128, CH * WC], BF16)
    xc_sb = ar.take("xc_sb", [128, CH * TCH], BF16)
    xs_sb = [ar.take(f"xs{i}", [128, CH * TCH], BF16) for i in range(2)]
    sq_sb = [ar.take(f"sq{i}", [128, TCH], BF16) for i in range(2)]
    EVW = max(TCH, UNITS * 128)
    ev_sb = [ar.take(f"ev{i}", [128, EVW], BF16) for i in range(2)]
    t1_sb = ar.take("t1_sb", [128, TCH], BF16)
    mb_sb = ar.take("mb_sb", [128, TCH], BF16)
    rb_sb = ar.take("rb_sb", [128, TCH], BF16)
    stat1 = ar.take("stat1", [1, 4 * TCH], F32)
    ar.reset()
    qt_sb = [ar.take(f"qt{u}", [128, G * S], BF16) for u in range(UNITS)]
    kt_sb = [ar.take(f"kt{u}", [128, S], BF16) for u in range(UNITS)]
    vt_sb = [ar.take(f"vt{u}", [128, S], BF16) for u in range(UNITS)]
    pr_sb = [ar.take(f"pr{i}", [128, QB], BF16) for i in range(4)]
    pm_sb = [ar.take(f"pm{i}", [128, QB], BF16) for i in range(4)]
    ib_sb = [ar.take(f"ib{i}", [128, QB], F32) for i in range(2)]
    GPQ = 2 * UNITS * G
    aev_sb = [ar.take(f"aev{i}", [128, QB], BF16) for i in range(GPQ)]
    inv_sb = [ar.take(f"inv{i}", [1, QB], F32) for i in range(2)]
    masks_sb = ar.take("masks_sb", [128, DIAG * QB], BF16)
    ar.reset()
    xs2_sb = ar.take("xs2_sb", [128, CH * TC], BF16)
    xs2_end = ar.off
    ao_sb = ar.take("ao_sb", [128, CH * TC], BF16)
    wo_sb = [ar.take(f"wo{i}", [128, CH * 128], BF16) for i in range(2)]
    xo_sb = [ar.take(f"xo{i}", [128, TC], F32) for i in range(2)]
    o1_sb = [ar.take(f"o1{i}", [128, TC], F32) for i in range(2)]
    sq2_sb = [ar.take(f"sq2{i}", [128, TC], BF16) for i in range(2)]
    o1r_sb = [ar.take(f"o1r{i}", [128, TC], F32) for i in range(2)]
    t2_sb = ar.take("t2_sb", [128, TC], BF16)
    mb2_sb = ar.take("mb2_sb", [128, TC], BF16)
    rb2_sb = ar.take("rb2_sb", [128, TC], BF16)
    stat2 = ar.take("stat2", [1, 4 * TC], F32)
    ar.reset()
    ar.off = xs2_end
    hA_sb = ar.take("hA_sb", [128, FCMAX * TC], BF16)
    wg_sb = [ar.take(f"wgs{i}", [128, CH * 128], BF16) for i in range(2)]
    wu_sb = [ar.take(f"wus{i}", [128, CH * 128], BF16) for i in range(2)]
    wd_sb = [ar.take(f"wds{i}", [128, FCMAX * 128], BF16) for i in range(2)]
    sg_sb = [ar.take(f"sg{i}", [128, TC], BF16) for i in range(2)]
    o2r_sb = [ar.take(f"o2r{i}", [128, TC], F32) for i in range(2)]
    dat_sb = [ar.take(f"dat{i}", [128, TC], F32) for i in range(2)]

    pb = [stack.enter_context(nc.psum_tensor(f"pb{i}", [128, 512], F32))
          for i in range(8)]

    semnames = ["g_w0", "g_w1", "g_w2", "g_w3", "g_wkv", "g_x", "g_ao", "g_wo0", "g_wo1", "g_fw0", "g_fw1",
                "g_wd0", "g_wd1",
                "s_ms", "s_kv", "s_xo0", "s_xo1", "s_or0", "s_or1",
                "s_r20", "s_r21", "s_q0", "s_q1", "s_q2", "s_q3",
                "s_o1a", "s_o1b",
                "s_da0", "s_da1", "s_db0", "s_db1", "s_ou0", "s_ou1",
                "a_q0", "a_q1",
                "cc", "g_ms",
                "act_sq", "pe_st", "act_mv", "dve_mv", "pe_bc", "act_bc",
                "dve_x", "dve_xs", "pe_mm", "act_ev",
                "pe_s", "act_pr", "dve_pm", "pe_a", "dve_iv", "pe_b",
                "act_ib", "dve_no",
                "pe_o", "dve_o1", "act_sq2", "pe_o2", "dve_x2",
                "pe_f", "act_sl", "dve_h", "pe_d", "dve_d"]
    sems = {n: stack.enter_context(nc.semaphore(n)) for n in semnames}

    w = Walk()

    w.op("gpsimd", lambda E: E.memset(ones_bf[:, :], 1.0))
    w.op("gpsimd", lambda E: E.memset(onesp_f[:, :], 1.0))
    w.op("gpsimd", lambda E: E.memset(ones_f[:, :], 1.0), inc="g_ms")
    w.op("sync", lambda E: E.dma_start(out=sel_sb[:, :], in_=sel[:, :]),
         inc="s_ms", inck=16)
    NSP = 4

    rec_xsfree = [0, 0]
    rec_ev = []
    evst = {"evg": 0}
    n_groups = QNC + UNITS + VSUB

    def p1_stats(t):
        w.op("gpsimd", lambda E, t=t: E.dma_start(out=xc_sb[:, :], in_=xseq[t, :, :]),
             waits=[("act_sq", t * CH), ("pe_st", t * CH), ("dve_x", t * CH)],
             inc="g_x", inck=16)
        ld_n = w.at("g_x")
        if t == 0:
            for i in range(NSP):
                lo, hi = i * (CH // NSP), (i + 1) * (CH // NSP)
                w.op("gpsimd", lambda E, lo=lo, hi=hi: E.dma_start(
                    out=wq_sb[:, lo * DQL:hi * DQL],
                    in_=wq[:, lo * DQL:hi * DQL]),
                    inc="g_w%d" % i, inck=16)
            w.op("gpsimd", lambda E: E.dma_start(out=wkv_sb[:, :], in_=wkv[:, :]),
                 inc="g_wkv", inck=16)
        for ci in range(CH):
            w.op("scalar", lambda E, ci=ci: E.activation(
                sq_sb[ci % 2][:, :], xc_sb[:, ci * TCH:(ci + 1) * TCH], AF.Square),
                waits=[("g_x", ld_n), ("pe_st", t * CH + ci - 1)],
                inc="act_sq")
            w.op("tensor", lambda E, ci=ci: E.matmul(
                pb[6][0:1, 0:TCH], ones_bf[:, :], xc_sb[:, ci * TCH:(ci + 1) * TCH],
                start=(ci == 0), stop=(ci == CH - 1)),
                waits=[("g_x", ld_n), ("act_mv", 3 * t), ("g_ms", 1)])
            w.op("tensor", lambda E, ci=ci: E.matmul(
                pb[7][0:1, 0:TCH], ones_bf[:, :], sq_sb[ci % 2][:, :],
                start=(ci == 0), stop=(ci == CH - 1)),
                waits=[("act_sq", t * CH + ci + 1)],
                inc="pe_st")
        pe_st_n = w.at("pe_st")

        m_sl = stat1[0:1, 0:TCH]
        q_sl = stat1[0:1, TCH:2 * TCH]
        r_sl = stat1[0:1, 2 * TCH:3 * TCH]
        tp_sl = stat1[0:1, 3 * TCH:4 * TCH]
        w.op("scalar", lambda E: E.mul(m_sl, pb[6][0:1, 0:TCH], 1.0 / D),
             waits=[("pe_st", pe_st_n)])
        w.op("scalar", lambda E: E.mul(q_sl, pb[7][0:1, 0:TCH], 1.0 / D),
             inc="act_mv")
        a_mv = w.at("act_mv")
        w.op("vector", lambda E: E.tensor_mul(tp_sl, m_sl, m_sl),
             waits=[("act_mv", a_mv)], inc="dve_mv")
        w.op("vector", lambda E: E.tensor_sub(r_sl, q_sl, tp_sl),
             waits=[("dve_mv", w.at("dve_mv"))], inc="dve_mv")
        w.op("vector", lambda E: E.tensor_scalar_add(tp_sl, r_sl, 1e-5),
             waits=[("dve_mv", w.at("dve_mv"))], inc="dve_mv")
        d_mv = w.at("dve_mv")
        w.op("scalar", lambda E: E.activation(q_sl, tp_sl, AF.Ln),
             waits=[("dve_mv", d_mv)], inc="act_mv")
        w.op("scalar", lambda E: E.activation(r_sl, q_sl, AF.Exp, scale=-0.5),
             waits=[("act_mv", w.at("act_mv"))], inc="act_mv")
        a_mv = w.at("act_mv")
        w.op("tensor", lambda E: E.matmul(pb[4][:, 0:TCH], ones_f[:, :], m_sl,
                                          start=True, stop=True),
             waits=[("act_mv", a_mv), ("act_bc", 2 * t)])
        w.op("tensor", lambda E: E.matmul(pb[5][:, 0:TCH], ones_f[:, :], r_sl,
                                          start=True, stop=True), inc="pe_bc")
        p_bc = w.at("pe_bc")
        w.op("scalar", lambda E: E.copy(mb_sb[:, :], pb[4][:, 0:TCH]),
             waits=[("pe_bc", p_bc), ("dve_xs", t * CH)])
        w.op("scalar", lambda E: E.copy(rb_sb[:, :], pb[5][:, 0:TCH]),
             inc="act_bc", inck=2)
        a_bc = w.at("act_bc")
        xs = xs_sb[t % 2]
        for ci in range(CH):
            w.op("vector", lambda E, ci=ci: E.tensor_sub(
                t1_sb[:, :], xc_sb[:, ci * TCH:(ci + 1) * TCH], mb_sb[:, :]),
                waits=[("act_bc", a_bc), ("pe_mm", rec_xsfree[t % 2]),
                       ("dve_xs", t * CH + ci)],
                inc="dve_x")
            w.op("vector", lambda E, ci=ci, xs=xs: E.tensor_mul(
                xs[:, ci * TCH:(ci + 1) * TCH], t1_sb[:, :], rb_sb[:, :]),
                waits=[("dve_x", t * CH + ci + 1)],
                inc="dve_xs")

    def p1_group(t, gi, kind, idx):
        xs = xs_sb[t % 2]
        gbase = t * n_groups
        evg = evst["evg"]
        if kind == "q":
            nq = idx
            bank = gi % 2
            for ci in range(CH):
                w.op("tensor", lambda E, ci=ci, nq=nq, bank=bank, xs=xs: E.matmul(
                    pb[bank][:, 0:TCH],
                    wq_sb[:, ci * DQL + nq * 128: ci * DQL + (nq + 1) * 128],
                    xs[:, ci * TCH:(ci + 1) * TCH],
                    start=(ci == 0), stop=(ci == CH - 1)),
                    waits=[("g_w%d" % (ci * NSP // CH), 16),
                           ("dve_xs", t * CH + ci + 1),
                           ("act_ev", gbase + gi - 1)],
                    inc=("pe_mm" if ci == CH - 1 else None))
            mm_n = w.at("pe_mm")
            w.op("scalar", lambda E, bank=bank, evg=evg: E.copy(
                ev_sb[evg % 2][:, 0:TCH], pb[bank][:, 0:TCH]),
                waits=[("pe_mm", mm_n),
                       ("a_q%d" % (evg % 2), rec_ev[evg - 2] if evg >= 2 else 0)],
                inc="act_ev")
            a_ev = w.at("act_ev")
            stn = w.op("scalar", lambda E, nq=nq, t=t, evg=evg: E.dma_start(
                out=q_scr[nq * 128:(nq + 1) * 128, t * TCH:(t + 1) * TCH],
                in_=ev_sb[evg % 2][:, 0:TCH]),
                waits=[("act_ev", a_ev)], inc="a_q%d" % (evg % 2), inck=16)
            rec_ev.append(stn)
        elif kind == "k":
            u = idx
            bank = gi % 2
            for ci in range(CH):
                w.op("tensor", lambda E, ci=ci, u=u, bank=bank, xs=xs: E.matmul(
                    pb[bank][:, 0:TCH],
                    wkv_sb[:, ci * WC + u * 128: ci * WC + (u + 1) * 128],
                    xs[:, ci * TCH:(ci + 1) * TCH],
                    start=(ci == 0), stop=(ci == CH - 1)),
                    waits=[("g_wkv", 16), ("dve_xs", t * CH + ci + 1),
                           ("act_ev", gbase + gi - 1)],
                    inc=("pe_mm" if ci == CH - 1 else None))
            mm_n = w.at("pe_mm")
            w.op("scalar", lambda E, bank=bank, evg=evg: E.copy(
                ev_sb[evg % 2][:, 0:TCH], pb[bank][:, 0:TCH]),
                waits=[("pe_mm", mm_n),
                       ("a_q%d" % (evg % 2), rec_ev[evg - 2] if evg >= 2 else 0)],
                inc="act_ev")
            a_ev = w.at("act_ev")
            stn = w.op("scalar", lambda E, u=u, t=t, evg=evg: E.dma_start(
                out=k_scr[u, :, t * TCH:(t + 1) * TCH], in_=ev_sb[evg % 2][:, 0:TCH]),
                waits=[("act_ev", a_ev)], inc="a_q%d" % (evg % 2), inck=16)
            rec_ev.append(stn)
        else:
            sub = idx
            bank = 2 + sub % 2
            vg = t * VSUB + sub
            for ci in range(CH):
                w.op("tensor", lambda E, ci=ci, sub=sub, bank=bank, xs=xs: E.matmul(
                    pb[bank][:, 0:UNITS * 128],
                    xs[:, ci * TCH + sub * 128: ci * TCH + (sub + 1) * 128],
                    wkv_sb[:, ci * WC + UNITS * 128:(ci + 1) * WC],
                    start=(ci == 0), stop=(ci == CH - 1)),
                    waits=[("g_wkv", 16), ("dve_xs", t * CH + ci + 1),
                           ("act_ev", gbase + gi - 1)],
                    inc=("pe_mm" if ci == CH - 1 else None))
            mm_n = w.at("pe_mm")
            w.op("scalar", lambda E, bank=bank, evg=evg: E.copy(
                ev_sb[evg % 2][:, 0:UNITS * 128], pb[bank][:, 0:UNITS * 128]),
                waits=[("pe_mm", mm_n),
                       ("a_q%d" % (evg % 2), rec_ev[evg - 2] if evg >= 2 else 0)],
                inc="act_ev")
            a_ev = w.at("act_ev")
            for u in range(UNITS):
                stn = w.op("scalar", lambda E, u=u, vg=vg, evg=evg: E.dma_start(
                    out=v_scr[u, :, vg * 128:(vg + 1) * 128],
                    in_=ev_sb[evg % 2][:, u * 128:(u + 1) * 128]),
                    waits=[("act_ev", a_ev)], inc="a_q%d" % (evg % 2), inck=16)
            rec_ev.append(stn)
        evst["evg"] += 1

    groups = ([("q", nq) for nq in range(QNC)] + [("k", u) for u in range(UNITS)]
              + [("v", sub) for sub in range(VSUB)])
    SPLIT = max(2, n_groups // 2)
    p1_stats(0)
    for t in range(NCHUNK):
        for gi, (kind, idx) in enumerate(groups[:SPLIT]):
            p1_group(t, gi, kind, idx)
        if t + 1 < NCHUNK:
            p1_stats(t + 1)
        for gi, (kind, idx) in enumerate(groups[SPLIT:], start=SPLIT):
            p1_group(t, gi, kind, idx)
        rec_xsfree[t % 2] = w.at("pe_mm")

    P1T = {n: w.at(n) for n in ("a_q0", "a_q1")}

    p1waits = [(n, v) for n, v in P1T.items()]
    w.op("sync", lambda E: E.dma_start(out=masks_sb[:, :], in_=masks_in[:, :]),
         waits=p1waits, inc="s_ms", inck=16)
    S_MS = w.at("s_ms")
    for u in range(UNITS):
        w.op("sync", lambda E, u=u: E.dma_start(
            out=qt_sb[u][:, :].rearrange("p (g t) -> p g t", g=G),
            in_=q_scr[u * G * 128:(u + 1) * G * 128, :].rearrange(
                "(g p) t -> p g t", p=128)),
            waits=p1waits, inc="s_kv", inck=16)
        w.op("sync", lambda E, u=u: E.dma_start(
            out=kt_sb[u][:, :], in_=k_scr[u, :, :]),
            waits=p1waits, inc="s_kv", inck=16)
        w.op("sync", lambda E, u=u: E.dma_start(
            out=vt_sb[u][:, :], in_=v_scr[u, :, :]),
            waits=p1waits, inc="s_kv", inck=16)
    P2_LD = w.at("s_kv")

    it = 0
    qbg = 0
    rec_av = []
    rec_no = [0, 0]
    rec_pb2 = [0, 0]
    rec_aib = [0, 0]
    qtot = [0, 0, 0, 0]
    pending = []

    def p2_evac(qb, u, g, par, pe_a_n, k, ai):
        w.op("vector", lambda E, par=par: E.reciprocal(
            inv_sb[par][:, :], pb[4 + par][0:1, 0:QB]),
            waits=[("pe_a", pe_a_n), ("pe_b", rec_pb2[par])],
            inc="dve_iv")
        d_iv = w.at("dve_iv")
        w.op("tensor", lambda E, par=par: E.matmul(
            pb[6 + par][:, 0:QB], ones_f[:, :], inv_sb[par][:, :],
            start=True, stop=True),
            waits=[("dve_iv", d_iv), ("act_ib", rec_aib[par]),
                   ("act_bc", 2 * NCHUNK), ("act_mv", 3 * NCHUNK)],
            inc="pe_b")
        rec_pb2[par] = w.at("pe_b")
        w.op("scalar", lambda E, par=par: E.copy(
            ib_sb[par][:, :], pb[6 + par][:, 0:QB]),
            waits=[("pe_b", rec_pb2[par])], inc="act_ib")
        rec_aib[par] = w.at("act_ib")
        w.op("vector", lambda E, par=par, ai=ai: E.tensor_mul(
            aev_sb[ai][:, :], pb[2 + par][:, 0:QB], ib_sb[par][:, :]),
            waits=[("act_ib", rec_aib[par]),
                   ("s_q%d" % (k - 1), qtot[k - 1] if k >= 1 else 0)],
            inc="dve_no")
        rec_no[par] = w.at("dve_no")
        w.op("sync", lambda E, u=u, g=g, qb=qb, ai=ai, k=k: E.dma_start(
            out=attn_scr[k, (u * G + g) * 128:(u * G + g + 1) * 128,
                         (qb % 2) * QB:(qb % 2 + 1) * QB],
            in_=aev_sb[ai][:, :]),
            waits=[("dve_no", rec_no[par])], inc="s_q%d" % k, inck=16)

    for qb in range(8):
        k = qb // 2
        nkc = (qb + 1) * DIAG
        kc0d = qb * DIAG
        for u in range(UNITS):
            for g in range(G):
                par = qbg % 2
                for kc in range(nkc):
                    rot2, rot4 = it % 2, it % 4
                    w.op("tensor", lambda E, u=u, kc=kc, g=g, qb=qb, rot2=rot2: E.matmul(
                        pb[rot2][:, 0:QB], kt_sb[u][:, kc * 128:(kc + 1) * 128],
                        qt_sb[u][:, g * S + qb * QB: g * S + (qb + 1) * QB],
                        start=True, stop=True),
                        waits=[("s_kv", P2_LD), ("act_pr", it - 1)],
                        inc="pe_s")
                    w.op("scalar", lambda E, rot2=rot2, rot4=rot4: E.activation(
                        pr_sb[rot4][:, :], pb[rot2][:, 0:QB], AF.Exp, scale=SCALE),
                        waits=[("pe_s", it + 1),
                               ("pe_a", rec_av[it - 4] if it >= 4 else 0)],
                        inc="act_pr")
                    diag = kc >= kc0d
                    if diag:
                        r = kc - kc0d
                        w.op("vector", lambda E, rot4=rot4, r=r: E.tensor_mul(
                            pm_sb[rot4][:, :], pr_sb[rot4][:, :],
                            masks_sb[:, r * QB:(r + 1) * QB]),
                            waits=[("act_pr", it + 1), ("s_ms", S_MS),
                                   ("pe_a", rec_av[it - 4] if it >= 4 else 0)],
                            inc="dve_pm")
                        rhs = pm_sb[rot4]
                        av_w = [("dve_pm", w.at("dve_pm"))]
                    else:
                        rhs = pr_sb[rot4]
                        av_w = [("act_pr", it + 1)]
                    if kc == 0:
                        av_w.append(("dve_no", rec_no[par]))
                    w.op("tensor", lambda E, u=u, kc=kc, par=par, rhs=rhs,
                         nkc=nkc: E.matmul(
                        pb[2 + par][:, 0:QB], vt_sb[u][:, kc * 128:(kc + 1) * 128],
                        rhs[:, :], start=(kc == 0), stop=(kc == nkc - 1)),
                        waits=av_w)
                    w.op("tensor", lambda E, par=par, rhs=rhs, kc=kc, nkc=nkc: E.matmul(
                        pb[4 + par][0:1, 0:QB], ones_bf[:, :], rhs[:, :],
                        start=(kc == 0), stop=(kc == nkc - 1)),
                        inc="pe_a")
                    rec_av.append(w.at("pe_a"))
                    it += 1
                pending.append((qb, u, g, par, w.at("pe_a"), k, qbg % GPQ))
                if len(pending) > 1:
                    p2_evac(*pending.pop(0))
                qbg += 1
        if qb % 2 == 1:
            while pending:
                p2_evac(*pending.pop(0))
            qtot[k] = w.at("s_q%d" % k)
            w.op("gpsimd", lambda E, k=k: E.collective_compute(
                "AllGather", ALU.bypass, replica_groups=[list(range(8))],
                ins=[attn_scr_h.ap()[k].opt()], outs=[gath_scr_h.ap()[k].opt()]),
                waits=[("s_q%d" % k, qtot[k])], inc="cc")
    NG = w.at("cc")

    gre = {"base": None, "tmp": None}
    GSPL = 4 if CH % 4 == 0 else 1
    CPS = CH // GSPL

    def p3_gather(E, k):
        if gre["base"] is None:
            gre["base"] = E.alloc_register("gbase")
            gre["tmp"] = E.alloc_register("gtmp")
            E.reg_load(gre["base"], sel_sb[0:1, 0:1])
        E.reg_mov(gre["tmp"], k * CPS * 128 * TC)
        E.reg_add(gre["tmp"], gre["tmp"], gre["base"])
        return E.dma_start(
            out=ao_sb[:, k * CPS * TC:(k + 1) * CPS * TC].rearrange(
                "p (c t) -> p c t", c=CPS),
            in_=bass.AP(gath_scr_h, gre["tmp"],
                        [[TC, 128], [128 * TC, CPS], [1, TC]]))

    for k in range(GSPL):
        w.op("gpsimd", lambda E, k=k: p3_gather(E, k),
             waits=[("cc", NG), ("s_ms", S_MS)], inc="g_ao", inck=16)
    P3G_LD = w.at("g_ao")

    rec_wofree = [0, 0]
    rec_wog = [0, 0]
    rec_o1st = []
    rec_o1pe = {}
    for i in range(CH):
        stn = w.op("gpsimd", lambda E, i=i: E.dma_start(
            out=wo_sb[i % 2][:, :], in_=wo[i, :, :]),
            waits=[("cc", NG), ("pe_o", rec_wofree[i % 2])],
            inc="g_wo%d" % (i % 2), inck=16)
        rec_wog[i % 2] = stn
        w.op("sync", lambda E, i=i: E.dma_start(
            out=xo_sb[i % 2][:, :], in_=xown[i * 128:(i + 1) * 128, :]),
            waits=[("cc", NG), ("dve_o1", max(0, TH * (i - 1)))],
            inc="s_xo%d" % (i % 2), inck=16)
        sld = w.at("s_xo%d" % (i % 2))
        for th in range(TH):
            bank = (2 * i + th) % 4 if TH > 1 else i % 4
            for ci in range(CH):
                w.op("tensor", lambda E, i=i, th=th, ci=ci, bank=bank: E.matmul(
                    pb[bank][:, 0:TN],
                    wo_sb[i % 2][:, ci * 128:(ci + 1) * 128],
                    ao_sb[:, ci * TC + th * TN: ci * TC + (th + 1) * TN],
                    start=(ci == 0), stop=(ci == CH - 1)),
                    waits=[("g_wo%d" % (i % 2), rec_wog[i % 2]),
                           ("g_ao", P3G_LD),
                           ("dve_o1", max(0, TH * (i - 2) + th + 1))],
                    inc=("pe_o" if ci == CH - 1 else None))
            pe_o_n = w.at("pe_o")
            w.op("vector", lambda E, i=i, th=th, bank=bank: E.tensor_add(
                o1_sb[i % 2][:, th * TN:(th + 1) * TN], pb[bank][:, 0:TN],
                xo_sb[i % 2][:, th * TN:(th + 1) * TN]),
                waits=[("pe_o", pe_o_n), ("s_xo%d" % (i % 2), sld),
                       ("s_o1%s" % "ab"[i % 2], rec_o1st[i - 2] if i >= 2 else 0),
                       ("pe_o2", rec_o1pe.get(i - 2, 0)),
                       ("act_sq2", max(0, i - 1))],
                inc="dve_o1")
        d_o1 = w.at("dve_o1")
        w.op("scalar", lambda E, i=i: E.activation(
            sq2_sb[i % 2][:, :], o1_sb[i % 2][:, :], AF.Square),
            waits=[("dve_o1", d_o1), ("pe_o2", max(0, i - 1))], inc="act_sq2")
        a_sq2 = w.at("act_sq2")
        for th in range(TH):
            w.op("tensor", lambda E, i=i, th=th: E.matmul(
                pb[4 + th][0:1, 0:TN], onesp_f[:, :],
                o1_sb[i % 2][:, th * TN:(th + 1) * TN],
                start=(i == 0), stop=(i == CH - 1)),
                waits=[("dve_o1", d_o1)])
            w.op("tensor", lambda E, i=i, th=th: E.matmul(
                pb[6 + th][0:1, 0:TN], ones_bf[:, :],
                sq2_sb[i % 2][:, th * TN:(th + 1) * TN],
                start=(i == 0), stop=(i == CH - 1)),
                waits=[("act_sq2", a_sq2)],
                inc=("pe_o2" if th == TH - 1 else None))
        rec_o1pe[i] = w.at("pe_o2")
        stn = w.op("sync", lambda E, i=i: E.dma_start(
            out=out1_scr[i * 128:(i + 1) * 128, :], in_=o1_sb[i % 2][:, :]),
            waits=[("dve_o1", d_o1)], inc="s_o1%s" % "ab"[i % 2], inck=16)
        rec_o1st.append(stn)
        rec_wofree[i % 2] = w.at("pe_o")
    P3_PEO2 = w.at("pe_o2")
    P3_O1A, P3_O1B = w.at("s_o1a"), w.at("s_o1b")

    m2_sl = stat2[0:1, 0:TC]
    q2_sl = stat2[0:1, TC:2 * TC]
    r2_sl = stat2[0:1, 2 * TC:3 * TC]
    t2_sl = stat2[0:1, 3 * TC:4 * TC]
    for th in range(TH):
        w.op("scalar", lambda E, th=th: E.mul(
            m2_sl[0:1, th * TN:(th + 1) * TN], pb[4 + th][0:1, 0:TN], 1.0 / D),
            waits=[("pe_o2", P3_PEO2)])
        w.op("scalar", lambda E, th=th: E.mul(
            q2_sl[0:1, th * TN:(th + 1) * TN], pb[6 + th][0:1, 0:TN], 1.0 / D),
            inc=("act_sq2" if th == TH - 1 else None))
    A_MV2 = w.at("act_sq2")
    w.op("vector", lambda E: E.tensor_mul(t2_sl, m2_sl, m2_sl),
         waits=[("act_sq2", A_MV2)], inc="dve_x2")
    w.op("vector", lambda E: E.tensor_sub(r2_sl, q2_sl, t2_sl),
         waits=[("dve_x2", w.at("dve_x2"))], inc="dve_x2")
    w.op("vector", lambda E: E.tensor_scalar_add(t2_sl, r2_sl, 1e-5),
         waits=[("dve_x2", w.at("dve_x2"))], inc="dve_x2")
    D_MV2 = w.at("dve_x2")
    w.op("scalar", lambda E: E.activation(q2_sl, t2_sl, AF.Ln),
         waits=[("dve_x2", D_MV2)], inc="act_sq2")
    w.op("scalar", lambda E: E.activation(r2_sl, q2_sl, AF.Exp, scale=-0.5),
         waits=[("act_sq2", w.at("act_sq2"))], inc="act_sq2")
    A_MV2 = w.at("act_sq2")
    for th in range(TH):
        w.op("tensor", lambda E, th=th: E.matmul(
            pb[th][:, 0:TN], ones_f[:, :], m2_sl[0:1, th * TN:(th + 1) * TN],
            start=True, stop=True),
            waits=[("act_sq2", A_MV2), ("dve_o1", TH * CH)])
        w.op("tensor", lambda E, th=th: E.matmul(
            pb[2 + th][:, 0:TN], ones_f[:, :], r2_sl[0:1, th * TN:(th + 1) * TN],
            start=True, stop=True),
            inc=("pe_o" if th == TH - 1 else None))
    PE_BC2 = w.at("pe_o")
    for th in range(TH):
        w.op("scalar", lambda E, th=th: E.copy(
            mb2_sb[:, th * TN:(th + 1) * TN], pb[th][:, 0:TN]),
            waits=[("pe_o", PE_BC2)])
        w.op("scalar", lambda E, th=th: E.copy(
            rb2_sb[:, th * TN:(th + 1) * TN], pb[2 + th][:, 0:TN]),
            inc=("act_sq2" if th == TH - 1 else None))
    A_BC2 = w.at("act_sq2")

    X2B = w.at("dve_x2")
    for i in range(CH):
        w.op("sync", lambda E, i=i: E.dma_start(
            out=o1r_sb[i % 2][:, :], in_=out1_scr[i * 128:(i + 1) * 128, :]),
            waits=[("s_o1a", P3_O1A), ("s_o1b", P3_O1B),
                   ("dve_x2", max(0, X2B + 2 * (i - 1)))],
            inc="s_or%d" % (i % 2), inck=16)
        sld = w.at("s_or%d" % (i % 2))
        w.op("vector", lambda E, i=i: E.tensor_sub(
            t2_sb[:, :], o1r_sb[i % 2][:, :], mb2_sb[:, :]),
            waits=[("s_or%d" % (i % 2), sld), ("act_sq2", A_BC2),
                   ("dve_x2", X2B + 2 * i)],
            inc="dve_x2")
        w.op("vector", lambda E, i=i: E.tensor_mul(
            xs2_sb[:, i * TC:(i + 1) * TC], t2_sb[:, :], rb2_sb[:, :]),
            waits=[("dve_x2", X2B + 2 * i + 1)],
            inc="dve_x2")
    P3_X2 = w.at("dve_x2")

    P4_FENCE = [("pe_o2", P3_PEO2), ("dve_x2", P3_X2), ("pe_o", PE_BC2),
                ("act_sq2", A_BC2), ("s_o1a", P3_O1A), ("s_o1b", P3_O1B)]
    WG_FENCE = [("pe_o", PE_BC2), ("dve_o1", TH * CH), ("act_sq2", A_BC2)]

    def xs2_ready(ci):
        return X2B + 2 * ci + 2

    rd_srcs = [out1_scr, da_scr, db_scr]
    st_dsts = [da_scr, db_scr, None]
    st_name = ["s_da", "s_db", "s_ou"]
    rec_dd = [0, 0]
    rec_fwg = [0, 0]
    rec_wdg = [0, 0]
    rec_dast = {}
    prev_tot = None
    gfc = 0
    for sw in range(SW):
        NF = FCL[sw]
        fb = sum(FCL[:sw])
        for fc in range(NF):
            w.op("gpsimd", lambda E, gfc=gfc: E.dma_start(
                out=wg_sb[gfc % 2][:, :], in_=wg[gfc, :, :]),
                waits=WG_FENCE + [("pe_f", max(0, 2 * TH * (gfc - 1)))],
                inc="g_fw%d" % (gfc % 2), inck=16)
            w.op("gpsimd", lambda E, gfc=gfc: E.dma_start(
                out=wu_sb[gfc % 2][:, :], in_=wu[gfc, :, :]),
                inc="g_fw%d" % (gfc % 2), inck=16)
            rec_fwg[gfc % 2] = w.at("g_fw%d" % (gfc % 2))
            base = 4 * (gfc % 2)
            pf0 = w.at("pe_f")
            for role in range(2):
                wsb = wg_sb[gfc % 2] if role == 0 else wu_sb[gfc % 2]
                for th in range(TH):
                    bank = base + 2 * role + th
                    for ci in range(CH):
                        fw = [("g_fw%d" % (gfc % 2), rec_fwg[gfc % 2]),
                              ("dve_x2", xs2_ready(ci)),
                              ("act_sl", max(0, TH * (gfc - 1))),
                              ("dve_h", max(0, TH * (gfc - 1))),
                              ("dve_d", rec_dd[0] if fc == 0 else 0)]
                        w.op("tensor", lambda E, ci=ci, th=th, bank=bank,
                             wsb=wsb: E.matmul(
                            pb[bank][:, 0:TN],
                            wsb[:, ci * 128:(ci + 1) * 128],
                            xs2_sb[:, ci * TC + th * TN: ci * TC + (th + 1) * TN],
                            start=(ci == 0), stop=(ci == CH - 1)),
                            waits=fw,
                            inc=("pe_f" if ci == CH - 1 else None))
            for th in range(TH):
                w.op("scalar", lambda E, th=th, base=base, gfc=gfc: E.activation(
                    sg_sb[gfc % 2][:, th * TN:(th + 1) * TN],
                    pb[base + th][:, 0:TN], AF.Silu),
                    waits=[("pe_f", pf0 + th + 1),
                           ("dve_h", max(0, TH * (gfc - 2) + TH))],
                    inc="act_sl")
            asl = w.at("act_sl")
            for th in range(TH):
                w.op("vector", lambda E, th=th, base=base, gfc=gfc, fc=fc: E.tensor_mul(
                    hA_sb[:, fc * TC + th * TN: fc * TC + (th + 1) * TN],
                    pb[base + 2 + th][:, 0:TN],
                    sg_sb[gfc % 2][:, th * TN:(th + 1) * TN]),
                    waits=[("pe_f", pf0 + TH + th + 1), ("act_sl", asl),
                           ("pe_d", rec_dd[1] if (sw >= 1 and fc < 2) else 0)],
                    inc="dve_h")
            gfc += 1
        DVE_H_S = w.at("dve_h")

        for dmc in range(CH):
            dg = sw * CH + dmc
            stn = w.op("gpsimd", lambda E, dmc=dmc, fb=fb, NF=NF, dg=dg: E.dma_start(
                out=wd_sb[dg % 2][:, 0:NF * 128],
                in_=wd[dmc, :, fb * 128:(fb + NF) * 128]),
                waits=[("pe_d", max(0, TH * (dg - 1)))],
                inc="g_wd%d" % (dg % 2), inck=16)
            rec_wdg[dg % 2] = stn
            if sw == 0:
                w.op("sync", lambda E, dmc=dmc, dg=dg: E.dma_start(
                    out=o2r_sb[dg % 2][:, :],
                    in_=rd_srcs[0][dmc * 128:(dmc + 1) * 128, :]),
                    waits=P4_FENCE + [("dve_d", max(0, TH * (dg - 1)))],
                    inc="s_r2%d" % (dg % 2), inck=16)
            else:
                w.op("sync", lambda E, dmc=dmc, dg=dg, sw=sw: E.dma_start(
                    out=o2r_sb[dg % 2][:, :],
                    in_=rd_srcs[sw][dmc * 128:(dmc + 1) * 128, :]),
                    waits=[(st_name[sw - 1] + "0", prev_tot[0]),
                           (st_name[sw - 1] + "1", prev_tot[1]),
                           ("dve_d", max(0, TH * (dg - 1)))],
                    inc="s_r2%d" % (dg % 2), inck=16)
            sld = w.at("s_r2%d" % (dg % 2))
            for th in range(TH):
                bank = (2 * dg + th) % 4 if TH > 1 else dg % 4
                for fc in range(NF):
                    w.op("tensor", lambda E, fc=fc, th=th, bank=bank, dg=dg,
                         NF=NF: E.matmul(
                        pb[bank][:, 0:TN],
                        wd_sb[dg % 2][:, fc * 128:(fc + 1) * 128],
                        hA_sb[:, fc * TC + th * TN: fc * TC + (th + 1) * TN],
                        start=(fc == 0), stop=(fc == NF - 1)),
                        waits=[("g_wd%d" % (dg % 2), rec_wdg[dg % 2]),
                               ("dve_h", DVE_H_S),
                               ("dve_d", max(0, TH * (dg - 2) + th + 1))],
                        inc=("pe_d" if fc == NF - 1 else None))
                pd = w.at("pe_d")
                ds_sem, ds_cnt = rec_dast.get(dg - 2, ("s_da0", 0))
                w.op("vector", lambda E, th=th, bank=bank, dg=dg: E.tensor_add(
                    dat_sb[dg % 2][:, th * TN:(th + 1) * TN], pb[bank][:, 0:TN],
                    o2r_sb[dg % 2][:, th * TN:(th + 1) * TN]),
                    waits=[("pe_d", pd), ("s_r2%d" % (dg % 2), sld),
                           (ds_sem, ds_cnt)],
                    inc="dve_d")
            dd = w.at("dve_d")
            sname = "%s%d" % (st_name[sw], dg % 2)
            if sw < SW - 1:
                stn = w.op("sync", lambda E, dmc=dmc, dg=dg, sw=sw: E.dma_start(
                    out=st_dsts[sw][dmc * 128:(dmc + 1) * 128, :],
                    in_=dat_sb[dg % 2][:, :]),
                    waits=[("dve_d", dd)], inc=sname, inck=16)
            else:
                stn = w.op("sync", lambda E, dmc=dmc, dg=dg: E.dma_start(
                    out=out_fm[dmc * 128:(dmc + 1) * 128, :],
                    in_=dat_sb[dg % 2][:, :]),
                    waits=[("dve_d", dd)], inc=sname, inck=16)
            rec_dast[dg] = (sname, stn)
        rec_dd = [w.at("dve_d"), w.at("pe_d")]
        prev_tot = [w.at(st_name[sw] + "0"), w.at(st_name[sw] + "1")]

    w.op("sync", lambda E: E.nop(),
         waits=[("s_ou0", w.at("s_ou0")), ("s_ou1", w.at("s_ou1"))])

    with nc.Block() as block:
        @block.tensor
        def _(E):
            w.emit("tensor", E, sems)

        @block.scalar
        def _(E):
            w.emit("scalar", E, sems)

        @block.vector
        def _(E):
            w.emit("vector", E, sems)

        @block.sync
        def _(E):
            w.emit("sync", E, sems)

        @block.gpsimd
        def _(E):
            w.emit("gpsimd", E, sems)

    stack.pop_all()
    return nc


def prep_inputs(cfg, x, w_q, w_k, w_v, w_o, w_gate, w_up, w_down):
    B, S, D, HKV, G, HD, F = (cfg[k] for k in ("B", "S", "D", "HKV", "G", "HD", "F"))
    CH, TCH, NCHUNK, DIAG, QB = (cfg[k] for k in ("CH", "TCH", "NCHUNK", "DIAG", "QB"))
    UNITS, DQL, WC, TC, FC = (cfg[k] for k in ("UNITS", "DQL", "WC", "TC", "FC"))

    f32 = np.float32
    x = np.asarray(x, dtype=f32)
    w_q = np.asarray(w_q, dtype=f32)
    w_k = np.asarray(w_k, dtype=f32)
    w_v = np.asarray(w_v, dtype=f32)
    w_o = np.asarray(w_o, dtype=f32)
    w_gate = np.asarray(w_gate, dtype=f32)
    w_up = np.asarray(w_up, dtype=f32)
    w_down = np.asarray(w_down, dtype=f32)

    wo_r = np.ascontiguousarray(
        w_o.reshape(CH, 128, CH, 128).transpose(2, 1, 0, 3).reshape(CH, 128, CH * 128))
    wgT = w_gate.T
    wuT = w_up.T
    wdT = w_down.T
    wg_r = np.ascontiguousarray(
        wgT.reshape(CH, 128, FC, 128).transpose(2, 1, 0, 3).reshape(FC, 128, CH * 128))
    wu_r = np.ascontiguousarray(
        wuT.reshape(CH, 128, FC, 128).transpose(2, 1, 0, 3).reshape(FC, 128, CH * 128))
    wd_r = np.ascontiguousarray(
        wdT.reshape(FC, 128, CH, 128).transpose(2, 1, 0, 3).reshape(CH, 128, FC * 128))

    masks = np.zeros((128, DIAG * QB), dtype=ml_dtypes.bfloat16)
    for r in range(DIAG):
        for p in range(128):
            q0 = r * 128 + p
            if q0 < QB:
                masks[p, r * QB + q0:(r + 1) * QB] = 1.0

    xseq_b = []
    for b in range(B):
        xb = np.ascontiguousarray(x[b].T)
        xr = np.ascontiguousarray(
            xb.reshape(CH, 128, NCHUNK, TCH).transpose(2, 1, 0, 3)
            .reshape(NCHUNK, 128, CH * TCH))
        xseq_b.append(xr)

    in_maps = []
    for c in range(8):
        b, j = c // 4, c % 4
        wq_sl = np.ascontiguousarray(
            w_q[:, j * DQL:(j + 1) * DQL].reshape(CH, 128, DQL)
            .transpose(1, 0, 2).reshape(128, CH * DQL))
        h0 = UNITS * j
        kcols = w_k[:, h0 * 128:(h0 + UNITS) * 128]
        vcols = w_v[:, h0 * 128:(h0 + UNITS) * 128]
        wkv_sl = np.concatenate([kcols, vcols], axis=1)
        wkv_sl = np.ascontiguousarray(
            wkv_sl.reshape(CH, 128, WC).transpose(1, 0, 2).reshape(128, CH * WC))
        xown = np.ascontiguousarray(x[b, j * TC:(j + 1) * TC, :].T)
        selv = np.array([[j * 8 * DQL * TC + 4 * b * DQL * TC]], dtype=np.int32)
        in_maps.append({
            "xseq": xseq_b[b], "xown": xown, "wq": wq_sl, "wkv": wkv_sl,
            "wo": wo_r, "wg": wg_r, "wu": wu_r, "wd": wd_r,
            "masks": masks, "sel": selv,
        })
    return in_maps


def assemble_output(cfg, results):
    B, S, D, TC = cfg["B"], cfg["S"], cfg["D"], cfg["TC"]
    out = np.empty((B, S, D), dtype=np.float32)
    for c in range(8):
        b, j = c // 4, c % 4
        out[b, j * TC:(j + 1) * TC, :] = results[c]["out_fm"].T
    return out


def kernel(L0_attn_hidden_states, L0_attn_w_q, L0_attn_w_k, L0_attn_w_v,
           L0_attn_w_o, L0_ffn_W_gate, L0_ffn_W_up, L0_ffn_W_down):
    cfg = make_cfg(small=False)
    in_maps = prep_inputs(cfg, L0_attn_hidden_states, L0_attn_w_q, L0_attn_w_k,
                          L0_attn_w_v, L0_attn_w_o, L0_ffn_W_gate, L0_ffn_W_up,
                          L0_ffn_W_down)
    nc = build_program(cfg)
    res = run_bass_kernel_spmd(nc, in_maps, core_ids=list(range(8)))
    return assemble_output(cfg, res.results)


# version 05
# speedup vs baseline: 1.3133x; 1.2208x over previous
import numpy as np
import ml_dtypes
import concourse.bass as bass
import concourse.mybir as mybir
from concourse.bass_utils import run_bass_kernel_spmd

F32 = mybir.dt.float32
BF16 = mybir.dt.bfloat16
U8 = mybir.dt.uint8
I32 = mybir.dt.int32
AF = mybir.ActivationFunctionType
ALU = mybir.AluOpType

ENGS = ("tensor", "scalar", "vector", "sync", "gpsimd")


def make_cfg(small=False):
    if small:
        c = dict(B=2, S=1024, D=1024, HKV=8, G=1, HD=128, F=1536)
    else:
        c = dict(B=2, S=4096, D=4096, HKV=8, G=4, HD=128, F=11008)
    c["CH"] = c["D"] // 128
    c["QB"] = c["S"] // 8
    c["TCH"] = c["QB"]
    c["NCHUNK"] = c["S"] // c["TCH"]
    c["DIAG"] = c["QB"] // 128
    c["UNITS"] = c["B"] * c["HKV"] // 8
    c["DQL"] = c["UNITS"] * c["G"] * 128
    c["QNC"] = c["DQL"] // 128
    c["WC"] = c["UNITS"] * 128 * 2
    c["TC"] = c["B"] * c["S"] // 8
    c["TH"] = max(1, c["TC"] // 512)
    c["TN"] = c["TC"] // c["TH"]
    c["FC"] = c["F"] // 128
    c["SW"] = 3
    base = c["FC"] // c["SW"]
    rem = c["FC"] - base * c["SW"]
    c["FCL"] = [base + (1 if i < rem else 0) for i in range(c["SW"])]
    c["FCMAX"] = max(c["FCL"])
    c["SCALE"] = float(c["HD"]) ** -0.5
    return c


class Walk:

    def __init__(self):
        self.ops = {e: [] for e in ENGS}
        self.cnt = {}

    def at(self, sem):
        return self.cnt.get(sem, 0)

    def op(self, eng, emit, waits=(), inc=None, inck=1):
        ws = tuple((s, v) for (s, v) in waits if v > 0)
        self.ops[eng].append((emit, ws, inc, inck))
        if inc is not None:
            self.cnt[inc] = self.cnt.get(inc, 0) + inck
        return self.cnt.get(inc, 0) if inc else None

    def emit(self, eng_name, eng, sems):
        last = {}
        for (emit, waits, inc, inck) in self.ops[eng_name]:
            for (s, v) in waits:
                if last.get(s, -1) >= v:
                    continue
                eng.wait_ge(sems[s], v)
                last[s] = v
            ins = emit(eng)
            if inc is not None:
                ins.then_inc(sems[inc], inck)


def build_program(cfg):
    B, S, D, HKV, G, HD, F = (cfg[k] for k in ("B", "S", "D", "HKV", "G", "HD", "F"))
    CH, QB, TCH, NCHUNK, DIAG = (cfg[k] for k in ("CH", "QB", "TCH", "NCHUNK", "DIAG"))
    UNITS, DQL, QNC, WC = (cfg[k] for k in ("UNITS", "DQL", "QNC", "WC"))
    TC, TH, TN, FC = (cfg[k] for k in ("TC", "TH", "TN", "FC"))
    SW, FCL, FCMAX = cfg["SW"], cfg["FCL"], cfg["FCMAX"]
    SCALE = cfg["SCALE"]
    VSUB = TCH // 128

    from contextlib import ExitStack
    stack = ExitStack()
    nc = bass.Bass("TRN2", target_bir_lowering=False)

    xseq = nc.dram_tensor("xseq", [NCHUNK, 128, CH * TCH], F32, kind="ExternalInput").ap()
    xown = nc.dram_tensor("xown", [D, TC], F32, kind="ExternalInput").ap()
    wq = nc.dram_tensor("wq", [128, CH * DQL], F32, kind="ExternalInput").ap()
    wkv = nc.dram_tensor("wkv", [128, CH * WC], F32, kind="ExternalInput").ap()
    wo = nc.dram_tensor("wo", [CH, 128, CH * 128], F32, kind="ExternalInput").ap()
    wg = nc.dram_tensor("wg", [FC, 128, CH * 128], F32, kind="ExternalInput").ap()
    wu = nc.dram_tensor("wu", [FC, 128, CH * 128], F32, kind="ExternalInput").ap()
    wd = nc.dram_tensor("wd", [CH, 128, FC * 128], F32, kind="ExternalInput").ap()
    masks_in = nc.dram_tensor("masks", [128, DIAG * QB], BF16, kind="ExternalInput").ap()
    sel = nc.dram_tensor("sel", [1, 1], I32, kind="ExternalInput").ap()
    out_fm = nc.dram_tensor("out_fm", [D, TC], F32, kind="ExternalOutput").ap()

    q_scr = nc.dram_tensor("q_scr", [DQL, S], BF16).ap()
    k_scr = nc.dram_tensor("k_scr", [UNITS, 128, S], BF16).ap()
    v_scr = nc.dram_tensor("v_scr", [UNITS, 128, S], BF16).ap()
    attn_scr_h = nc.dram_tensor("attn_scr", [4, DQL, TC], BF16)
    gath_scr_h = nc.dram_tensor("gath_scr", [4, 8 * DQL, TC], BF16,
                                addr_space="Shared")
    attn_scr = attn_scr_h.ap()
    out1_scr = nc.dram_tensor("out1_scr", [D, TC], F32).ap()
    da_scr = nc.dram_tensor("da_scr", [D, TC], F32).ap()
    db_scr = nc.dram_tensor("db_scr", [D, TC], F32).ap()

    ones_bf = stack.enter_context(nc.sbuf_tensor("ones_bf", [128, 1], BF16))
    onesp_f = stack.enter_context(nc.sbuf_tensor("onesp_f", [128, 1], F32))
    ones_f = stack.enter_context(nc.sbuf_tensor("ones_f", [1, 128], F32))
    sel_sb = stack.enter_context(nc.sbuf_tensor("sel_sb", [1, 1], I32))

    def roundup(x, a=32):
        return (x + a - 1) // a * a

    class Arena:
        def __init__(self):
            cap = nc.sbuf_bytes_remaining - 64
            self.big = stack.enter_context(
                nc.sbuf_tensor("arena", [nc.NUM_PARTITIONS, cap], U8))
            self.base = nc.lookup_mloc(self.big).addr
            self.cap = cap
            self.off = 0
            self.maxoff = 0

        def take(self, name, shape, dtype):
            per = int(np.prod(shape[1:])) * mybir.dt.size(dtype)
            o = self.off
            self.off = roundup(self.off + per)
            self.maxoff = max(self.maxoff, self.off)
            assert self.off <= self.cap, (name, self.off, self.cap)
            return nc.alloc_sbuf_tensor_at(name, list(shape), dtype,
                                           offset=self.base + o)

        def reset(self):
            self.off = 0

    ar = Arena()
    wq_sb = ar.take("wq_sb", [128, CH * DQL], BF16)
    wkv_sb = ar.take("wkv_sb", [128, CH * WC], BF16)
    xc_sb = ar.take("xc_sb", [128, CH * TCH], BF16)
    xs_sb = [ar.take(f"xs{i}", [128, CH * TCH], BF16) for i in range(2)]
    sq_sb = [ar.take(f"sq{i}", [128, TCH], BF16) for i in range(2)]
    EVW = max(TCH, UNITS * 128)
    ev_sb = [ar.take(f"ev{i}", [128, EVW], BF16) for i in range(2)]
    t1_sb = ar.take("t1_sb", [128, TCH], BF16)
    mb_sb = ar.take("mb_sb", [128, TCH], BF16)
    rb_sb = ar.take("rb_sb", [128, TCH], BF16)
    stat1 = ar.take("stat1", [1, 4 * TCH], F32)
    ar.reset()
    qt_sb = [ar.take(f"qt{u}", [128, G * S], BF16) for u in range(UNITS)]
    kt_sb = [ar.take(f"kt{u}", [128, S], BF16) for u in range(UNITS)]
    vt_sb = [ar.take(f"vt{u}", [128, S], BF16) for u in range(UNITS)]
    pr_sb = [ar.take(f"pr{i}", [128, QB], BF16) for i in range(4)]
    pm_sb = [ar.take(f"pm{i}", [128, QB], BF16) for i in range(4)]
    ib_sb = [ar.take(f"ib{i}", [128, QB], F32) for i in range(2)]
    GPQ = 2 * UNITS * G
    aev_sb = [ar.take(f"aev{i}", [128, QB], BF16) for i in range(GPQ)]
    inv_sb = [ar.take(f"inv{i}", [1, QB], F32) for i in range(2)]
    masks_sb = ar.take("masks_sb", [128, DIAG * QB], BF16)
    ar.reset()
    xs2_sb = ar.take("xs2_sb", [128, CH * TC], BF16)
    xs2_end = ar.off
    ao_sb = ar.take("ao_sb", [128, CH * TC], BF16)
    wo_sb = [ar.take(f"wo{i}", [128, CH * 128], BF16) for i in range(2)]
    xo_sb = [ar.take(f"xo{i}", [128, TC], F32) for i in range(2)]
    o1_sb = [ar.take(f"o1{i}", [128, TC], F32) for i in range(2)]
    sq2_sb = [ar.take(f"sq2{i}", [128, TC], BF16) for i in range(2)]
    o1r_sb = [ar.take(f"o1r{i}", [128, TC], F32) for i in range(2)]
    t2_sb = ar.take("t2_sb", [128, TC], BF16)
    mb2_sb = ar.take("mb2_sb", [128, TC], BF16)
    rb2_sb = ar.take("rb2_sb", [128, TC], BF16)
    stat2 = ar.take("stat2", [1, 4 * TC], F32)
    ar.reset()
    ar.off = xs2_end
    hA_sb = ar.take("hA_sb", [128, FCMAX * TC], BF16)
    wg_sb = [ar.take(f"wgs{i}", [128, CH * 128], BF16) for i in range(2)]
    wu_sb = [ar.take(f"wus{i}", [128, CH * 128], BF16) for i in range(2)]
    wd_sb = [ar.take(f"wds{i}", [128, FCMAX * 128], BF16) for i in range(2)]
    sg_sb = [ar.take(f"sg{i}", [128, TC], BF16) for i in range(2)]
    o2r_sb = [ar.take(f"o2r{i}", [128, TC], F32) for i in range(2)]
    dat_sb = [ar.take(f"dat{i}", [128, TC], F32) for i in range(2)]

    pb = [stack.enter_context(nc.psum_tensor(f"pb{i}", [128, 512], F32))
          for i in range(8)]

    semnames = ["g_w0", "g_w1", "g_w2", "g_w3", "g_wkv", "g_x", "g_ao", "g_wo0", "g_wo1", "g_fw0", "g_fw1",
                "g_wd0", "g_wd1",
                "s_ms", "s_kv", "s_xo0", "s_xo1", "s_or0", "s_or1",
                "s_r20", "s_r21", "s_q0", "s_q1", "s_q2", "s_q3",
                "s_o1a", "s_o1b",
                "s_da0", "s_da1", "s_db0", "s_db1", "s_ou0", "s_ou1",
                "a_q0", "a_q1",
                "cc", "g_ms",
                "act_sq", "pe_st", "act_mv", "dve_mv", "pe_bc", "act_bc",
                "dve_x", "dve_xs", "pe_mm", "act_ev",
                "pe_s", "act_pr", "dve_pm", "pe_a", "dve_iv", "pe_b",
                "act_ib", "dve_no",
                "pe_o", "dve_o1", "act_sq2", "pe_o2", "dve_x2",
                "pe_f", "act_sl", "dve_h", "pe_d", "dve_d"]
    sems = {n: stack.enter_context(nc.semaphore(n)) for n in semnames}

    w = Walk()

    w.op("gpsimd", lambda E: E.memset(ones_bf[:, :], 1.0))
    w.op("gpsimd", lambda E: E.memset(onesp_f[:, :], 1.0))
    w.op("gpsimd", lambda E: E.memset(ones_f[:, :], 1.0), inc="g_ms")
    w.op("sync", lambda E: E.dma_start(out=sel_sb[:, :], in_=sel[:, :]),
         inc="s_ms", inck=16)
    NSP = 4

    rec_xsfree = [0, 0]
    rec_ev = []
    evst = {"evg": 0}
    n_groups = QNC + UNITS + VSUB

    def p1_stats(t):
        w.op("gpsimd", lambda E, t=t: E.dma_start(out=xc_sb[:, :], in_=xseq[t, :, :]),
             waits=[("act_sq", t * CH), ("pe_st", t * CH), ("dve_x", t * CH)],
             inc="g_x", inck=16)
        ld_n = w.at("g_x")
        if t == 0:
            for i in range(NSP):
                lo, hi = i * (CH // NSP), (i + 1) * (CH // NSP)
                w.op("gpsimd", lambda E, lo=lo, hi=hi: E.dma_start(
                    out=wq_sb[:, lo * DQL:hi * DQL],
                    in_=wq[:, lo * DQL:hi * DQL]),
                    inc="g_w%d" % i, inck=16)
            w.op("gpsimd", lambda E: E.dma_start(out=wkv_sb[:, :], in_=wkv[:, :]),
                 inc="g_wkv", inck=16)
        for ci in range(CH):
            w.op("scalar", lambda E, ci=ci: E.activation(
                sq_sb[ci % 2][:, :], xc_sb[:, ci * TCH:(ci + 1) * TCH], AF.Square),
                waits=[("g_x", ld_n), ("pe_st", t * CH + ci - 1)],
                inc="act_sq")
            w.op("tensor", lambda E, ci=ci: E.matmul(
                pb[6][0:1, 0:TCH], ones_bf[:, :], xc_sb[:, ci * TCH:(ci + 1) * TCH],
                start=(ci == 0), stop=(ci == CH - 1)),
                waits=[("g_x", ld_n), ("act_mv", 3 * t), ("g_ms", 1)])
            w.op("tensor", lambda E, ci=ci: E.matmul(
                pb[7][0:1, 0:TCH], ones_bf[:, :], sq_sb[ci % 2][:, :],
                start=(ci == 0), stop=(ci == CH - 1)),
                waits=[("act_sq", t * CH + ci + 1)],
                inc="pe_st")
        pe_st_n = w.at("pe_st")

        m_sl = stat1[0:1, 0:TCH]
        q_sl = stat1[0:1, TCH:2 * TCH]
        r_sl = stat1[0:1, 2 * TCH:3 * TCH]
        tp_sl = stat1[0:1, 3 * TCH:4 * TCH]
        w.op("scalar", lambda E: E.mul(m_sl, pb[6][0:1, 0:TCH], 1.0 / D),
             waits=[("pe_st", pe_st_n)])
        w.op("scalar", lambda E: E.mul(q_sl, pb[7][0:1, 0:TCH], 1.0 / D),
             inc="act_mv")
        a_mv = w.at("act_mv")
        w.op("vector", lambda E: E.tensor_mul(tp_sl, m_sl, m_sl),
             waits=[("act_mv", a_mv)], inc="dve_mv")
        w.op("vector", lambda E: E.tensor_sub(r_sl, q_sl, tp_sl),
             waits=[("dve_mv", w.at("dve_mv"))], inc="dve_mv")
        w.op("vector", lambda E: E.tensor_scalar_add(tp_sl, r_sl, 1e-5),
             waits=[("dve_mv", w.at("dve_mv"))], inc="dve_mv")
        d_mv = w.at("dve_mv")
        w.op("scalar", lambda E: E.activation(q_sl, tp_sl, AF.Ln),
             waits=[("dve_mv", d_mv)], inc="act_mv")
        w.op("scalar", lambda E: E.activation(r_sl, q_sl, AF.Exp, scale=-0.5),
             waits=[("act_mv", w.at("act_mv"))], inc="act_mv")
        a_mv = w.at("act_mv")
        w.op("tensor", lambda E: E.matmul(pb[4][:, 0:TCH], ones_f[:, :], m_sl,
                                          start=True, stop=True),
             waits=[("act_mv", a_mv), ("act_bc", 2 * t)])
        w.op("tensor", lambda E: E.matmul(pb[5][:, 0:TCH], ones_f[:, :], r_sl,
                                          start=True, stop=True), inc="pe_bc")
        p_bc = w.at("pe_bc")
        w.op("scalar", lambda E: E.copy(mb_sb[:, :], pb[4][:, 0:TCH]),
             waits=[("pe_bc", p_bc), ("dve_xs", t * CH)])
        w.op("scalar", lambda E: E.copy(rb_sb[:, :], pb[5][:, 0:TCH]),
             inc="act_bc", inck=2)
        a_bc = w.at("act_bc")
        xs = xs_sb[t % 2]
        for ci in range(CH):
            w.op("vector", lambda E, ci=ci: E.tensor_sub(
                t1_sb[:, :], xc_sb[:, ci * TCH:(ci + 1) * TCH], mb_sb[:, :]),
                waits=[("act_bc", a_bc), ("pe_mm", rec_xsfree[t % 2]),
                       ("dve_xs", t * CH + ci)],
                inc="dve_x")
            w.op("vector", lambda E, ci=ci, xs=xs: E.tensor_mul(
                xs[:, ci * TCH:(ci + 1) * TCH], t1_sb[:, :], rb_sb[:, :]),
                waits=[("dve_x", t * CH + ci + 1)],
                inc="dve_xs")

    def p1_group(t, gi, kind, idx):
        xs = xs_sb[t % 2]
        gbase = t * n_groups
        evg = evst["evg"]
        if kind == "q":
            nq = idx
            bank = gi % 2
            for ci in range(CH):
                w.op("tensor", lambda E, ci=ci, nq=nq, bank=bank, xs=xs: E.matmul(
                    pb[bank][:, 0:TCH],
                    wq_sb[:, ci * DQL + nq * 128: ci * DQL + (nq + 1) * 128],
                    xs[:, ci * TCH:(ci + 1) * TCH],
                    start=(ci == 0), stop=(ci == CH - 1)),
                    waits=[("g_w%d" % (ci * NSP // CH), 16),
                           ("dve_xs", t * CH + ci + 1),
                           ("act_ev", gbase + gi - 1)],
                    inc=("pe_mm" if ci == CH - 1 else None))
            mm_n = w.at("pe_mm")
            w.op("scalar", lambda E, bank=bank, evg=evg: E.copy(
                ev_sb[evg % 2][:, 0:TCH], pb[bank][:, 0:TCH]),
                waits=[("pe_mm", mm_n),
                       ("a_q%d" % (evg % 2), rec_ev[evg - 2] if evg >= 2 else 0)],
                inc="act_ev")
            a_ev = w.at("act_ev")
            stn = w.op("scalar", lambda E, nq=nq, t=t, evg=evg: E.dma_start(
                out=q_scr[nq * 128:(nq + 1) * 128, t * TCH:(t + 1) * TCH],
                in_=ev_sb[evg % 2][:, 0:TCH]),
                waits=[("act_ev", a_ev)], inc="a_q%d" % (evg % 2), inck=16)
            rec_ev.append(stn)
        elif kind == "k":
            u = idx
            bank = gi % 2
            for ci in range(CH):
                w.op("tensor", lambda E, ci=ci, u=u, bank=bank, xs=xs: E.matmul(
                    pb[bank][:, 0:TCH],
                    wkv_sb[:, ci * WC + u * 128: ci * WC + (u + 1) * 128],
                    xs[:, ci * TCH:(ci + 1) * TCH],
                    start=(ci == 0), stop=(ci == CH - 1)),
                    waits=[("g_wkv", 16), ("dve_xs", t * CH + ci + 1),
                           ("act_ev", gbase + gi - 1)],
                    inc=("pe_mm" if ci == CH - 1 else None))
            mm_n = w.at("pe_mm")
            w.op("scalar", lambda E, bank=bank, evg=evg: E.copy(
                ev_sb[evg % 2][:, 0:TCH], pb[bank][:, 0:TCH]),
                waits=[("pe_mm", mm_n),
                       ("a_q%d" % (evg % 2), rec_ev[evg - 2] if evg >= 2 else 0)],
                inc="act_ev")
            a_ev = w.at("act_ev")
            stn = w.op("scalar", lambda E, u=u, t=t, evg=evg: E.dma_start(
                out=k_scr[u, :, t * TCH:(t + 1) * TCH], in_=ev_sb[evg % 2][:, 0:TCH]),
                waits=[("act_ev", a_ev)], inc="a_q%d" % (evg % 2), inck=16)
            rec_ev.append(stn)
        else:
            sub = idx
            bank = 2 + sub % 2
            vg = t * VSUB + sub
            for ci in range(CH):
                w.op("tensor", lambda E, ci=ci, sub=sub, bank=bank, xs=xs: E.matmul(
                    pb[bank][:, 0:UNITS * 128],
                    xs[:, ci * TCH + sub * 128: ci * TCH + (sub + 1) * 128],
                    wkv_sb[:, ci * WC + UNITS * 128:(ci + 1) * WC],
                    start=(ci == 0), stop=(ci == CH - 1)),
                    waits=[("g_wkv", 16), ("dve_xs", t * CH + ci + 1),
                           ("act_ev", gbase + gi - 1)],
                    inc=("pe_mm" if ci == CH - 1 else None))
            mm_n = w.at("pe_mm")
            w.op("scalar", lambda E, bank=bank, evg=evg: E.copy(
                ev_sb[evg % 2][:, 0:UNITS * 128], pb[bank][:, 0:UNITS * 128]),
                waits=[("pe_mm", mm_n),
                       ("a_q%d" % (evg % 2), rec_ev[evg - 2] if evg >= 2 else 0)],
                inc="act_ev")
            a_ev = w.at("act_ev")
            for u in range(UNITS):
                stn = w.op("scalar", lambda E, u=u, vg=vg, evg=evg: E.dma_start(
                    out=v_scr[u, :, vg * 128:(vg + 1) * 128],
                    in_=ev_sb[evg % 2][:, u * 128:(u + 1) * 128]),
                    waits=[("act_ev", a_ev)], inc="a_q%d" % (evg % 2), inck=16)
            rec_ev.append(stn)
        evst["evg"] += 1

    groups = ([("q", nq) for nq in range(QNC)] + [("k", u) for u in range(UNITS)]
              + [("v", sub) for sub in range(VSUB)])
    SPLIT = max(2, n_groups // 2)
    p1_stats(0)
    for t in range(NCHUNK):
        for gi, (kind, idx) in enumerate(groups[:SPLIT]):
            p1_group(t, gi, kind, idx)
        if t + 1 < NCHUNK:
            p1_stats(t + 1)
        for gi, (kind, idx) in enumerate(groups[SPLIT:], start=SPLIT):
            p1_group(t, gi, kind, idx)
        rec_xsfree[t % 2] = w.at("pe_mm")

    P1T = {n: w.at(n) for n in ("a_q0", "a_q1")}

    p1waits = [(n, v) for n, v in P1T.items()]
    w.op("sync", lambda E: E.dma_start(out=masks_sb[:, :], in_=masks_in[:, :]),
         waits=p1waits, inc="s_ms", inck=16)
    S_MS = w.at("s_ms")
    for u in range(UNITS):
        w.op("sync", lambda E, u=u: E.dma_start(
            out=qt_sb[u][:, :].rearrange("p (g t) -> p g t", g=G),
            in_=q_scr[u * G * 128:(u + 1) * G * 128, :].rearrange(
                "(g p) t -> p g t", p=128)),
            waits=p1waits, inc="s_kv", inck=16)
        w.op("sync", lambda E, u=u: E.dma_start(
            out=kt_sb[u][:, :], in_=k_scr[u, :, :]),
            waits=p1waits, inc="s_kv", inck=16)
        w.op("sync", lambda E, u=u: E.dma_start(
            out=vt_sb[u][:, :], in_=v_scr[u, :, :]),
            waits=p1waits, inc="s_kv", inck=16)
    P2_LD = w.at("s_kv")

    it = 0
    qbg = 0
    rec_av = []
    rec_no = [0, 0]
    rec_pb2 = [0, 0]
    rec_aib = [0, 0]
    qtot = [0, 0, 0, 0]
    pending = []

    def p2_evac(qb, u, g, par, pe_a_n, k, ai):
        w.op("vector", lambda E, par=par: E.reciprocal(
            inv_sb[par][:, :], pb[4 + par][0:1, 0:QB]),
            waits=[("pe_a", pe_a_n), ("pe_b", rec_pb2[par])],
            inc="dve_iv")
        d_iv = w.at("dve_iv")
        w.op("tensor", lambda E, par=par: E.matmul(
            pb[6 + par][:, 0:QB], ones_f[:, :], inv_sb[par][:, :],
            start=True, stop=True),
            waits=[("dve_iv", d_iv), ("act_ib", rec_aib[par]),
                   ("act_bc", 2 * NCHUNK), ("act_mv", 3 * NCHUNK)],
            inc="pe_b")
        rec_pb2[par] = w.at("pe_b")
        w.op("scalar", lambda E, par=par: E.copy(
            ib_sb[par][:, :], pb[6 + par][:, 0:QB]),
            waits=[("pe_b", rec_pb2[par])], inc="act_ib")
        rec_aib[par] = w.at("act_ib")
        w.op("vector", lambda E, par=par, ai=ai: E.tensor_mul(
            aev_sb[ai][:, :], pb[2 + par][:, 0:QB], ib_sb[par][:, :]),
            waits=[("act_ib", rec_aib[par]),
                   ("s_q%d" % (k - 1), qtot[k - 1] if k >= 1 else 0)],
            inc="dve_no")
        rec_no[par] = w.at("dve_no")
        w.op("sync", lambda E, u=u, g=g, qb=qb, ai=ai, k=k: E.dma_start(
            out=attn_scr[k, (u * G + g) * 128:(u * G + g + 1) * 128,
                         (qb % 2) * QB:(qb % 2 + 1) * QB],
            in_=aev_sb[ai][:, :]),
            waits=[("dve_no", rec_no[par])], inc="s_q%d" % k, inck=16)

    items = []
    for qb in range(8):
        nkc = (qb + 1) * DIAG
        for u in range(UNITS):
            for g in range(G):
                for kc in range(nkc):
                    items.append((qb, u, g, kc, nkc))
    NIT = len(items)

    def p2_scores(i):
        qb, u, g, kc, nkc = items[i]
        rot2 = i % 2
        w.op("tensor", lambda E, u=u, kc=kc, g=g, qb=qb, rot2=rot2: E.matmul(
            pb[rot2][:, 0:QB], kt_sb[u][:, kc * 128:(kc + 1) * 128],
            qt_sb[u][:, g * S + qb * QB: g * S + (qb + 1) * QB],
            start=True, stop=True),
            waits=[("s_kv", P2_LD), ("act_pr", i - 1)],
            inc="pe_s")

    st = {"qbg": 0}

    def p2_rest(i):
        qb, u, g, kc, nkc = items[i]
        qbg = st["qbg"]
        par = qbg % 2
        kc0d = qb * DIAG
        rot2, rot4 = i % 2, i % 4
        w.op("scalar", lambda E, rot2=rot2, rot4=rot4: E.activation(
            pr_sb[rot4][:, :], pb[rot2][:, 0:QB], AF.Exp, scale=SCALE),
            waits=[("pe_s", i + 1),
                   ("pe_a", rec_av[i - 4] if i >= 4 else 0)],
            inc="act_pr")
        diag = kc >= kc0d
        if diag:
            r = kc - kc0d
            w.op("vector", lambda E, rot4=rot4, r=r: E.tensor_mul(
                pm_sb[rot4][:, :], pr_sb[rot4][:, :],
                masks_sb[:, r * QB:(r + 1) * QB]),
                waits=[("act_pr", i + 1), ("s_ms", S_MS),
                       ("pe_a", rec_av[i - 4] if i >= 4 else 0)],
                inc="dve_pm")
            rhs = pm_sb[rot4]
            av_w = [("dve_pm", w.at("dve_pm"))]
        else:
            rhs = pr_sb[rot4]
            av_w = [("act_pr", i + 1)]
        if kc == 0:
            av_w.append(("dve_no", rec_no[par]))
        w.op("tensor", lambda E, u=u, kc=kc, par=par, rhs=rhs, nkc=nkc: E.matmul(
            pb[2 + par][:, 0:QB], vt_sb[u][:, kc * 128:(kc + 1) * 128],
            rhs[:, :], start=(kc == 0), stop=(kc == nkc - 1)),
            waits=av_w)
        w.op("tensor", lambda E, par=par, rhs=rhs, kc=kc, nkc=nkc: E.matmul(
            pb[4 + par][0:1, 0:QB], ones_bf[:, :], rhs[:, :],
            start=(kc == 0), stop=(kc == nkc - 1)),
            inc="pe_a")
        rec_av.append(w.at("pe_a"))
        if kc == nkc - 1:
            k = qb // 2
            pending.append((qb, u, g, par, w.at("pe_a"), k, qbg % GPQ))
            if len(pending) > 1:
                p2_evac(*pending.pop(0))
            st["qbg"] += 1
            if u == UNITS - 1 and g == G - 1 and qb % 2 == 1:
                while pending:
                    p2_evac(*pending.pop(0))
                qtot[k] = w.at("s_q%d" % k)
                w.op("gpsimd", lambda E, k=k: E.collective_compute(
                    "AllGather", ALU.bypass, replica_groups=[list(range(8))],
                    ins=[attn_scr_h.ap()[k].opt()],
                    outs=[gath_scr_h.ap()[k].opt()]),
                    waits=[("s_q%d" % k, qtot[k])], inc="cc")

    p2_scores(0)
    for i in range(1, NIT):
        p2_scores(i)
        p2_rest(i - 1)
    p2_rest(NIT - 1)
    NG = w.at("cc")

    gre = {"base": None, "tmp": None}
    GSPL = 4 if CH % 4 == 0 else 1
    CPS = CH // GSPL

    def p3_gather(E, k):
        if gre["base"] is None:
            gre["base"] = E.alloc_register("gbase")
            gre["tmp"] = E.alloc_register("gtmp")
            E.reg_load(gre["base"], sel_sb[0:1, 0:1])
        E.reg_mov(gre["tmp"], k * CPS * 128 * TC)
        E.reg_add(gre["tmp"], gre["tmp"], gre["base"])
        return E.dma_start(
            out=ao_sb[:, k * CPS * TC:(k + 1) * CPS * TC].rearrange(
                "p (c t) -> p c t", c=CPS),
            in_=bass.AP(gath_scr_h, gre["tmp"],
                        [[TC, 128], [128 * TC, CPS], [1, TC]]))

    for k in range(GSPL):
        w.op("gpsimd", lambda E, k=k: p3_gather(E, k),
             waits=[("cc", NG), ("s_ms", S_MS)], inc="g_ao", inck=16)
    P3G_LD = w.at("g_ao")

    rec_wofree = [0, 0]
    rec_wog = [0, 0]
    rec_o1st = []
    rec_o1pe = {}
    for i in range(CH):
        stn = w.op("gpsimd", lambda E, i=i: E.dma_start(
            out=wo_sb[i % 2][:, :], in_=wo[i, :, :]),
            waits=[("cc", NG), ("pe_o", rec_wofree[i % 2])],
            inc="g_wo%d" % (i % 2), inck=16)
        rec_wog[i % 2] = stn
        w.op("sync", lambda E, i=i: E.dma_start(
            out=xo_sb[i % 2][:, :], in_=xown[i * 128:(i + 1) * 128, :]),
            waits=[("cc", NG), ("dve_o1", max(0, TH * (i - 1)))],
            inc="s_xo%d" % (i % 2), inck=16)
        sld = w.at("s_xo%d" % (i % 2))
        for th in range(TH):
            bank = (2 * i + th) % 4 if TH > 1 else i % 4
            for ci in range(CH):
                w.op("tensor", lambda E, i=i, th=th, ci=ci, bank=bank: E.matmul(
                    pb[bank][:, 0:TN],
                    wo_sb[i % 2][:, ci * 128:(ci + 1) * 128],
                    ao_sb[:, ci * TC + th * TN: ci * TC + (th + 1) * TN],
                    start=(ci == 0), stop=(ci == CH - 1)),
                    waits=[("g_wo%d" % (i % 2), rec_wog[i % 2]),
                           ("g_ao", P3G_LD),
                           ("dve_o1", max(0, TH * (i - 2) + th + 1))],
                    inc=("pe_o" if ci == CH - 1 else None))
            pe_o_n = w.at("pe_o")
            w.op("vector", lambda E, i=i, th=th, bank=bank: E.tensor_add(
                o1_sb[i % 2][:, th * TN:(th + 1) * TN], pb[bank][:, 0:TN],
                xo_sb[i % 2][:, th * TN:(th + 1) * TN]),
                waits=[("pe_o", pe_o_n), ("s_xo%d" % (i % 2), sld),
                       ("s_o1%s" % "ab"[i % 2], rec_o1st[i - 2] if i >= 2 else 0),
                       ("pe_o2", rec_o1pe.get(i - 2, 0)),
                       ("act_sq2", max(0, i - 1))],
                inc="dve_o1")
        d_o1 = w.at("dve_o1")
        w.op("scalar", lambda E, i=i: E.activation(
            sq2_sb[i % 2][:, :], o1_sb[i % 2][:, :], AF.Square),
            waits=[("dve_o1", d_o1), ("pe_o2", max(0, i - 1))], inc="act_sq2")
        a_sq2 = w.at("act_sq2")
        for th in range(TH):
            w.op("tensor", lambda E, i=i, th=th: E.matmul(
                pb[4 + th][0:1, 0:TN], onesp_f[:, :],
                o1_sb[i % 2][:, th * TN:(th + 1) * TN],
                start=(i == 0), stop=(i == CH - 1)),
                waits=[("dve_o1", d_o1)])
            w.op("tensor", lambda E, i=i, th=th: E.matmul(
                pb[6 + th][0:1, 0:TN], ones_bf[:, :],
                sq2_sb[i % 2][:, th * TN:(th + 1) * TN],
                start=(i == 0), stop=(i == CH - 1)),
                waits=[("act_sq2", a_sq2)],
                inc=("pe_o2" if th == TH - 1 else None))
        rec_o1pe[i] = w.at("pe_o2")
        stn = w.op("sync", lambda E, i=i: E.dma_start(
            out=out1_scr[i * 128:(i + 1) * 128, :], in_=o1_sb[i % 2][:, :]),
            waits=[("dve_o1", d_o1)], inc="s_o1%s" % "ab"[i % 2], inck=16)
        rec_o1st.append(stn)
        rec_wofree[i % 2] = w.at("pe_o")
    P3_PEO2 = w.at("pe_o2")
    P3_O1A, P3_O1B = w.at("s_o1a"), w.at("s_o1b")

    m2_sl = stat2[0:1, 0:TC]
    q2_sl = stat2[0:1, TC:2 * TC]
    r2_sl = stat2[0:1, 2 * TC:3 * TC]
    t2_sl = stat2[0:1, 3 * TC:4 * TC]
    for th in range(TH):
        w.op("scalar", lambda E, th=th: E.mul(
            m2_sl[0:1, th * TN:(th + 1) * TN], pb[4 + th][0:1, 0:TN], 1.0 / D),
            waits=[("pe_o2", P3_PEO2)])
        w.op("scalar", lambda E, th=th: E.mul(
            q2_sl[0:1, th * TN:(th + 1) * TN], pb[6 + th][0:1, 0:TN], 1.0 / D),
            inc=("act_sq2" if th == TH - 1 else None))
    A_MV2 = w.at("act_sq2")
    w.op("vector", lambda E: E.tensor_mul(t2_sl, m2_sl, m2_sl),
         waits=[("act_sq2", A_MV2)], inc="dve_x2")
    w.op("vector", lambda E: E.tensor_sub(r2_sl, q2_sl, t2_sl),
         waits=[("dve_x2", w.at("dve_x2"))], inc="dve_x2")
    w.op("vector", lambda E: E.tensor_scalar_add(t2_sl, r2_sl, 1e-5),
         waits=[("dve_x2", w.at("dve_x2"))], inc="dve_x2")
    D_MV2 = w.at("dve_x2")
    w.op("scalar", lambda E: E.activation(q2_sl, t2_sl, AF.Ln),
         waits=[("dve_x2", D_MV2)], inc="act_sq2")
    w.op("scalar", lambda E: E.activation(r2_sl, q2_sl, AF.Exp, scale=-0.5),
         waits=[("act_sq2", w.at("act_sq2"))], inc="act_sq2")
    A_MV2 = w.at("act_sq2")
    for th in range(TH):
        w.op("tensor", lambda E, th=th: E.matmul(
            pb[th][:, 0:TN], ones_f[:, :], m2_sl[0:1, th * TN:(th + 1) * TN],
            start=True, stop=True),
            waits=[("act_sq2", A_MV2), ("dve_o1", TH * CH)])
        w.op("tensor", lambda E, th=th: E.matmul(
            pb[2 + th][:, 0:TN], ones_f[:, :], r2_sl[0:1, th * TN:(th + 1) * TN],
            start=True, stop=True),
            inc=("pe_o" if th == TH - 1 else None))
    PE_BC2 = w.at("pe_o")
    for th in range(TH):
        w.op("scalar", lambda E, th=th: E.copy(
            mb2_sb[:, th * TN:(th + 1) * TN], pb[th][:, 0:TN]),
            waits=[("pe_o", PE_BC2)])
        w.op("scalar", lambda E, th=th: E.copy(
            rb2_sb[:, th * TN:(th + 1) * TN], pb[2 + th][:, 0:TN]),
            inc=("act_sq2" if th == TH - 1 else None))
    A_BC2 = w.at("act_sq2")

    X2B = w.at("dve_x2")
    for i in range(CH):
        w.op("sync", lambda E, i=i: E.dma_start(
            out=o1r_sb[i % 2][:, :], in_=out1_scr[i * 128:(i + 1) * 128, :]),
            waits=[("s_o1a", P3_O1A), ("s_o1b", P3_O1B),
                   ("dve_x2", max(0, X2B + 2 * (i - 1)))],
            inc="s_or%d" % (i % 2), inck=16)
        sld = w.at("s_or%d" % (i % 2))
        w.op("vector", lambda E, i=i: E.tensor_sub(
            t2_sb[:, :], o1r_sb[i % 2][:, :], mb2_sb[:, :]),
            waits=[("s_or%d" % (i % 2), sld), ("act_sq2", A_BC2),
                   ("dve_x2", X2B + 2 * i)],
            inc="dve_x2")
        w.op("vector", lambda E, i=i: E.tensor_mul(
            xs2_sb[:, i * TC:(i + 1) * TC], t2_sb[:, :], rb2_sb[:, :]),
            waits=[("dve_x2", X2B + 2 * i + 1)],
            inc="dve_x2")
    P3_X2 = w.at("dve_x2")

    P4_FENCE = [("pe_o2", P3_PEO2), ("dve_x2", P3_X2), ("pe_o", PE_BC2),
                ("act_sq2", A_BC2), ("s_o1a", P3_O1A), ("s_o1b", P3_O1B)]
    WG_FENCE = [("pe_o", PE_BC2), ("dve_o1", TH * CH), ("act_sq2", A_BC2)]

    def xs2_ready(ci):
        return X2B + 2 * ci + 2

    rd_srcs = [out1_scr, da_scr, db_scr]
    st_dsts = [da_scr, db_scr, None]
    st_name = ["s_da", "s_db", "s_ou"]
    rec_dd = [0, 0]
    rec_fwg = [0, 0]
    rec_wdg = [0, 0]
    rec_dast = {}
    prev_tot = None
    gfc = 0
    for sw in range(SW):
        NF = FCL[sw]
        fb = sum(FCL[:sw])
        for fc in range(NF):
            w.op("gpsimd", lambda E, gfc=gfc: E.dma_start(
                out=wg_sb[gfc % 2][:, :], in_=wg[gfc, :, :]),
                waits=WG_FENCE + [("pe_f", max(0, 2 * TH * (gfc - 1)))],
                inc="g_fw%d" % (gfc % 2), inck=16)
            w.op("gpsimd", lambda E, gfc=gfc: E.dma_start(
                out=wu_sb[gfc % 2][:, :], in_=wu[gfc, :, :]),
                inc="g_fw%d" % (gfc % 2), inck=16)
            rec_fwg[gfc % 2] = w.at("g_fw%d" % (gfc % 2))
            base = 4 * (gfc % 2)
            pf0 = w.at("pe_f")
            for role in range(2):
                wsb = wg_sb[gfc % 2] if role == 0 else wu_sb[gfc % 2]
                for th in range(TH):
                    bank = base + 2 * role + th
                    for ci in range(CH):
                        fw = [("g_fw%d" % (gfc % 2), rec_fwg[gfc % 2]),
                              ("dve_x2", xs2_ready(ci)),
                              ("act_sl", max(0, TH * (gfc - 1))),
                              ("dve_h", max(0, TH * (gfc - 1))),
                              ("dve_d", rec_dd[0] if fc == 0 else 0)]
                        w.op("tensor", lambda E, ci=ci, th=th, bank=bank,
                             wsb=wsb: E.matmul(
                            pb[bank][:, 0:TN],
                            wsb[:, ci * 128:(ci + 1) * 128],
                            xs2_sb[:, ci * TC + th * TN: ci * TC + (th + 1) * TN],
                            start=(ci == 0), stop=(ci == CH - 1)),
                            waits=fw,
                            inc=("pe_f" if ci == CH - 1 else None))
            for th in range(TH):
                w.op("scalar", lambda E, th=th, base=base, gfc=gfc: E.activation(
                    sg_sb[gfc % 2][:, th * TN:(th + 1) * TN],
                    pb[base + th][:, 0:TN], AF.Silu),
                    waits=[("pe_f", pf0 + th + 1),
                           ("dve_h", max(0, TH * (gfc - 2) + TH))],
                    inc="act_sl")
            asl = w.at("act_sl")
            for th in range(TH):
                w.op("vector", lambda E, th=th, base=base, gfc=gfc, fc=fc: E.tensor_mul(
                    hA_sb[:, fc * TC + th * TN: fc * TC + (th + 1) * TN],
                    pb[base + 2 + th][:, 0:TN],
                    sg_sb[gfc % 2][:, th * TN:(th + 1) * TN]),
                    waits=[("pe_f", pf0 + TH + th + 1), ("act_sl", asl),
                           ("pe_d", rec_dd[1] if (sw >= 1 and fc < 2) else 0)],
                    inc="dve_h")
            gfc += 1
        DVE_H_S = w.at("dve_h")

        for dmc in range(CH):
            dg = sw * CH + dmc
            stn = w.op("gpsimd", lambda E, dmc=dmc, fb=fb, NF=NF, dg=dg: E.dma_start(
                out=wd_sb[dg % 2][:, 0:NF * 128],
                in_=wd[dmc, :, fb * 128:(fb + NF) * 128]),
                waits=[("pe_d", max(0, TH * (dg - 1)))],
                inc="g_wd%d" % (dg % 2), inck=16)
            rec_wdg[dg % 2] = stn
            if sw == 0:
                w.op("sync", lambda E, dmc=dmc, dg=dg: E.dma_start(
                    out=o2r_sb[dg % 2][:, :],
                    in_=rd_srcs[0][dmc * 128:(dmc + 1) * 128, :]),
                    waits=P4_FENCE + [("dve_d", max(0, TH * (dg - 1)))],
                    inc="s_r2%d" % (dg % 2), inck=16)
            else:
                w.op("sync", lambda E, dmc=dmc, dg=dg, sw=sw: E.dma_start(
                    out=o2r_sb[dg % 2][:, :],
                    in_=rd_srcs[sw][dmc * 128:(dmc + 1) * 128, :]),
                    waits=[(st_name[sw - 1] + "0", prev_tot[0]),
                           (st_name[sw - 1] + "1", prev_tot[1]),
                           ("dve_d", max(0, TH * (dg - 1)))],
                    inc="s_r2%d" % (dg % 2), inck=16)
            sld = w.at("s_r2%d" % (dg % 2))
            for th in range(TH):
                bank = (2 * dg + th) % 4 if TH > 1 else dg % 4
                for fc in range(NF):
                    w.op("tensor", lambda E, fc=fc, th=th, bank=bank, dg=dg,
                         NF=NF: E.matmul(
                        pb[bank][:, 0:TN],
                        wd_sb[dg % 2][:, fc * 128:(fc + 1) * 128],
                        hA_sb[:, fc * TC + th * TN: fc * TC + (th + 1) * TN],
                        start=(fc == 0), stop=(fc == NF - 1)),
                        waits=[("g_wd%d" % (dg % 2), rec_wdg[dg % 2]),
                               ("dve_h", DVE_H_S),
                               ("dve_d", max(0, TH * (dg - 2) + th + 1))],
                        inc=("pe_d" if fc == NF - 1 else None))
                pd = w.at("pe_d")
                ds_sem, ds_cnt = rec_dast.get(dg - 2, ("s_da0", 0))
                w.op("vector", lambda E, th=th, bank=bank, dg=dg: E.tensor_add(
                    dat_sb[dg % 2][:, th * TN:(th + 1) * TN], pb[bank][:, 0:TN],
                    o2r_sb[dg % 2][:, th * TN:(th + 1) * TN]),
                    waits=[("pe_d", pd), ("s_r2%d" % (dg % 2), sld),
                           (ds_sem, ds_cnt)],
                    inc="dve_d")
            dd = w.at("dve_d")
            sname = "%s%d" % (st_name[sw], dg % 2)
            if sw < SW - 1:
                stn = w.op("sync", lambda E, dmc=dmc, dg=dg, sw=sw: E.dma_start(
                    out=st_dsts[sw][dmc * 128:(dmc + 1) * 128, :],
                    in_=dat_sb[dg % 2][:, :]),
                    waits=[("dve_d", dd)], inc=sname, inck=16)
            else:
                stn = w.op("sync", lambda E, dmc=dmc, dg=dg: E.dma_start(
                    out=out_fm[dmc * 128:(dmc + 1) * 128, :],
                    in_=dat_sb[dg % 2][:, :]),
                    waits=[("dve_d", dd)], inc=sname, inck=16)
            rec_dast[dg] = (sname, stn)
        rec_dd = [w.at("dve_d"), w.at("pe_d")]
        prev_tot = [w.at(st_name[sw] + "0"), w.at(st_name[sw] + "1")]

    w.op("sync", lambda E: E.nop(),
         waits=[("s_ou0", w.at("s_ou0")), ("s_ou1", w.at("s_ou1"))])

    with nc.Block() as block:
        @block.tensor
        def _(E):
            w.emit("tensor", E, sems)

        @block.scalar
        def _(E):
            w.emit("scalar", E, sems)

        @block.vector
        def _(E):
            w.emit("vector", E, sems)

        @block.sync
        def _(E):
            w.emit("sync", E, sems)

        @block.gpsimd
        def _(E):
            w.emit("gpsimd", E, sems)

    stack.pop_all()
    return nc


def prep_inputs(cfg, x, w_q, w_k, w_v, w_o, w_gate, w_up, w_down):
    B, S, D, HKV, G, HD, F = (cfg[k] for k in ("B", "S", "D", "HKV", "G", "HD", "F"))
    CH, TCH, NCHUNK, DIAG, QB = (cfg[k] for k in ("CH", "TCH", "NCHUNK", "DIAG", "QB"))
    UNITS, DQL, WC, TC, FC = (cfg[k] for k in ("UNITS", "DQL", "WC", "TC", "FC"))

    f32 = np.float32
    x = np.asarray(x, dtype=f32)
    w_q = np.asarray(w_q, dtype=f32)
    w_k = np.asarray(w_k, dtype=f32)
    w_v = np.asarray(w_v, dtype=f32)
    w_o = np.asarray(w_o, dtype=f32)
    w_gate = np.asarray(w_gate, dtype=f32)
    w_up = np.asarray(w_up, dtype=f32)
    w_down = np.asarray(w_down, dtype=f32)

    wo_r = np.ascontiguousarray(
        w_o.reshape(CH, 128, CH, 128).transpose(2, 1, 0, 3).reshape(CH, 128, CH * 128))
    wgT = w_gate.T
    wuT = w_up.T
    wdT = w_down.T
    wg_r = np.ascontiguousarray(
        wgT.reshape(CH, 128, FC, 128).transpose(2, 1, 0, 3).reshape(FC, 128, CH * 128))
    wu_r = np.ascontiguousarray(
        wuT.reshape(CH, 128, FC, 128).transpose(2, 1, 0, 3).reshape(FC, 128, CH * 128))
    wd_r = np.ascontiguousarray(
        wdT.reshape(FC, 128, CH, 128).transpose(2, 1, 0, 3).reshape(CH, 128, FC * 128))

    masks = np.zeros((128, DIAG * QB), dtype=ml_dtypes.bfloat16)
    for r in range(DIAG):
        for p in range(128):
            q0 = r * 128 + p
            if q0 < QB:
                masks[p, r * QB + q0:(r + 1) * QB] = 1.0

    xseq_b = []
    for b in range(B):
        xb = np.ascontiguousarray(x[b].T)
        xr = np.ascontiguousarray(
            xb.reshape(CH, 128, NCHUNK, TCH).transpose(2, 1, 0, 3)
            .reshape(NCHUNK, 128, CH * TCH))
        xseq_b.append(xr)

    in_maps = []
    for c in range(8):
        b, j = c // 4, c % 4
        wq_sl = np.ascontiguousarray(
            w_q[:, j * DQL:(j + 1) * DQL].reshape(CH, 128, DQL)
            .transpose(1, 0, 2).reshape(128, CH * DQL))
        h0 = UNITS * j
        kcols = w_k[:, h0 * 128:(h0 + UNITS) * 128]
        vcols = w_v[:, h0 * 128:(h0 + UNITS) * 128]
        wkv_sl = np.concatenate([kcols, vcols], axis=1)
        wkv_sl = np.ascontiguousarray(
            wkv_sl.reshape(CH, 128, WC).transpose(1, 0, 2).reshape(128, CH * WC))
        xown = np.ascontiguousarray(x[b, j * TC:(j + 1) * TC, :].T)
        selv = np.array([[j * 8 * DQL * TC + 4 * b * DQL * TC]], dtype=np.int32)
        in_maps.append({
            "xseq": xseq_b[b], "xown": xown, "wq": wq_sl, "wkv": wkv_sl,
            "wo": wo_r, "wg": wg_r, "wu": wu_r, "wd": wd_r,
            "masks": masks, "sel": selv,
        })
    return in_maps


def assemble_output(cfg, results):
    B, S, D, TC = cfg["B"], cfg["S"], cfg["D"], cfg["TC"]
    out = np.empty((B, S, D), dtype=np.float32)
    for c in range(8):
        b, j = c // 4, c % 4
        out[b, j * TC:(j + 1) * TC, :] = results[c]["out_fm"].T
    return out


def kernel(L0_attn_hidden_states, L0_attn_w_q, L0_attn_w_k, L0_attn_w_v,
           L0_attn_w_o, L0_ffn_W_gate, L0_ffn_W_up, L0_ffn_W_down):
    cfg = make_cfg(small=False)
    in_maps = prep_inputs(cfg, L0_attn_hidden_states, L0_attn_w_q, L0_attn_w_k,
                          L0_attn_w_v, L0_attn_w_o, L0_ffn_W_gate, L0_ffn_W_up,
                          L0_ffn_W_down)
    nc = build_program(cfg)
    res = run_bass_kernel_spmd(nc, in_maps, core_ids=list(range(8)))
    return assemble_output(cfg, res.results)


# version 07
# speedup vs baseline: 1.3437x; 1.0231x over previous
import numpy as np
import ml_dtypes
import concourse.bass as bass
import concourse.mybir as mybir
from concourse.bass_utils import run_bass_kernel_spmd

F32 = mybir.dt.float32
BF16 = mybir.dt.bfloat16
U8 = mybir.dt.uint8
I32 = mybir.dt.int32
AF = mybir.ActivationFunctionType
ALU = mybir.AluOpType

ENGS = ("tensor", "scalar", "vector", "sync", "gpsimd")


def make_cfg(small=False):
    if small:
        c = dict(B=2, S=1024, D=1024, HKV=8, G=1, HD=128, F=1536)
    else:
        c = dict(B=2, S=4096, D=4096, HKV=8, G=4, HD=128, F=11008)
    c["CH"] = c["D"] // 128
    c["QB"] = c["S"] // 8
    c["TCH"] = c["QB"]
    c["NCHUNK"] = c["S"] // c["TCH"]
    c["DIAG"] = c["QB"] // 128
    c["UNITS"] = c["B"] * c["HKV"] // 8
    c["DQL"] = c["UNITS"] * c["G"] * 128
    c["QNC"] = c["DQL"] // 128
    c["WC"] = c["UNITS"] * 128 * 2
    c["TC"] = c["B"] * c["S"] // 8
    c["TH"] = max(1, c["TC"] // 512)
    c["TN"] = c["TC"] // c["TH"]
    c["FC"] = c["F"] // 128
    c["SW"] = 3
    base = c["FC"] // c["SW"]
    rem = c["FC"] - base * c["SW"]
    c["FCL"] = [base + (1 if i < rem else 0) for i in range(c["SW"])]
    c["FCMAX"] = max(c["FCL"])
    c["SCALE"] = float(c["HD"]) ** -0.5
    return c


class Walk:

    def __init__(self):
        self.ops = {e: [] for e in ENGS}
        self.cnt = {}

    def at(self, sem):
        return self.cnt.get(sem, 0)

    def op(self, eng, emit, waits=(), inc=None, inck=1):
        ws = tuple((s, v) for (s, v) in waits if v > 0)
        self.ops[eng].append((emit, ws, inc, inck))
        if inc is not None:
            self.cnt[inc] = self.cnt.get(inc, 0) + inck
        return self.cnt.get(inc, 0) if inc else None

    def emit(self, eng_name, eng, sems):
        last = {}
        for (emit, waits, inc, inck) in self.ops[eng_name]:
            for (s, v) in waits:
                if last.get(s, -1) >= v:
                    continue
                eng.wait_ge(sems[s], v)
                last[s] = v
            ins = emit(eng)
            if inc is not None:
                ins.then_inc(sems[inc], inck)


def build_program(cfg):
    B, S, D, HKV, G, HD, F = (cfg[k] for k in ("B", "S", "D", "HKV", "G", "HD", "F"))
    CH, QB, TCH, NCHUNK, DIAG = (cfg[k] for k in ("CH", "QB", "TCH", "NCHUNK", "DIAG"))
    UNITS, DQL, QNC, WC = (cfg[k] for k in ("UNITS", "DQL", "QNC", "WC"))
    TC, TH, TN, FC = (cfg[k] for k in ("TC", "TH", "TN", "FC"))
    SW, FCL, FCMAX = cfg["SW"], cfg["FCL"], cfg["FCMAX"]
    SCALE = cfg["SCALE"]
    VSUB = TCH // 128

    from contextlib import ExitStack
    stack = ExitStack()
    nc = bass.Bass("TRN2", target_bir_lowering=False)

    xseq = nc.dram_tensor("xseq", [NCHUNK, 128, CH * TCH], F32, kind="ExternalInput").ap()
    xown = nc.dram_tensor("xown", [D, TC], F32, kind="ExternalInput").ap()
    wq = nc.dram_tensor("wq", [128, CH * DQL], F32, kind="ExternalInput").ap()
    wkv = nc.dram_tensor("wkv", [128, CH * WC], F32, kind="ExternalInput").ap()
    wo = nc.dram_tensor("wo", [CH, 128, CH * 128], F32, kind="ExternalInput").ap()
    wg = nc.dram_tensor("wg", [FC, 128, CH * 128], F32, kind="ExternalInput").ap()
    wu = nc.dram_tensor("wu", [FC, 128, CH * 128], F32, kind="ExternalInput").ap()
    wd = nc.dram_tensor("wd", [CH, 128, FC * 128], F32, kind="ExternalInput").ap()
    masks_in = nc.dram_tensor("masks", [128, DIAG * QB], BF16, kind="ExternalInput").ap()
    sel = nc.dram_tensor("sel", [1, 1], I32, kind="ExternalInput").ap()
    out_fm = nc.dram_tensor("out_fm", [D, TC], F32, kind="ExternalOutput").ap()

    q_scr = nc.dram_tensor("q_scr", [DQL, S], BF16).ap()
    k_scr = nc.dram_tensor("k_scr", [UNITS, 128, S], BF16).ap()
    v_scr = nc.dram_tensor("v_scr", [UNITS, 128, S], BF16).ap()
    attn_scr_h = nc.dram_tensor("attn_scr", [8, DQL, QB], BF16)
    gath_scr_h = nc.dram_tensor("gath_scr", [8, 8 * DQL, QB], BF16,
                                addr_space="Shared")
    attn_scr = attn_scr_h.ap()
    iv_scr_h = nc.dram_tensor("iv_scr", [2, 1, QB], F32)
    iv_scr = iv_scr_h.ap()
    out1_scr = nc.dram_tensor("out1_scr", [D, TC], F32).ap()
    da_scr = nc.dram_tensor("da_scr", [D, TC], F32).ap()
    db_scr = nc.dram_tensor("db_scr", [D, TC], F32).ap()

    ones_bf = stack.enter_context(nc.sbuf_tensor("ones_bf", [128, 1], BF16))
    onesp_f = stack.enter_context(nc.sbuf_tensor("onesp_f", [128, 1], F32))
    ones_f = stack.enter_context(nc.sbuf_tensor("ones_f", [1, 128], F32))
    sel_sb = stack.enter_context(nc.sbuf_tensor("sel_sb", [1, 1], I32))

    def roundup(x, a=32):
        return (x + a - 1) // a * a

    class Arena:
        def __init__(self):
            cap = nc.sbuf_bytes_remaining - 64
            self.big = stack.enter_context(
                nc.sbuf_tensor("arena", [nc.NUM_PARTITIONS, cap], U8))
            self.base = nc.lookup_mloc(self.big).addr
            self.cap = cap
            self.off = 0
            self.maxoff = 0

        def take(self, name, shape, dtype):
            per = int(np.prod(shape[1:])) * mybir.dt.size(dtype)
            o = self.off
            self.off = roundup(self.off + per)
            self.maxoff = max(self.maxoff, self.off)
            assert self.off <= self.cap, (name, self.off, self.cap)
            return nc.alloc_sbuf_tensor_at(name, list(shape), dtype,
                                           offset=self.base + o)

        def reset(self):
            self.off = 0

    ar = Arena()
    wq_sb = ar.take("wq_sb", [128, CH * DQL], BF16)
    wkv_sb = ar.take("wkv_sb", [128, CH * WC], BF16)
    xc_sb = ar.take("xc_sb", [128, CH * TCH], BF16)
    xs_sb = [ar.take(f"xs{i}", [128, CH * TCH], BF16) for i in range(2)]
    sq_sb = [ar.take(f"sq{i}", [128, TCH], BF16) for i in range(2)]
    EVW = max(TCH, UNITS * 128)
    ev_sb = [ar.take(f"ev{i}", [128, EVW], BF16) for i in range(2)]
    t1_sb = ar.take("t1_sb", [128, TCH], BF16)
    mb_sb = ar.take("mb_sb", [128, TCH], BF16)
    rb_sb = ar.take("rb_sb", [128, TCH], BF16)
    stat1 = ar.take("stat1", [1, 4 * TCH], F32)
    ar.reset()
    qt_sb = [ar.take(f"qt{u}", [128, G * S], BF16) for u in range(UNITS)]
    kt_sb = [ar.take(f"kt{u}", [128, S], BF16) for u in range(UNITS)]
    vt_sb = [ar.take(f"vt{u}", [128, S], BF16) for u in range(UNITS)]
    pr_sb = [ar.take(f"pr{i}", [128, 2 * QB], BF16) for i in range(2)]
    pm_sb = [ar.take(f"pm{i}", [128, 2 * QB], BF16) for i in range(2)]
    sp_sb = [ar.take(f"sp{i}", [128, QB], BF16) for i in range(2)]
    ib_sb = [ar.take(f"ib{i}", [128, QB], F32) for i in range(2)]
    aev_sb = [ar.take(f"aev{i}", [128, QB], BF16) for i in range(2)]
    inv_sb = [ar.take(f"inv{i}", [1, QB], F32) for i in range(2)]
    masks_sb = ar.take("masks_sb", [128, DIAG * QB], BF16)
    P2_END = ar.off
    ar.reset()
    xs2_sb = ar.take("xs2_sb", [128, CH * TC], BF16)
    xs2_end = ar.off
    stat2 = ar.take("stat2", [1, 4 * TC], F32)
    o1_sb = [ar.take(f"o1{i}", [128, TC], F32) for i in range(2)]
    sq2_sb = [ar.take(f"sq2{i}", [128, TC], BF16) for i in range(2)]
    xo1_off = ar.off
    xo_sb = [None, ar.take("xo1", [128, TC], F32)]
    ar.off = xs2_end
    hA_sb = ar.take("hA_sb", [128, FCMAX * TC], BF16)
    HA_END = ar.off
    ar.off = P2_END
    wo_sb = [ar.take(f"wo{i}", [128, CH * 128], BF16) for i in range(2)]
    xo_sb[0] = ar.take("xo0", [128, TC], F32)
    E_END = ar.off
    ar.off = max(P2_END, HA_END)
    wg_sb = [ar.take(f"wgs{i}", [128, CH * 128], BF16) for i in range(2)]
    wu_sb = [ar.take(f"wus{i}", [128, CH * 128], BF16) for i in range(2)]
    wd_sb = [ar.take(f"wds{i}", [128, FCMAX * 128], BF16) for i in range(2)]
    sg_sb = [ar.take(f"sg{i}", [128, TC], BF16) for i in range(2)]
    o2r_sb = [ar.take(f"o2r{i}", [128, TC], F32) for i in range(2)]
    dat_sb = [ar.take(f"dat{i}", [128, TC], F32) for i in range(2)]
    C_END = ar.off
    ar.off = E_END
    AO_OFF = ar.off
    ao_sb = ar.take("ao_sb", [128, CH * TC], BF16)
    AO_END = ar.off
    ao3_sb = nc.alloc_sbuf_tensor_at("ao3", [128, CH, TC], BF16,
                                     offset=ar.base + AO_OFF)
    ar.off = max(C_END, AO_END - 16384)
    o1r_sb = [ar.take(f"o1r{i}", [128, TC], F32) for i in range(2)]
    t2_sb = ar.take("t2_sb", [128, TC], BF16)
    mb2_sb = ar.take("mb2_sb", [128, TC], BF16)
    rb2_sb = ar.take("rb2_sb", [128, TC], BF16)

    sc_ps = [stack.enter_context(nc.psum_tensor(f"sc{i}", [128, 1024], F32))
             for i in range(2)]
    sm_ps = [stack.enter_context(nc.psum_tensor(f"sm{i}", [128, 512], F32))
             for i in range(4)]

    class PBView:
        def __init__(self, h, coff):
            self.h, self.coff = h, coff

        def __getitem__(self, idx):
            r, c = idx
            return self.h[r, c.start + self.coff:c.stop + self.coff]

    pb = [PBView(sc_ps[0], 0), PBView(sc_ps[0], 512),
          PBView(sc_ps[1], 0), PBView(sc_ps[1], 512),
          PBView(sm_ps[0], 0), PBView(sm_ps[1], 0),
          PBView(sm_ps[2], 0), PBView(sm_ps[3], 0)]

    semnames = ["g_w0", "g_w1", "g_w2", "g_w3", "g_wkv", "g_x", "g_ao", "g_wo0", "g_wo1", "g_fw0", "g_fw1",
                "g_wd0", "g_wd1",
                "s_ms", "s_kv", "s_xo0", "s_xo1", "s_or0", "s_or1",
                "s_r20", "s_r21", "s_q", "g_pre",
                "s_o1a", "s_o1b",
                "s_da0", "s_da1", "s_db0", "s_db1", "s_ou0", "s_ou1",
                "a_q0", "a_q1",
                "cc", "g_ms",
                "act_sq", "pe_st", "act_mv", "dve_mv", "pe_bc", "act_bc",
                "dve_x", "dve_xs", "pe_mm", "act_ev",
                "pe_s", "act_pr", "pe_av", "dve2", "v_iv",
                "pe_o", "dve_o1", "act_sq2", "pe_o2", "dve_x2",
                "pe_f", "act_sl", "dve_h", "pe_d", "dve_d"]
    sems = {n: stack.enter_context(nc.semaphore(n)) for n in semnames}

    w = Walk()

    w.op("gpsimd", lambda E: E.memset(ones_bf[:, :], 1.0))
    w.op("gpsimd", lambda E: E.memset(onesp_f[:, :], 1.0))
    w.op("gpsimd", lambda E: E.memset(ones_f[:, :], 1.0), inc="g_ms")
    w.op("sync", lambda E: E.dma_start(out=sel_sb[:, :], in_=sel[:, :]),
         inc="s_ms", inck=16)
    NSP = 4

    rec_xsfree = [0, 0]
    rec_ev = []
    evst = {"evg": 0}
    n_groups = QNC + UNITS + VSUB

    def p1_stats(t):
        w.op("gpsimd", lambda E, t=t: E.dma_start(out=xc_sb[:, :], in_=xseq[t, :, :]),
             waits=[("act_sq", t * CH), ("pe_st", t * CH), ("dve_x", t * CH)],
             inc="g_x", inck=16)
        ld_n = w.at("g_x")
        if t == 0:
            for i in range(NSP):
                lo, hi = i * (CH // NSP), (i + 1) * (CH // NSP)
                w.op("gpsimd", lambda E, lo=lo, hi=hi: E.dma_start(
                    out=wq_sb[:, lo * DQL:hi * DQL],
                    in_=wq[:, lo * DQL:hi * DQL]),
                    inc="g_w%d" % i, inck=16)
            w.op("gpsimd", lambda E: E.dma_start(out=wkv_sb[:, :], in_=wkv[:, :]),
                 inc="g_wkv", inck=16)
        for ci in range(CH):
            w.op("scalar", lambda E, ci=ci: E.activation(
                sq_sb[ci % 2][:, :], xc_sb[:, ci * TCH:(ci + 1) * TCH], AF.Square),
                waits=[("g_x", ld_n), ("pe_st", t * CH + ci - 1)],
                inc="act_sq")
            w.op("tensor", lambda E, ci=ci: E.matmul(
                pb[6][0:1, 0:TCH], ones_bf[:, :], xc_sb[:, ci * TCH:(ci + 1) * TCH],
                start=(ci == 0), stop=(ci == CH - 1)),
                waits=[("g_x", ld_n), ("act_mv", 3 * t), ("g_ms", 1)])
            w.op("tensor", lambda E, ci=ci: E.matmul(
                pb[7][0:1, 0:TCH], ones_bf[:, :], sq_sb[ci % 2][:, :],
                start=(ci == 0), stop=(ci == CH - 1)),
                waits=[("act_sq", t * CH + ci + 1)],
                inc="pe_st")
        pe_st_n = w.at("pe_st")

        m_sl = stat1[0:1, 0:TCH]
        q_sl = stat1[0:1, TCH:2 * TCH]
        r_sl = stat1[0:1, 2 * TCH:3 * TCH]
        tp_sl = stat1[0:1, 3 * TCH:4 * TCH]
        w.op("scalar", lambda E: E.mul(m_sl, pb[6][0:1, 0:TCH], 1.0 / D),
             waits=[("pe_st", pe_st_n)])
        w.op("scalar", lambda E: E.mul(q_sl, pb[7][0:1, 0:TCH], 1.0 / D),
             inc="act_mv")
        a_mv = w.at("act_mv")
        w.op("vector", lambda E: E.tensor_mul(tp_sl, m_sl, m_sl),
             waits=[("act_mv", a_mv)], inc="dve_mv")
        w.op("vector", lambda E: E.tensor_sub(r_sl, q_sl, tp_sl),
             waits=[("dve_mv", w.at("dve_mv"))], inc="dve_mv")
        w.op("vector", lambda E: E.tensor_scalar_add(tp_sl, r_sl, 1e-5),
             waits=[("dve_mv", w.at("dve_mv"))], inc="dve_mv")
        d_mv = w.at("dve_mv")
        w.op("scalar", lambda E: E.activation(q_sl, tp_sl, AF.Ln),
             waits=[("dve_mv", d_mv)], inc="act_mv")
        w.op("scalar", lambda E: E.activation(r_sl, q_sl, AF.Exp, scale=-0.5),
             waits=[("act_mv", w.at("act_mv"))], inc="act_mv")
        a_mv = w.at("act_mv")
        w.op("tensor", lambda E: E.matmul(pb[4][:, 0:TCH], ones_f[:, :], m_sl,
                                          start=True, stop=True),
             waits=[("act_mv", a_mv), ("act_bc", 2 * t)])
        w.op("tensor", lambda E: E.matmul(pb[5][:, 0:TCH], ones_f[:, :], r_sl,
                                          start=True, stop=True), inc="pe_bc")
        p_bc = w.at("pe_bc")
        w.op("scalar", lambda E: E.copy(mb_sb[:, :], pb[4][:, 0:TCH]),
             waits=[("pe_bc", p_bc), ("dve_xs", t * CH)])
        w.op("scalar", lambda E: E.copy(rb_sb[:, :], pb[5][:, 0:TCH]),
             inc="act_bc", inck=2)
        a_bc = w.at("act_bc")
        xs = xs_sb[t % 2]
        for ci in range(CH):
            w.op("vector", lambda E, ci=ci: E.tensor_sub(
                t1_sb[:, :], xc_sb[:, ci * TCH:(ci + 1) * TCH], mb_sb[:, :]),
                waits=[("act_bc", a_bc), ("pe_mm", rec_xsfree[t % 2]),
                       ("dve_xs", t * CH + ci)],
                inc="dve_x")
            w.op("vector", lambda E, ci=ci, xs=xs: E.tensor_mul(
                xs[:, ci * TCH:(ci + 1) * TCH], t1_sb[:, :], rb_sb[:, :]),
                waits=[("dve_x", t * CH + ci + 1)],
                inc="dve_xs")

    def p1_group(t, gi, kind, idx):
        xs = xs_sb[t % 2]
        gbase = t * n_groups
        evg = evst["evg"]
        if kind == "q":
            nq = idx
            bank = gi % 2
            for ci in range(CH):
                w.op("tensor", lambda E, ci=ci, nq=nq, bank=bank, xs=xs: E.matmul(
                    pb[bank][:, 0:TCH],
                    wq_sb[:, ci * DQL + nq * 128: ci * DQL + (nq + 1) * 128],
                    xs[:, ci * TCH:(ci + 1) * TCH],
                    start=(ci == 0), stop=(ci == CH - 1)),
                    waits=[("g_w%d" % (ci * NSP // CH), 16),
                           ("dve_xs", t * CH + ci + 1),
                           ("act_ev", gbase + gi - 1)],
                    inc=("pe_mm" if ci == CH - 1 else None))
            mm_n = w.at("pe_mm")
            w.op("scalar", lambda E, bank=bank, evg=evg: E.copy(
                ev_sb[evg % 2][:, 0:TCH], pb[bank][:, 0:TCH]),
                waits=[("pe_mm", mm_n),
                       ("a_q%d" % (evg % 2), rec_ev[evg - 2] if evg >= 2 else 0)],
                inc="act_ev")
            a_ev = w.at("act_ev")
            stn = w.op("scalar", lambda E, nq=nq, t=t, evg=evg: E.dma_start(
                out=q_scr[nq * 128:(nq + 1) * 128, t * TCH:(t + 1) * TCH],
                in_=ev_sb[evg % 2][:, 0:TCH]),
                waits=[("act_ev", a_ev)], inc="a_q%d" % (evg % 2), inck=16)
            rec_ev.append(stn)
        elif kind == "k":
            u = idx
            bank = gi % 2
            for ci in range(CH):
                w.op("tensor", lambda E, ci=ci, u=u, bank=bank, xs=xs: E.matmul(
                    pb[bank][:, 0:TCH],
                    wkv_sb[:, ci * WC + u * 128: ci * WC + (u + 1) * 128],
                    xs[:, ci * TCH:(ci + 1) * TCH],
                    start=(ci == 0), stop=(ci == CH - 1)),
                    waits=[("g_wkv", 16), ("dve_xs", t * CH + ci + 1),
                           ("act_ev", gbase + gi - 1)],
                    inc=("pe_mm" if ci == CH - 1 else None))
            mm_n = w.at("pe_mm")
            w.op("scalar", lambda E, bank=bank, evg=evg: E.copy(
                ev_sb[evg % 2][:, 0:TCH], pb[bank][:, 0:TCH]),
                waits=[("pe_mm", mm_n),
                       ("a_q%d" % (evg % 2), rec_ev[evg - 2] if evg >= 2 else 0)],
                inc="act_ev")
            a_ev = w.at("act_ev")
            stn = w.op("scalar", lambda E, u=u, t=t, evg=evg: E.dma_start(
                out=k_scr[u, :, t * TCH:(t + 1) * TCH], in_=ev_sb[evg % 2][:, 0:TCH]),
                waits=[("act_ev", a_ev)], inc="a_q%d" % (evg % 2), inck=16)
            rec_ev.append(stn)
        else:
            sub = idx
            bank = 2 + sub % 2
            vg = t * VSUB + sub
            for ci in range(CH):
                w.op("tensor", lambda E, ci=ci, sub=sub, bank=bank, xs=xs: E.matmul(
                    pb[bank][:, 0:UNITS * 128],
                    xs[:, ci * TCH + sub * 128: ci * TCH + (sub + 1) * 128],
                    wkv_sb[:, ci * WC + UNITS * 128:(ci + 1) * WC],
                    start=(ci == 0), stop=(ci == CH - 1)),
                    waits=[("g_wkv", 16), ("dve_xs", t * CH + ci + 1),
                           ("act_ev", gbase + gi - 1)],
                    inc=("pe_mm" if ci == CH - 1 else None))
            mm_n = w.at("pe_mm")
            w.op("scalar", lambda E, bank=bank, evg=evg: E.copy(
                ev_sb[evg % 2][:, 0:UNITS * 128], pb[bank][:, 0:UNITS * 128]),
                waits=[("pe_mm", mm_n),
                       ("a_q%d" % (evg % 2), rec_ev[evg - 2] if evg >= 2 else 0)],
                inc="act_ev")
            a_ev = w.at("act_ev")
            for u in range(UNITS):
                stn = w.op("scalar", lambda E, u=u, vg=vg, evg=evg: E.dma_start(
                    out=v_scr[u, :, vg * 128:(vg + 1) * 128],
                    in_=ev_sb[evg % 2][:, u * 128:(u + 1) * 128]),
                    waits=[("act_ev", a_ev)], inc="a_q%d" % (evg % 2), inck=16)
            rec_ev.append(stn)
        evst["evg"] += 1

    groups = ([("q", nq) for nq in range(QNC)] + [("k", u) for u in range(UNITS)]
              + [("v", sub) for sub in range(VSUB)])
    SPLIT = max(2, n_groups // 2)
    p1_stats(0)
    for t in range(NCHUNK):
        for gi, (kind, idx) in enumerate(groups[:SPLIT]):
            p1_group(t, gi, kind, idx)
        if t + 1 < NCHUNK:
            p1_stats(t + 1)
        for gi, (kind, idx) in enumerate(groups[SPLIT:], start=SPLIT):
            p1_group(t, gi, kind, idx)
        rec_xsfree[t % 2] = w.at("pe_mm")

    P1T = {n: w.at(n) for n in ("a_q0", "a_q1")}

    p1waits = [(n, v) for n, v in P1T.items()]
    P1_EV = w.at("act_ev")
    P1_BC = w.at("act_bc")
    P1_MV = w.at("act_mv")
    w.op("sync", lambda E: E.dma_start(out=masks_sb[:, :], in_=masks_in[:, :]),
         waits=p1waits, inc="s_ms", inck=16)
    S_MS = w.at("s_ms")
    kv_ready = []
    for u in range(UNITS):
        w.op("sync", lambda E, u=u: E.dma_start(
            out=qt_sb[u][:, :].rearrange("p (g t) -> p g t", g=G),
            in_=q_scr[u * G * 128:(u + 1) * G * 128, :].rearrange(
                "(g p) t -> p g t", p=128)),
            waits=p1waits, inc="s_kv", inck=16)
        w.op("sync", lambda E, u=u: E.dma_start(
            out=kt_sb[u][:, :], in_=k_scr[u, :, :]), inc="s_kv", inck=16)
        w.op("sync", lambda E, u=u: E.dma_start(
            out=vt_sb[u][:, :], in_=v_scr[u, :, :]), inc="s_kv", inck=16)
        kv_ready.append(w.at("s_kv"))
    w.op("gpsimd", lambda E: E.dma_start(out=wo_sb[0][:, :], in_=wo[0, :, :]),
         waits=p1waits, inc="g_pre", inck=16)
    w.op("gpsimd", lambda E: E.dma_start(out=wo_sb[1][:, :], in_=wo[1, :, :]),
         inc="g_pre", inck=16)
    w.op("gpsimd", lambda E: E.dma_start(out=xo_sb[0][:, :], in_=xown[0:128, :]),
         inc="g_pre", inck=16)
    G_PRE = w.at("g_pre")

    pairs = []
    pair_gidx = []
    gi = -1
    for qb in range(8):
        nkc = (qb + 1) * DIAG
        assert nkc % 2 == 0
        for u in range(UNITS):
            for g in range(G):
                for pi in range(nkc // 2):
                    if pi == 0:
                        gi += 1
                    pairs.append((qb, u, g, 2 * pi, nkc))
                    pair_gidx.append(gi)
    NPAIR = len(pairs)

    rec_exp, rec_avh, rec_suh, rec_dvp = [], [], [], []
    rec_recip, rec_ivst, rec_ivbc, rec_aevm, rec_store = {}, {}, {}, {}, {}
    rec_pmav = [0, 0]
    stash = {}
    st2 = {"dpi": 0}

    def p2_scores(pi):
        qb, u, g, kc, nkc = pairs[pi]
        b2 = pi % 2
        ws = [("s_kv", kv_ready[u])]
        if pi >= 2:
            ws.append(("act_pr", rec_exp[pi - 2]))
        else:
            ws.append(("act_ev", P1_EV))
        for h in range(2):
            w.op("tensor", lambda E, u=u, kc=kc, g=g, qb=qb, b2=b2, h=h: E.matmul(
                sc_ps[b2][:, h * QB:(h + 1) * QB],
                kt_sb[u][:, (kc + h) * 128:(kc + h + 1) * 128],
                qt_sb[u][:, g * S + qb * QB: g * S + (qb + 1) * QB],
                start=True, stop=True),
                waits=(ws if h == 0 else []),
                inc=("pe_s" if h == 1 else None))

    def p2_actdve(pi):
        qb, u, g, kc, nkc = pairs[pi]
        b2 = pi % 2
        diag = kc >= qb * DIAG
        ws = [("pe_s", pi + 1)]
        if pi >= 2:
            ws += [("pe_av", rec_avh[pi - 2]), ("dve2", rec_dvp[pi - 2])]
        if diag:
            r = kc - qb * DIAG
            pmb = st2["dpi"] % 2
            st2["dpi"] += 1
            w.op("scalar", lambda E, b2=b2: E.activation(
                pr_sb[b2][:, 0:QB], sc_ps[b2][:, 0:QB], AF.Exp, scale=SCALE),
                waits=ws, inc="act_pr")
            e_lo = w.at("act_pr")
            w.op("vector", lambda E, b2=b2, pmb=pmb, r=r: E.tensor_mul(
                pm_sb[pmb][:, 0:QB], pr_sb[b2][:, 0:QB],
                masks_sb[:, r * QB:(r + 1) * QB]),
                waits=[("act_pr", e_lo), ("s_ms", S_MS),
                       ("pe_av", rec_pmav[pmb])],
                inc="dve2")
            m_lo = w.at("dve2")
            w.op("scalar", lambda E, b2=b2: E.activation(
                pr_sb[b2][:, QB:2 * QB], sc_ps[b2][:, QB:2 * QB],
                AF.Exp, scale=SCALE), inc="act_pr")
            e_hi = w.at("act_pr")
            w.op("vector", lambda E, b2=b2, pmb=pmb, r=r: E.tensor_mul(
                pm_sb[pmb][:, QB:2 * QB], pr_sb[b2][:, QB:2 * QB],
                masks_sb[:, (r + 1) * QB:(r + 2) * QB]),
                waits=[("act_pr", e_hi)], inc="dve2")
            m_hi = w.at("dve2")
            rhs_t = pm_sb[pmb]
            avw = ([("dve2", m_lo)], [("dve2", m_hi)])
        else:
            pmb = None
            w.op("scalar", lambda E, b2=b2: E.activation(
                pr_sb[b2][:, :], sc_ps[b2][:, :], AF.Exp, scale=SCALE),
                waits=ws, inc="act_pr")
            e = w.at("act_pr")
            rhs_t = pr_sb[b2]
            avw = ([("act_pr", e)], [])
        rec_exp.append(w.at("act_pr"))
        paw = [] if diag else [("act_pr", rec_exp[pi])]
        if pi >= 2:
            paw.append(("pe_av", rec_suh[pi - 2]))
        w.op("vector", lambda E, rhs_t=rhs_t, s2=pi % 2: E.tensor_add(
            sp_sb[s2][:, :], rhs_t[:, 0:QB], rhs_t[:, QB:2 * QB]),
            waits=paw, inc="dve2")
        rec_dvp.append(w.at("dve2"))
        stash[pi] = (rhs_t, avw, pmb)

    def p2_av(pi):
        qb, u, g, kc, nkc = pairs[pi]
        gidx = pair_gidx[pi]
        par = gidx % 2
        rhs_t, avw, pmb = stash.pop(pi)
        for h in range(2):
            ws = list(avw[h])
            if kc == 0 and h == 0:
                ws.append(("dve2", rec_aevm[gidx - 2]) if gidx >= 2
                          else ("act_bc", P1_BC))
            w.op("tensor", lambda E, u=u, kc=kc, h=h, par=par, rhs_t=rhs_t,
                 nkc=nkc: E.matmul(
                sm_ps[par][:, 0:QB],
                vt_sb[u][:, (kc + h) * 128:(kc + h + 1) * 128],
                rhs_t[:, h * QB:(h + 1) * QB],
                start=(kc + h == 0), stop=(kc + h == nkc - 1)),
                waits=ws, inc=("pe_av" if h == 1 else None))
        rec_avh.append(w.at("pe_av"))
        if pmb is not None:
            rec_pmav[pmb] = w.at("pe_av")
        ws = [("dve2", rec_dvp[pi])]
        if kc == 0:
            ws.append(("dve2", rec_recip[gidx - 2]) if gidx >= 2
                      else ("act_mv", P1_MV))
        w.op("tensor", lambda E, par=par, s2=pi % 2, kc=kc, nkc=nkc: E.matmul(
            sm_ps[2 + par][0:1, 0:QB], ones_bf[:, :], sp_sb[s2][:, :],
            start=(kc == 0), stop=(kc == nkc - 2)),
            waits=ws, inc="pe_av")
        rec_suh.append(w.at("pe_av"))

    def p2_groupend(pi):
        qb, u, g, kc, nkc = pairs[pi]
        gidx = pair_gidx[pi]
        par = gidx % 2
        ws = [("pe_av", rec_suh[pi])]
        if gidx >= 2:
            ws.append(("v_iv", rec_ivst[gidx - 2]))
        w.op("vector", lambda E, par=par: E.reciprocal(
            inv_sb[par][:, :], sm_ps[2 + par][0:1, 0:QB]),
            waits=ws, inc="dve2")
        rec_recip[gidx] = w.at("dve2")
        w.op("sync", lambda E, par=par: E.dma_start(
            out=iv_scr[par, :, :], in_=inv_sb[par][:, :]),
            waits=[("dve2", rec_recip[gidx])], inc="v_iv", inck=16)
        rec_ivst[gidx] = w.at("v_iv")
        ws = [("v_iv", rec_ivst[gidx])]
        if gidx >= 2:
            ws.append(("dve2", rec_aevm[gidx - 2]))
        w.op("sync", lambda E, par=par: E.dma_start(
            out=ib_sb[par][:, :],
            in_=bass.AP(iv_scr_h, par * QB, [[0, 128], [1, QB]])),
            waits=ws, inc="v_iv", inck=16)
        rec_ivbc[gidx] = w.at("v_iv")
        ws = [("v_iv", rec_ivbc[gidx]), ("pe_av", rec_avh[pi])]
        if gidx >= 2:
            ws.append(("s_q", rec_store[gidx - 2]))
        w.op("vector", lambda E, par=par: E.tensor_mul(
            aev_sb[par][:, :], sm_ps[par][:, 0:QB], ib_sb[par][:, :]),
            waits=ws, inc="dve2")
        rec_aevm[gidx] = w.at("dve2")
        w.op("sync", lambda E, qb=qb, u=u, g=g, par=par: E.dma_start(
            out=attn_scr[qb, (u * G + g) * 128:(u * G + g + 1) * 128, :],
            in_=aev_sb[par][:, :]),
            waits=[("dve2", rec_aevm[gidx])], inc="s_q", inck=16)
        rec_store[gidx] = w.at("s_q")
        if u == UNITS - 1 and g == G - 1:
            w.op("gpsimd", lambda E, qb=qb: E.collective_compute(
                "AllGather", ALU.bypass, replica_groups=[list(range(8))],
                ins=[attn_scr_h.ap()[qb].opt()],
                outs=[gath_scr_h.ap()[qb].opt()]),
                waits=[("s_q", rec_store[gidx])], inc="cc")

    p2_scores(0)
    p2_actdve(0)
    for pi in range(1, NPAIR):
        p2_scores(pi)
        p2_actdve(pi)
        p2_av(pi - 1)
        if pairs[pi - 1][3] == pairs[pi - 1][4] - 2:
            p2_groupend(pi - 1)
    p2_av(NPAIR - 1)
    p2_groupend(NPAIR - 1)
    NG = w.at("cc")
    P2_ACT = w.at("act_pr")
    P2_DVE = w.at("dve2")

    gre = {"base": None, "tmp": None}
    GSPL = 4 if CH % 4 == 0 else 1
    CPS = CH // GSPL

    def p3_gather(E, k, h):
        if gre["base"] is None:
            gre["base"] = E.alloc_register("gbase")
            gre["tmp"] = E.alloc_register("gtmp")
            E.reg_load(gre["base"], sel_sb[0:1, 0:1])
        E.reg_mov(gre["tmp"], h * 8 * DQL * QB + k * CPS * 128 * QB)
        E.reg_add(gre["tmp"], gre["tmp"], gre["base"])
        return E.dma_start(
            out=ao3_sb[:, k * CPS:(k + 1) * CPS, h * QB:(h + 1) * QB],
            in_=bass.AP(gath_scr_h, gre["tmp"],
                        [[QB, 128], [128 * QB, CPS], [1, QB]]))

    P3G_H = []
    for h in range(2):
        for k in range(GSPL):
            w.op("gpsimd", lambda E, k=k, h=h: p3_gather(E, k, h),
                 waits=[("cc", NG - 1 if h == 0 else NG), ("s_ms", S_MS)],
                 inc="g_ao", inck=16)
        P3G_H.append(w.at("g_ao"))
    P3G_LD = w.at("g_ao")

    rec_wofree = [0, 0]
    rec_wog = [0, 0]
    rec_o1st = []
    rec_o1pe = {}
    for i in range(CH):
        if i >= 2:
            stn = w.op("gpsimd", lambda E, i=i: E.dma_start(
                out=wo_sb[i % 2][:, :], in_=wo[i, :, :]),
                waits=[("pe_o", rec_wofree[i % 2])],
                inc="g_wo%d" % (i % 2), inck=16)
            rec_wog[i % 2] = stn
            wo_w = ("g_wo%d" % (i % 2), rec_wog[i % 2])
        else:
            wo_w = ("g_pre", G_PRE)
        if i >= 1:
            w.op("sync", lambda E, i=i: E.dma_start(
                out=xo_sb[i % 2][:, :], in_=xown[i * 128:(i + 1) * 128, :]),
                waits=[("cc", NG), ("dve_o1", max(0, TH * (i - 1)))],
                inc="s_xo%d" % (i % 2), inck=16)
            xo_w = ("s_xo%d" % (i % 2), w.at("s_xo%d" % (i % 2)))
        else:
            xo_w = ("g_pre", G_PRE)
        for th in range(TH):
            bank = (2 * i + th) % 4 if TH > 1 else i % 4
            for ci in range(CH):
                fw = [wo_w, ("g_ao", P3G_H[th] if TH == 2 else P3G_LD),
                      ("dve_o1", max(0, TH * (i - 2) + th + 1))]
                if i < 2:
                    fw.append(("act_pr", P2_ACT))
                w.op("tensor", lambda E, i=i, th=th, ci=ci, bank=bank: E.matmul(
                    pb[bank][:, 0:TN],
                    wo_sb[i % 2][:, ci * 128:(ci + 1) * 128],
                    ao_sb[:, ci * TC + th * TN: ci * TC + (th + 1) * TN],
                    start=(ci == 0), stop=(ci == CH - 1)),
                    waits=fw,
                    inc=("pe_o" if ci == CH - 1 else None))
            pe_o_n = w.at("pe_o")
            w.op("vector", lambda E, i=i, th=th, bank=bank: E.tensor_add(
                o1_sb[i % 2][:, th * TN:(th + 1) * TN], pb[bank][:, 0:TN],
                xo_sb[i % 2][:, th * TN:(th + 1) * TN]),
                waits=[("pe_o", pe_o_n), xo_w,
                       ("s_o1%s" % "ab"[i % 2], rec_o1st[i - 2] if i >= 2 else 0),
                       ("pe_o2", rec_o1pe.get(i - 2, 0)),
                       ("act_sq2", max(0, i - 1))],
                inc="dve_o1")
        d_o1 = w.at("dve_o1")
        w.op("scalar", lambda E, i=i: E.activation(
            sq2_sb[i % 2][:, :], o1_sb[i % 2][:, :], AF.Square),
            waits=[("dve_o1", d_o1), ("pe_o2", max(0, i - 1))], inc="act_sq2")
        a_sq2 = w.at("act_sq2")
        for th in range(TH):
            sw = [("dve_o1", d_o1)]
            if i == 0:
                sw.append(("dve2", P2_DVE))
            w.op("tensor", lambda E, i=i, th=th: E.matmul(
                pb[4 + th][0:1, 0:TN], onesp_f[:, :],
                o1_sb[i % 2][:, th * TN:(th + 1) * TN],
                start=(i == 0), stop=(i == CH - 1)),
                waits=sw)
            w.op("tensor", lambda E, i=i, th=th: E.matmul(
                pb[6 + th][0:1, 0:TN], ones_bf[:, :],
                sq2_sb[i % 2][:, th * TN:(th + 1) * TN],
                start=(i == 0), stop=(i == CH - 1)),
                waits=[("act_sq2", a_sq2)],
                inc=("pe_o2" if th == TH - 1 else None))
        rec_o1pe[i] = w.at("pe_o2")
        stn = w.op("sync", lambda E, i=i: E.dma_start(
            out=out1_scr[i * 128:(i + 1) * 128, :], in_=o1_sb[i % 2][:, :]),
            waits=[("dve_o1", d_o1)], inc="s_o1%s" % "ab"[i % 2], inck=16)
        rec_o1st.append(stn)
        rec_wofree[i % 2] = w.at("pe_o")
    P3_PEO2 = w.at("pe_o2")
    P3_O1A, P3_O1B = w.at("s_o1a"), w.at("s_o1b")

    m2_sl = stat2[0:1, 0:TC]
    q2_sl = stat2[0:1, TC:2 * TC]
    r2_sl = stat2[0:1, 2 * TC:3 * TC]
    t2_sl = stat2[0:1, 3 * TC:4 * TC]
    for th in range(TH):
        w.op("scalar", lambda E, th=th: E.mul(
            m2_sl[0:1, th * TN:(th + 1) * TN], pb[4 + th][0:1, 0:TN], 1.0 / D),
            waits=[("pe_o2", P3_PEO2)])
        w.op("scalar", lambda E, th=th: E.mul(
            q2_sl[0:1, th * TN:(th + 1) * TN], pb[6 + th][0:1, 0:TN], 1.0 / D),
            inc=("act_sq2" if th == TH - 1 else None))
    A_MV2 = w.at("act_sq2")
    w.op("vector", lambda E: E.tensor_mul(t2_sl, m2_sl, m2_sl),
         waits=[("act_sq2", A_MV2)], inc="dve_x2")
    w.op("vector", lambda E: E.tensor_sub(r2_sl, q2_sl, t2_sl),
         waits=[("dve_x2", w.at("dve_x2"))], inc="dve_x2")
    w.op("vector", lambda E: E.tensor_scalar_add(t2_sl, r2_sl, 1e-5),
         waits=[("dve_x2", w.at("dve_x2"))], inc="dve_x2")
    D_MV2 = w.at("dve_x2")
    w.op("scalar", lambda E: E.activation(q2_sl, t2_sl, AF.Ln),
         waits=[("dve_x2", D_MV2)], inc="act_sq2")
    w.op("scalar", lambda E: E.activation(r2_sl, q2_sl, AF.Exp, scale=-0.5),
         waits=[("act_sq2", w.at("act_sq2"))], inc="act_sq2")
    A_MV2 = w.at("act_sq2")
    for th in range(TH):
        w.op("tensor", lambda E, th=th: E.matmul(
            pb[th][:, 0:TN], ones_f[:, :], m2_sl[0:1, th * TN:(th + 1) * TN],
            start=True, stop=True),
            waits=[("act_sq2", A_MV2), ("dve_o1", TH * CH)])
        w.op("tensor", lambda E, th=th: E.matmul(
            pb[2 + th][:, 0:TN], ones_f[:, :], r2_sl[0:1, th * TN:(th + 1) * TN],
            start=True, stop=True),
            inc=("pe_o" if th == TH - 1 else None))
    PE_BC2 = w.at("pe_o")
    for th in range(TH):
        w.op("scalar", lambda E, th=th: E.copy(
            mb2_sb[:, th * TN:(th + 1) * TN], pb[th][:, 0:TN]),
            waits=[("pe_o", PE_BC2)])
        w.op("scalar", lambda E, th=th: E.copy(
            rb2_sb[:, th * TN:(th + 1) * TN], pb[2 + th][:, 0:TN]),
            inc=("act_sq2" if th == TH - 1 else None))
    A_BC2 = w.at("act_sq2")

    X2B = w.at("dve_x2")
    for i in range(CH):
        w.op("sync", lambda E, i=i: E.dma_start(
            out=o1r_sb[i % 2][:, :], in_=out1_scr[i * 128:(i + 1) * 128, :]),
            waits=[("s_o1a", P3_O1A), ("s_o1b", P3_O1B),
                   ("dve_x2", max(0, X2B + 2 * (i - 1)))],
            inc="s_or%d" % (i % 2), inck=16)
        sld = w.at("s_or%d" % (i % 2))
        w.op("vector", lambda E, i=i: E.tensor_sub(
            t2_sb[:, :], o1r_sb[i % 2][:, :], mb2_sb[:, :]),
            waits=[("s_or%d" % (i % 2), sld), ("act_sq2", A_BC2),
                   ("dve_x2", X2B + 2 * i)],
            inc="dve_x2")
        w.op("vector", lambda E, i=i: E.tensor_mul(
            xs2_sb[:, i * TC:(i + 1) * TC], t2_sb[:, :], rb2_sb[:, :]),
            waits=[("dve_x2", X2B + 2 * i + 1)],
            inc="dve_x2")
    P3_X2 = w.at("dve_x2")

    P4_FENCE = [("pe_o2", P3_PEO2), ("dve_x2", P3_X2), ("pe_o", PE_BC2),
                ("act_sq2", A_BC2), ("s_o1a", P3_O1A), ("s_o1b", P3_O1B)]
    WG_FENCE = [("pe_o", PE_BC2), ("dve_o1", TH * CH), ("act_sq2", A_BC2)]

    def xs2_ready(ci):
        return X2B + 2 * ci + 2

    rd_srcs = [out1_scr, da_scr, db_scr]
    st_dsts = [da_scr, db_scr, None]
    st_name = ["s_da", "s_db", "s_ou"]
    rec_dd = [0, 0]
    rec_fwg = [0, 0]
    rec_wdg = [0, 0]
    rec_dast = {}
    prev_tot = None
    gfc = 0
    for sw in range(SW):
        NF = FCL[sw]
        fb = sum(FCL[:sw])
        for fc in range(NF):
            w.op("gpsimd", lambda E, gfc=gfc: E.dma_start(
                out=wg_sb[gfc % 2][:, :], in_=wg[gfc, :, :]),
                waits=WG_FENCE + [("pe_f", max(0, 2 * TH * (gfc - 1)))],
                inc="g_fw%d" % (gfc % 2), inck=16)
            w.op("gpsimd", lambda E, gfc=gfc: E.dma_start(
                out=wu_sb[gfc % 2][:, :], in_=wu[gfc, :, :]),
                inc="g_fw%d" % (gfc % 2), inck=16)
            rec_fwg[gfc % 2] = w.at("g_fw%d" % (gfc % 2))
            base = 4 * (gfc % 2)
            pf0 = w.at("pe_f")
            for role in range(2):
                wsb = wg_sb[gfc % 2] if role == 0 else wu_sb[gfc % 2]
                for th in range(TH):
                    bank = base + 2 * role + th
                    for ci in range(CH):
                        fw = [("g_fw%d" % (gfc % 2), rec_fwg[gfc % 2]),
                              ("dve_x2", xs2_ready(ci)),
                              ("act_sl", max(0, TH * (gfc - 1))),
                              ("dve_h", max(0, TH * (gfc - 1))),
                              ("dve_d", rec_dd[0] if fc == 0 else 0)]
                        w.op("tensor", lambda E, ci=ci, th=th, bank=bank,
                             wsb=wsb: E.matmul(
                            pb[bank][:, 0:TN],
                            wsb[:, ci * 128:(ci + 1) * 128],
                            xs2_sb[:, ci * TC + th * TN: ci * TC + (th + 1) * TN],
                            start=(ci == 0), stop=(ci == CH - 1)),
                            waits=fw,
                            inc=("pe_f" if ci == CH - 1 else None))
            for th in range(TH):
                w.op("scalar", lambda E, th=th, base=base, gfc=gfc: E.activation(
                    sg_sb[gfc % 2][:, th * TN:(th + 1) * TN],
                    pb[base + th][:, 0:TN], AF.Silu),
                    waits=[("pe_f", pf0 + th + 1),
                           ("dve_h", max(0, TH * (gfc - 2) + TH))],
                    inc="act_sl")
            asl = w.at("act_sl")
            for th in range(TH):
                w.op("vector", lambda E, th=th, base=base, gfc=gfc, fc=fc: E.tensor_mul(
                    hA_sb[:, fc * TC + th * TN: fc * TC + (th + 1) * TN],
                    pb[base + 2 + th][:, 0:TN],
                    sg_sb[gfc % 2][:, th * TN:(th + 1) * TN]),
                    waits=[("pe_f", pf0 + TH + th + 1), ("act_sl", asl),
                           ("pe_d", rec_dd[1] if (sw >= 1 and fc < 2) else 0)],
                    inc="dve_h")
            gfc += 1
        DVE_H_S = w.at("dve_h")

        for dmc in range(CH):
            dg = sw * CH + dmc
            wdw = [("pe_d", max(0, TH * (dg - 1)))]
            if dg < 2:
                wdw.append(("pe_o", PE_BC2))
            stn = w.op("gpsimd", lambda E, dmc=dmc, fb=fb, NF=NF, dg=dg: E.dma_start(
                out=wd_sb[dg % 2][:, 0:NF * 128],
                in_=wd[dmc, :, fb * 128:(fb + NF) * 128]),
                waits=wdw,
                inc="g_wd%d" % (dg % 2), inck=16)
            rec_wdg[dg % 2] = stn
            if sw == 0:
                w.op("sync", lambda E, dmc=dmc, dg=dg: E.dma_start(
                    out=o2r_sb[dg % 2][:, :],
                    in_=rd_srcs[0][dmc * 128:(dmc + 1) * 128, :]),
                    waits=P4_FENCE + [("dve_d", max(0, TH * (dg - 1)))],
                    inc="s_r2%d" % (dg % 2), inck=16)
            else:
                w.op("sync", lambda E, dmc=dmc, dg=dg, sw=sw: E.dma_start(
                    out=o2r_sb[dg % 2][:, :],
                    in_=rd_srcs[sw][dmc * 128:(dmc + 1) * 128, :]),
                    waits=[(st_name[sw - 1] + "0", prev_tot[0]),
                           (st_name[sw - 1] + "1", prev_tot[1]),
                           ("dve_d", max(0, TH * (dg - 1)))],
                    inc="s_r2%d" % (dg % 2), inck=16)
            sld = w.at("s_r2%d" % (dg % 2))
            for th in range(TH):
                bank = (2 * dg + th) % 4 if TH > 1 else dg % 4
                for fc in range(NF):
                    w.op("tensor", lambda E, fc=fc, th=th, bank=bank, dg=dg,
                         NF=NF: E.matmul(
                        pb[bank][:, 0:TN],
                        wd_sb[dg % 2][:, fc * 128:(fc + 1) * 128],
                        hA_sb[:, fc * TC + th * TN: fc * TC + (th + 1) * TN],
                        start=(fc == 0), stop=(fc == NF - 1)),
                        waits=[("g_wd%d" % (dg % 2), rec_wdg[dg % 2]),
                               ("dve_h", DVE_H_S),
                               ("dve_d", max(0, TH * (dg - 2) + th + 1))],
                        inc=("pe_d" if fc == NF - 1 else None))
                pd = w.at("pe_d")
                ds_sem, ds_cnt = rec_dast.get(dg - 2, ("s_da0", 0))
                w.op("vector", lambda E, th=th, bank=bank, dg=dg: E.tensor_add(
                    dat_sb[dg % 2][:, th * TN:(th + 1) * TN], pb[bank][:, 0:TN],
                    o2r_sb[dg % 2][:, th * TN:(th + 1) * TN]),
                    waits=[("pe_d", pd), ("s_r2%d" % (dg % 2), sld),
                           (ds_sem, ds_cnt)],
                    inc="dve_d")
            dd = w.at("dve_d")
            sname = "%s%d" % (st_name[sw], dg % 2)
            if sw < SW - 1:
                stn = w.op("sync", lambda E, dmc=dmc, dg=dg, sw=sw: E.dma_start(
                    out=st_dsts[sw][dmc * 128:(dmc + 1) * 128, :],
                    in_=dat_sb[dg % 2][:, :]),
                    waits=[("dve_d", dd)], inc=sname, inck=16)
            else:
                stn = w.op("sync", lambda E, dmc=dmc, dg=dg: E.dma_start(
                    out=out_fm[dmc * 128:(dmc + 1) * 128, :],
                    in_=dat_sb[dg % 2][:, :]),
                    waits=[("dve_d", dd)], inc=sname, inck=16)
            rec_dast[dg] = (sname, stn)
        rec_dd = [w.at("dve_d"), w.at("pe_d")]
        prev_tot = [w.at(st_name[sw] + "0"), w.at(st_name[sw] + "1")]

    w.op("sync", lambda E: E.nop(),
         waits=[("s_ou0", w.at("s_ou0")), ("s_ou1", w.at("s_ou1"))])

    with nc.Block() as block:
        @block.tensor
        def _(E):
            w.emit("tensor", E, sems)

        @block.scalar
        def _(E):
            w.emit("scalar", E, sems)

        @block.vector
        def _(E):
            w.emit("vector", E, sems)

        @block.sync
        def _(E):
            w.emit("sync", E, sems)

        @block.gpsimd
        def _(E):
            w.emit("gpsimd", E, sems)

    stack.pop_all()
    return nc


def prep_inputs(cfg, x, w_q, w_k, w_v, w_o, w_gate, w_up, w_down):
    B, S, D, HKV, G, HD, F = (cfg[k] for k in ("B", "S", "D", "HKV", "G", "HD", "F"))
    CH, TCH, NCHUNK, DIAG, QB = (cfg[k] for k in ("CH", "TCH", "NCHUNK", "DIAG", "QB"))
    UNITS, DQL, WC, TC, FC = (cfg[k] for k in ("UNITS", "DQL", "WC", "TC", "FC"))

    f32 = np.float32
    x = np.asarray(x, dtype=f32)
    w_q = np.asarray(w_q, dtype=f32)
    w_k = np.asarray(w_k, dtype=f32)
    w_v = np.asarray(w_v, dtype=f32)
    w_o = np.asarray(w_o, dtype=f32)
    w_gate = np.asarray(w_gate, dtype=f32)
    w_up = np.asarray(w_up, dtype=f32)
    w_down = np.asarray(w_down, dtype=f32)

    wo_r = np.ascontiguousarray(
        w_o.reshape(CH, 128, CH, 128).transpose(2, 1, 0, 3).reshape(CH, 128, CH * 128))
    wgT = w_gate.T
    wuT = w_up.T
    wdT = w_down.T
    wg_r = np.ascontiguousarray(
        wgT.reshape(CH, 128, FC, 128).transpose(2, 1, 0, 3).reshape(FC, 128, CH * 128))
    wu_r = np.ascontiguousarray(
        wuT.reshape(CH, 128, FC, 128).transpose(2, 1, 0, 3).reshape(FC, 128, CH * 128))
    wd_r = np.ascontiguousarray(
        wdT.reshape(FC, 128, CH, 128).transpose(2, 1, 0, 3).reshape(CH, 128, FC * 128))

    masks = np.zeros((128, DIAG * QB), dtype=ml_dtypes.bfloat16)
    for r in range(DIAG):
        for p in range(128):
            q0 = r * 128 + p
            if q0 < QB:
                masks[p, r * QB + q0:(r + 1) * QB] = 1.0

    xseq_b = []
    for b in range(B):
        xb = np.ascontiguousarray(x[b].T)
        xr = np.ascontiguousarray(
            xb.reshape(CH, 128, NCHUNK, TCH).transpose(2, 1, 0, 3)
            .reshape(NCHUNK, 128, CH * TCH))
        xseq_b.append(xr)

    in_maps = []
    for c in range(8):
        b, j = c // 4, c % 4
        wq_sl = np.ascontiguousarray(
            w_q[:, j * DQL:(j + 1) * DQL].reshape(CH, 128, DQL)
            .transpose(1, 0, 2).reshape(128, CH * DQL))
        h0 = UNITS * j
        kcols = w_k[:, h0 * 128:(h0 + UNITS) * 128]
        vcols = w_v[:, h0 * 128:(h0 + UNITS) * 128]
        wkv_sl = np.concatenate([kcols, vcols], axis=1)
        wkv_sl = np.ascontiguousarray(
            wkv_sl.reshape(CH, 128, WC).transpose(1, 0, 2).reshape(128, CH * WC))
        xown = np.ascontiguousarray(x[b, j * TC:(j + 1) * TC, :].T)
        selv = np.array([[(2 * j * 8 * DQL + 4 * b * DQL) * QB]], dtype=np.int32)
        in_maps.append({
            "xseq": xseq_b[b], "xown": xown, "wq": wq_sl, "wkv": wkv_sl,
            "wo": wo_r, "wg": wg_r, "wu": wu_r, "wd": wd_r,
            "masks": masks, "sel": selv,
        })
    return in_maps


def assemble_output(cfg, results):
    B, S, D, TC = cfg["B"], cfg["S"], cfg["D"], cfg["TC"]
    out = np.empty((B, S, D), dtype=np.float32)
    for c in range(8):
        b, j = c // 4, c % 4
        out[b, j * TC:(j + 1) * TC, :] = results[c]["out_fm"].T
    return out


def kernel(L0_attn_hidden_states, L0_attn_w_q, L0_attn_w_k, L0_attn_w_v,
           L0_attn_w_o, L0_ffn_W_gate, L0_ffn_W_up, L0_ffn_W_down):
    cfg = make_cfg(small=False)
    in_maps = prep_inputs(cfg, L0_attn_hidden_states, L0_attn_w_q, L0_attn_w_k,
                          L0_attn_w_v, L0_attn_w_o, L0_ffn_W_gate, L0_ffn_W_up,
                          L0_ffn_W_down)
    nc = build_program(cfg)
    res = run_bass_kernel_spmd(nc, in_maps, core_ids=list(range(8)))
    return assemble_output(cfg, res.results)


# version 08
# speedup vs baseline: 1.4285x; 1.0631x over previous
import numpy as np
import ml_dtypes
import concourse.bass as bass
import concourse.mybir as mybir
from concourse.bass_utils import run_bass_kernel_spmd

F32 = mybir.dt.float32
BF16 = mybir.dt.bfloat16
U8 = mybir.dt.uint8
I32 = mybir.dt.int32
AF = mybir.ActivationFunctionType
ALU = mybir.AluOpType

ENGS = ("tensor", "scalar", "vector", "sync", "gpsimd")


def make_cfg(small=False):
    if small:
        c = dict(B=2, S=1024, D=1024, HKV=8, G=1, HD=128, F=1536)
    else:
        c = dict(B=2, S=4096, D=4096, HKV=8, G=4, HD=128, F=11008)
    c["CH"] = c["D"] // 128
    c["QB"] = c["S"] // 8
    c["TCH"] = c["QB"]
    c["NCHUNK"] = c["S"] // c["TCH"]
    c["DIAG"] = c["QB"] // 128
    c["UNITS"] = c["B"] * c["HKV"] // 8
    c["DQL"] = c["UNITS"] * c["G"] * 128
    c["QNC"] = c["DQL"] // 128
    c["WC"] = c["UNITS"] * 128 * 2
    c["TC"] = c["B"] * c["S"] // 8
    c["TH"] = max(1, c["TC"] // 512)
    c["TN"] = c["TC"] // c["TH"]
    c["FC"] = c["F"] // 128
    c["SW"] = 3
    base = c["FC"] // c["SW"]
    rem = c["FC"] - base * c["SW"]
    c["FCL"] = [base + (1 if i < rem else 0) for i in range(c["SW"])]
    c["FCMAX"] = max(c["FCL"])
    c["SCALE"] = float(c["HD"]) ** -0.5
    return c


class Walk:

    def __init__(self):
        self.ops = {e: [] for e in ENGS}
        self.cnt = {}

    def at(self, sem):
        return self.cnt.get(sem, 0)

    def op(self, eng, emit, waits=(), inc=None, inck=1):
        ws = tuple((s, v) for (s, v) in waits if v > 0)
        self.ops[eng].append((emit, ws, inc, inck))
        if inc is not None:
            self.cnt[inc] = self.cnt.get(inc, 0) + inck
        return self.cnt.get(inc, 0) if inc else None

    def emit(self, eng_name, eng, sems):
        last = {}
        for (emit, waits, inc, inck) in self.ops[eng_name]:
            for (s, v) in waits:
                if last.get(s, -1) >= v:
                    continue
                eng.wait_ge(sems[s], v)
                last[s] = v
            ins = emit(eng)
            if inc is not None:
                ins.then_inc(sems[inc], inck)


def build_program(cfg):
    B, S, D, HKV, G, HD, F = (cfg[k] for k in ("B", "S", "D", "HKV", "G", "HD", "F"))
    CH, QB, TCH, NCHUNK, DIAG = (cfg[k] for k in ("CH", "QB", "TCH", "NCHUNK", "DIAG"))
    UNITS, DQL, QNC, WC = (cfg[k] for k in ("UNITS", "DQL", "QNC", "WC"))
    TC, TH, TN, FC = (cfg[k] for k in ("TC", "TH", "TN", "FC"))
    SW, FCL, FCMAX = cfg["SW"], cfg["FCL"], cfg["FCMAX"]
    SCALE = cfg["SCALE"]
    VSUB = TCH // 128

    from contextlib import ExitStack
    stack = ExitStack()
    nc = bass.Bass("TRN2", target_bir_lowering=False)

    xseq = nc.dram_tensor("xseq", [NCHUNK, 128, CH * TCH], F32, kind="ExternalInput").ap()
    xown = nc.dram_tensor("xown", [D, TC], F32, kind="ExternalInput").ap()
    wq = nc.dram_tensor("wq", [128, CH * DQL], F32, kind="ExternalInput").ap()
    wkv = nc.dram_tensor("wkv", [128, CH * WC], F32, kind="ExternalInput").ap()
    wo = nc.dram_tensor("wo", [CH, 128, CH * 128], F32, kind="ExternalInput").ap()
    wg = nc.dram_tensor("wg", [FC, 128, CH * 128], F32, kind="ExternalInput").ap()
    wu = nc.dram_tensor("wu", [FC, 128, CH * 128], F32, kind="ExternalInput").ap()
    wd = nc.dram_tensor("wd", [CH, 128, FC * 128], F32, kind="ExternalInput").ap()
    masks_in = nc.dram_tensor("masks", [128, DIAG * QB], BF16, kind="ExternalInput").ap()
    sel = nc.dram_tensor("sel", [1, 1], I32, kind="ExternalInput").ap()
    out_fm = nc.dram_tensor("out_fm", [D, TC], F32, kind="ExternalOutput").ap()

    q_scr = nc.dram_tensor("q_scr", [DQL, S], BF16).ap()
    k_scr = nc.dram_tensor("k_scr", [UNITS, 128, S], BF16).ap()
    v_scr = nc.dram_tensor("v_scr", [UNITS, 128, S], BF16).ap()
    attn_scr_h = nc.dram_tensor("attn_scr", [8, DQL, QB], BF16)
    gath_scr_h = nc.dram_tensor("gath_scr", [8, 8 * DQL, QB], BF16,
                                addr_space="Shared")
    attn_scr = attn_scr_h.ap()
    out1_scr = nc.dram_tensor("out1_scr", [D, TC], F32).ap()
    da_scr = nc.dram_tensor("da_scr", [D, TC], F32).ap()
    db_scr = nc.dram_tensor("db_scr", [D, TC], F32).ap()

    ones_bf = stack.enter_context(nc.sbuf_tensor("ones_bf", [128, 1], BF16))
    onesp_f = stack.enter_context(nc.sbuf_tensor("onesp_f", [128, 1], F32))
    ones_f = stack.enter_context(nc.sbuf_tensor("ones_f", [1, 128], F32))
    sel_sb = stack.enter_context(nc.sbuf_tensor("sel_sb", [1, 1], I32))

    def roundup(x, a=32):
        return (x + a - 1) // a * a

    class Arena:
        def __init__(self):
            cap = nc.sbuf_bytes_remaining - 64
            self.big = stack.enter_context(
                nc.sbuf_tensor("arena", [nc.NUM_PARTITIONS, cap], U8))
            self.base = nc.lookup_mloc(self.big).addr
            self.cap = cap
            self.off = 0
            self.maxoff = 0

        def take(self, name, shape, dtype):
            per = int(np.prod(shape[1:])) * mybir.dt.size(dtype)
            o = self.off
            self.off = roundup(self.off + per)
            self.maxoff = max(self.maxoff, self.off)
            assert self.off <= self.cap, (name, self.off, self.cap)
            return nc.alloc_sbuf_tensor_at(name, list(shape), dtype,
                                           offset=self.base + o)

        def reset(self):
            self.off = 0

    ar = Arena()
    wq_sb = ar.take("wq_sb", [128, CH * DQL], BF16)
    wkv_sb = ar.take("wkv_sb", [128, CH * WC], BF16)
    xc_sb = ar.take("xc_sb", [128, CH * TCH], BF16)
    xs_sb = [ar.take(f"xs{i}", [128, CH * TCH], BF16) for i in range(2)]
    sq_sb = [ar.take(f"sq{i}", [128, TCH], BF16) for i in range(2)]
    EVW = max(TCH, UNITS * 128)
    ev_sb = [ar.take(f"ev{i}", [128, EVW], BF16) for i in range(2)]
    t1_sb = ar.take("t1_sb", [128, TCH], BF16)
    mb_sb = ar.take("mb_sb", [128, TCH], BF16)
    rb_sb = ar.take("rb_sb", [128, TCH], BF16)
    stat1 = ar.take("stat1", [1, 4 * TCH], F32)
    ar.reset()
    qt_sb = [ar.take(f"qt{u}", [128, G * S], BF16) for u in range(UNITS)]
    kt_sb = [ar.take(f"kt{u}", [128, S], BF16) for u in range(UNITS)]
    vt_sb = [ar.take(f"vt{u}", [128, S], BF16) for u in range(UNITS)]
    pr_sb = [ar.take(f"pr{i}", [128, 2 * QB], BF16) for i in range(2)]
    pm_sb = [ar.take(f"pm{i}", [128, 2 * QB], BF16) for i in range(2)]
    sp_sb = [ar.take(f"sp{i}", [128, QB], BF16) for i in range(2)]
    ib_sb = [ar.take(f"ib{i}", [128, QB], F32) for i in range(2)]
    aev_sb = [ar.take(f"aev{i}", [128, QB], BF16) for i in range(2)]
    onesmm_sb = ar.take("onesmm", [128, 128], BF16)
    masks_sb = ar.take("masks_sb", [128, DIAG * QB], BF16)
    P2_END = ar.off
    ar.reset()
    xs2_sb = ar.take("xs2_sb", [128, CH * TC], BF16)
    xs2_end = ar.off
    stat2 = ar.take("stat2", [1, 4 * TC], F32)
    o1_sb = [ar.take(f"o1{i}", [128, TC], F32) for i in range(2)]
    sq2_sb = [ar.take(f"sq2{i}", [128, TC], BF16) for i in range(2)]
    xo1_off = ar.off
    xo_sb = [None, ar.take("xo1", [128, TC], F32)]
    ar.off = xs2_end
    hA_sb = ar.take("hA_sb", [128, FCMAX * TC], BF16)
    HA_END = ar.off
    ar.off = P2_END
    wo_sb = [ar.take(f"wo{i}", [128, CH * 128], BF16) for i in range(2)]
    xo_sb[0] = ar.take("xo0", [128, TC], F32)
    E_END = ar.off
    ar.off = max(P2_END, HA_END)
    wg_sb = [ar.take(f"wgs{i}", [128, CH * 128], BF16) for i in range(2)]
    wu_sb = [ar.take(f"wus{i}", [128, CH * 128], BF16) for i in range(2)]
    wd_sb = [ar.take(f"wds{i}", [128, FCMAX * 128], BF16) for i in range(2)]
    sg_sb = [ar.take(f"sg{i}", [128, TC], BF16) for i in range(2)]
    o2r_sb = [ar.take(f"o2r{i}", [128, TC], F32) for i in range(2)]
    dat_sb = [ar.take(f"dat{i}", [128, TC], F32) for i in range(2)]
    C_END = ar.off
    ar.off = E_END
    AO_OFF = ar.off
    ao_sb = ar.take("ao_sb", [128, CH * TC], BF16)
    AO_END = ar.off
    ao3_sb = nc.alloc_sbuf_tensor_at("ao3", [128, CH, TC], BF16,
                                     offset=ar.base + AO_OFF)
    ar.off = max(C_END, AO_END - 16384)
    o1r_sb = [ar.take(f"o1r{i}", [128, TC], F32) for i in range(2)]
    t2_sb = ar.take("t2_sb", [128, TC], BF16)
    mb2_sb = ar.take("mb2_sb", [128, TC], BF16)
    rb2_sb = ar.take("rb2_sb", [128, TC], BF16)

    sc_ps = [stack.enter_context(nc.psum_tensor(f"sc{i}", [128, 1024], F32))
             for i in range(2)]
    sm_ps = [stack.enter_context(nc.psum_tensor(f"sm{i}", [128, 512], F32))
             for i in range(4)]

    class PBView:
        def __init__(self, h, coff):
            self.h, self.coff = h, coff

        def __getitem__(self, idx):
            r, c = idx
            return self.h[r, c.start + self.coff:c.stop + self.coff]

    pb = [PBView(sc_ps[0], 0), PBView(sc_ps[0], 512),
          PBView(sc_ps[1], 0), PBView(sc_ps[1], 512),
          PBView(sm_ps[0], 0), PBView(sm_ps[1], 0),
          PBView(sm_ps[2], 0), PBView(sm_ps[3], 0)]

    semnames = ["g_w0", "g_w1", "g_w2", "g_w3", "g_wkv", "g_x", "g_ao", "g_wo0", "g_wo1", "g_fw0", "g_fw1",
                "g_wd0", "g_wd1",
                "s_ms", "s_kv", "s_xo0", "s_xo1", "s_or0", "s_or1",
                "s_r20", "s_r21", "s_q", "g_pre",
                "s_o1a", "s_o1b",
                "s_da0", "s_da1", "s_db0", "s_db1", "s_ou0", "s_ou1",
                "a_q0", "a_q1",
                "cc", "g_ms",
                "act_sq", "pe_st", "act_mv", "dve_mv", "pe_bc", "act_bc",
                "dve_x", "dve_xs", "pe_mm", "act_ev",
                "pe_s", "act_pr", "pe_av", "dve2",
                "pe_o", "dve_o1", "act_sq2", "pe_o2", "dve_x2",
                "pe_f", "act_sl", "dve_h", "pe_d", "dve_d"]
    sems = {n: stack.enter_context(nc.semaphore(n)) for n in semnames}

    w = Walk()

    w.op("gpsimd", lambda E: E.memset(ones_bf[:, :], 1.0))
    w.op("gpsimd", lambda E: E.memset(onesp_f[:, :], 1.0))
    w.op("gpsimd", lambda E: E.memset(ones_f[:, :], 1.0), inc="g_ms")
    w.op("sync", lambda E: E.dma_start(out=sel_sb[:, :], in_=sel[:, :]),
         inc="s_ms", inck=16)
    NSP = 4

    rec_xsfree = [0, 0]
    rec_ev = []
    evst = {"evg": 0}
    n_groups = QNC + UNITS + VSUB

    def p1_stats(t):
        w.op("gpsimd", lambda E, t=t: E.dma_start(out=xc_sb[:, :], in_=xseq[t, :, :]),
             waits=[("act_sq", t * CH), ("pe_st", t * CH), ("dve_x", t * CH)],
             inc="g_x", inck=16)
        ld_n = w.at("g_x")
        if t == 0:
            for i in range(NSP):
                lo, hi = i * (CH // NSP), (i + 1) * (CH // NSP)
                w.op("gpsimd", lambda E, lo=lo, hi=hi: E.dma_start(
                    out=wq_sb[:, lo * DQL:hi * DQL],
                    in_=wq[:, lo * DQL:hi * DQL]),
                    inc="g_w%d" % i, inck=16)
            w.op("gpsimd", lambda E: E.dma_start(out=wkv_sb[:, :], in_=wkv[:, :]),
                 inc="g_wkv", inck=16)
        for ci in range(CH):
            w.op("scalar", lambda E, ci=ci: E.activation(
                sq_sb[ci % 2][:, :], xc_sb[:, ci * TCH:(ci + 1) * TCH], AF.Square),
                waits=[("g_x", ld_n), ("pe_st", t * CH + ci - 1)],
                inc="act_sq")
            w.op("tensor", lambda E, ci=ci: E.matmul(
                pb[6][0:1, 0:TCH], ones_bf[:, :], xc_sb[:, ci * TCH:(ci + 1) * TCH],
                start=(ci == 0), stop=(ci == CH - 1)),
                waits=[("g_x", ld_n), ("act_mv", 3 * t), ("g_ms", 1)])
            w.op("tensor", lambda E, ci=ci: E.matmul(
                pb[7][0:1, 0:TCH], ones_bf[:, :], sq_sb[ci % 2][:, :],
                start=(ci == 0), stop=(ci == CH - 1)),
                waits=[("act_sq", t * CH + ci + 1)],
                inc="pe_st")
        pe_st_n = w.at("pe_st")

        m_sl = stat1[0:1, 0:TCH]
        q_sl = stat1[0:1, TCH:2 * TCH]
        r_sl = stat1[0:1, 2 * TCH:3 * TCH]
        tp_sl = stat1[0:1, 3 * TCH:4 * TCH]
        w.op("scalar", lambda E: E.mul(m_sl, pb[6][0:1, 0:TCH], 1.0 / D),
             waits=[("pe_st", pe_st_n)])
        w.op("scalar", lambda E: E.mul(q_sl, pb[7][0:1, 0:TCH], 1.0 / D),
             inc="act_mv")
        a_mv = w.at("act_mv")
        w.op("vector", lambda E: E.tensor_mul(tp_sl, m_sl, m_sl),
             waits=[("act_mv", a_mv)], inc="dve_mv")
        w.op("vector", lambda E: E.tensor_sub(r_sl, q_sl, tp_sl),
             waits=[("dve_mv", w.at("dve_mv"))], inc="dve_mv")
        w.op("vector", lambda E: E.tensor_scalar_add(tp_sl, r_sl, 1e-5),
             waits=[("dve_mv", w.at("dve_mv"))], inc="dve_mv")
        d_mv = w.at("dve_mv")
        w.op("scalar", lambda E: E.activation(q_sl, tp_sl, AF.Ln),
             waits=[("dve_mv", d_mv)], inc="act_mv")
        w.op("scalar", lambda E: E.activation(r_sl, q_sl, AF.Exp, scale=-0.5),
             waits=[("act_mv", w.at("act_mv"))], inc="act_mv")
        a_mv = w.at("act_mv")
        w.op("tensor", lambda E: E.matmul(pb[4][:, 0:TCH], ones_f[:, :], m_sl,
                                          start=True, stop=True),
             waits=[("act_mv", a_mv), ("act_bc", 2 * t)])
        w.op("tensor", lambda E: E.matmul(pb[5][:, 0:TCH], ones_f[:, :], r_sl,
                                          start=True, stop=True), inc="pe_bc")
        p_bc = w.at("pe_bc")
        w.op("scalar", lambda E: E.copy(mb_sb[:, :], pb[4][:, 0:TCH]),
             waits=[("pe_bc", p_bc), ("dve_xs", t * CH)])
        w.op("scalar", lambda E: E.copy(rb_sb[:, :], pb[5][:, 0:TCH]),
             inc="act_bc", inck=2)
        a_bc = w.at("act_bc")
        xs = xs_sb[t % 2]
        for ci in range(CH):
            w.op("vector", lambda E, ci=ci: E.tensor_sub(
                t1_sb[:, :], xc_sb[:, ci * TCH:(ci + 1) * TCH], mb_sb[:, :]),
                waits=[("act_bc", a_bc), ("pe_mm", rec_xsfree[t % 2]),
                       ("dve_xs", t * CH + ci)],
                inc="dve_x")
            w.op("vector", lambda E, ci=ci, xs=xs: E.tensor_mul(
                xs[:, ci * TCH:(ci + 1) * TCH], t1_sb[:, :], rb_sb[:, :]),
                waits=[("dve_x", t * CH + ci + 1)],
                inc="dve_xs")

    def p1_group(t, gi, kind, idx):
        xs = xs_sb[t % 2]
        gbase = t * n_groups
        evg = evst["evg"]
        if kind == "q":
            nq = idx
            bank = gi % 2
            for ci in range(CH):
                w.op("tensor", lambda E, ci=ci, nq=nq, bank=bank, xs=xs: E.matmul(
                    pb[bank][:, 0:TCH],
                    wq_sb[:, ci * DQL + nq * 128: ci * DQL + (nq + 1) * 128],
                    xs[:, ci * TCH:(ci + 1) * TCH],
                    start=(ci == 0), stop=(ci == CH - 1)),
                    waits=[("g_w%d" % (ci * NSP // CH), 16),
                           ("dve_xs", t * CH + ci + 1),
                           ("act_ev", gbase + gi - 1)],
                    inc=("pe_mm" if ci == CH - 1 else None))
            mm_n = w.at("pe_mm")
            w.op("scalar", lambda E, bank=bank, evg=evg: E.copy(
                ev_sb[evg % 2][:, 0:TCH], pb[bank][:, 0:TCH]),
                waits=[("pe_mm", mm_n),
                       ("a_q%d" % (evg % 2), rec_ev[evg - 2] if evg >= 2 else 0)],
                inc="act_ev")
            a_ev = w.at("act_ev")
            stn = w.op("scalar", lambda E, nq=nq, t=t, evg=evg: E.dma_start(
                out=q_scr[nq * 128:(nq + 1) * 128, t * TCH:(t + 1) * TCH],
                in_=ev_sb[evg % 2][:, 0:TCH]),
                waits=[("act_ev", a_ev)], inc="a_q%d" % (evg % 2), inck=16)
            rec_ev.append(stn)
        elif kind == "k":
            u = idx
            bank = gi % 2
            for ci in range(CH):
                w.op("tensor", lambda E, ci=ci, u=u, bank=bank, xs=xs: E.matmul(
                    pb[bank][:, 0:TCH],
                    wkv_sb[:, ci * WC + u * 128: ci * WC + (u + 1) * 128],
                    xs[:, ci * TCH:(ci + 1) * TCH],
                    start=(ci == 0), stop=(ci == CH - 1)),
                    waits=[("g_wkv", 16), ("dve_xs", t * CH + ci + 1),
                           ("act_ev", gbase + gi - 1)],
                    inc=("pe_mm" if ci == CH - 1 else None))
            mm_n = w.at("pe_mm")
            w.op("scalar", lambda E, bank=bank, evg=evg: E.copy(
                ev_sb[evg % 2][:, 0:TCH], pb[bank][:, 0:TCH]),
                waits=[("pe_mm", mm_n),
                       ("a_q%d" % (evg % 2), rec_ev[evg - 2] if evg >= 2 else 0)],
                inc="act_ev")
            a_ev = w.at("act_ev")
            stn = w.op("scalar", lambda E, u=u, t=t, evg=evg: E.dma_start(
                out=k_scr[u, :, t * TCH:(t + 1) * TCH], in_=ev_sb[evg % 2][:, 0:TCH]),
                waits=[("act_ev", a_ev)], inc="a_q%d" % (evg % 2), inck=16)
            rec_ev.append(stn)
        else:
            sub = idx
            bank = 2 + sub % 2
            vg = t * VSUB + sub
            for ci in range(CH):
                w.op("tensor", lambda E, ci=ci, sub=sub, bank=bank, xs=xs: E.matmul(
                    pb[bank][:, 0:UNITS * 128],
                    xs[:, ci * TCH + sub * 128: ci * TCH + (sub + 1) * 128],
                    wkv_sb[:, ci * WC + UNITS * 128:(ci + 1) * WC],
                    start=(ci == 0), stop=(ci == CH - 1)),
                    waits=[("g_wkv", 16), ("dve_xs", t * CH + ci + 1),
                           ("act_ev", gbase + gi - 1)],
                    inc=("pe_mm" if ci == CH - 1 else None))
            mm_n = w.at("pe_mm")
            w.op("scalar", lambda E, bank=bank, evg=evg: E.copy(
                ev_sb[evg % 2][:, 0:UNITS * 128], pb[bank][:, 0:UNITS * 128]),
                waits=[("pe_mm", mm_n),
                       ("a_q%d" % (evg % 2), rec_ev[evg - 2] if evg >= 2 else 0)],
                inc="act_ev")
            a_ev = w.at("act_ev")
            for u in range(UNITS):
                stn = w.op("scalar", lambda E, u=u, vg=vg, evg=evg: E.dma_start(
                    out=v_scr[u, :, vg * 128:(vg + 1) * 128],
                    in_=ev_sb[evg % 2][:, u * 128:(u + 1) * 128]),
                    waits=[("act_ev", a_ev)], inc="a_q%d" % (evg % 2), inck=16)
            rec_ev.append(stn)
        evst["evg"] += 1

    groups = ([("q", nq) for nq in range(QNC)] + [("k", u) for u in range(UNITS)]
              + [("v", sub) for sub in range(VSUB)])
    SPLIT = max(2, n_groups // 2)
    p1_stats(0)
    for t in range(NCHUNK):
        for gi, (kind, idx) in enumerate(groups[:SPLIT]):
            p1_group(t, gi, kind, idx)
        if t + 1 < NCHUNK:
            p1_stats(t + 1)
        for gi, (kind, idx) in enumerate(groups[SPLIT:], start=SPLIT):
            p1_group(t, gi, kind, idx)
        rec_xsfree[t % 2] = w.at("pe_mm")

    P1T = {n: w.at(n) for n in ("a_q0", "a_q1")}

    p1waits = [(n, v) for n, v in P1T.items()]
    P1_EV = w.at("act_ev")
    P1_BC = w.at("act_bc")
    P1_MV = w.at("act_mv")
    w.op("sync", lambda E: E.dma_start(out=masks_sb[:, :], in_=masks_in[:, :]),
         waits=p1waits, inc="s_ms", inck=16)
    S_MS = w.at("s_ms")
    kv_ready = []
    for u in range(UNITS):
        w.op("sync", lambda E, u=u: E.dma_start(
            out=qt_sb[u][:, :].rearrange("p (g t) -> p g t", g=G),
            in_=q_scr[u * G * 128:(u + 1) * G * 128, :].rearrange(
                "(g p) t -> p g t", p=128)),
            waits=p1waits, inc="s_kv", inck=16)
        w.op("sync", lambda E, u=u: E.dma_start(
            out=kt_sb[u][:, :], in_=k_scr[u, :, :]), inc="s_kv", inck=16)
        w.op("sync", lambda E, u=u: E.dma_start(
            out=vt_sb[u][:, :], in_=v_scr[u, :, :]), inc="s_kv", inck=16)
        kv_ready.append(w.at("s_kv"))
    w.op("gpsimd", lambda E: E.memset(onesmm_sb[:, :], 1.0),
         waits=p1waits, inc="g_ms")
    G_MS2 = w.at("g_ms")
    w.op("gpsimd", lambda E: E.dma_start(out=wo_sb[0][:, :], in_=wo[0, :, :]),
         waits=p1waits, inc="g_pre", inck=16)
    w.op("gpsimd", lambda E: E.dma_start(out=wo_sb[1][:, :], in_=wo[1, :, :]),
         inc="g_pre", inck=16)
    w.op("gpsimd", lambda E: E.dma_start(out=xo_sb[0][:, :], in_=xown[0:128, :]),
         inc="g_pre", inck=16)
    G_PRE = w.at("g_pre")

    pairs = []
    pair_gidx = []
    gi = -1
    for qb in range(8):
        nkc = (qb + 1) * DIAG
        assert nkc % 2 == 0
        for u in range(UNITS):
            for g in range(G):
                for pi in range(nkc // 2):
                    if pi == 0:
                        gi += 1
                    pairs.append((qb, u, g, 2 * pi, nkc))
                    pair_gidx.append(gi)
    NPAIR = len(pairs)

    rec_exp, rec_avh, rec_suh, rec_dvp = [], [], [], []
    rec_recip, rec_aevm, rec_store = {}, {}, {}
    rec_pmav = [0, 0]
    stash = {}
    st2 = {"dpi": 0}

    def p2_scores(pi):
        qb, u, g, kc, nkc = pairs[pi]
        b2 = pi % 2
        ws = [("s_kv", kv_ready[u])]
        if pi >= 2:
            ws.append(("act_pr", rec_exp[pi - 2]))
        else:
            ws.append(("act_ev", P1_EV))
        for h in range(2):
            w.op("tensor", lambda E, u=u, kc=kc, g=g, qb=qb, b2=b2, h=h: E.matmul(
                sc_ps[b2][:, h * QB:(h + 1) * QB],
                kt_sb[u][:, (kc + h) * 128:(kc + h + 1) * 128],
                qt_sb[u][:, g * S + qb * QB: g * S + (qb + 1) * QB],
                start=True, stop=True),
                waits=(ws if h == 0 else []),
                inc=("pe_s" if h == 1 else None))

    def p2_actdve(pi):
        qb, u, g, kc, nkc = pairs[pi]
        b2 = pi % 2
        diag = kc >= qb * DIAG
        ws = [("pe_s", pi + 1)]
        if pi >= 2:
            ws += [("pe_av", rec_avh[pi - 2]), ("dve2", rec_dvp[pi - 2])]
        if diag:
            r = kc - qb * DIAG
            pmb = st2["dpi"] % 2
            st2["dpi"] += 1
            w.op("scalar", lambda E, b2=b2: E.activation(
                pr_sb[b2][:, 0:QB], sc_ps[b2][:, 0:QB], AF.Exp, scale=SCALE),
                waits=ws, inc="act_pr")
            e_lo = w.at("act_pr")
            w.op("vector", lambda E, b2=b2, pmb=pmb, r=r: E.tensor_mul(
                pm_sb[pmb][:, 0:QB], pr_sb[b2][:, 0:QB],
                masks_sb[:, r * QB:(r + 1) * QB]),
                waits=[("act_pr", e_lo), ("s_ms", S_MS),
                       ("pe_av", rec_pmav[pmb])],
                inc="dve2")
            m_lo = w.at("dve2")
            w.op("scalar", lambda E, b2=b2: E.activation(
                pr_sb[b2][:, QB:2 * QB], sc_ps[b2][:, QB:2 * QB],
                AF.Exp, scale=SCALE), inc="act_pr")
            e_hi = w.at("act_pr")
            w.op("vector", lambda E, b2=b2, pmb=pmb, r=r: E.tensor_mul(
                pm_sb[pmb][:, QB:2 * QB], pr_sb[b2][:, QB:2 * QB],
                masks_sb[:, (r + 1) * QB:(r + 2) * QB]),
                waits=[("act_pr", e_hi)], inc="dve2")
            m_hi = w.at("dve2")
            rhs_t = pm_sb[pmb]
            avw = ([("dve2", m_lo)], [("dve2", m_hi)])
        else:
            pmb = None
            w.op("scalar", lambda E, b2=b2: E.activation(
                pr_sb[b2][:, :], sc_ps[b2][:, :], AF.Exp, scale=SCALE),
                waits=ws, inc="act_pr")
            e = w.at("act_pr")
            rhs_t = pr_sb[b2]
            avw = ([("act_pr", e)], [])
        rec_exp.append(w.at("act_pr"))
        paw = [] if diag else [("act_pr", rec_exp[pi])]
        if pi >= 2:
            paw.append(("pe_av", rec_suh[pi - 2]))
        w.op("vector", lambda E, rhs_t=rhs_t, s2=pi % 2: E.tensor_add(
            sp_sb[s2][:, :], rhs_t[:, 0:QB], rhs_t[:, QB:2 * QB]),
            waits=paw, inc="dve2")
        rec_dvp.append(w.at("dve2"))
        stash[pi] = (rhs_t, avw, pmb)

    def p2_av(pi):
        qb, u, g, kc, nkc = pairs[pi]
        gidx = pair_gidx[pi]
        par = gidx % 2
        rhs_t, avw, pmb = stash.pop(pi)
        for h in range(2):
            ws = list(avw[h])
            if kc == 0 and h == 0:
                ws.append(("dve2", rec_aevm[gidx - 2]) if gidx >= 2
                          else ("act_bc", P1_BC))
            w.op("tensor", lambda E, u=u, kc=kc, h=h, par=par, rhs_t=rhs_t,
                 nkc=nkc: E.matmul(
                sm_ps[par][:, 0:QB],
                vt_sb[u][:, (kc + h) * 128:(kc + h + 1) * 128],
                rhs_t[:, h * QB:(h + 1) * QB],
                start=(kc + h == 0), stop=(kc + h == nkc - 1)),
                waits=ws, inc=("pe_av" if h == 1 else None))
        rec_avh.append(w.at("pe_av"))
        if pmb is not None:
            rec_pmav[pmb] = w.at("pe_av")
        ws = [("dve2", rec_dvp[pi])]
        if kc == 0:
            ws.append(("dve2", rec_recip[gidx - 2]) if gidx >= 2
                      else ("act_mv", P1_MV))
            if gidx < 2:
                ws.append(("g_ms", G_MS2))
        w.op("tensor", lambda E, par=par, s2=pi % 2, kc=kc, nkc=nkc: E.matmul(
            sm_ps[2 + par][:, 0:QB], onesmm_sb[:, :], sp_sb[s2][:, :],
            start=(kc == 0), stop=(kc == nkc - 2)),
            waits=ws, inc="pe_av")
        rec_suh.append(w.at("pe_av"))

    def p2_groupend(pi):
        qb, u, g, kc, nkc = pairs[pi]
        gidx = pair_gidx[pi]
        par = gidx % 2
        w.op("vector", lambda E, par=par: E.reciprocal(
            ib_sb[par][:, :], sm_ps[2 + par][:, 0:QB]),
            waits=[("pe_av", rec_suh[pi])], inc="dve2")
        rec_recip[gidx] = w.at("dve2")
        ws = [("pe_av", rec_avh[pi])]
        if gidx >= 2:
            ws.append(("s_q", rec_store[gidx - 2]))
        w.op("vector", lambda E, par=par: E.tensor_mul(
            aev_sb[par][:, :], sm_ps[par][:, 0:QB], ib_sb[par][:, :]),
            waits=ws, inc="dve2")
        rec_aevm[gidx] = w.at("dve2")
        w.op("sync", lambda E, qb=qb, u=u, g=g, par=par: E.dma_start(
            out=attn_scr[qb, (u * G + g) * 128:(u * G + g + 1) * 128, :],
            in_=aev_sb[par][:, :]),
            waits=[("dve2", rec_aevm[gidx])], inc="s_q", inck=16)
        rec_store[gidx] = w.at("s_q")
        if u == UNITS - 1 and g == G - 1:
            w.op("gpsimd", lambda E, qb=qb: E.collective_compute(
                "AllGather", ALU.bypass, replica_groups=[list(range(8))],
                ins=[attn_scr_h.ap()[qb].opt()],
                outs=[gath_scr_h.ap()[qb].opt()]),
                waits=[("s_q", rec_store[gidx])], inc="cc")

    p2_scores(0)
    p2_actdve(0)
    for pi in range(1, NPAIR):
        p2_scores(pi)
        p2_actdve(pi)
        p2_av(pi - 1)
        if pairs[pi - 1][3] == pairs[pi - 1][4] - 2:
            p2_groupend(pi - 1)
    p2_av(NPAIR - 1)
    p2_groupend(NPAIR - 1)
    NG = w.at("cc")
    P2_ACT = w.at("act_pr")
    P2_DVE = w.at("dve2")

    gre = {"base": None, "tmp": None}
    GSPL = 4 if CH % 4 == 0 else 1
    CPS = CH // GSPL

    def p3_gather(E, k, h):
        if gre["base"] is None:
            gre["base"] = E.alloc_register("gbase")
            gre["tmp"] = E.alloc_register("gtmp")
            E.reg_load(gre["base"], sel_sb[0:1, 0:1])
        E.reg_mov(gre["tmp"], h * 8 * DQL * QB + k * CPS * 128 * QB)
        E.reg_add(gre["tmp"], gre["tmp"], gre["base"])
        return E.dma_start(
            out=ao3_sb[:, k * CPS:(k + 1) * CPS, h * QB:(h + 1) * QB],
            in_=bass.AP(gath_scr_h, gre["tmp"],
                        [[QB, 128], [128 * QB, CPS], [1, QB]]))

    P3G_H = []
    for h in range(2):
        for k in range(GSPL):
            w.op("gpsimd", lambda E, k=k, h=h: p3_gather(E, k, h),
                 waits=[("cc", NG - 1 if h == 0 else NG), ("s_ms", S_MS)],
                 inc="g_ao", inck=16)
        P3G_H.append(w.at("g_ao"))
    P3G_LD = w.at("g_ao")

    rec_wofree = [0, 0]
    rec_wog = [0, 0]
    rec_o1st = []
    rec_o1pe = {}
    for i in range(CH):
        if i >= 2:
            stn = w.op("gpsimd", lambda E, i=i: E.dma_start(
                out=wo_sb[i % 2][:, :], in_=wo[i, :, :]),
                waits=[("pe_o", rec_wofree[i % 2])],
                inc="g_wo%d" % (i % 2), inck=16)
            rec_wog[i % 2] = stn
            wo_w = ("g_wo%d" % (i % 2), rec_wog[i % 2])
        else:
            wo_w = ("g_pre", G_PRE)
        if i >= 1:
            w.op("sync", lambda E, i=i: E.dma_start(
                out=xo_sb[i % 2][:, :], in_=xown[i * 128:(i + 1) * 128, :]),
                waits=[("cc", NG), ("dve_o1", max(0, TH * (i - 1)))],
                inc="s_xo%d" % (i % 2), inck=16)
            xo_w = ("s_xo%d" % (i % 2), w.at("s_xo%d" % (i % 2)))
        else:
            xo_w = ("g_pre", G_PRE)
        for th in range(TH):
            bank = (2 * i + th) % 4 if TH > 1 else i % 4
            for ci in range(CH):
                fw = [wo_w, ("g_ao", P3G_H[th] if TH == 2 else P3G_LD),
                      ("dve_o1", max(0, TH * (i - 2) + th + 1))]
                if i < 2:
                    fw.append(("act_pr", P2_ACT))
                w.op("tensor", lambda E, i=i, th=th, ci=ci, bank=bank: E.matmul(
                    pb[bank][:, 0:TN],
                    wo_sb[i % 2][:, ci * 128:(ci + 1) * 128],
                    ao_sb[:, ci * TC + th * TN: ci * TC + (th + 1) * TN],
                    start=(ci == 0), stop=(ci == CH - 1)),
                    waits=fw,
                    inc=("pe_o" if ci == CH - 1 else None))
            pe_o_n = w.at("pe_o")
            w.op("vector", lambda E, i=i, th=th, bank=bank: E.tensor_add(
                o1_sb[i % 2][:, th * TN:(th + 1) * TN], pb[bank][:, 0:TN],
                xo_sb[i % 2][:, th * TN:(th + 1) * TN]),
                waits=[("pe_o", pe_o_n), xo_w,
                       ("s_o1%s" % "ab"[i % 2], rec_o1st[i - 2] if i >= 2 else 0),
                       ("pe_o2", rec_o1pe.get(i - 2, 0)),
                       ("act_sq2", max(0, i - 1))],
                inc="dve_o1")
        d_o1 = w.at("dve_o1")
        w.op("scalar", lambda E, i=i: E.activation(
            sq2_sb[i % 2][:, :], o1_sb[i % 2][:, :], AF.Square),
            waits=[("dve_o1", d_o1), ("pe_o2", max(0, i - 1))], inc="act_sq2")
        a_sq2 = w.at("act_sq2")
        for th in range(TH):
            sw = [("dve_o1", d_o1)]
            if i == 0:
                sw.append(("dve2", P2_DVE))
            w.op("tensor", lambda E, i=i, th=th: E.matmul(
                pb[4 + th][0:1, 0:TN], onesp_f[:, :],
                o1_sb[i % 2][:, th * TN:(th + 1) * TN],
                start=(i == 0), stop=(i == CH - 1)),
                waits=sw)
            w.op("tensor", lambda E, i=i, th=th: E.matmul(
                pb[6 + th][0:1, 0:TN], ones_bf[:, :],
                sq2_sb[i % 2][:, th * TN:(th + 1) * TN],
                start=(i == 0), stop=(i == CH - 1)),
                waits=[("act_sq2", a_sq2)],
                inc=("pe_o2" if th == TH - 1 else None))
        rec_o1pe[i] = w.at("pe_o2")
        stn = w.op("sync", lambda E, i=i: E.dma_start(
            out=out1_scr[i * 128:(i + 1) * 128, :], in_=o1_sb[i % 2][:, :]),
            waits=[("dve_o1", d_o1)], inc="s_o1%s" % "ab"[i % 2], inck=16)
        rec_o1st.append(stn)
        rec_wofree[i % 2] = w.at("pe_o")
    P3_PEO2 = w.at("pe_o2")
    P3_O1A, P3_O1B = w.at("s_o1a"), w.at("s_o1b")

    m2_sl = stat2[0:1, 0:TC]
    q2_sl = stat2[0:1, TC:2 * TC]
    r2_sl = stat2[0:1, 2 * TC:3 * TC]
    t2_sl = stat2[0:1, 3 * TC:4 * TC]
    for th in range(TH):
        w.op("scalar", lambda E, th=th: E.mul(
            m2_sl[0:1, th * TN:(th + 1) * TN], pb[4 + th][0:1, 0:TN], 1.0 / D),
            waits=[("pe_o2", P3_PEO2)])
        w.op("scalar", lambda E, th=th: E.mul(
            q2_sl[0:1, th * TN:(th + 1) * TN], pb[6 + th][0:1, 0:TN], 1.0 / D),
            inc=("act_sq2" if th == TH - 1 else None))
    A_MV2 = w.at("act_sq2")
    w.op("vector", lambda E: E.tensor_mul(t2_sl, m2_sl, m2_sl),
         waits=[("act_sq2", A_MV2)], inc="dve_x2")
    w.op("vector", lambda E: E.tensor_sub(r2_sl, q2_sl, t2_sl),
         waits=[("dve_x2", w.at("dve_x2"))], inc="dve_x2")
    w.op("vector", lambda E: E.tensor_scalar_add(t2_sl, r2_sl, 1e-5),
         waits=[("dve_x2", w.at("dve_x2"))], inc="dve_x2")
    D_MV2 = w.at("dve_x2")
    w.op("scalar", lambda E: E.activation(q2_sl, t2_sl, AF.Ln),
         waits=[("dve_x2", D_MV2)], inc="act_sq2")
    w.op("scalar", lambda E: E.activation(r2_sl, q2_sl, AF.Exp, scale=-0.5),
         waits=[("act_sq2", w.at("act_sq2"))], inc="act_sq2")
    A_MV2 = w.at("act_sq2")
    for th in range(TH):
        w.op("tensor", lambda E, th=th: E.matmul(
            pb[th][:, 0:TN], ones_f[:, :], m2_sl[0:1, th * TN:(th + 1) * TN],
            start=True, stop=True),
            waits=[("act_sq2", A_MV2), ("dve_o1", TH * CH)])
        w.op("tensor", lambda E, th=th: E.matmul(
            pb[2 + th][:, 0:TN], ones_f[:, :], r2_sl[0:1, th * TN:(th + 1) * TN],
            start=True, stop=True),
            inc=("pe_o" if th == TH - 1 else None))
    PE_BC2 = w.at("pe_o")
    for th in range(TH):
        w.op("scalar", lambda E, th=th: E.copy(
            mb2_sb[:, th * TN:(th + 1) * TN], pb[th][:, 0:TN]),
            waits=[("pe_o", PE_BC2)])
        w.op("scalar", lambda E, th=th: E.copy(
            rb2_sb[:, th * TN:(th + 1) * TN], pb[2 + th][:, 0:TN]),
            inc=("act_sq2" if th == TH - 1 else None))
    A_BC2 = w.at("act_sq2")

    X2B = w.at("dve_x2")
    for i in range(CH):
        w.op("sync", lambda E, i=i: E.dma_start(
            out=o1r_sb[i % 2][:, :], in_=out1_scr[i * 128:(i + 1) * 128, :]),
            waits=[("s_o1a", P3_O1A), ("s_o1b", P3_O1B),
                   ("dve_x2", max(0, X2B + 2 * (i - 1)))],
            inc="s_or%d" % (i % 2), inck=16)
        sld = w.at("s_or%d" % (i % 2))
        w.op("vector", lambda E, i=i: E.tensor_sub(
            t2_sb[:, :], o1r_sb[i % 2][:, :], mb2_sb[:, :]),
            waits=[("s_or%d" % (i % 2), sld), ("act_sq2", A_BC2),
                   ("dve_x2", X2B + 2 * i)],
            inc="dve_x2")
        w.op("vector", lambda E, i=i: E.tensor_mul(
            xs2_sb[:, i * TC:(i + 1) * TC], t2_sb[:, :], rb2_sb[:, :]),
            waits=[("dve_x2", X2B + 2 * i + 1)],
            inc="dve_x2")
    P3_X2 = w.at("dve_x2")

    P4_FENCE = [("pe_o2", P3_PEO2), ("dve_x2", P3_X2), ("pe_o", PE_BC2),
                ("act_sq2", A_BC2), ("s_o1a", P3_O1A), ("s_o1b", P3_O1B)]
    WG_FENCE = [("pe_o", PE_BC2), ("dve_o1", TH * CH), ("act_sq2", A_BC2)]

    def xs2_ready(ci):
        return X2B + 2 * ci + 2

    rd_srcs = [out1_scr, da_scr, db_scr]
    st_dsts = [da_scr, db_scr, None]
    st_name = ["s_da", "s_db", "s_ou"]
    rec_dd = [0, 0]
    rec_fwg = [0, 0]
    rec_wdg = [0, 0]
    rec_dast = {}
    prev_tot = None
    gfc = 0
    for sw in range(SW):
        NF = FCL[sw]
        fb = sum(FCL[:sw])
        for fc in range(NF):
            w.op("gpsimd", lambda E, gfc=gfc: E.dma_start(
                out=wg_sb[gfc % 2][:, :], in_=wg[gfc, :, :]),
                waits=WG_FENCE + [("pe_f", max(0, 2 * TH * (gfc - 1)))],
                inc="g_fw%d" % (gfc % 2), inck=16)
            w.op("gpsimd", lambda E, gfc=gfc: E.dma_start(
                out=wu_sb[gfc % 2][:, :], in_=wu[gfc, :, :]),
                inc="g_fw%d" % (gfc % 2), inck=16)
            rec_fwg[gfc % 2] = w.at("g_fw%d" % (gfc % 2))
            base = 4 * (gfc % 2)
            pf0 = w.at("pe_f")
            for role in range(2):
                wsb = wg_sb[gfc % 2] if role == 0 else wu_sb[gfc % 2]
                for th in range(TH):
                    bank = base + 2 * role + th
                    for ci in range(CH):
                        fw = [("g_fw%d" % (gfc % 2), rec_fwg[gfc % 2]),
                              ("dve_x2", xs2_ready(ci)),
                              ("act_sl", max(0, TH * (gfc - 1))),
                              ("dve_h", max(0, TH * (gfc - 1))),
                              ("dve_d", rec_dd[0] if fc == 0 else 0)]
                        w.op("tensor", lambda E, ci=ci, th=th, bank=bank,
                             wsb=wsb: E.matmul(
                            pb[bank][:, 0:TN],
                            wsb[:, ci * 128:(ci + 1) * 128],
                            xs2_sb[:, ci * TC + th * TN: ci * TC + (th + 1) * TN],
                            start=(ci == 0), stop=(ci == CH - 1)),
                            waits=fw,
                            inc=("pe_f" if ci == CH - 1 else None))
            for th in range(TH):
                w.op("scalar", lambda E, th=th, base=base, gfc=gfc: E.activation(
                    sg_sb[gfc % 2][:, th * TN:(th + 1) * TN],
                    pb[base + th][:, 0:TN], AF.Silu),
                    waits=[("pe_f", pf0 + th + 1),
                           ("dve_h", max(0, TH * (gfc - 2) + TH))],
                    inc="act_sl")
            asl = w.at("act_sl")
            for th in range(TH):
                w.op("vector", lambda E, th=th, base=base, gfc=gfc, fc=fc: E.tensor_mul(
                    hA_sb[:, fc * TC + th * TN: fc * TC + (th + 1) * TN],
                    pb[base + 2 + th][:, 0:TN],
                    sg_sb[gfc % 2][:, th * TN:(th + 1) * TN]),
                    waits=[("pe_f", pf0 + TH + th + 1), ("act_sl", asl),
                           ("pe_d", rec_dd[1] if (sw >= 1 and fc < 2) else 0)],
                    inc="dve_h")
            gfc += 1
        DVE_H_S = w.at("dve_h")

        for dmc in range(CH):
            dg = sw * CH + dmc
            wdw = [("pe_d", max(0, TH * (dg - 1)))]
            if dg < 2:
                wdw.append(("pe_o", PE_BC2))
            stn = w.op("gpsimd", lambda E, dmc=dmc, fb=fb, NF=NF, dg=dg: E.dma_start(
                out=wd_sb[dg % 2][:, 0:NF * 128],
                in_=wd[dmc, :, fb * 128:(fb + NF) * 128]),
                waits=wdw,
                inc="g_wd%d" % (dg % 2), inck=16)
            rec_wdg[dg % 2] = stn
            if sw == 0:
                w.op("sync", lambda E, dmc=dmc, dg=dg: E.dma_start(
                    out=o2r_sb[dg % 2][:, :],
                    in_=rd_srcs[0][dmc * 128:(dmc + 1) * 128, :]),
                    waits=P4_FENCE + [("dve_d", max(0, TH * (dg - 1)))],
                    inc="s_r2%d" % (dg % 2), inck=16)
            else:
                w.op("sync", lambda E, dmc=dmc, dg=dg, sw=sw: E.dma_start(
                    out=o2r_sb[dg % 2][:, :],
                    in_=rd_srcs[sw][dmc * 128:(dmc + 1) * 128, :]),
                    waits=[(st_name[sw - 1] + "0", prev_tot[0]),
                           (st_name[sw - 1] + "1", prev_tot[1]),
                           ("dve_d", max(0, TH * (dg - 1)))],
                    inc="s_r2%d" % (dg % 2), inck=16)
            sld = w.at("s_r2%d" % (dg % 2))
            for th in range(TH):
                bank = (2 * dg + th) % 4 if TH > 1 else dg % 4
                for fc in range(NF):
                    w.op("tensor", lambda E, fc=fc, th=th, bank=bank, dg=dg,
                         NF=NF: E.matmul(
                        pb[bank][:, 0:TN],
                        wd_sb[dg % 2][:, fc * 128:(fc + 1) * 128],
                        hA_sb[:, fc * TC + th * TN: fc * TC + (th + 1) * TN],
                        start=(fc == 0), stop=(fc == NF - 1)),
                        waits=[("g_wd%d" % (dg % 2), rec_wdg[dg % 2]),
                               ("dve_h", DVE_H_S),
                               ("dve_d", max(0, TH * (dg - 2) + th + 1))],
                        inc=("pe_d" if fc == NF - 1 else None))
                pd = w.at("pe_d")
                ds_sem, ds_cnt = rec_dast.get(dg - 2, ("s_da0", 0))
                w.op("vector", lambda E, th=th, bank=bank, dg=dg: E.tensor_add(
                    dat_sb[dg % 2][:, th * TN:(th + 1) * TN], pb[bank][:, 0:TN],
                    o2r_sb[dg % 2][:, th * TN:(th + 1) * TN]),
                    waits=[("pe_d", pd), ("s_r2%d" % (dg % 2), sld),
                           (ds_sem, ds_cnt)],
                    inc="dve_d")
            dd = w.at("dve_d")
            sname = "%s%d" % (st_name[sw], dg % 2)
            if sw < SW - 1:
                stn = w.op("sync", lambda E, dmc=dmc, dg=dg, sw=sw: E.dma_start(
                    out=st_dsts[sw][dmc * 128:(dmc + 1) * 128, :],
                    in_=dat_sb[dg % 2][:, :]),
                    waits=[("dve_d", dd)], inc=sname, inck=16)
            else:
                stn = w.op("sync", lambda E, dmc=dmc, dg=dg: E.dma_start(
                    out=out_fm[dmc * 128:(dmc + 1) * 128, :],
                    in_=dat_sb[dg % 2][:, :]),
                    waits=[("dve_d", dd)], inc=sname, inck=16)
            rec_dast[dg] = (sname, stn)
        rec_dd = [w.at("dve_d"), w.at("pe_d")]
        prev_tot = [w.at(st_name[sw] + "0"), w.at(st_name[sw] + "1")]

    w.op("sync", lambda E: E.nop(),
         waits=[("s_ou0", w.at("s_ou0")), ("s_ou1", w.at("s_ou1"))])

    with nc.Block() as block:
        @block.tensor
        def _(E):
            w.emit("tensor", E, sems)

        @block.scalar
        def _(E):
            w.emit("scalar", E, sems)

        @block.vector
        def _(E):
            w.emit("vector", E, sems)

        @block.sync
        def _(E):
            w.emit("sync", E, sems)

        @block.gpsimd
        def _(E):
            w.emit("gpsimd", E, sems)

    stack.pop_all()
    return nc


def prep_inputs(cfg, x, w_q, w_k, w_v, w_o, w_gate, w_up, w_down):
    B, S, D, HKV, G, HD, F = (cfg[k] for k in ("B", "S", "D", "HKV", "G", "HD", "F"))
    CH, TCH, NCHUNK, DIAG, QB = (cfg[k] for k in ("CH", "TCH", "NCHUNK", "DIAG", "QB"))
    UNITS, DQL, WC, TC, FC = (cfg[k] for k in ("UNITS", "DQL", "WC", "TC", "FC"))

    f32 = np.float32
    x = np.asarray(x, dtype=f32)
    w_q = np.asarray(w_q, dtype=f32)
    w_k = np.asarray(w_k, dtype=f32)
    w_v = np.asarray(w_v, dtype=f32)
    w_o = np.asarray(w_o, dtype=f32)
    w_gate = np.asarray(w_gate, dtype=f32)
    w_up = np.asarray(w_up, dtype=f32)
    w_down = np.asarray(w_down, dtype=f32)

    wo_r = np.ascontiguousarray(
        w_o.reshape(CH, 128, CH, 128).transpose(2, 1, 0, 3).reshape(CH, 128, CH * 128))
    wgT = w_gate.T
    wuT = w_up.T
    wdT = w_down.T
    wg_r = np.ascontiguousarray(
        wgT.reshape(CH, 128, FC, 128).transpose(2, 1, 0, 3).reshape(FC, 128, CH * 128))
    wu_r = np.ascontiguousarray(
        wuT.reshape(CH, 128, FC, 128).transpose(2, 1, 0, 3).reshape(FC, 128, CH * 128))
    wd_r = np.ascontiguousarray(
        wdT.reshape(FC, 128, CH, 128).transpose(2, 1, 0, 3).reshape(CH, 128, FC * 128))

    masks = np.zeros((128, DIAG * QB), dtype=ml_dtypes.bfloat16)
    for r in range(DIAG):
        for p in range(128):
            q0 = r * 128 + p
            if q0 < QB:
                masks[p, r * QB + q0:(r + 1) * QB] = 1.0

    xseq_b = []
    for b in range(B):
        xb = np.ascontiguousarray(x[b].T)
        xr = np.ascontiguousarray(
            xb.reshape(CH, 128, NCHUNK, TCH).transpose(2, 1, 0, 3)
            .reshape(NCHUNK, 128, CH * TCH))
        xseq_b.append(xr)

    in_maps = []
    for c in range(8):
        b, j = c // 4, c % 4
        wq_sl = np.ascontiguousarray(
            w_q[:, j * DQL:(j + 1) * DQL].reshape(CH, 128, DQL)
            .transpose(1, 0, 2).reshape(128, CH * DQL))
        h0 = UNITS * j
        kcols = w_k[:, h0 * 128:(h0 + UNITS) * 128]
        vcols = w_v[:, h0 * 128:(h0 + UNITS) * 128]
        wkv_sl = np.concatenate([kcols, vcols], axis=1)
        wkv_sl = np.ascontiguousarray(
            wkv_sl.reshape(CH, 128, WC).transpose(1, 0, 2).reshape(128, CH * WC))
        xown = np.ascontiguousarray(x[b, j * TC:(j + 1) * TC, :].T)
        selv = np.array([[(2 * j * 8 * DQL + 4 * b * DQL) * QB]], dtype=np.int32)
        in_maps.append({
            "xseq": xseq_b[b], "xown": xown, "wq": wq_sl, "wkv": wkv_sl,
            "wo": wo_r, "wg": wg_r, "wu": wu_r, "wd": wd_r,
            "masks": masks, "sel": selv,
        })
    return in_maps


def assemble_output(cfg, results):
    B, S, D, TC = cfg["B"], cfg["S"], cfg["D"], cfg["TC"]
    out = np.empty((B, S, D), dtype=np.float32)
    for c in range(8):
        b, j = c // 4, c % 4
        out[b, j * TC:(j + 1) * TC, :] = results[c]["out_fm"].T
    return out


def kernel(L0_attn_hidden_states, L0_attn_w_q, L0_attn_w_k, L0_attn_w_v,
           L0_attn_w_o, L0_ffn_W_gate, L0_ffn_W_up, L0_ffn_W_down):
    cfg = make_cfg(small=False)
    in_maps = prep_inputs(cfg, L0_attn_hidden_states, L0_attn_w_q, L0_attn_w_k,
                          L0_attn_w_v, L0_attn_w_o, L0_ffn_W_gate, L0_ffn_W_up,
                          L0_ffn_W_down)
    nc = build_program(cfg)
    res = run_bass_kernel_spmd(nc, in_maps, core_ids=list(range(8)))
    return assemble_output(cfg, res.results)


# version 09
# speedup vs baseline: 1.4665x; 1.0266x over previous
import numpy as np
import ml_dtypes
import concourse.bass as bass
import concourse.mybir as mybir
from concourse.bass_utils import run_bass_kernel_spmd

F32 = mybir.dt.float32
BF16 = mybir.dt.bfloat16
U8 = mybir.dt.uint8
I32 = mybir.dt.int32
AF = mybir.ActivationFunctionType
ALU = mybir.AluOpType

ENGS = ("tensor", "scalar", "vector", "sync", "gpsimd")


def make_cfg(small=False):
    if small:
        c = dict(B=2, S=1024, D=1024, HKV=8, G=1, HD=128, F=1536)
    else:
        c = dict(B=2, S=4096, D=4096, HKV=8, G=4, HD=128, F=11008)
    c["CH"] = c["D"] // 128
    c["QB"] = c["S"] // 8
    c["TCH"] = c["QB"]
    c["NCHUNK"] = c["S"] // c["TCH"]
    c["DIAG"] = c["QB"] // 128
    c["UNITS"] = c["B"] * c["HKV"] // 8
    c["DQL"] = c["UNITS"] * c["G"] * 128
    c["QNC"] = c["DQL"] // 128
    c["WC"] = c["UNITS"] * 128 * 2
    c["TC"] = c["B"] * c["S"] // 8
    c["TH"] = max(1, c["TC"] // 512)
    c["TN"] = c["TC"] // c["TH"]
    c["FC"] = c["F"] // 128
    c["SW"] = 3
    base = c["FC"] // c["SW"]
    rem = c["FC"] - base * c["SW"]
    c["FCL"] = [base + (1 if i < rem else 0) for i in range(c["SW"])]
    c["FCMAX"] = max(c["FCL"])
    c["SCALE"] = float(c["HD"]) ** -0.5
    return c


class Walk:

    def __init__(self):
        self.ops = {e: [] for e in ENGS}
        self.cnt = {}

    def at(self, sem):
        return self.cnt.get(sem, 0)

    def op(self, eng, emit, waits=(), inc=None, inck=1):
        ws = tuple((s, v) for (s, v) in waits if v > 0)
        self.ops[eng].append((emit, ws, inc, inck))
        if inc is not None:
            self.cnt[inc] = self.cnt.get(inc, 0) + inck
        return self.cnt.get(inc, 0) if inc else None

    def emit(self, eng_name, eng, sems):
        last = {}
        for (emit, waits, inc, inck) in self.ops[eng_name]:
            for (s, v) in waits:
                if last.get(s, -1) >= v:
                    continue
                eng.wait_ge(sems[s], v)
                last[s] = v
            ins = emit(eng)
            if inc is not None:
                ins.then_inc(sems[inc], inck)


def build_program(cfg):
    B, S, D, HKV, G, HD, F = (cfg[k] for k in ("B", "S", "D", "HKV", "G", "HD", "F"))
    CH, QB, TCH, NCHUNK, DIAG = (cfg[k] for k in ("CH", "QB", "TCH", "NCHUNK", "DIAG"))
    UNITS, DQL, QNC, WC = (cfg[k] for k in ("UNITS", "DQL", "QNC", "WC"))
    TC, TH, TN, FC = (cfg[k] for k in ("TC", "TH", "TN", "FC"))
    SW, FCL, FCMAX = cfg["SW"], cfg["FCL"], cfg["FCMAX"]
    SCALE = cfg["SCALE"]
    VSUB = TCH // 128

    from contextlib import ExitStack
    stack = ExitStack()
    nc = bass.Bass("TRN2", target_bir_lowering=False)

    xseq = nc.dram_tensor("xseq", [NCHUNK, 128, CH * TCH], F32, kind="ExternalInput").ap()
    xown = nc.dram_tensor("xown", [D, TC], F32, kind="ExternalInput").ap()
    wq = nc.dram_tensor("wq", [128, CH * DQL], F32, kind="ExternalInput").ap()
    wkv = nc.dram_tensor("wkv", [128, CH * WC], F32, kind="ExternalInput").ap()
    wo = nc.dram_tensor("wo", [CH, 128, CH * 128], F32, kind="ExternalInput").ap()
    wg = nc.dram_tensor("wg", [FC, 128, CH * 128], F32, kind="ExternalInput").ap()
    wu = nc.dram_tensor("wu", [FC, 128, CH * 128], F32, kind="ExternalInput").ap()
    wd = nc.dram_tensor("wd", [CH, 128, FC * 128], F32, kind="ExternalInput").ap()
    masks_in = nc.dram_tensor("masks", [128, DIAG * QB], BF16, kind="ExternalInput").ap()
    sel = nc.dram_tensor("sel", [1, 1], I32, kind="ExternalInput").ap()
    out_fm = nc.dram_tensor("out_fm", [D, TC], F32, kind="ExternalOutput").ap()

    q_scr = nc.dram_tensor("q_scr", [DQL, S], BF16).ap()
    k_scr = nc.dram_tensor("k_scr", [UNITS, 128, S], BF16).ap()
    v_scr = nc.dram_tensor("v_scr", [UNITS, 128, S], BF16).ap()
    attn_scr_h = nc.dram_tensor("attn_scr", [8, DQL, QB], BF16)
    gath_scr_h = nc.dram_tensor("gath_scr", [8, 8 * DQL, QB], BF16,
                                addr_space="Shared")
    attn_scr = attn_scr_h.ap()
    out1_scr = nc.dram_tensor("out1_scr", [D, TC], F32).ap()
    da_scr = nc.dram_tensor("da_scr", [D, TC], F32).ap()
    db_scr = nc.dram_tensor("db_scr", [D, TC], F32).ap()

    ones_bf = stack.enter_context(nc.sbuf_tensor("ones_bf", [128, 1], BF16))
    onesp_f = stack.enter_context(nc.sbuf_tensor("onesp_f", [128, 1], F32))
    ones_f = stack.enter_context(nc.sbuf_tensor("ones_f", [1, 128], F32))
    sel_sb = stack.enter_context(nc.sbuf_tensor("sel_sb", [1, 1], I32))

    def roundup(x, a=32):
        return (x + a - 1) // a * a

    class Arena:
        def __init__(self):
            cap = nc.sbuf_bytes_remaining - 64
            self.big = stack.enter_context(
                nc.sbuf_tensor("arena", [nc.NUM_PARTITIONS, cap], U8))
            self.base = nc.lookup_mloc(self.big).addr
            self.cap = cap
            self.off = 0
            self.maxoff = 0

        def take(self, name, shape, dtype):
            per = int(np.prod(shape[1:])) * mybir.dt.size(dtype)
            o = self.off
            self.off = roundup(self.off + per)
            self.maxoff = max(self.maxoff, self.off)
            assert self.off <= self.cap, (name, self.off, self.cap)
            return nc.alloc_sbuf_tensor_at(name, list(shape), dtype,
                                           offset=self.base + o)

        def reset(self):
            self.off = 0

    ar = Arena()
    wq_sb = ar.take("wq_sb", [128, CH * DQL], BF16)
    wkv_sb = ar.take("wkv_sb", [128, CH * WC], BF16)
    xc_sb = ar.take("xc_sb", [128, CH * TCH], BF16)
    xs_sb = [ar.take(f"xs{i}", [128, CH * TCH], BF16) for i in range(2)]
    sq_sb = [ar.take(f"sq{i}", [128, TCH], BF16) for i in range(2)]
    EVW = max(TCH, UNITS * 128)
    ev_sb = [ar.take(f"ev{i}", [128, EVW], BF16) for i in range(2)]
    t1_sb = ar.take("t1_sb", [128, TCH], BF16)
    mb_sb = ar.take("mb_sb", [128, TCH], BF16)
    rb_sb = ar.take("rb_sb", [128, TCH], BF16)
    stat1 = ar.take("stat1", [1, 4 * TCH], F32)
    ar.reset()
    qt_sb = [ar.take(f"qt{u}", [128, G * S], BF16) for u in range(UNITS)]
    kt_sb = [ar.take(f"kt{u}", [128, S], BF16) for u in range(UNITS)]
    vt_sb = [ar.take(f"vt{u}", [128, S], BF16) for u in range(UNITS)]
    pr_sb = [ar.take(f"pr{i}", [128, 2 * QB], BF16) for i in range(2)]
    pm_sb = [ar.take(f"pm{i}", [128, 2 * QB], BF16) for i in range(2)]
    sp_sb = [ar.take(f"sp{i}", [128, QB], BF16) for i in range(3)]
    ib_sb = [ar.take(f"ib{i}", [128, QB], F32) for i in range(2)]
    aev_sb = [ar.take(f"aev{i}", [128, QB], BF16) for i in range(2)]
    onesmm_sb = ar.take("onesmm", [128, 128], BF16)
    masks_sb = ar.take("masks_sb", [128, DIAG * QB], BF16)
    P2_END = ar.off
    ar.reset()
    xs2_sb = ar.take("xs2_sb", [128, CH * TC], BF16)
    xs2_end = ar.off
    stat2 = ar.take("stat2", [1, 4 * TC], F32)
    o1_sb = [ar.take(f"o1{i}", [128, TC], F32) for i in range(2)]
    sq2_sb = [ar.take(f"sq2{i}", [128, TC], BF16) for i in range(2)]
    xo1_off = ar.off
    xo_sb = [None, ar.take("xo1", [128, TC], F32)]
    ar.off = xs2_end
    hA_sb = ar.take("hA_sb", [128, FCMAX * TC], BF16)
    HA_END = ar.off
    ar.off = P2_END
    wo_sb = [ar.take(f"wo{i}", [128, CH * 128], BF16) for i in range(2)]
    xo_sb[0] = ar.take("xo0", [128, TC], F32)
    E_END = ar.off
    ar.off = max(P2_END, HA_END)
    wg_sb = [ar.take(f"wgs{i}", [128, CH * 128], BF16) for i in range(2)]
    wu_sb = [ar.take(f"wus{i}", [128, CH * 128], BF16) for i in range(2)]
    wd_sb = [ar.take(f"wds{i}", [128, FCMAX * 128], BF16) for i in range(2)]
    sg_sb = [ar.take(f"sg{i}", [128, TC], BF16) for i in range(2)]
    o2r_sb = [ar.take(f"o2r{i}", [128, TC], F32) for i in range(2)]
    dat_sb = [ar.take(f"dat{i}", [128, TC], F32) for i in range(2)]
    C_END = ar.off
    ar.off = E_END
    AO_OFF = ar.off
    ao_sb = ar.take("ao_sb", [128, CH * TC], BF16)
    AO_END = ar.off
    ao3_sb = nc.alloc_sbuf_tensor_at("ao3", [128, CH, TC], BF16,
                                     offset=ar.base + AO_OFF)
    ar.off = max(C_END, AO_END - 16384)
    o1r_sb = [ar.take(f"o1r{i}", [128, TC], F32) for i in range(2)]
    t2_sb = ar.take("t2_sb", [128, TC], BF16)
    mb2_sb = ar.take("mb2_sb", [128, TC], BF16)
    rb2_sb = ar.take("rb2_sb", [128, TC], BF16)

    sc_ps = [stack.enter_context(nc.psum_tensor(f"sc{i}", [128, 1024], F32))
             for i in range(2)]
    sm_ps = [stack.enter_context(nc.psum_tensor(f"sm{i}", [128, 512], F32))
             for i in range(4)]

    class PBView:
        def __init__(self, h, coff):
            self.h, self.coff = h, coff

        def __getitem__(self, idx):
            r, c = idx
            return self.h[r, c.start + self.coff:c.stop + self.coff]

    pb = [PBView(sc_ps[0], 0), PBView(sc_ps[0], 512),
          PBView(sc_ps[1], 0), PBView(sc_ps[1], 512),
          PBView(sm_ps[0], 0), PBView(sm_ps[1], 0),
          PBView(sm_ps[2], 0), PBView(sm_ps[3], 0)]

    semnames = ["g_w0", "g_w1", "g_w2", "g_w3", "g_wkv", "g_x", "g_ao", "g_wo0", "g_wo1", "g_fw0", "g_fw1",
                "g_wd0", "g_wd1",
                "s_ms", "s_kv", "s_xo0", "s_xo1", "s_or0", "s_or1",
                "s_r20", "s_r21", "s_q", "g_pre",
                "s_o1a", "s_o1b",
                "s_da0", "s_da1", "s_db0", "s_db1", "s_ou0", "s_ou1",
                "a_q0", "a_q1",
                "cc", "g_ms",
                "act_sq", "pe_st", "act_mv", "dve_mv", "pe_bc", "act_bc",
                "dve_x", "dve_xs", "pe_mm", "act_ev",
                "pe_s", "act_pr", "pe_av", "dve2",
                "pe_o", "dve_o1", "act_sq2", "pe_o2", "dve_x2",
                "pe_f", "act_sl", "dve_h", "pe_d", "dve_d"]
    sems = {n: stack.enter_context(nc.semaphore(n)) for n in semnames}

    w = Walk()

    w.op("gpsimd", lambda E: E.memset(ones_bf[:, :], 1.0))
    w.op("gpsimd", lambda E: E.memset(onesp_f[:, :], 1.0))
    w.op("gpsimd", lambda E: E.memset(ones_f[:, :], 1.0), inc="g_ms")
    w.op("sync", lambda E: E.dma_start(out=sel_sb[:, :], in_=sel[:, :]),
         inc="s_ms", inck=16)
    NSP = 4

    rec_xsfree = [0, 0]
    rec_ev = []
    evst = {"evg": 0}
    n_groups = QNC + UNITS + VSUB

    def p1_stats(t):
        w.op("gpsimd", lambda E, t=t: E.dma_start(out=xc_sb[:, :], in_=xseq[t, :, :]),
             waits=[("act_sq", t * CH), ("pe_st", t * CH), ("dve_x", t * CH)],
             inc="g_x", inck=16)
        ld_n = w.at("g_x")
        if t == 0:
            for i in range(NSP):
                lo, hi = i * (CH // NSP), (i + 1) * (CH // NSP)
                w.op("gpsimd", lambda E, lo=lo, hi=hi: E.dma_start(
                    out=wq_sb[:, lo * DQL:hi * DQL],
                    in_=wq[:, lo * DQL:hi * DQL]),
                    inc="g_w%d" % i, inck=16)
            w.op("gpsimd", lambda E: E.dma_start(out=wkv_sb[:, :], in_=wkv[:, :]),
                 inc="g_wkv", inck=16)
        for ci in range(CH):
            w.op("scalar", lambda E, ci=ci: E.activation(
                sq_sb[ci % 2][:, :], xc_sb[:, ci * TCH:(ci + 1) * TCH], AF.Square),
                waits=[("g_x", ld_n), ("pe_st", t * CH + ci - 1)],
                inc="act_sq")
            w.op("tensor", lambda E, ci=ci: E.matmul(
                pb[6][0:1, 0:TCH], ones_bf[:, :], xc_sb[:, ci * TCH:(ci + 1) * TCH],
                start=(ci == 0), stop=(ci == CH - 1)),
                waits=[("g_x", ld_n), ("act_mv", 3 * t), ("g_ms", 1)])
            w.op("tensor", lambda E, ci=ci: E.matmul(
                pb[7][0:1, 0:TCH], ones_bf[:, :], sq_sb[ci % 2][:, :],
                start=(ci == 0), stop=(ci == CH - 1)),
                waits=[("act_sq", t * CH + ci + 1)],
                inc="pe_st")
        pe_st_n = w.at("pe_st")

        m_sl = stat1[0:1, 0:TCH]
        q_sl = stat1[0:1, TCH:2 * TCH]
        r_sl = stat1[0:1, 2 * TCH:3 * TCH]
        tp_sl = stat1[0:1, 3 * TCH:4 * TCH]
        w.op("scalar", lambda E: E.mul(m_sl, pb[6][0:1, 0:TCH], 1.0 / D),
             waits=[("pe_st", pe_st_n)])
        w.op("scalar", lambda E: E.mul(q_sl, pb[7][0:1, 0:TCH], 1.0 / D),
             inc="act_mv")
        a_mv = w.at("act_mv")
        w.op("vector", lambda E: E.tensor_mul(tp_sl, m_sl, m_sl),
             waits=[("act_mv", a_mv)], inc="dve_mv")
        w.op("vector", lambda E: E.tensor_sub(r_sl, q_sl, tp_sl),
             waits=[("dve_mv", w.at("dve_mv"))], inc="dve_mv")
        w.op("vector", lambda E: E.tensor_scalar_add(tp_sl, r_sl, 1e-5),
             waits=[("dve_mv", w.at("dve_mv"))], inc="dve_mv")
        d_mv = w.at("dve_mv")
        w.op("scalar", lambda E: E.activation(q_sl, tp_sl, AF.Ln),
             waits=[("dve_mv", d_mv)], inc="act_mv")
        w.op("scalar", lambda E: E.activation(r_sl, q_sl, AF.Exp, scale=-0.5),
             waits=[("act_mv", w.at("act_mv"))], inc="act_mv")
        a_mv = w.at("act_mv")
        w.op("tensor", lambda E: E.matmul(pb[4][:, 0:TCH], ones_f[:, :], m_sl,
                                          start=True, stop=True),
             waits=[("act_mv", a_mv), ("act_bc", 2 * t)])
        w.op("tensor", lambda E: E.matmul(pb[5][:, 0:TCH], ones_f[:, :], r_sl,
                                          start=True, stop=True), inc="pe_bc")
        p_bc = w.at("pe_bc")
        w.op("scalar", lambda E: E.copy(mb_sb[:, :], pb[4][:, 0:TCH]),
             waits=[("pe_bc", p_bc), ("dve_xs", t * CH)])
        w.op("scalar", lambda E: E.copy(rb_sb[:, :], pb[5][:, 0:TCH]),
             inc="act_bc", inck=2)
        a_bc = w.at("act_bc")
        xs = xs_sb[t % 2]
        for ci in range(CH):
            w.op("vector", lambda E, ci=ci: E.tensor_sub(
                t1_sb[:, :], xc_sb[:, ci * TCH:(ci + 1) * TCH], mb_sb[:, :]),
                waits=[("act_bc", a_bc), ("pe_mm", rec_xsfree[t % 2]),
                       ("dve_xs", t * CH + ci)],
                inc="dve_x")
            w.op("vector", lambda E, ci=ci, xs=xs: E.tensor_mul(
                xs[:, ci * TCH:(ci + 1) * TCH], t1_sb[:, :], rb_sb[:, :]),
                waits=[("dve_x", t * CH + ci + 1)],
                inc="dve_xs")

    def p1_group(t, gi, kind, idx):
        xs = xs_sb[t % 2]
        gbase = t * n_groups
        evg = evst["evg"]
        if kind == "q":
            nq = idx
            bank = gi % 2
            for ci in range(CH):
                w.op("tensor", lambda E, ci=ci, nq=nq, bank=bank, xs=xs: E.matmul(
                    pb[bank][:, 0:TCH],
                    wq_sb[:, ci * DQL + nq * 128: ci * DQL + (nq + 1) * 128],
                    xs[:, ci * TCH:(ci + 1) * TCH],
                    start=(ci == 0), stop=(ci == CH - 1)),
                    waits=[("g_w%d" % (ci * NSP // CH), 16),
                           ("dve_xs", t * CH + ci + 1),
                           ("act_ev", gbase + gi - 1)],
                    inc=("pe_mm" if ci == CH - 1 else None))
            mm_n = w.at("pe_mm")
            w.op("scalar", lambda E, bank=bank, evg=evg: E.copy(
                ev_sb[evg % 2][:, 0:TCH], pb[bank][:, 0:TCH]),
                waits=[("pe_mm", mm_n),
                       ("a_q%d" % (evg % 2), rec_ev[evg - 2] if evg >= 2 else 0)],
                inc="act_ev")
            a_ev = w.at("act_ev")
            stn = w.op("scalar", lambda E, nq=nq, t=t, evg=evg: E.dma_start(
                out=q_scr[nq * 128:(nq + 1) * 128, t * TCH:(t + 1) * TCH],
                in_=ev_sb[evg % 2][:, 0:TCH]),
                waits=[("act_ev", a_ev)], inc="a_q%d" % (evg % 2), inck=16)
            rec_ev.append(stn)
        elif kind == "k":
            u = idx
            bank = gi % 2
            for ci in range(CH):
                w.op("tensor", lambda E, ci=ci, u=u, bank=bank, xs=xs: E.matmul(
                    pb[bank][:, 0:TCH],
                    wkv_sb[:, ci * WC + u * 128: ci * WC + (u + 1) * 128],
                    xs[:, ci * TCH:(ci + 1) * TCH],
                    start=(ci == 0), stop=(ci == CH - 1)),
                    waits=[("g_wkv", 16), ("dve_xs", t * CH + ci + 1),
                           ("act_ev", gbase + gi - 1)],
                    inc=("pe_mm" if ci == CH - 1 else None))
            mm_n = w.at("pe_mm")
            w.op("scalar", lambda E, bank=bank, evg=evg: E.copy(
                ev_sb[evg % 2][:, 0:TCH], pb[bank][:, 0:TCH]),
                waits=[("pe_mm", mm_n),
                       ("a_q%d" % (evg % 2), rec_ev[evg - 2] if evg >= 2 else 0)],
                inc="act_ev")
            a_ev = w.at("act_ev")
            stn = w.op("scalar", lambda E, u=u, t=t, evg=evg: E.dma_start(
                out=k_scr[u, :, t * TCH:(t + 1) * TCH], in_=ev_sb[evg % 2][:, 0:TCH]),
                waits=[("act_ev", a_ev)], inc="a_q%d" % (evg % 2), inck=16)
            rec_ev.append(stn)
        else:
            sub = idx
            bank = 2 + sub % 2
            vg = t * VSUB + sub
            for ci in range(CH):
                w.op("tensor", lambda E, ci=ci, sub=sub, bank=bank, xs=xs: E.matmul(
                    pb[bank][:, 0:UNITS * 128],
                    xs[:, ci * TCH + sub * 128: ci * TCH + (sub + 1) * 128],
                    wkv_sb[:, ci * WC + UNITS * 128:(ci + 1) * WC],
                    start=(ci == 0), stop=(ci == CH - 1)),
                    waits=[("g_wkv", 16), ("dve_xs", t * CH + ci + 1),
                           ("act_ev", gbase + gi - 1)],
                    inc=("pe_mm" if ci == CH - 1 else None))
            mm_n = w.at("pe_mm")
            w.op("scalar", lambda E, bank=bank, evg=evg: E.copy(
                ev_sb[evg % 2][:, 0:UNITS * 128], pb[bank][:, 0:UNITS * 128]),
                waits=[("pe_mm", mm_n),
                       ("a_q%d" % (evg % 2), rec_ev[evg - 2] if evg >= 2 else 0)],
                inc="act_ev")
            a_ev = w.at("act_ev")
            for u in range(UNITS):
                stn = w.op("scalar", lambda E, u=u, vg=vg, evg=evg: E.dma_start(
                    out=v_scr[u, :, vg * 128:(vg + 1) * 128],
                    in_=ev_sb[evg % 2][:, u * 128:(u + 1) * 128]),
                    waits=[("act_ev", a_ev)], inc="a_q%d" % (evg % 2), inck=16)
            rec_ev.append(stn)
        evst["evg"] += 1

    groups = ([("q", nq) for nq in range(QNC)] + [("k", u) for u in range(UNITS)]
              + [("v", sub) for sub in range(VSUB)])
    SPLIT = max(2, n_groups // 2)
    p1_stats(0)
    for t in range(NCHUNK):
        for gi, (kind, idx) in enumerate(groups[:SPLIT]):
            p1_group(t, gi, kind, idx)
        if t + 1 < NCHUNK:
            p1_stats(t + 1)
        for gi, (kind, idx) in enumerate(groups[SPLIT:], start=SPLIT):
            p1_group(t, gi, kind, idx)
        rec_xsfree[t % 2] = w.at("pe_mm")

    P1T = {n: w.at(n) for n in ("a_q0", "a_q1")}

    p1waits = [(n, v) for n, v in P1T.items()]
    P1_EV = w.at("act_ev")
    P1_BC = w.at("act_bc")
    P1_MV = w.at("act_mv")
    w.op("sync", lambda E: E.dma_start(out=masks_sb[:, :], in_=masks_in[:, :]),
         waits=p1waits, inc="s_ms", inck=16)
    S_MS = w.at("s_ms")
    kv_ready = []
    for u in range(UNITS):
        w.op("sync", lambda E, u=u: E.dma_start(
            out=qt_sb[u][:, :].rearrange("p (g t) -> p g t", g=G),
            in_=q_scr[u * G * 128:(u + 1) * G * 128, :].rearrange(
                "(g p) t -> p g t", p=128)),
            waits=p1waits, inc="s_kv", inck=16)
        w.op("sync", lambda E, u=u: E.dma_start(
            out=kt_sb[u][:, :], in_=k_scr[u, :, :]), inc="s_kv", inck=16)
        w.op("sync", lambda E, u=u: E.dma_start(
            out=vt_sb[u][:, :], in_=v_scr[u, :, :]), inc="s_kv", inck=16)
        kv_ready.append(w.at("s_kv"))
    w.op("gpsimd", lambda E: E.memset(onesmm_sb[:, :], 1.0),
         waits=p1waits, inc="g_ms")
    G_MS2 = w.at("g_ms")
    w.op("gpsimd", lambda E: E.dma_start(out=wo_sb[0][:, :], in_=wo[0, :, :]),
         waits=p1waits, inc="g_pre", inck=16)
    w.op("gpsimd", lambda E: E.dma_start(out=wo_sb[1][:, :], in_=wo[1, :, :]),
         inc="g_pre", inck=16)
    w.op("gpsimd", lambda E: E.dma_start(out=xo_sb[0][:, :], in_=xown[0:128, :]),
         inc="g_pre", inck=16)
    G_PRE = w.at("g_pre")

    pairs = []
    pair_gidx = []
    gi = -1
    for qb in range(8):
        nkc = (qb + 1) * DIAG
        assert nkc % 2 == 0
        for u in range(UNITS):
            for g in range(G):
                for pi in range(nkc // 2):
                    if pi == 0:
                        gi += 1
                    pairs.append((qb, u, g, 2 * pi, nkc))
                    pair_gidx.append(gi)
    NPAIR = len(pairs)

    rec_exp, rec_avh, rec_suh, rec_dvp = [], [], [], []
    rec_recip, rec_aevm, rec_store = {}, {}, {}
    rec_pmav = [0, 0]
    stash = {}
    st2 = {"dpi": 0}

    def p2_scores(pi):
        qb, u, g, kc, nkc = pairs[pi]
        b2 = pi % 2
        ws = [("s_kv", kv_ready[u])]
        if pi >= 2:
            ws.append(("act_pr", rec_exp[pi - 2]))
        else:
            ws.append(("act_ev", P1_EV))
        for h in range(2):
            w.op("tensor", lambda E, u=u, kc=kc, g=g, qb=qb, b2=b2, h=h: E.matmul(
                sc_ps[b2][:, h * QB:(h + 1) * QB],
                kt_sb[u][:, (kc + h) * 128:(kc + h + 1) * 128],
                qt_sb[u][:, g * S + qb * QB: g * S + (qb + 1) * QB],
                start=True, stop=True),
                waits=(ws if h == 0 else []),
                inc=("pe_s" if h == 1 else None))

    def p2_actdve(pi):
        qb, u, g, kc, nkc = pairs[pi]
        b2 = pi % 2
        diag = kc >= qb * DIAG
        ws = [("pe_s", pi + 1)]
        if pi >= 2:
            ws += [("pe_av", rec_avh[pi - 2]), ("dve2", rec_dvp[pi - 2])]
        if diag:
            r = kc - qb * DIAG
            pmb = st2["dpi"] % 2
            st2["dpi"] += 1
            w.op("scalar", lambda E, b2=b2: E.activation(
                pr_sb[b2][:, 0:QB], sc_ps[b2][:, 0:QB], AF.Exp, scale=SCALE),
                waits=ws, inc="act_pr")
            e_lo = w.at("act_pr")
            w.op("vector", lambda E, b2=b2, pmb=pmb, r=r: E.tensor_mul(
                pm_sb[pmb][:, 0:QB], pr_sb[b2][:, 0:QB],
                masks_sb[:, r * QB:(r + 1) * QB]),
                waits=[("act_pr", e_lo), ("s_ms", S_MS),
                       ("pe_av", rec_pmav[pmb])],
                inc="dve2")
            m_lo = w.at("dve2")
            w.op("scalar", lambda E, b2=b2: E.activation(
                pr_sb[b2][:, QB:2 * QB], sc_ps[b2][:, QB:2 * QB],
                AF.Exp, scale=SCALE), inc="act_pr")
            e_hi = w.at("act_pr")
            w.op("vector", lambda E, b2=b2, pmb=pmb, r=r: E.tensor_mul(
                pm_sb[pmb][:, QB:2 * QB], pr_sb[b2][:, QB:2 * QB],
                masks_sb[:, (r + 1) * QB:(r + 2) * QB]),
                waits=[("act_pr", e_hi)], inc="dve2")
            m_hi = w.at("dve2")
            rhs_t = pm_sb[pmb]
            avw = ([("dve2", m_lo)], [("dve2", m_hi)])
        else:
            pmb = None
            w.op("scalar", lambda E, b2=b2: E.activation(
                pr_sb[b2][:, :], sc_ps[b2][:, :], AF.Exp, scale=SCALE),
                waits=ws, inc="act_pr")
            e = w.at("act_pr")
            rhs_t = pr_sb[b2]
            avw = ([("act_pr", e)], [])
        rec_exp.append(w.at("act_pr"))
        paw = [] if diag else [("act_pr", rec_exp[pi])]
        if pi >= 3:
            paw.append(("pe_av", rec_suh[pi - 3]))
        w.op("vector", lambda E, rhs_t=rhs_t, s3=pi % 3: E.tensor_add(
            sp_sb[s3][:, :], rhs_t[:, 0:QB], rhs_t[:, QB:2 * QB]),
            waits=paw, inc="dve2")
        rec_dvp.append(w.at("dve2"))
        stash[pi] = (rhs_t, avw, pmb)

    def p2_av(pi):
        qb, u, g, kc, nkc = pairs[pi]
        gidx = pair_gidx[pi]
        par = gidx % 2
        rhs_t, avw, pmb = stash.pop(pi)
        for h in range(2):
            ws = list(avw[h])
            if kc == 0 and h == 0:
                ws.append(("dve2", rec_aevm[gidx - 2]) if gidx >= 2
                          else ("act_bc", P1_BC))
            w.op("tensor", lambda E, u=u, kc=kc, h=h, par=par, rhs_t=rhs_t,
                 nkc=nkc: E.matmul(
                sm_ps[par][:, 0:QB],
                vt_sb[u][:, (kc + h) * 128:(kc + h + 1) * 128],
                rhs_t[:, h * QB:(h + 1) * QB],
                start=(kc + h == 0), stop=(kc + h == nkc - 1)),
                waits=ws, inc=("pe_av" if h == 1 else None))
        rec_avh.append(w.at("pe_av"))
        if pmb is not None:
            rec_pmav[pmb] = w.at("pe_av")

    def p2_su(pi):
        qb, u, g, kc, nkc = pairs[pi]
        gidx = pair_gidx[pi]
        par = gidx % 2
        ws = [("dve2", rec_dvp[pi])]
        if kc == 0:
            ws.append(("dve2", rec_recip[gidx - 2]) if gidx >= 2
                      else ("act_mv", P1_MV))
            if gidx < 2:
                ws.append(("g_ms", G_MS2))
        w.op("tensor", lambda E, par=par, s3=pi % 3, kc=kc, nkc=nkc: E.matmul(
            sm_ps[2 + par][:, 0:QB], onesmm_sb[:, :], sp_sb[s3][:, :],
            start=(kc == 0), stop=(kc == nkc - 2)),
            waits=ws, inc="pe_av")
        rec_suh.append(w.at("pe_av"))

    def p2_groupend(pi):
        qb, u, g, kc, nkc = pairs[pi]
        gidx = pair_gidx[pi]
        par = gidx % 2
        w.op("vector", lambda E, par=par: E.reciprocal(
            ib_sb[par][:, :], sm_ps[2 + par][:, 0:QB]),
            waits=[("pe_av", rec_suh[pi])], inc="dve2")
        rec_recip[gidx] = w.at("dve2")
        ws = [("pe_av", rec_avh[pi])]
        if gidx >= 2:
            ws.append(("s_q", rec_store[gidx - 2]))
        w.op("vector", lambda E, par=par: E.tensor_mul(
            aev_sb[par][:, :], sm_ps[par][:, 0:QB], ib_sb[par][:, :]),
            waits=ws, inc="dve2")
        rec_aevm[gidx] = w.at("dve2")
        w.op("sync", lambda E, qb=qb, u=u, g=g, par=par: E.dma_start(
            out=attn_scr[qb, (u * G + g) * 128:(u * G + g + 1) * 128, :],
            in_=aev_sb[par][:, :]),
            waits=[("dve2", rec_aevm[gidx])], inc="s_q", inck=16)
        rec_store[gidx] = w.at("s_q")
        if u == UNITS - 1 and g == G - 1:
            w.op("gpsimd", lambda E, qb=qb: E.collective_compute(
                "AllGather", ALU.bypass, replica_groups=[list(range(8))],
                ins=[attn_scr_h.ap()[qb].opt()],
                outs=[gath_scr_h.ap()[qb].opt()]),
                waits=[("s_q", rec_store[gidx])], inc="cc")

    for pi in range(NPAIR + 3):
        if pi < NPAIR:
            p2_scores(pi)
        if 0 <= pi - 2 < NPAIR:
            p2_av(pi - 2)
        if 0 <= pi - 3 < NPAIR:
            p2_su(pi - 3)
            if pairs[pi - 3][3] == pairs[pi - 3][4] - 2:
                p2_groupend(pi - 3)
        if pi < NPAIR:
            p2_actdve(pi)
    NG = w.at("cc")
    P2_ACT = w.at("act_pr")
    P2_DVE = w.at("dve2")

    gre = {"base": None, "tmp": None}
    GSPL = 4 if CH % 4 == 0 else 1
    CPS = CH // GSPL

    def p3_gather(E, k, h):
        if gre["base"] is None:
            gre["base"] = E.alloc_register("gbase")
            gre["tmp"] = E.alloc_register("gtmp")
            E.reg_load(gre["base"], sel_sb[0:1, 0:1])
        E.reg_mov(gre["tmp"], h * 8 * DQL * QB + k * CPS * 128 * QB)
        E.reg_add(gre["tmp"], gre["tmp"], gre["base"])
        return E.dma_start(
            out=ao3_sb[:, k * CPS:(k + 1) * CPS, h * QB:(h + 1) * QB],
            in_=bass.AP(gath_scr_h, gre["tmp"],
                        [[QB, 128], [128 * QB, CPS], [1, QB]]))

    P3G_H = []
    for h in range(2):
        for k in range(GSPL):
            w.op("gpsimd", lambda E, k=k, h=h: p3_gather(E, k, h),
                 waits=[("cc", NG - 1 if h == 0 else NG), ("s_ms", S_MS)],
                 inc="g_ao", inck=16)
        P3G_H.append(w.at("g_ao"))
    P3G_LD = w.at("g_ao")

    rec_wofree = [0, 0]
    rec_wog = [0, 0]
    rec_o1st = []
    rec_o1pe = {}
    for i in range(CH):
        if i >= 2:
            stn = w.op("gpsimd", lambda E, i=i: E.dma_start(
                out=wo_sb[i % 2][:, :], in_=wo[i, :, :]),
                waits=[("pe_o", rec_wofree[i % 2])],
                inc="g_wo%d" % (i % 2), inck=16)
            rec_wog[i % 2] = stn
            wo_w = ("g_wo%d" % (i % 2), rec_wog[i % 2])
        else:
            wo_w = ("g_pre", G_PRE)
        if i >= 1:
            w.op("sync", lambda E, i=i: E.dma_start(
                out=xo_sb[i % 2][:, :], in_=xown[i * 128:(i + 1) * 128, :]),
                waits=[("cc", NG), ("dve_o1", max(0, TH * (i - 1)))],
                inc="s_xo%d" % (i % 2), inck=16)
            xo_w = ("s_xo%d" % (i % 2), w.at("s_xo%d" % (i % 2)))
        else:
            xo_w = ("g_pre", G_PRE)
        for th in range(TH):
            bank = (2 * i + th) % 4 if TH > 1 else i % 4
            for ci in range(CH):
                fw = [wo_w, ("g_ao", P3G_H[th] if TH == 2 else P3G_LD),
                      ("dve_o1", max(0, TH * (i - 2) + th + 1))]
                if i < 2:
                    fw.append(("act_pr", P2_ACT))
                w.op("tensor", lambda E, i=i, th=th, ci=ci, bank=bank: E.matmul(
                    pb[bank][:, 0:TN],
                    wo_sb[i % 2][:, ci * 128:(ci + 1) * 128],
                    ao_sb[:, ci * TC + th * TN: ci * TC + (th + 1) * TN],
                    start=(ci == 0), stop=(ci == CH - 1)),
                    waits=fw,
                    inc=("pe_o" if ci == CH - 1 else None))
            pe_o_n = w.at("pe_o")
            w.op("vector", lambda E, i=i, th=th, bank=bank: E.tensor_add(
                o1_sb[i % 2][:, th * TN:(th + 1) * TN], pb[bank][:, 0:TN],
                xo_sb[i % 2][:, th * TN:(th + 1) * TN]),
                waits=[("pe_o", pe_o_n), xo_w,
                       ("s_o1%s" % "ab"[i % 2], rec_o1st[i - 2] if i >= 2 else 0),
                       ("pe_o2", rec_o1pe.get(i - 2, 0)),
                       ("act_sq2", max(0, i - 1))],
                inc="dve_o1")
        d_o1 = w.at("dve_o1")
        w.op("scalar", lambda E, i=i: E.activation(
            sq2_sb[i % 2][:, :], o1_sb[i % 2][:, :], AF.Square),
            waits=[("dve_o1", d_o1), ("pe_o2", max(0, i - 1))], inc="act_sq2")
        a_sq2 = w.at("act_sq2")
        for th in range(TH):
            sw = [("dve_o1", d_o1)]
            if i == 0:
                sw.append(("dve2", P2_DVE))
            w.op("tensor", lambda E, i=i, th=th: E.matmul(
                pb[4 + th][0:1, 0:TN], onesp_f[:, :],
                o1_sb[i % 2][:, th * TN:(th + 1) * TN],
                start=(i == 0), stop=(i == CH - 1)),
                waits=sw)
            w.op("tensor", lambda E, i=i, th=th: E.matmul(
                pb[6 + th][0:1, 0:TN], ones_bf[:, :],
                sq2_sb[i % 2][:, th * TN:(th + 1) * TN],
                start=(i == 0), stop=(i == CH - 1)),
                waits=[("act_sq2", a_sq2)],
                inc=("pe_o2" if th == TH - 1 else None))
        rec_o1pe[i] = w.at("pe_o2")
        stn = w.op("sync", lambda E, i=i: E.dma_start(
            out=out1_scr[i * 128:(i + 1) * 128, :], in_=o1_sb[i % 2][:, :]),
            waits=[("dve_o1", d_o1)], inc="s_o1%s" % "ab"[i % 2], inck=16)
        rec_o1st.append(stn)
        rec_wofree[i % 2] = w.at("pe_o")
    P3_PEO2 = w.at("pe_o2")
    P3_O1A, P3_O1B = w.at("s_o1a"), w.at("s_o1b")

    m2_sl = stat2[0:1, 0:TC]
    q2_sl = stat2[0:1, TC:2 * TC]
    r2_sl = stat2[0:1, 2 * TC:3 * TC]
    t2_sl = stat2[0:1, 3 * TC:4 * TC]
    for th in range(TH):
        w.op("scalar", lambda E, th=th: E.mul(
            m2_sl[0:1, th * TN:(th + 1) * TN], pb[4 + th][0:1, 0:TN], 1.0 / D),
            waits=[("pe_o2", P3_PEO2)])
        w.op("scalar", lambda E, th=th: E.mul(
            q2_sl[0:1, th * TN:(th + 1) * TN], pb[6 + th][0:1, 0:TN], 1.0 / D),
            inc=("act_sq2" if th == TH - 1 else None))
    A_MV2 = w.at("act_sq2")
    w.op("vector", lambda E: E.tensor_mul(t2_sl, m2_sl, m2_sl),
         waits=[("act_sq2", A_MV2)], inc="dve_x2")
    w.op("vector", lambda E: E.tensor_sub(r2_sl, q2_sl, t2_sl),
         waits=[("dve_x2", w.at("dve_x2"))], inc="dve_x2")
    w.op("vector", lambda E: E.tensor_scalar_add(t2_sl, r2_sl, 1e-5),
         waits=[("dve_x2", w.at("dve_x2"))], inc="dve_x2")
    D_MV2 = w.at("dve_x2")
    w.op("scalar", lambda E: E.activation(q2_sl, t2_sl, AF.Ln),
         waits=[("dve_x2", D_MV2)], inc="act_sq2")
    w.op("scalar", lambda E: E.activation(r2_sl, q2_sl, AF.Exp, scale=-0.5),
         waits=[("act_sq2", w.at("act_sq2"))], inc="act_sq2")
    A_MV2 = w.at("act_sq2")
    for th in range(TH):
        w.op("tensor", lambda E, th=th: E.matmul(
            pb[th][:, 0:TN], ones_f[:, :], m2_sl[0:1, th * TN:(th + 1) * TN],
            start=True, stop=True),
            waits=[("act_sq2", A_MV2), ("dve_o1", TH * CH)])
        w.op("tensor", lambda E, th=th: E.matmul(
            pb[2 + th][:, 0:TN], ones_f[:, :], r2_sl[0:1, th * TN:(th + 1) * TN],
            start=True, stop=True),
            inc=("pe_o" if th == TH - 1 else None))
    PE_BC2 = w.at("pe_o")
    for th in range(TH):
        w.op("scalar", lambda E, th=th: E.copy(
            mb2_sb[:, th * TN:(th + 1) * TN], pb[th][:, 0:TN]),
            waits=[("pe_o", PE_BC2)])
        w.op("scalar", lambda E, th=th: E.copy(
            rb2_sb[:, th * TN:(th + 1) * TN], pb[2 + th][:, 0:TN]),
            inc=("act_sq2" if th == TH - 1 else None))
    A_BC2 = w.at("act_sq2")

    X2B = w.at("dve_x2")
    for i in range(CH):
        w.op("sync", lambda E, i=i: E.dma_start(
            out=o1r_sb[i % 2][:, :], in_=out1_scr[i * 128:(i + 1) * 128, :]),
            waits=[("s_o1a", P3_O1A), ("s_o1b", P3_O1B),
                   ("dve_x2", max(0, X2B + 2 * (i - 1)))],
            inc="s_or%d" % (i % 2), inck=16)
        sld = w.at("s_or%d" % (i % 2))
        w.op("vector", lambda E, i=i: E.tensor_sub(
            t2_sb[:, :], o1r_sb[i % 2][:, :], mb2_sb[:, :]),
            waits=[("s_or%d" % (i % 2), sld), ("act_sq2", A_BC2),
                   ("dve_x2", X2B + 2 * i)],
            inc="dve_x2")
        w.op("vector", lambda E, i=i: E.tensor_mul(
            xs2_sb[:, i * TC:(i + 1) * TC], t2_sb[:, :], rb2_sb[:, :]),
            waits=[("dve_x2", X2B + 2 * i + 1)],
            inc="dve_x2")
    P3_X2 = w.at("dve_x2")

    P4_FENCE = [("pe_o2", P3_PEO2), ("dve_x2", P3_X2), ("pe_o", PE_BC2),
                ("act_sq2", A_BC2), ("s_o1a", P3_O1A), ("s_o1b", P3_O1B)]
    WG_FENCE = [("pe_o", PE_BC2), ("dve_o1", TH * CH), ("act_sq2", A_BC2)]

    def xs2_ready(ci):
        return X2B + 2 * ci + 2

    rd_srcs = [out1_scr, da_scr, db_scr]
    st_dsts = [da_scr, db_scr, None]
    st_name = ["s_da", "s_db", "s_ou"]
    rec_dd = [0, 0]
    rec_fwg = [0, 0]
    rec_wdg = [0, 0]
    rec_dast = {}
    prev_tot = None
    gfc = 0
    for sw in range(SW):
        NF = FCL[sw]
        fb = sum(FCL[:sw])
        for fc in range(NF):
            w.op("gpsimd", lambda E, gfc=gfc: E.dma_start(
                out=wg_sb[gfc % 2][:, :], in_=wg[gfc, :, :]),
                waits=WG_FENCE + [("pe_f", max(0, 2 * TH * (gfc - 1)))],
                inc="g_fw%d" % (gfc % 2), inck=16)
            w.op("gpsimd", lambda E, gfc=gfc: E.dma_start(
                out=wu_sb[gfc % 2][:, :], in_=wu[gfc, :, :]),
                inc="g_fw%d" % (gfc % 2), inck=16)
            rec_fwg[gfc % 2] = w.at("g_fw%d" % (gfc % 2))
            base = 4 * (gfc % 2)
            pf0 = w.at("pe_f")
            for role in range(2):
                wsb = wg_sb[gfc % 2] if role == 0 else wu_sb[gfc % 2]
                for th in range(TH):
                    bank = base + 2 * role + th
                    for ci in range(CH):
                        fw = [("g_fw%d" % (gfc % 2), rec_fwg[gfc % 2]),
                              ("dve_x2", xs2_ready(ci)),
                              ("act_sl", max(0, TH * (gfc - 1))),
                              ("dve_h", max(0, TH * (gfc - 1))),
                              ("dve_d", rec_dd[0] if fc == 0 else 0)]
                        w.op("tensor", lambda E, ci=ci, th=th, bank=bank,
                             wsb=wsb: E.matmul(
                            pb[bank][:, 0:TN],
                            wsb[:, ci * 128:(ci + 1) * 128],
                            xs2_sb[:, ci * TC + th * TN: ci * TC + (th + 1) * TN],
                            start=(ci == 0), stop=(ci == CH - 1)),
                            waits=fw,
                            inc=("pe_f" if ci == CH - 1 else None))
            for th in range(TH):
                w.op("scalar", lambda E, th=th, base=base, gfc=gfc: E.activation(
                    sg_sb[gfc % 2][:, th * TN:(th + 1) * TN],
                    pb[base + th][:, 0:TN], AF.Silu),
                    waits=[("pe_f", pf0 + th + 1),
                           ("dve_h", max(0, TH * (gfc - 2) + TH))],
                    inc="act_sl")
            asl = w.at("act_sl")
            for th in range(TH):
                w.op("vector", lambda E, th=th, base=base, gfc=gfc, fc=fc: E.tensor_mul(
                    hA_sb[:, fc * TC + th * TN: fc * TC + (th + 1) * TN],
                    pb[base + 2 + th][:, 0:TN],
                    sg_sb[gfc % 2][:, th * TN:(th + 1) * TN]),
                    waits=[("pe_f", pf0 + TH + th + 1), ("act_sl", asl),
                           ("pe_d", rec_dd[1] if (sw >= 1 and fc < 2) else 0)],
                    inc="dve_h")
            gfc += 1
        DVE_H_S = w.at("dve_h")

        for dmc in range(CH):
            dg = sw * CH + dmc
            wdw = [("pe_d", max(0, TH * (dg - 1)))]
            if dg < 2:
                wdw.append(("pe_o", PE_BC2))
            stn = w.op("gpsimd", lambda E, dmc=dmc, fb=fb, NF=NF, dg=dg: E.dma_start(
                out=wd_sb[dg % 2][:, 0:NF * 128],
                in_=wd[dmc, :, fb * 128:(fb + NF) * 128]),
                waits=wdw,
                inc="g_wd%d" % (dg % 2), inck=16)
            rec_wdg[dg % 2] = stn
            if sw == 0:
                w.op("sync", lambda E, dmc=dmc, dg=dg: E.dma_start(
                    out=o2r_sb[dg % 2][:, :],
                    in_=rd_srcs[0][dmc * 128:(dmc + 1) * 128, :]),
                    waits=P4_FENCE + [("dve_d", max(0, TH * (dg - 1)))],
                    inc="s_r2%d" % (dg % 2), inck=16)
            else:
                w.op("sync", lambda E, dmc=dmc, dg=dg, sw=sw: E.dma_start(
                    out=o2r_sb[dg % 2][:, :],
                    in_=rd_srcs[sw][dmc * 128:(dmc + 1) * 128, :]),
                    waits=[(st_name[sw - 1] + "0", prev_tot[0]),
                           (st_name[sw - 1] + "1", prev_tot[1]),
                           ("dve_d", max(0, TH * (dg - 1)))],
                    inc="s_r2%d" % (dg % 2), inck=16)
            sld = w.at("s_r2%d" % (dg % 2))
            for th in range(TH):
                bank = (2 * dg + th) % 4 if TH > 1 else dg % 4
                for fc in range(NF):
                    w.op("tensor", lambda E, fc=fc, th=th, bank=bank, dg=dg,
                         NF=NF: E.matmul(
                        pb[bank][:, 0:TN],
                        wd_sb[dg % 2][:, fc * 128:(fc + 1) * 128],
                        hA_sb[:, fc * TC + th * TN: fc * TC + (th + 1) * TN],
                        start=(fc == 0), stop=(fc == NF - 1)),
                        waits=[("g_wd%d" % (dg % 2), rec_wdg[dg % 2]),
                               ("dve_h", DVE_H_S),
                               ("dve_d", max(0, TH * (dg - 2) + th + 1))],
                        inc=("pe_d" if fc == NF - 1 else None))
                pd = w.at("pe_d")
                ds_sem, ds_cnt = rec_dast.get(dg - 2, ("s_da0", 0))
                w.op("vector", lambda E, th=th, bank=bank, dg=dg: E.tensor_add(
                    dat_sb[dg % 2][:, th * TN:(th + 1) * TN], pb[bank][:, 0:TN],
                    o2r_sb[dg % 2][:, th * TN:(th + 1) * TN]),
                    waits=[("pe_d", pd), ("s_r2%d" % (dg % 2), sld),
                           (ds_sem, ds_cnt)],
                    inc="dve_d")
            dd = w.at("dve_d")
            sname = "%s%d" % (st_name[sw], dg % 2)
            if sw < SW - 1:
                stn = w.op("sync", lambda E, dmc=dmc, dg=dg, sw=sw: E.dma_start(
                    out=st_dsts[sw][dmc * 128:(dmc + 1) * 128, :],
                    in_=dat_sb[dg % 2][:, :]),
                    waits=[("dve_d", dd)], inc=sname, inck=16)
            else:
                stn = w.op("sync", lambda E, dmc=dmc, dg=dg: E.dma_start(
                    out=out_fm[dmc * 128:(dmc + 1) * 128, :],
                    in_=dat_sb[dg % 2][:, :]),
                    waits=[("dve_d", dd)], inc=sname, inck=16)
            rec_dast[dg] = (sname, stn)
        rec_dd = [w.at("dve_d"), w.at("pe_d")]
        prev_tot = [w.at(st_name[sw] + "0"), w.at(st_name[sw] + "1")]

    w.op("sync", lambda E: E.nop(),
         waits=[("s_ou0", w.at("s_ou0")), ("s_ou1", w.at("s_ou1"))])

    with nc.Block() as block:
        @block.tensor
        def _(E):
            w.emit("tensor", E, sems)

        @block.scalar
        def _(E):
            w.emit("scalar", E, sems)

        @block.vector
        def _(E):
            w.emit("vector", E, sems)

        @block.sync
        def _(E):
            w.emit("sync", E, sems)

        @block.gpsimd
        def _(E):
            w.emit("gpsimd", E, sems)

    stack.pop_all()
    return nc


def prep_inputs(cfg, x, w_q, w_k, w_v, w_o, w_gate, w_up, w_down):
    B, S, D, HKV, G, HD, F = (cfg[k] for k in ("B", "S", "D", "HKV", "G", "HD", "F"))
    CH, TCH, NCHUNK, DIAG, QB = (cfg[k] for k in ("CH", "TCH", "NCHUNK", "DIAG", "QB"))
    UNITS, DQL, WC, TC, FC = (cfg[k] for k in ("UNITS", "DQL", "WC", "TC", "FC"))

    f32 = np.float32
    x = np.asarray(x, dtype=f32)
    w_q = np.asarray(w_q, dtype=f32)
    w_k = np.asarray(w_k, dtype=f32)
    w_v = np.asarray(w_v, dtype=f32)
    w_o = np.asarray(w_o, dtype=f32)
    w_gate = np.asarray(w_gate, dtype=f32)
    w_up = np.asarray(w_up, dtype=f32)
    w_down = np.asarray(w_down, dtype=f32)

    wo_r = np.ascontiguousarray(
        w_o.reshape(CH, 128, CH, 128).transpose(2, 1, 0, 3).reshape(CH, 128, CH * 128))
    wgT = w_gate.T
    wuT = w_up.T
    wdT = w_down.T
    wg_r = np.ascontiguousarray(
        wgT.reshape(CH, 128, FC, 128).transpose(2, 1, 0, 3).reshape(FC, 128, CH * 128))
    wu_r = np.ascontiguousarray(
        wuT.reshape(CH, 128, FC, 128).transpose(2, 1, 0, 3).reshape(FC, 128, CH * 128))
    wd_r = np.ascontiguousarray(
        wdT.reshape(FC, 128, CH, 128).transpose(2, 1, 0, 3).reshape(CH, 128, FC * 128))

    masks = np.zeros((128, DIAG * QB), dtype=ml_dtypes.bfloat16)
    for r in range(DIAG):
        for p in range(128):
            q0 = r * 128 + p
            if q0 < QB:
                masks[p, r * QB + q0:(r + 1) * QB] = 1.0

    xseq_b = []
    for b in range(B):
        xb = np.ascontiguousarray(x[b].T)
        xr = np.ascontiguousarray(
            xb.reshape(CH, 128, NCHUNK, TCH).transpose(2, 1, 0, 3)
            .reshape(NCHUNK, 128, CH * TCH))
        xseq_b.append(xr)

    in_maps = []
    for c in range(8):
        b, j = c // 4, c % 4
        wq_sl = np.ascontiguousarray(
            w_q[:, j * DQL:(j + 1) * DQL].reshape(CH, 128, DQL)
            .transpose(1, 0, 2).reshape(128, CH * DQL))
        h0 = UNITS * j
        kcols = w_k[:, h0 * 128:(h0 + UNITS) * 128]
        vcols = w_v[:, h0 * 128:(h0 + UNITS) * 128]
        wkv_sl = np.concatenate([kcols, vcols], axis=1)
        wkv_sl = np.ascontiguousarray(
            wkv_sl.reshape(CH, 128, WC).transpose(1, 0, 2).reshape(128, CH * WC))
        xown = np.ascontiguousarray(x[b, j * TC:(j + 1) * TC, :].T)
        selv = np.array([[(2 * j * 8 * DQL + 4 * b * DQL) * QB]], dtype=np.int32)
        in_maps.append({
            "xseq": xseq_b[b], "xown": xown, "wq": wq_sl, "wkv": wkv_sl,
            "wo": wo_r, "wg": wg_r, "wu": wu_r, "wd": wd_r,
            "masks": masks, "sel": selv,
        })
    return in_maps


def assemble_output(cfg, results):
    B, S, D, TC = cfg["B"], cfg["S"], cfg["D"], cfg["TC"]
    out = np.empty((B, S, D), dtype=np.float32)
    for c in range(8):
        b, j = c // 4, c % 4
        out[b, j * TC:(j + 1) * TC, :] = results[c]["out_fm"].T
    return out


def kernel(L0_attn_hidden_states, L0_attn_w_q, L0_attn_w_k, L0_attn_w_v,
           L0_attn_w_o, L0_ffn_W_gate, L0_ffn_W_up, L0_ffn_W_down):
    cfg = make_cfg(small=False)
    in_maps = prep_inputs(cfg, L0_attn_hidden_states, L0_attn_w_q, L0_attn_w_k,
                          L0_attn_w_v, L0_attn_w_o, L0_ffn_W_gate, L0_ffn_W_up,
                          L0_ffn_W_down)
    nc = build_program(cfg)
    res = run_bass_kernel_spmd(nc, in_maps, core_ids=list(range(8)))
    return assemble_output(cfg, res.results)


# version 11
# speedup vs baseline: 1.5277x; 1.0417x over previous
import numpy as np
import ml_dtypes
import concourse.bass as bass
import concourse.mybir as mybir
from concourse.bass_utils import run_bass_kernel_spmd

F32 = mybir.dt.float32
BF16 = mybir.dt.bfloat16
U8 = mybir.dt.uint8
I32 = mybir.dt.int32
AF = mybir.ActivationFunctionType
ALU = mybir.AluOpType

ENGS = ("tensor", "scalar", "vector", "sync", "gpsimd")


def make_cfg(small=False):
    if small:
        c = dict(B=2, S=1024, D=1024, HKV=8, G=1, HD=128, F=1536)
    else:
        c = dict(B=2, S=4096, D=4096, HKV=8, G=4, HD=128, F=11008)
    c["CH"] = c["D"] // 128
    c["QB"] = c["S"] // 8
    c["TCH"] = c["QB"]
    c["NCHUNK"] = c["S"] // c["TCH"]
    c["DIAG"] = c["QB"] // 128
    c["UNITS"] = c["B"] * c["HKV"] // 8
    c["DQL"] = c["UNITS"] * c["G"] * 128
    c["QNC"] = c["DQL"] // 128
    c["WC"] = c["UNITS"] * 128 * 2
    c["TC"] = c["B"] * c["S"] // 8
    c["TH"] = max(1, c["TC"] // 512)
    c["TN"] = c["TC"] // c["TH"]
    c["FC"] = c["F"] // 128
    c["SW"] = 3
    base = c["FC"] // c["SW"]
    rem = c["FC"] - base * c["SW"]
    c["FCL"] = [base + (1 if i < rem else 0) for i in range(c["SW"])]
    c["FCMAX"] = max(c["FCL"])
    c["SCALE"] = float(c["HD"]) ** -0.5
    return c


class Walk:

    def __init__(self):
        self.ops = {e: [] for e in ENGS}
        self.cnt = {}

    def at(self, sem):
        return self.cnt.get(sem, 0)

    def op(self, eng, emit, waits=(), inc=None, inck=1):
        ws = tuple((s, v) for (s, v) in waits if v > 0)
        self.ops[eng].append((emit, ws, inc, inck))
        if inc is not None:
            self.cnt[inc] = self.cnt.get(inc, 0) + inck
        return self.cnt.get(inc, 0) if inc else None

    def emit(self, eng_name, eng, sems):
        last = {}
        for (emit, waits, inc, inck) in self.ops[eng_name]:
            for (s, v) in waits:
                if last.get(s, -1) >= v:
                    continue
                eng.wait_ge(sems[s], v)
                last[s] = v
            ins = emit(eng)
            if inc is not None:
                ins.then_inc(sems[inc], inck)


def build_program(cfg):
    B, S, D, HKV, G, HD, F = (cfg[k] for k in ("B", "S", "D", "HKV", "G", "HD", "F"))
    CH, QB, TCH, NCHUNK, DIAG = (cfg[k] for k in ("CH", "QB", "TCH", "NCHUNK", "DIAG"))
    UNITS, DQL, QNC, WC = (cfg[k] for k in ("UNITS", "DQL", "QNC", "WC"))
    TC, TH, TN, FC = (cfg[k] for k in ("TC", "TH", "TN", "FC"))
    SW, FCL, FCMAX = cfg["SW"], cfg["FCL"], cfg["FCMAX"]
    SCALE = cfg["SCALE"]
    VSUB = TCH // 128

    from contextlib import ExitStack
    stack = ExitStack()
    nc = bass.Bass("TRN2", target_bir_lowering=False)

    xseq = nc.dram_tensor("xseq", [NCHUNK, 128, CH * TCH], F32, kind="ExternalInput").ap()
    xown = nc.dram_tensor("xown", [D, TC], F32, kind="ExternalInput").ap()
    wq = nc.dram_tensor("wq", [128, CH * DQL], F32, kind="ExternalInput").ap()
    wkv = nc.dram_tensor("wkv", [128, CH * WC], F32, kind="ExternalInput").ap()
    wo = nc.dram_tensor("wo", [CH, 128, CH * 128], F32, kind="ExternalInput").ap()
    wg = nc.dram_tensor("wg", [FC, 128, CH * 128], F32, kind="ExternalInput").ap()
    wu = nc.dram_tensor("wu", [FC, 128, CH * 128], F32, kind="ExternalInput").ap()
    wd = nc.dram_tensor("wd", [CH, 128, FC * 128], F32, kind="ExternalInput").ap()
    masks_in = nc.dram_tensor("masks", [128, DIAG * QB], BF16, kind="ExternalInput").ap()
    sel = nc.dram_tensor("sel", [1, 1], I32, kind="ExternalInput").ap()
    out_fm = nc.dram_tensor("out_fm", [D, TC], F32, kind="ExternalOutput").ap()

    q_scr = nc.dram_tensor("q_scr", [DQL, S], BF16).ap()
    k_scr = nc.dram_tensor("k_scr", [UNITS, 128, S], BF16).ap()
    v_scr = nc.dram_tensor("v_scr", [UNITS, 128, S], BF16).ap()
    attn_scr_h = nc.dram_tensor("attn_scr", [8, DQL, QB], BF16)
    gath_scr_h = nc.dram_tensor("gath_scr", [8, 8 * DQL, QB], BF16,
                                addr_space="Shared")
    attn_scr = attn_scr_h.ap()
    out1_scr = nc.dram_tensor("out1_scr", [D, TC], F32).ap()
    da_scr = nc.dram_tensor("da_scr", [D, TC], F32).ap()
    db_scr = nc.dram_tensor("db_scr", [D, TC], F32).ap()

    ones_bf = stack.enter_context(nc.sbuf_tensor("ones_bf", [128, 1], BF16))
    onesp_f = stack.enter_context(nc.sbuf_tensor("onesp_f", [128, 1], F32))
    ones_f = stack.enter_context(nc.sbuf_tensor("ones_f", [1, 128], F32))
    sel_sb = stack.enter_context(nc.sbuf_tensor("sel_sb", [1, 1], I32))

    def roundup(x, a=32):
        return (x + a - 1) // a * a

    class Arena:
        def __init__(self):
            cap = nc.sbuf_bytes_remaining - 64
            self.big = stack.enter_context(
                nc.sbuf_tensor("arena", [nc.NUM_PARTITIONS, cap], U8))
            self.base = nc.lookup_mloc(self.big).addr
            self.cap = cap
            self.off = 0
            self.maxoff = 0

        def take(self, name, shape, dtype):
            per = int(np.prod(shape[1:])) * mybir.dt.size(dtype)
            o = self.off
            self.off = roundup(self.off + per)
            self.maxoff = max(self.maxoff, self.off)
            assert self.off <= self.cap, (name, self.off, self.cap)
            return nc.alloc_sbuf_tensor_at(name, list(shape), dtype,
                                           offset=self.base + o)

        def reset(self):
            self.off = 0

    ar = Arena()
    wq_sb = ar.take("wq_sb", [128, CH * DQL], BF16)
    wkv_sb = ar.take("wkv_sb", [128, CH * WC], BF16)
    xc_sb = ar.take("xc_sb", [128, CH * TCH], BF16)
    xs_sb = [ar.take(f"xs{i}", [128, CH * TCH], BF16) for i in range(2)]
    sq_sb = [ar.take(f"sq{i}", [128, TCH], BF16) for i in range(2)]
    EVW = max(TCH, UNITS * 128)
    ev_sb = [ar.take(f"ev{i}", [128, EVW], BF16) for i in range(2)]
    t1_off = ar.off
    t1_sb = ar.take("t1_sb", [128, TCH], BF16)
    mb_sb = ar.take("mb_sb", [128, TCH], BF16)
    rb_sb = ar.take("rb_sb", [128, TCH], BF16)
    st1_off = ar.off
    stat1 = ar.take("stat1", [1, 4 * TCH], F32)
    xacc_sb = nc.alloc_sbuf_tensor_at("xacc", [128, TCH], F32,
                                      offset=ar.base + t1_off)
    sqacc_sb = nc.alloc_sbuf_tensor_at("sqacc", [128, TCH], F32,
                                       offset=ar.base + st1_off)
    ar.reset()
    qt_sb = [ar.take(f"qt{u}", [128, G, S], BF16) for u in range(UNITS)]
    kt_sb = [ar.take(f"kt{u}", [128, S], BF16) for u in range(UNITS)]
    vt_sb = [ar.take(f"vt{u}", [128, S], BF16) for u in range(UNITS)]
    pr_sb = [ar.take(f"pr{i}", [128, 2 * QB], BF16) for i in range(2)]
    pm_sb = [ar.take(f"pm{i}", [128, 2 * QB], BF16) for i in range(2)]
    sp_sb = [ar.take(f"sp{i}", [128, QB], BF16) for i in range(3)]
    pr2_sb = [ar.take(f"pr2{i}", [128, QB], BF16) for i in range(2)]
    ib_sb = [ar.take(f"ib{i}", [128, QB], F32) for i in range(2)]
    aev_sb = [ar.take(f"aev{i}", [128, QB], BF16) for i in range(2)]
    onesmm_sb = ar.take("onesmm", [128, 128], BF16)
    masks_sb = ar.take("masks_sb", [128, DIAG * QB], BF16)
    P2_END = ar.off
    ar.reset()
    xs2_sb = ar.take("xs2_sb", [128, CH * TC], BF16)
    xs2_end = ar.off
    stat2 = ar.take("stat2", [1, 4 * TC], F32)
    o1_sb = [ar.take(f"o1{i}", [128, TC], F32) for i in range(2)]
    sq2_sb = [ar.take(f"sq2{i}", [128, TC], BF16) for i in range(2)]
    m2acc_sb = ar.take("m2acc", [128, TC], F32)
    q2acc_sb = ar.take("q2acc", [128, TC], F32)
    xo1_off = ar.off
    xo_sb = [None, ar.take("xo1", [128, TC], F32)]
    ar.off = xs2_end
    hA_sb = ar.take("hA_sb", [128, FCMAX * TC], BF16)
    HA_END = ar.off
    ar.off = P2_END
    wo_sb = [ar.take(f"wo{i}", [128, CH * 128], BF16) for i in range(2)]
    xo_sb[0] = ar.take("xo0", [128, TC], F32)
    E_END = ar.off
    ar.off = max(P2_END, HA_END)
    wg_sb = [ar.take(f"wgs{i}", [128, CH * 128], BF16) for i in range(2)]
    wu_sb = [ar.take(f"wus{i}", [128, CH * 128], BF16) for i in range(2)]
    wd_sb = [ar.take(f"wds{i}", [128, FCMAX * 128], BF16) for i in range(2)]
    sg_sb = [ar.take(f"sg{i}", [128, TC], BF16) for i in range(2)]
    o2r_sb = [ar.take(f"o2r{i}", [128, TC], F32) for i in range(2)]
    dat_sb = [ar.take(f"dat{i}", [128, TC], F32) for i in range(2)]
    C_END = ar.off
    ar.off = E_END
    AO_OFF = ar.off
    ao_sb = ar.take("ao_sb", [128, CH * TC], BF16)
    AO_END = ar.off
    ao3_sb = nc.alloc_sbuf_tensor_at("ao3", [128, CH, TC], BF16,
                                     offset=ar.base + AO_OFF)
    ar.off = max(C_END, AO_END - 16384)
    o1r_sb = [ar.take(f"o1r{i}", [128, TC], F32) for i in range(2)]
    t2_sb = ar.take("t2_sb", [128, TC], BF16)
    mb2_sb = ar.take("mb2_sb", [128, TC], BF16)
    rb2_sb = ar.take("rb2_sb", [128, TC], BF16)

    sc_ps = [stack.enter_context(nc.psum_tensor(f"sc{i}", [128, 1024], F32))
             for i in range(2)]
    sm_ps = [stack.enter_context(nc.psum_tensor(f"sm{i}", [128, 512], F32))
             for i in range(4)]

    class PBView:
        def __init__(self, h, coff):
            self.h, self.coff = h, coff

        def __getitem__(self, idx):
            r, c = idx
            return self.h[r, c.start + self.coff:c.stop + self.coff]

    pb = [PBView(sc_ps[0], 0), PBView(sc_ps[0], 512),
          PBView(sc_ps[1], 0), PBView(sc_ps[1], 512),
          PBView(sm_ps[0], 0), PBView(sm_ps[1], 0),
          PBView(sm_ps[2], 0), PBView(sm_ps[3], 0)]

    semnames = ["g_w0", "g_w1", "g_w2", "g_w3", "g_wkv", "g_x", "g_ao", "g_wo0", "g_wo1", "g_fw0", "g_fw1",
                "g_wd0", "g_wd1",
                "s_ms", "s_kv", "s_xo0", "s_xo1", "s_or0", "s_or1",
                "s_r20", "s_r21", "s_q", "g_pre",
                "s_o1a", "s_o1b",
                "s_da0", "s_da1", "s_db0", "s_db1", "s_ou0", "s_ou1",
                "a_q0", "a_q1",
                "cc", "g_ms", "dve_ac", "dve_m2",
                "act_sq", "pe_st", "act_mv", "dve_mv", "pe_bc", "act_bc",
                "dve_x", "dve_xs", "pe_mm", "act_ev",
                "pe_s", "act_pr", "pe_av", "dve2",
                "pe_o", "dve_o1", "act_sq2", "pe_o2", "dve_x2",
                "pe_f", "act_sl", "dve_h", "pe_d", "dve_d"]
    sems = {n: stack.enter_context(nc.semaphore(n)) for n in semnames}

    w = Walk()

    w.op("gpsimd", lambda E: E.memset(ones_bf[:, :], 1.0))
    w.op("gpsimd", lambda E: E.memset(onesp_f[:, :], 1.0))
    w.op("gpsimd", lambda E: E.memset(ones_f[:, :], 1.0), inc="g_ms")
    w.op("sync", lambda E: E.dma_start(out=sel_sb[:, :], in_=sel[:, :]),
         inc="s_ms", inck=16)
    NSP = 4

    rec_xsfree = [0, 0]
    rec_ev = []
    evst = {"evg": 0}
    n_groups = QNC + UNITS + VSUB
    NXP = 4
    CPP = CH // NXP
    rec_sqd = {}

    rec_gx0 = {}

    def p1_dma(t):
        rec_gx0[t] = w.at("g_x")
        for p in range(NXP):
            ws = ([("act_sq", t * CH), ("dve_x", t * CH),
                   ("dve_ac", 2 * CH * t)] if p == 0 else [])
            w.op("gpsimd", lambda E, t=t, p=p: E.dma_start(
                out=xc_sb[:, p * CPP * TCH:(p + 1) * CPP * TCH],
                in_=xseq[t, :, p * CPP * TCH:(p + 1) * CPP * TCH]),
                waits=ws, inc="g_x", inck=16)
        if t == 0:
            for i in range(NSP):
                lo, hi = i * (CH // NSP), (i + 1) * (CH // NSP)
                w.op("gpsimd", lambda E, lo=lo, hi=hi: E.dma_start(
                    out=wq_sb[:, lo * DQL:hi * DQL],
                    in_=wq[:, lo * DQL:hi * DQL]),
                    inc="g_w%d" % i, inck=16)
            w.op("gpsimd", lambda E: E.dma_start(out=wkv_sb[:, :], in_=wkv[:, :]),
                 inc="g_wkv", inck=16)

    def p1_acc(t, p):
        need_gx = rec_gx0[t] + 16 * (p + 1)
        for ci in range(p * CPP, (p + 1) * CPP):
            sq_free = (rec_sqd.get((t, ci - 2), 0) if ci >= 2
                       else rec_sqd.get((t - 1, CH + ci - 2), 0))
            w.op("scalar", lambda E, ci=ci: E.activation(
                sq_sb[ci % 2][:, :], xc_sb[:, ci * TCH:(ci + 1) * TCH], AF.Square),
                waits=[("g_x", need_gx), ("dve_ac", sq_free)],
                inc="act_sq")
            sq_n = w.at("act_sq")
            if ci == 0:
                w.op("vector", lambda E: E.tensor_scalar_add(
                    xacc_sb[:, :], xc_sb[:, 0:TCH], 0.0),
                    waits=[("g_x", need_gx), ("pe_st", 2 * t), ("pe_bc", t)],
                    inc="dve_ac")
                w.op("vector", lambda E: E.tensor_scalar_add(
                    sqacc_sb[:, :], sq_sb[0][:, :], 0.0),
                    waits=[("act_sq", sq_n)], inc="dve_ac")
            else:
                w.op("vector", lambda E, ci=ci: E.tensor_add(
                    xacc_sb[:, :], xacc_sb[:, :],
                    xc_sb[:, ci * TCH:(ci + 1) * TCH]),
                    waits=[("g_x", need_gx)], inc="dve_ac")
                w.op("vector", lambda E, ci=ci: E.tensor_add(
                    sqacc_sb[:, :], sqacc_sb[:, :], sq_sb[ci % 2][:, :]),
                    waits=[("act_sq", sq_n)], inc="dve_ac")
            rec_sqd[(t, ci)] = w.at("dve_ac")

    def p1_fin(t):
        dve_ac_n = rec_sqd[(t, CH - 1)]
        w.op("tensor", lambda E: E.matmul(
            pb[6][0:1, 0:TCH], onesp_f[:, :], xacc_sb[:, :],
            start=True, stop=True),
            waits=[("dve_ac", dve_ac_n), ("act_mv", 3 * t), ("g_ms", 1)])
        w.op("tensor", lambda E: E.matmul(
            pb[7][0:1, 0:TCH], onesp_f[:, :], sqacc_sb[:, :],
            start=True, stop=True),
            inc="pe_st", inck=2)
        pe_st_n = w.at("pe_st")

        m_sl = stat1[0:1, 0:TCH]
        q_sl = stat1[0:1, TCH:2 * TCH]
        r_sl = stat1[0:1, 2 * TCH:3 * TCH]
        tp_sl = stat1[0:1, 3 * TCH:4 * TCH]
        w.op("scalar", lambda E: E.mul(m_sl, pb[6][0:1, 0:TCH], 1.0 / D),
             waits=[("pe_st", pe_st_n)])
        w.op("scalar", lambda E: E.mul(q_sl, pb[7][0:1, 0:TCH], 1.0 / D),
             inc="act_mv")
        a_mv = w.at("act_mv")
        w.op("vector", lambda E: E.tensor_mul(tp_sl, m_sl, m_sl),
             waits=[("act_mv", a_mv)], inc="dve_mv")
        w.op("vector", lambda E: E.tensor_sub(r_sl, q_sl, tp_sl),
             waits=[("dve_mv", w.at("dve_mv"))], inc="dve_mv")
        w.op("vector", lambda E: E.tensor_scalar_add(tp_sl, r_sl, 1e-5),
             waits=[("dve_mv", w.at("dve_mv"))], inc="dve_mv")
        d_mv = w.at("dve_mv")
        w.op("scalar", lambda E: E.activation(q_sl, tp_sl, AF.Ln),
             waits=[("dve_mv", d_mv)], inc="act_mv")
        w.op("scalar", lambda E: E.activation(r_sl, q_sl, AF.Exp, scale=-0.5),
             waits=[("act_mv", w.at("act_mv"))], inc="act_mv")
        a_mv = w.at("act_mv")
        w.op("tensor", lambda E: E.matmul(pb[4][:, 0:TCH], ones_f[:, :], m_sl,
                                          start=True, stop=True),
             waits=[("act_mv", a_mv), ("act_bc", 2 * t)])
        w.op("tensor", lambda E: E.matmul(pb[5][:, 0:TCH], ones_f[:, :], r_sl,
                                          start=True, stop=True), inc="pe_bc")
        p_bc = w.at("pe_bc")
        w.op("scalar", lambda E: E.copy(mb_sb[:, :], pb[4][:, 0:TCH]),
             waits=[("pe_bc", p_bc), ("dve_xs", t * CH)])
        w.op("scalar", lambda E: E.copy(rb_sb[:, :], pb[5][:, 0:TCH]),
             inc="act_bc", inck=2)
        a_bc = w.at("act_bc")
        xs = xs_sb[t % 2]
        for ci in range(CH):
            w.op("vector", lambda E, ci=ci: E.tensor_sub(
                t1_sb[:, :], xc_sb[:, ci * TCH:(ci + 1) * TCH], mb_sb[:, :]),
                waits=[("act_bc", a_bc), ("pe_mm", rec_xsfree[t % 2]),
                       ("dve_xs", t * CH + ci)],
                inc="dve_x")
            w.op("vector", lambda E, ci=ci, xs=xs: E.tensor_mul(
                xs[:, ci * TCH:(ci + 1) * TCH], t1_sb[:, :], rb_sb[:, :]),
                waits=[("dve_x", t * CH + ci + 1)],
                inc="dve_xs")

    def p1_group(t, gi, kind, idx):
        xs = xs_sb[t % 2]
        gbase = t * n_groups
        evg = evst["evg"]
        if kind == "q":
            nq = idx
            bank = gi % 2
            for ci in range(CH):
                w.op("tensor", lambda E, ci=ci, nq=nq, bank=bank, xs=xs: E.matmul(
                    pb[bank][:, 0:TCH],
                    wq_sb[:, ci * DQL + nq * 128: ci * DQL + (nq + 1) * 128],
                    xs[:, ci * TCH:(ci + 1) * TCH],
                    start=(ci == 0), stop=(ci == CH - 1)),
                    waits=[("g_w%d" % (ci * NSP // CH), 16),
                           ("dve_xs", t * CH + ci + 1),
                           ("act_ev", gbase + gi - 1)],
                    inc=("pe_mm" if ci == CH - 1 else None))
            mm_n = w.at("pe_mm")
            w.op("scalar", lambda E, bank=bank, evg=evg: E.copy(
                ev_sb[evg % 2][:, 0:TCH], pb[bank][:, 0:TCH]),
                waits=[("pe_mm", mm_n),
                       ("a_q%d" % (evg % 2), rec_ev[evg - 2] if evg >= 2 else 0)],
                inc="act_ev")
            a_ev = w.at("act_ev")
            stn = w.op("scalar", lambda E, nq=nq, t=t, evg=evg: E.dma_start(
                out=q_scr[nq * 128:(nq + 1) * 128, t * TCH:(t + 1) * TCH],
                in_=ev_sb[evg % 2][:, 0:TCH]),
                waits=[("act_ev", a_ev)], inc="a_q%d" % (evg % 2), inck=16)
            rec_ev.append(stn)
        elif kind == "k":
            u = idx
            bank = gi % 2
            for ci in range(CH):
                w.op("tensor", lambda E, ci=ci, u=u, bank=bank, xs=xs: E.matmul(
                    pb[bank][:, 0:TCH],
                    wkv_sb[:, ci * WC + u * 128: ci * WC + (u + 1) * 128],
                    xs[:, ci * TCH:(ci + 1) * TCH],
                    start=(ci == 0), stop=(ci == CH - 1)),
                    waits=[("g_wkv", 16), ("dve_xs", t * CH + ci + 1),
                           ("act_ev", gbase + gi - 1)],
                    inc=("pe_mm" if ci == CH - 1 else None))
            mm_n = w.at("pe_mm")
            w.op("scalar", lambda E, bank=bank, evg=evg: E.copy(
                ev_sb[evg % 2][:, 0:TCH], pb[bank][:, 0:TCH]),
                waits=[("pe_mm", mm_n),
                       ("a_q%d" % (evg % 2), rec_ev[evg - 2] if evg >= 2 else 0)],
                inc="act_ev")
            a_ev = w.at("act_ev")
            stn = w.op("scalar", lambda E, u=u, t=t, evg=evg: E.dma_start(
                out=k_scr[u, :, t * TCH:(t + 1) * TCH], in_=ev_sb[evg % 2][:, 0:TCH]),
                waits=[("act_ev", a_ev)], inc="a_q%d" % (evg % 2), inck=16)
            rec_ev.append(stn)
        else:
            sub = idx
            bank = 2 + sub % 2
            vg = t * VSUB + sub
            for ci in range(CH):
                w.op("tensor", lambda E, ci=ci, sub=sub, bank=bank, xs=xs: E.matmul(
                    pb[bank][:, 0:UNITS * 128],
                    xs[:, ci * TCH + sub * 128: ci * TCH + (sub + 1) * 128],
                    wkv_sb[:, ci * WC + UNITS * 128:(ci + 1) * WC],
                    start=(ci == 0), stop=(ci == CH - 1)),
                    waits=[("g_wkv", 16), ("dve_xs", t * CH + ci + 1),
                           ("act_ev", gbase + gi - 1)],
                    inc=("pe_mm" if ci == CH - 1 else None))
            mm_n = w.at("pe_mm")
            w.op("scalar", lambda E, bank=bank, evg=evg: E.copy(
                ev_sb[evg % 2][:, 0:UNITS * 128], pb[bank][:, 0:UNITS * 128]),
                waits=[("pe_mm", mm_n),
                       ("a_q%d" % (evg % 2), rec_ev[evg - 2] if evg >= 2 else 0)],
                inc="act_ev")
            a_ev = w.at("act_ev")
            for u in range(UNITS):
                stn = w.op("scalar", lambda E, u=u, vg=vg, evg=evg: E.dma_start(
                    out=v_scr[u, :, vg * 128:(vg + 1) * 128],
                    in_=ev_sb[evg % 2][:, u * 128:(u + 1) * 128]),
                    waits=[("act_ev", a_ev)], inc="a_q%d" % (evg % 2), inck=16)
            rec_ev.append(stn)
        evst["evg"] += 1

    groups = ([("q", nq) for nq in range(QNC)] + [("k", u) for u in range(UNITS)]
              + [("v", sub) for sub in range(VSUB)])
    SPLIT = max(2, n_groups // 2)
    aq_chunk = []
    p1_dma(0)
    for p in range(NXP):
        p1_acc(0, p)
    p1_fin(0)
    for t in range(NCHUNK):
        for gi, (kind, idx) in enumerate(groups):
            p1_group(t, gi, kind, idx)
            if t + 1 < NCHUNK:
                if gi == 0:
                    p1_dma(t + 1)
                elif 1 <= gi <= NXP:
                    p1_acc(t + 1, gi - 1)
                elif gi == SPLIT - 1:
                    p1_fin(t + 1)
        rec_xsfree[t % 2] = w.at("pe_mm")
        aq_chunk.append((w.at("a_q0"), w.at("a_q1")))

    P1T = {n: w.at(n) for n in ("a_q0", "a_q1")}
    PE_FINAL = w.at("pe_mm")

    p1waits = [(n, v) for n, v in P1T.items()]
    P1_EV = w.at("act_ev")
    P1_BC = w.at("act_bc")
    P1_MV = w.at("act_mv")
    pmw = [("pe_mm", PE_FINAL)]
    w.op("sync", lambda E: E.dma_start(out=masks_sb[:, :], in_=masks_in[:, :]),
         waits=pmw, inc="s_ms", inck=16)
    S_MS = w.at("s_ms")
    skv_qb = []
    for qb in range(8):
        ws = pmw + [("a_q0", aq_chunk[qb][0]), ("a_q1", aq_chunk[qb][1])]
        for u in range(UNITS):
            w.op("gpsimd", lambda E, u=u, qb=qb: E.dma_start(
                out=qt_sb[u][:, :, qb * QB:(qb + 1) * QB],
                in_=q_scr[u * G * 128:(u + 1) * G * 128,
                          qb * QB:(qb + 1) * QB].rearrange(
                    "(g p) t -> p g t", p=128)),
                waits=(ws if u == 0 else []), inc="s_kv", inck=16)
            w.op("gpsimd", lambda E, u=u, qb=qb: E.dma_start(
                out=kt_sb[u][:, qb * TCH:(qb + 1) * TCH],
                in_=k_scr[u, :, qb * TCH:(qb + 1) * TCH]), inc="s_kv", inck=16)
            w.op("gpsimd", lambda E, u=u, qb=qb: E.dma_start(
                out=vt_sb[u][:, qb * TCH:(qb + 1) * TCH],
                in_=v_scr[u, :, qb * TCH:(qb + 1) * TCH]), inc="s_kv", inck=16)
        skv_qb.append(w.at("s_kv"))
    w.op("gpsimd", lambda E: E.memset(onesmm_sb[:, :], 1.0),
         waits=pmw, inc="g_ms")
    G_MS2 = w.at("g_ms")
    w.op("gpsimd", lambda E: E.dma_start(out=wo_sb[0][:, :], in_=wo[0, :, :]),
         waits=p1waits, inc="g_pre", inck=16)
    w.op("gpsimd", lambda E: E.dma_start(out=wo_sb[1][:, :], in_=wo[1, :, :]),
         inc="g_pre", inck=16)
    w.op("gpsimd", lambda E: E.dma_start(out=xo_sb[0][:, :], in_=xown[0:128, :]),
         inc="g_pre", inck=16)
    G_PRE = w.at("g_pre")

    pairs = []
    pair_gidx = []
    gi = -1
    for qb in range(8):
        nkc = (qb + 1) * DIAG
        assert nkc % 2 == 0
        for u in range(UNITS):
            for g in range(G):
                for pi in range(nkc // 2):
                    if pi == 0:
                        gi += 1
                    pairs.append((qb, u, g, 2 * pi, nkc))
                    pair_gidx.append(gi)
    NPAIR = len(pairs)

    rec_exp, rec_avh, rec_dvp = [], [], []
    rec_recip, rec_aevm, rec_store = {}, {}, {}
    rec_pmav = [0, 0]
    rec_tree = {}
    rec_sulast = {}
    pr2_free = [0, 0]
    stash = {}
    st2 = {"dpi": 0}

    def p2_scores(pi):
        qb, u, g, kc, nkc = pairs[pi]
        b2 = pi % 2
        ws = [("s_kv", skv_qb[qb])]
        if pi >= 2:
            ws.append(("act_pr", rec_exp[pi - 2]))
        else:
            ws.append(("act_ev", P1_EV))
        for h in range(2):
            w.op("tensor", lambda E, u=u, kc=kc, g=g, qb=qb, b2=b2, h=h: E.matmul(
                sc_ps[b2][:, h * QB:(h + 1) * QB],
                kt_sb[u][:, (kc + h) * 128:(kc + h + 1) * 128],
                qt_sb[u][:, g, qb * QB:(qb + 1) * QB],
                start=True, stop=True),
                waits=(ws if h == 0 else []),
                inc=("pe_s" if h == 1 else None))

    def p2_actdve(pi):
        qb, u, g, kc, nkc = pairs[pi]
        b2 = pi % 2
        diag = kc >= qb * DIAG
        ws = [("pe_s", pi + 1)]
        if pi >= 2:
            ws += [("pe_av", rec_avh[pi - 2]), ("dve2", rec_dvp[pi - 2])]
        if diag:
            r = kc - qb * DIAG
            pmb = st2["dpi"] % 2
            st2["dpi"] += 1
            w.op("scalar", lambda E, b2=b2: E.activation(
                pr_sb[b2][:, 0:QB], sc_ps[b2][:, 0:QB], AF.Exp, scale=SCALE),
                waits=ws, inc="act_pr")
            e_lo = w.at("act_pr")
            w.op("vector", lambda E, b2=b2, pmb=pmb, r=r: E.tensor_mul(
                pm_sb[pmb][:, 0:QB], pr_sb[b2][:, 0:QB],
                masks_sb[:, r * QB:(r + 1) * QB]),
                waits=[("act_pr", e_lo), ("s_ms", S_MS),
                       ("pe_av", rec_pmav[pmb])],
                inc="dve2")
            m_lo = w.at("dve2")
            w.op("scalar", lambda E, b2=b2: E.activation(
                pr_sb[b2][:, QB:2 * QB], sc_ps[b2][:, QB:2 * QB],
                AF.Exp, scale=SCALE), inc="act_pr")
            e_hi = w.at("act_pr")
            w.op("vector", lambda E, b2=b2, pmb=pmb, r=r: E.tensor_mul(
                pm_sb[pmb][:, QB:2 * QB], pr_sb[b2][:, QB:2 * QB],
                masks_sb[:, (r + 1) * QB:(r + 2) * QB]),
                waits=[("act_pr", e_hi)], inc="dve2")
            m_hi = w.at("dve2")
            rhs_t = pm_sb[pmb]
            avw = ([("dve2", m_lo)], [("dve2", m_hi)])
        else:
            pmb = None
            w.op("scalar", lambda E, b2=b2: E.activation(
                pr_sb[b2][:, :], sc_ps[b2][:, :], AF.Exp, scale=SCALE),
                waits=ws, inc="act_pr")
            e = w.at("act_pr")
            rhs_t = pr_sb[b2]
            avw = ([("act_pr", e)], [])
        rec_exp.append(w.at("act_pr"))
        paw = [] if diag else [("act_pr", rec_exp[pi])]
        w.op("vector", lambda E, rhs_t=rhs_t, s3=pi % 3: E.tensor_add(
            sp_sb[s3][:, :], rhs_t[:, 0:QB], rhs_t[:, QB:2 * QB]),
            waits=paw, inc="dve2")
        rec_dvp.append(w.at("dve2"))
        j = kc // 2
        if j % 2 == 1:
            gidx = pair_gidx[pi]
            par = gidx % 2
            w.op("vector", lambda E, par=par, a=(pi - 1) % 3, b=pi % 3:
                 E.tensor_add(pr2_sb[par][:, :], sp_sb[a][:, :], sp_sb[b][:, :]),
                 waits=[("pe_av", pr2_free[par])], inc="dve2")
            rec_tree[pi] = w.at("dve2")
        stash[pi] = (rhs_t, avw, pmb)

    def p2_av(pi):
        qb, u, g, kc, nkc = pairs[pi]
        gidx = pair_gidx[pi]
        par = gidx % 2
        rhs_t, avw, pmb = stash.pop(pi)
        for h in range(2):
            ws = list(avw[h])
            if kc == 0 and h == 0:
                ws.append(("dve2", rec_aevm[gidx - 2]) if gidx >= 2
                          else ("act_bc", P1_BC))
            w.op("tensor", lambda E, u=u, kc=kc, h=h, par=par, rhs_t=rhs_t,
                 nkc=nkc: E.matmul(
                sm_ps[par][:, 0:QB],
                vt_sb[u][:, (kc + h) * 128:(kc + h + 1) * 128],
                rhs_t[:, h * QB:(h + 1) * QB],
                start=(kc + h == 0), stop=(kc + h == nkc - 1)),
                waits=ws, inc=("pe_av" if h == 1 else None))
        rec_avh.append(w.at("pe_av"))
        if pmb is not None:
            rec_pmav[pmb] = w.at("pe_av")

    def p2_su(pi):
        qb, u, g, kc, nkc = pairs[pi]
        gidx = pair_gidx[pi]
        par = gidx % 2
        j = kc // 2
        npair = nkc // 2
        ws = [("dve2", rec_tree[pi])]
        if j == 1:
            ws.append(("dve2", rec_recip[gidx - 2]) if gidx >= 2
                      else ("act_mv", P1_MV))
            if gidx < 2:
                ws.append(("g_ms", G_MS2))
        w.op("tensor", lambda E, par=par, j=j, npair=npair: E.matmul(
            sm_ps[2 + par][:, 0:QB], onesmm_sb[:, :], pr2_sb[par][:, :],
            start=(j == 1), stop=(j == npair - 1)),
            waits=ws, inc="pe_av")
        pr2_free[par] = w.at("pe_av")
        rec_sulast[gidx] = w.at("pe_av")

    def p2_groupend(pi):
        qb, u, g, kc, nkc = pairs[pi]
        gidx = pair_gidx[pi]
        par = gidx % 2
        w.op("vector", lambda E, par=par: E.reciprocal(
            ib_sb[par][:, :], sm_ps[2 + par][:, 0:QB]),
            waits=[("pe_av", rec_sulast[gidx])], inc="dve2")
        rec_recip[gidx] = w.at("dve2")
        ws = [("pe_av", rec_avh[pi])]
        if gidx >= 2:
            ws.append(("s_q", rec_store[gidx - 2]))
        w.op("vector", lambda E, par=par: E.tensor_mul(
            aev_sb[par][:, :], sm_ps[par][:, 0:QB], ib_sb[par][:, :]),
            waits=ws, inc="dve2")
        rec_aevm[gidx] = w.at("dve2")
        w.op("sync", lambda E, qb=qb, u=u, g=g, par=par: E.dma_start(
            out=attn_scr[qb, (u * G + g) * 128:(u * G + g + 1) * 128, :],
            in_=aev_sb[par][:, :]),
            waits=[("dve2", rec_aevm[gidx])], inc="s_q", inck=16)
        rec_store[gidx] = w.at("s_q")
        if u == UNITS - 1 and g == G - 1:
            w.op("gpsimd", lambda E, qb=qb: E.collective_compute(
                "AllGather", ALU.bypass, replica_groups=[list(range(8))],
                ins=[attn_scr_h.ap()[qb].opt()],
                outs=[gath_scr_h.ap()[qb].opt()]),
                waits=[("s_q", rec_store[gidx])], inc="cc")

    for pi in range(NPAIR + 3):
        if pi < NPAIR:
            p2_scores(pi)
        if 0 <= pi - 2 < NPAIR:
            p2_av(pi - 2)
        if 0 <= pi - 3 < NPAIR:
            if (pairs[pi - 3][3] // 2) % 2 == 1:
                p2_su(pi - 3)
            if pairs[pi - 3][3] == pairs[pi - 3][4] - 2:
                p2_groupend(pi - 3)
        if pi < NPAIR:
            p2_actdve(pi)
    NG = w.at("cc")
    P2_ACT = w.at("act_pr")
    P2_DVE = w.at("dve2")

    gre = {"base": None, "tmp": None}
    GSPL = 4 if CH % 4 == 0 else 1
    CPS = CH // GSPL

    def p3_gather(E, k, h):
        if gre["base"] is None:
            gre["base"] = E.alloc_register("gbase")
            gre["tmp"] = E.alloc_register("gtmp")
            E.reg_load(gre["base"], sel_sb[0:1, 0:1])
        E.reg_mov(gre["tmp"], h * 8 * DQL * QB + k * CPS * 128 * QB)
        E.reg_add(gre["tmp"], gre["tmp"], gre["base"])
        return E.dma_start(
            out=ao3_sb[:, k * CPS:(k + 1) * CPS, h * QB:(h + 1) * QB],
            in_=bass.AP(gath_scr_h, gre["tmp"],
                        [[QB, 128], [128 * QB, CPS], [1, QB]]))

    P3G_H = []
    for h in range(2):
        for k in range(GSPL):
            w.op("gpsimd", lambda E, k=k, h=h: p3_gather(E, k, h),
                 waits=[("cc", NG - 1 if h == 0 else NG), ("s_ms", S_MS)],
                 inc="g_ao", inck=16)
        P3G_H.append(w.at("g_ao"))
    P3G_LD = w.at("g_ao")

    rec_wofree = [0, 0]
    rec_wog = [0, 0]
    rec_o1st = []
    rec_m2d = {}
    rec_q2d = {}
    for i in range(CH):
        if i >= 2:
            stn = w.op("gpsimd", lambda E, i=i: E.dma_start(
                out=wo_sb[i % 2][:, :], in_=wo[i, :, :]),
                waits=[("pe_o", rec_wofree[i % 2])],
                inc="g_wo%d" % (i % 2), inck=16)
            rec_wog[i % 2] = stn
            wo_w = ("g_wo%d" % (i % 2), rec_wog[i % 2])
        else:
            wo_w = ("g_pre", G_PRE)
        if i >= 1:
            w.op("sync", lambda E, i=i: E.dma_start(
                out=xo_sb[i % 2][:, :], in_=xown[i * 128:(i + 1) * 128, :]),
                waits=[("cc", NG), ("dve_o1", max(0, TH * (i - 1)))],
                inc="s_xo%d" % (i % 2), inck=16)
            xo_w = ("s_xo%d" % (i % 2), w.at("s_xo%d" % (i % 2)))
        else:
            xo_w = ("g_pre", G_PRE)
        for th in range(TH):
            bank = (2 * i + th) % 4 if TH > 1 else i % 4
            for ci in range(CH):
                fw = [wo_w, ("g_ao", P3G_H[th] if TH == 2 else P3G_LD),
                      ("dve_o1", max(0, TH * (i - 2) + th + 1))]
                if i < 2:
                    fw.append(("act_pr", P2_ACT))
                w.op("tensor", lambda E, i=i, th=th, ci=ci, bank=bank: E.matmul(
                    pb[bank][:, 0:TN],
                    wo_sb[i % 2][:, ci * 128:(ci + 1) * 128],
                    ao_sb[:, ci * TC + th * TN: ci * TC + (th + 1) * TN],
                    start=(ci == 0), stop=(ci == CH - 1)),
                    waits=fw,
                    inc=("pe_o" if ci == CH - 1 else None))
            pe_o_n = w.at("pe_o")
            w.op("vector", lambda E, i=i, th=th, bank=bank: E.tensor_add(
                o1_sb[i % 2][:, th * TN:(th + 1) * TN], pb[bank][:, 0:TN],
                xo_sb[i % 2][:, th * TN:(th + 1) * TN]),
                waits=[("pe_o", pe_o_n), xo_w,
                       ("s_o1%s" % "ab"[i % 2], rec_o1st[i - 2] if i >= 2 else 0),
                       ("dve_m2", rec_m2d.get(i - 2, 0)),
                       ("act_sq2", max(0, i - 1))],
                inc="dve_o1")
        d_o1 = w.at("dve_o1")
        w.op("scalar", lambda E, i=i: E.activation(
            sq2_sb[i % 2][:, :], o1_sb[i % 2][:, :], AF.Square),
            waits=[("dve_o1", d_o1), ("dve_m2", rec_q2d.get(i - 2, 0))],
            inc="act_sq2")
        a_sq2 = w.at("act_sq2")
        if i == 0:
            w.op("vector", lambda E: E.tensor_scalar_add(
                m2acc_sb[:, :], o1_sb[0][:, :], 0.0),
                waits=[("dve_o1", d_o1)], inc="dve_m2")
        else:
            w.op("vector", lambda E, i=i: E.tensor_add(
                m2acc_sb[:, :], m2acc_sb[:, :], o1_sb[i % 2][:, :]),
                waits=[("dve_o1", d_o1)], inc="dve_m2")
        rec_m2d[i] = w.at("dve_m2")
        if i == 0:
            w.op("vector", lambda E: E.tensor_scalar_add(
                q2acc_sb[:, :], sq2_sb[0][:, :], 0.0),
                waits=[("act_sq2", a_sq2)], inc="dve_m2")
        else:
            w.op("vector", lambda E, i=i: E.tensor_add(
                q2acc_sb[:, :], q2acc_sb[:, :], sq2_sb[i % 2][:, :]),
                waits=[("act_sq2", a_sq2)], inc="dve_m2")
        rec_q2d[i] = w.at("dve_m2")
        stn = w.op("sync", lambda E, i=i: E.dma_start(
            out=out1_scr[i * 128:(i + 1) * 128, :], in_=o1_sb[i % 2][:, :]),
            waits=[("dve_o1", d_o1)], inc="s_o1%s" % "ab"[i % 2], inck=16)
        rec_o1st.append(stn)
        rec_wofree[i % 2] = w.at("pe_o")
    D_M2 = w.at("dve_m2")
    for th in range(TH):
        w.op("tensor", lambda E, th=th: E.matmul(
            pb[4 + th][0:1, 0:TN], onesp_f[:, :],
            m2acc_sb[:, th * TN:(th + 1) * TN], start=True, stop=True),
            waits=([("dve_m2", D_M2), ("dve2", P2_DVE)] if th == 0 else []))
        w.op("tensor", lambda E, th=th: E.matmul(
            pb[6 + th][0:1, 0:TN], onesp_f[:, :],
            q2acc_sb[:, th * TN:(th + 1) * TN], start=True, stop=True),
            inc=("pe_o2" if th == TH - 1 else None))
    P3_PEO2 = w.at("pe_o2")
    P3_O1A, P3_O1B = w.at("s_o1a"), w.at("s_o1b")

    m2_sl = stat2[0:1, 0:TC]
    q2_sl = stat2[0:1, TC:2 * TC]
    r2_sl = stat2[0:1, 2 * TC:3 * TC]
    t2_sl = stat2[0:1, 3 * TC:4 * TC]
    for th in range(TH):
        w.op("scalar", lambda E, th=th: E.mul(
            m2_sl[0:1, th * TN:(th + 1) * TN], pb[4 + th][0:1, 0:TN], 1.0 / D),
            waits=[("pe_o2", P3_PEO2)])
        w.op("scalar", lambda E, th=th: E.mul(
            q2_sl[0:1, th * TN:(th + 1) * TN], pb[6 + th][0:1, 0:TN], 1.0 / D),
            inc=("act_sq2" if th == TH - 1 else None))
    A_MV2 = w.at("act_sq2")
    w.op("vector", lambda E: E.tensor_mul(t2_sl, m2_sl, m2_sl),
         waits=[("act_sq2", A_MV2)], inc="dve_x2")
    w.op("vector", lambda E: E.tensor_sub(r2_sl, q2_sl, t2_sl),
         waits=[("dve_x2", w.at("dve_x2"))], inc="dve_x2")
    w.op("vector", lambda E: E.tensor_scalar_add(t2_sl, r2_sl, 1e-5),
         waits=[("dve_x2", w.at("dve_x2"))], inc="dve_x2")
    D_MV2 = w.at("dve_x2")
    w.op("scalar", lambda E: E.activation(q2_sl, t2_sl, AF.Ln),
         waits=[("dve_x2", D_MV2)], inc="act_sq2")
    w.op("scalar", lambda E: E.activation(r2_sl, q2_sl, AF.Exp, scale=-0.5),
         waits=[("act_sq2", w.at("act_sq2"))], inc="act_sq2")
    A_MV2 = w.at("act_sq2")
    for th in range(TH):
        w.op("tensor", lambda E, th=th: E.matmul(
            pb[th][:, 0:TN], ones_f[:, :], m2_sl[0:1, th * TN:(th + 1) * TN],
            start=True, stop=True),
            waits=[("act_sq2", A_MV2), ("dve_o1", TH * CH)])
        w.op("tensor", lambda E, th=th: E.matmul(
            pb[2 + th][:, 0:TN], ones_f[:, :], r2_sl[0:1, th * TN:(th + 1) * TN],
            start=True, stop=True),
            inc=("pe_o" if th == TH - 1 else None))
    PE_BC2 = w.at("pe_o")
    for th in range(TH):
        w.op("scalar", lambda E, th=th: E.copy(
            mb2_sb[:, th * TN:(th + 1) * TN], pb[th][:, 0:TN]),
            waits=[("pe_o", PE_BC2)])
        w.op("scalar", lambda E, th=th: E.copy(
            rb2_sb[:, th * TN:(th + 1) * TN], pb[2 + th][:, 0:TN]),
            inc=("act_sq2" if th == TH - 1 else None))
    A_BC2 = w.at("act_sq2")

    X2B = w.at("dve_x2")
    for i in range(CH):
        w.op("scalar" if i % 2 else "sync", lambda E, i=i: E.dma_start(
            out=o1r_sb[i % 2][:, :], in_=out1_scr[i * 128:(i + 1) * 128, :]),
            waits=[("s_o1%s" % "ab"[i % 2], rec_o1st[i]),
                   ("dve_x2", max(0, X2B + 2 * (i - 1)))],
            inc="s_or%d" % (i % 2), inck=16)
        sld = w.at("s_or%d" % (i % 2))
        w.op("vector", lambda E, i=i: E.tensor_sub(
            t2_sb[:, :], o1r_sb[i % 2][:, :], mb2_sb[:, :]),
            waits=[("s_or%d" % (i % 2), sld), ("act_sq2", A_BC2),
                   ("dve_x2", X2B + 2 * i)],
            inc="dve_x2")
        w.op("vector", lambda E, i=i: E.tensor_mul(
            xs2_sb[:, i * TC:(i + 1) * TC], t2_sb[:, :], rb2_sb[:, :]),
            waits=[("dve_x2", X2B + 2 * i + 1)],
            inc="dve_x2")
    P3_X2 = w.at("dve_x2")

    P4_FENCE = [("pe_o2", P3_PEO2), ("dve_x2", P3_X2), ("pe_o", PE_BC2),
                ("act_sq2", A_BC2), ("s_o1a", P3_O1A), ("s_o1b", P3_O1B)]
    WG_FENCE = [("pe_o", PE_BC2), ("dve_o1", TH * CH), ("act_sq2", A_BC2)]

    def xs2_ready(ci):
        return X2B + 2 * ci + 2

    rd_srcs = [out1_scr, da_scr, db_scr]
    st_dsts = [da_scr, db_scr, None]
    st_name = ["s_da", "s_db", "s_ou"]
    rec_dd = [0, 0]
    rec_fwg = [0, 0]
    rec_wdg = [0, 0]
    rec_dast = {}
    prev_tot = None
    gfc = 0
    for sw in range(SW):
        NF = FCL[sw]
        fb = sum(FCL[:sw])
        for fc in range(NF):
            w.op("gpsimd", lambda E, gfc=gfc: E.dma_start(
                out=wg_sb[gfc % 2][:, :], in_=wg[gfc, :, :]),
                waits=WG_FENCE + [("pe_f", max(0, 2 * TH * (gfc - 1)))],
                inc="g_fw%d" % (gfc % 2), inck=16)
            w.op("gpsimd", lambda E, gfc=gfc: E.dma_start(
                out=wu_sb[gfc % 2][:, :], in_=wu[gfc, :, :]),
                inc="g_fw%d" % (gfc % 2), inck=16)
            rec_fwg[gfc % 2] = w.at("g_fw%d" % (gfc % 2))
            base = 4 * (gfc % 2)
            pf0 = w.at("pe_f")
            if gfc == 0:
                order = [(ci, role, th) for ci in range(CH)
                         for role in range(2) for th in range(TH)]
            else:
                order = [(ci, role, th) for role in range(2)
                         for th in range(TH) for ci in range(CH)]
            for (ci, role, th) in order:
                wsb = wg_sb[gfc % 2] if role == 0 else wu_sb[gfc % 2]
                bank = base + 2 * role + th
                fw = [("g_fw%d" % (gfc % 2), rec_fwg[gfc % 2]),
                      ("dve_x2", xs2_ready(ci)),
                      ("act_sl", max(0, TH * (gfc - 1))),
                      ("dve_h", max(0, TH * (gfc - 1))),
                      ("dve_d", rec_dd[0] if fc == 0 else 0)]
                w.op("tensor", lambda E, ci=ci, th=th, bank=bank,
                     wsb=wsb: E.matmul(
                    pb[bank][:, 0:TN],
                    wsb[:, ci * 128:(ci + 1) * 128],
                    xs2_sb[:, ci * TC + th * TN: ci * TC + (th + 1) * TN],
                    start=(ci == 0), stop=(ci == CH - 1)),
                    waits=fw,
                    inc=("pe_f" if ci == CH - 1 else None))
            for th in range(TH):
                w.op("scalar", lambda E, th=th, base=base, gfc=gfc: E.activation(
                    sg_sb[gfc % 2][:, th * TN:(th + 1) * TN],
                    pb[base + th][:, 0:TN], AF.Silu),
                    waits=[("pe_f", pf0 + th + 1),
                           ("dve_h", max(0, TH * (gfc - 2) + TH))],
                    inc="act_sl")
            asl = w.at("act_sl")
            for th in range(TH):
                w.op("vector", lambda E, th=th, base=base, gfc=gfc, fc=fc: E.tensor_mul(
                    hA_sb[:, fc * TC + th * TN: fc * TC + (th + 1) * TN],
                    pb[base + 2 + th][:, 0:TN],
                    sg_sb[gfc % 2][:, th * TN:(th + 1) * TN]),
                    waits=[("pe_f", pf0 + TH + th + 1), ("act_sl", asl),
                           ("pe_d", rec_dd[1] if (sw >= 1 and fc < 2) else 0)],
                    inc="dve_h")
            gfc += 1
        DVE_H_S = w.at("dve_h")

        for dmc in range(CH):
            dg = sw * CH + dmc
            wdw = [("pe_d", max(0, TH * (dg - 1)))]
            if dg < 2:
                wdw.append(("pe_o", PE_BC2))
            stn = w.op("gpsimd", lambda E, dmc=dmc, fb=fb, NF=NF, dg=dg: E.dma_start(
                out=wd_sb[dg % 2][:, 0:NF * 128],
                in_=wd[dmc, :, fb * 128:(fb + NF) * 128]),
                waits=wdw,
                inc="g_wd%d" % (dg % 2), inck=16)
            rec_wdg[dg % 2] = stn
            if sw == 0:
                w.op("sync", lambda E, dmc=dmc, dg=dg: E.dma_start(
                    out=o2r_sb[dg % 2][:, :],
                    in_=rd_srcs[0][dmc * 128:(dmc + 1) * 128, :]),
                    waits=P4_FENCE + [("dve_d", max(0, TH * (dg - 1)))],
                    inc="s_r2%d" % (dg % 2), inck=16)
            else:
                w.op("sync", lambda E, dmc=dmc, dg=dg, sw=sw: E.dma_start(
                    out=o2r_sb[dg % 2][:, :],
                    in_=rd_srcs[sw][dmc * 128:(dmc + 1) * 128, :]),
                    waits=[(st_name[sw - 1] + "0", prev_tot[0]),
                           (st_name[sw - 1] + "1", prev_tot[1]),
                           ("dve_d", max(0, TH * (dg - 1)))],
                    inc="s_r2%d" % (dg % 2), inck=16)
            sld = w.at("s_r2%d" % (dg % 2))
            for th in range(TH):
                bank = (2 * dg + th) % 4 if TH > 1 else dg % 4
                for fc in range(NF):
                    w.op("tensor", lambda E, fc=fc, th=th, bank=bank, dg=dg,
                         NF=NF: E.matmul(
                        pb[bank][:, 0:TN],
                        wd_sb[dg % 2][:, fc * 128:(fc + 1) * 128],
                        hA_sb[:, fc * TC + th * TN: fc * TC + (th + 1) * TN],
                        start=(fc == 0), stop=(fc == NF - 1)),
                        waits=[("g_wd%d" % (dg % 2), rec_wdg[dg % 2]),
                               ("dve_h", DVE_H_S),
                               ("dve_d", max(0, TH * (dg - 2) + th + 1))],
                        inc=("pe_d" if fc == NF - 1 else None))
                pd = w.at("pe_d")
                ds_sem, ds_cnt = rec_dast.get(dg - 2, ("s_da0", 0))
                w.op("vector", lambda E, th=th, bank=bank, dg=dg: E.tensor_add(
                    dat_sb[dg % 2][:, th * TN:(th + 1) * TN], pb[bank][:, 0:TN],
                    o2r_sb[dg % 2][:, th * TN:(th + 1) * TN]),
                    waits=[("pe_d", pd), ("s_r2%d" % (dg % 2), sld),
                           (ds_sem, ds_cnt)],
                    inc="dve_d")
            dd = w.at("dve_d")
            sname = "%s%d" % (st_name[sw], dg % 2)
            if sw < SW - 1:
                stn = w.op("sync", lambda E, dmc=dmc, dg=dg, sw=sw: E.dma_start(
                    out=st_dsts[sw][dmc * 128:(dmc + 1) * 128, :],
                    in_=dat_sb[dg % 2][:, :]),
                    waits=[("dve_d", dd)], inc=sname, inck=16)
            else:
                stn = w.op("sync", lambda E, dmc=dmc, dg=dg: E.dma_start(
                    out=out_fm[dmc * 128:(dmc + 1) * 128, :],
                    in_=dat_sb[dg % 2][:, :]),
                    waits=[("dve_d", dd)], inc=sname, inck=16)
            rec_dast[dg] = (sname, stn)
        rec_dd = [w.at("dve_d"), w.at("pe_d")]
        prev_tot = [w.at(st_name[sw] + "0"), w.at(st_name[sw] + "1")]

    w.op("sync", lambda E: E.nop(),
         waits=[("s_ou0", w.at("s_ou0")), ("s_ou1", w.at("s_ou1"))])

    with nc.Block() as block:
        @block.tensor
        def _(E):
            w.emit("tensor", E, sems)

        @block.scalar
        def _(E):
            w.emit("scalar", E, sems)

        @block.vector
        def _(E):
            w.emit("vector", E, sems)

        @block.sync
        def _(E):
            w.emit("sync", E, sems)

        @block.gpsimd
        def _(E):
            w.emit("gpsimd", E, sems)

    stack.pop_all()
    return nc


def prep_inputs(cfg, x, w_q, w_k, w_v, w_o, w_gate, w_up, w_down):
    B, S, D, HKV, G, HD, F = (cfg[k] for k in ("B", "S", "D", "HKV", "G", "HD", "F"))
    CH, TCH, NCHUNK, DIAG, QB = (cfg[k] for k in ("CH", "TCH", "NCHUNK", "DIAG", "QB"))
    UNITS, DQL, WC, TC, FC = (cfg[k] for k in ("UNITS", "DQL", "WC", "TC", "FC"))

    f32 = np.float32
    x = np.asarray(x, dtype=f32)
    w_q = np.asarray(w_q, dtype=f32)
    w_k = np.asarray(w_k, dtype=f32)
    w_v = np.asarray(w_v, dtype=f32)
    w_o = np.asarray(w_o, dtype=f32)
    w_gate = np.asarray(w_gate, dtype=f32)
    w_up = np.asarray(w_up, dtype=f32)
    w_down = np.asarray(w_down, dtype=f32)

    wo_r = np.ascontiguousarray(
        w_o.reshape(CH, 128, CH, 128).transpose(2, 1, 0, 3).reshape(CH, 128, CH * 128))
    wgT = w_gate.T
    wuT = w_up.T
    wdT = w_down.T
    wg_r = np.ascontiguousarray(
        wgT.reshape(CH, 128, FC, 128).transpose(2, 1, 0, 3).reshape(FC, 128, CH * 128))
    wu_r = np.ascontiguousarray(
        wuT.reshape(CH, 128, FC, 128).transpose(2, 1, 0, 3).reshape(FC, 128, CH * 128))
    wd_r = np.ascontiguousarray(
        wdT.reshape(FC, 128, CH, 128).transpose(2, 1, 0, 3).reshape(CH, 128, FC * 128))

    masks = np.zeros((128, DIAG * QB), dtype=ml_dtypes.bfloat16)
    for r in range(DIAG):
        for p in range(128):
            q0 = r * 128 + p
            if q0 < QB:
                masks[p, r * QB + q0:(r + 1) * QB] = 1.0

    xseq_b = []
    for b in range(B):
        xb = np.ascontiguousarray(x[b].T)
        xr = np.ascontiguousarray(
            xb.reshape(CH, 128, NCHUNK, TCH).transpose(2, 1, 0, 3)
            .reshape(NCHUNK, 128, CH * TCH))
        xseq_b.append(xr)

    in_maps = []
    for c in range(8):
        b, j = c // 4, c % 4
        wq_sl = np.ascontiguousarray(
            w_q[:, j * DQL:(j + 1) * DQL].reshape(CH, 128, DQL)
            .transpose(1, 0, 2).reshape(128, CH * DQL))
        h0 = UNITS * j
        kcols = w_k[:, h0 * 128:(h0 + UNITS) * 128]
        vcols = w_v[:, h0 * 128:(h0 + UNITS) * 128]
        wkv_sl = np.concatenate([kcols, vcols], axis=1)
        wkv_sl = np.ascontiguousarray(
            wkv_sl.reshape(CH, 128, WC).transpose(1, 0, 2).reshape(128, CH * WC))
        xown = np.ascontiguousarray(x[b, j * TC:(j + 1) * TC, :].T)
        selv = np.array([[(2 * j * 8 * DQL + 4 * b * DQL) * QB]], dtype=np.int32)
        in_maps.append({
            "xseq": xseq_b[b], "xown": xown, "wq": wq_sl, "wkv": wkv_sl,
            "wo": wo_r, "wg": wg_r, "wu": wu_r, "wd": wd_r,
            "masks": masks, "sel": selv,
        })
    return in_maps


def assemble_output(cfg, results):
    B, S, D, TC = cfg["B"], cfg["S"], cfg["D"], cfg["TC"]
    out = np.empty((B, S, D), dtype=np.float32)
    for c in range(8):
        b, j = c // 4, c % 4
        out[b, j * TC:(j + 1) * TC, :] = results[c]["out_fm"].T
    return out


def kernel(L0_attn_hidden_states, L0_attn_w_q, L0_attn_w_k, L0_attn_w_v,
           L0_attn_w_o, L0_ffn_W_gate, L0_ffn_W_up, L0_ffn_W_down):
    cfg = make_cfg(small=False)
    in_maps = prep_inputs(cfg, L0_attn_hidden_states, L0_attn_w_q, L0_attn_w_k,
                          L0_attn_w_v, L0_attn_w_o, L0_ffn_W_gate, L0_ffn_W_up,
                          L0_ffn_W_down)
    nc = build_program(cfg)
    res = run_bass_kernel_spmd(nc, in_maps, core_ids=list(range(8)))
    return assemble_output(cfg, res.results)
